# version 13
# baseline (speedup 1.0000x reference)
import contextlib
import types
import numpy as np
import concourse.bass as bass
import concourse.mybir as mybir
from concourse.bass_utils import run_bass_kernel_spmd

F32 = mybir.dt.float32
BF16 = mybir.dt.bfloat16
AF = mybir.ActivationFunctionType
OP = mybir.AluOpType
AXX = mybir.AxisListType.X

PI_LO = float(np.nextafter(np.float32(np.pi), np.float32(0)))
PI8_LO = float(np.nextafter(np.float32(np.pi / 8), np.float32(0)))
ISQHD = float(np.float32(1.0 / np.sqrt(128.0)))
INV32 = float(np.float32(1.0 / 32.0))
LN_EPS = 1e-5

B_LOC, S, D, H, HD, NQ, FF, E = 2, 2048, 1024, 8, 128, 8, 4096, 8
NT = S // 128
NC = D // 128
ENGS = ("tensor", "vector", "scalar", "gpsimd", "sync")


class Sem:
    __slots__ = ("h", "n", "name")

    def __init__(self, h, name):
        self.h, self.n, self.name = h, 0, name


class Prog:
    def __init__(self, nc):
        self.nc = nc
        self.q = {e: [] for e in ENGS}
        self._cms = []
        self.sems = {}

    def sem(self, name):
        if name not in self.sems:
            cm = self.nc.semaphore(name)
            self.sems[name] = Sem(cm.__enter__(), name)
            self._cms.append(cm)
        return self.sems[name]

    @staticmethod
    def _freeze(fn):
        if fn.__closure__:
            cells = tuple(types.CellType(c.cell_contents) for c in fn.__closure__)
            return types.FunctionType(fn.__code__, fn.__globals__, fn.__name__,
                                      fn.__defaults__, cells)
        return fn

    def emit(self, eng, fn, inc=None, incval=1):
        fn = self._freeze(fn)
        if inc is not None:
            inc.n += incval

        def op(h, fn=fn, inc=inc, incval=incval):
            r = fn(h)
            if inc is not None:
                r.then_inc(inc.h, incval)
        self.q[eng].append(op)

    def wait(self, eng, sm, v=None):
        v = sm.n if v is None else v
        self.q[eng].append(lambda h, sm=sm, v=v: h.wait_ge(sm.h, v))

    dma_sems = ()

    def barrier(self, engs=ENGS):
        bar = self.sem("bar_" + "".join(sorted(e[0] + e[1] for e in engs)))
        for e in engs:
            for sm in self.dma_sems:
                self.wait(e, sm)
            self.q[e].append(lambda h: h.drain())
            self.q[e].append(lambda h, b=bar: h.drain().then_inc(b.h, 1))
        bar.n += len(engs)
        tgt = bar.n
        for e in engs:
            self.q[e].append(lambda h, b=bar, t=tgt: h.wait_ge(b.h, t))

    def chain(self, eng, fns, sem, final_inc=True):
        for i, fn in enumerate(fns):
            if i > 0:
                self.wait(eng, sem)
            last = (i == len(fns) - 1)
            self.emit(eng, fn, inc=(sem if (not last or final_inc) else None))

    def replay(self):
        with self.nc.Block() as block:
            @block.tensor
            def _(t):
                for op in self.q["tensor"]:
                    op(t)

            @block.vector
            def _(v):
                for op in self.q["vector"]:
                    op(v)

            @block.scalar
            def _(a):
                for op in self.q["scalar"]:
                    op(a)

            @block.gpsimd
            def _(g):
                for op in self.q["gpsimd"]:
                    op(g)

            @block.sync
            def _(s):
                for op in self.q["sync"]:
                    op(s)

    def close(self):
        for cm in reversed(self._cms):
            cm.__exit__(None, None, None)


def A(handle, offset, dims):
    return bass.AP(handle, int(offset), [list(d) for d in dims])


def build(stage=None, taps=()):
    nc = bass.Bass("TRN2", target_bir_lowering=False)
    P = Prog(nc)
    stack = contextlib.ExitStack()

    def sbuf(name, shape, dtype):
        return stack.enter_context(nc.sbuf_tensor(name, shape, dtype))

    x_in = nc.dram_tensor("x", [B_LOC, S, D], F32, kind="ExternalInput")
    y_out = nc.dram_tensor("y", [B_LOC, S, D], F32, kind="ExternalOutput")
    wshape = {
        "aWq": [H, D, HD], "abq": [H, HD], "aWk": [H, D, HD], "abk": [H, HD],
        "aWv": [H, D, HD], "abv": [H, HD], "aWc2q": [H, HD, NQ], "abc2q": [H, NQ],
        "aWo": [D, D], "abo": [D], "ag1": [D], "abeta1": [D], "ag2": [D], "abeta2": [D],
        "aW1": [D, FF], "ab1": [FF], "aWm": [FF, NQ], "abm": [NQ],
        "aWqo": [NQ, FF], "abqo": [FF], "aW2": [FF, D], "ab2": [D],
        "fWq": [D, D], "fbq": [D], "fWk": [D, D], "fbk": [D],
        "fWv": [D, D], "fbv": [D], "fWo": [D, D], "fbo": [D],
        "fWg": [D, E], "fbg": [E], "fW1": [E, D, FF], "fb1": [E, FF],
        "fW2": [E, FF, D], "fb2": [E, D],
    }
    W = {k: nc.dram_tensor(k, v, F32, kind="ExternalInput") for k, v in wshape.items()}
    sc_xnt = nc.dram_tensor("sc_xnt", [B_LOC, D, S], BF16)
    sc_xn32 = nc.dram_tensor("sc_xn32", [B_LOC, D, S], F32)
    sc_aout = nc.dram_tensor("sc_aout", [B_LOC, S, D], F32)
    sc_xf32 = nc.dram_tensor("sc_xf32", [B_LOC, S, D], F32)
    sc_xft = nc.dram_tensor("sc_xft", [B_LOC, D, S], BF16)
    sc_vtok = nc.dram_tensor("sc_vtok", [B_LOC, 128, NT * D], BF16)

    s_ld = P.sem("ld")
    s_g = P.sem("g")
    s_lh = P.sem("lh")
    s_st = P.sem("st")
    s_t = P.sem("t")
    s_v = P.sem("v")
    s_a = P.sem("a")
    s_lx = P.sem("lx")
    s_lx2 = P.sem("lx2")
    s_lv = P.sem("lv")
    s_lv2 = P.sem("lv2")
    s_lb = P.sem("lb")
    s_pt = P.sem("pt")
    P.dma_sems = (s_ld, s_lh, s_st, s_lx, s_lx2, s_lv, s_lv2, s_lb, s_pt)

    tapset = set(taps)
    tap_req = []

    def tap(name, buf, rows, cols, dtype):
        if name in tapset:
            t = nc.dram_tensor("tap_" + name, [rows, cols], dtype, kind="ExternalOutput")
            tap_req.append((t, buf, rows, cols))

    open_spans = []

    def finish_stage():
        P.barrier()
        for (t, buf, rows, cols) in tap_req:
            P.emit("sync", lambda s, t=t, buf=buf, rows=rows, cols=cols: s.dma_start(
                A(t, 0, [[cols, rows], [1, cols]]),
                A(buf, 0, [[int(buf.shape[1]), rows], [1, cols]])), inc=s_st, incval=16)
        P.wait("sync", s_st)
        P.barrier()
        for sp in reversed(open_spans):
            sp.close()
        stack.close()
        P.replay()
        P.close()
        return nc

    ident_bf = sbuf("ident_bf", [128, 128], BF16)
    ident_f = sbuf("ident_f", [128, 128], F32)
    ones_c = sbuf("ones_c", [128, 1], F32)
    ones_r = sbuf("ones_r", [1, 128], F32)
    iota_r = sbuf("iota_r", [128, 128], F32)
    pidx = sbuf("pidx", [128, 1], F32)

    bq_c = sbuf("bq_c", [128, H], F32)
    bk_c = sbuf("bk_c", [128, H], F32)
    bv_r = sbuf("bv_r", [1, D], F32)
    bc2q_r = sbuf("bc2q_r", [1, H * NQ], F32)
    bc2q_rq = sbuf("bc2q_rq", [1, H * NQ], F32)
    bo_c = sbuf("bo_c", [128, NC], F32)
    g1_c = sbuf("g1_c", [128, NC], F32)
    be1_c = sbuf("be1_c", [128, NC], F32)
    g2_c = sbuf("g2_c", [128, NC], F32)
    be2_c = sbuf("be2_c", [128, NC], F32)
    fbq_c = sbuf("fbq_c", [128, NC], F32)
    fbk_c = sbuf("fbk_c", [128, NC], F32)
    fbv_r = sbuf("fbv_r", [1, D], F32)
    fbo_c = sbuf("fbo_c", [128, NC], F32)
    fbg_c = sbuf("fbg_c", [E, 1], F32)
    b1_c = sbuf("b1_c", [128, E * 32], F32)
    m1_bf = sbuf("m1_bf", [128, NC * NQ], BF16)
    bias1_c = sbuf("bias1_c", [NQ, 1], F32)
    m2_bf = sbuf("m2_bf", [NQ, D], BF16)
    bias2_c = sbuf("bias2_c", [128, NC], F32)
    wc2q_bf = sbuf("wc2q_bf", [128, H * NQ], BF16)
    qqT_h = sbuf("qqT_h", [NQ, S], BF16)
    kqT_h = sbuf("kqT_h", [NQ, S], BF16)
    wg_bf = sbuf("wg_bf", [128, NC * E], BF16)
    gates_t = sbuf("gates_t", [128, B_LOC * NT * E], F32)

    st_a = sbuf("st_a", [128, 32], F32)
    st_b = sbuf("st_b", [128, 32], F32)
    st_c = sbuf("st_c", [128, 32], F32)

    P.emit("gpsimd", lambda g: g.iota(iota_r[:, :], [[1, 128]], channel_multiplier=0,
                                      allow_small_or_imprecise_dtypes=True))
    P.emit("gpsimd", lambda g: g.iota(pidx[:, :], [[1, 1]], channel_multiplier=1,
                                      allow_small_or_imprecise_dtypes=True))
    P.emit("gpsimd", lambda g: g.memset(ones_c[:, :], 1.0))
    P.emit("gpsimd", lambda g: g.memset(ones_r[:, :], 1.0), inc=s_g)

    def ldpar(dst, w, dims, eng="sync", off=0):
        sem = s_lh if eng == "sync" else s_ld
        P.emit(eng, lambda h, dst=dst, w=w, dims=dims, off=off: h.dma_start(
            dst, A(W[w], off, dims), allow_slow_non_contiguous=True), inc=sem, incval=16)

    ldpar(bo_c[:, :], "abo", [[1, 128], [128, NC]])
    ldpar(g1_c[:, :], "ag1", [[1, 128], [128, NC]])
    ldpar(be1_c[:, :], "abeta1", [[1, 128], [128, NC]])
    ldpar(g2_c[:, :], "ag2", [[1, 128], [128, NC]])
    ldpar(be2_c[:, :], "abeta2", [[1, 128], [128, NC]])
    ldpar(fbq_c[:, :], "fbq", [[1, 128], [128, NC]])
    ldpar(fbk_c[:, :], "fbk", [[1, 128], [128, NC]])
    ldpar(fbo_c[:, :], "fbo", [[1, 128], [128, NC]])
    ldpar(bq_c[:, :], "abq", [[1, 128], [128, H]])
    ldpar(bk_c[:, :], "abk", [[1, 128], [128, H]])
    ldpar(bv_r[0:1, :], "abv", [[D, 1], [1, D]])
    ldpar(bc2q_r[0:1, :], "abc2q", [[H * NQ, 1], [1, H * NQ]])
    ldpar(fbg_c[0:E, 0:1], "fbg", [[1, E], [1, 1]])
    ldpar(fbv_r[0:1, :], "fbv", [[D, 1], [1, D]])
    ldpar(b1_c[:, :], "fb1", [[1, 128], [FF, E], [128, 32]])
    ldpar(wc2q_bf[:, :], "aWc2q", [[NQ, 128], [128 * NQ, H], [1, NQ]], eng="gpsimd")
    ldpar(wg_bf[:, :], "fWg", [[E, 128], [128 * E, NC], [1, E]], eng="gpsimd")
    P.barrier()
    P.emit("vector", lambda v: v.tensor_scalar(ident_bf[:, :], iota_r[:, :], pidx[:, 0:1], None, OP.is_equal))
    P.emit("vector", lambda v: v.tensor_scalar(ident_f[:, :], iota_r[:, :], pidx[:, 0:1], None, OP.is_equal))
    P.emit("vector", lambda v: v.tensor_scalar(bc2q_rq[0:1, :], bc2q_r[0:1, :], ISQHD, None, OP.mult), inc=s_v)
    P.barrier()

    with contextlib.ExitStack() as pp:
        w1blk = pp.enter_context(nc.sbuf_tensor("w1blk", [128, 1024], BF16))
        w1t = pp.enter_context(nc.sbuf_tensor("w1t", [128, 1024], BF16))
        wm_bf = pp.enter_context(nc.sbuf_tensor("wm_bf", [128, 32 * NQ], BF16))
        wm_32 = pp.enter_context(nc.sbuf_tensor("wm_32", [128, 32 * NQ], F32))
        b1a_c = pp.enter_context(nc.sbuf_tensor("b1a_c", [128, 32], F32))
        wqo_s = pp.enter_context(nc.sbuf_tensor("wqo_s", [NQ, FF], BF16))
        wqoT = pp.enter_context(nc.sbuf_tensor("wqoT", [128, 32 * NQ], BF16))
        bqo_c = pp.enter_context(nc.sbuf_tensor("bqo_c", [128, 32], F32))
        w2blk = pp.enter_context(nc.sbuf_tensor("w2blk", [128, 1024], BF16))
        w2b32 = pp.enter_context(nc.sbuf_tensor("w2b32", [128, 1024], F32))
        m1row = pp.enter_context(nc.sbuf_tensor("m1row", [NQ, D], BF16))
        brow = pp.enter_context(nc.sbuf_tensor("brow", [1, D + NQ], F32))
        exrow = pp.enter_context(nc.sbuf_tensor("exrow", [1, D], F32))
        with contextlib.ExitStack() as pps:
            p_tr = pps.enter_context(nc.psum_tensor("pp_tr", [128, 1024], BF16))
            p_m1 = pps.enter_context(nc.psum_tensor("pp_m1", [NQ, 1024], F32))
            p_m2 = pps.enter_context(nc.psum_tensor("pp_m2", [NQ, 1024], F32))
            p_bb = pps.enter_context(nc.psum_tensor("pp_bb", [1, 1536], F32))

            ldpar(wm_bf[:, :], "aWm", [[NQ, 128], [128 * NQ, 32], [1, NQ]], eng="gpsimd")
            ldpar(wm_32[:, :], "aWm", [[NQ, 128], [128 * NQ, 32], [1, NQ]])
            ldpar(b1a_c[:, :], "ab1", [[1, 128], [128, 32]])
            ldpar(wqo_s[0:NQ, :], "aWqo", [[FF, NQ], [1, FF]], eng="gpsimd")
            ldpar(bqo_c[:, :], "abqo", [[1, 128], [128, 32]])
            P.barrier()
            for c in range(32):
                P.emit("gpsimd", lambda g, c=c: g.dma_start(
                    w1blk[:, :], A(W["aW1"], c * 128, [[FF, 128], [128 * FF, NC], [1, 128]])),
                    inc=s_ld, incval=16)
                P.wait("tensor", s_ld)
                for f in range(NC):
                    P.emit("tensor", lambda t, f=f: t.transpose(
                        p_tr[:, f * 128:(f + 1) * 128], w1blk[:, f * 128:(f + 1) * 128], ident_bf[:, :]),
                        inc=(s_t if f == NC - 1 else None))
                P.wait("vector", s_t)
                P.emit("vector", lambda v: v.tensor_copy(w1t[:, :], p_tr[:, :]), inc=s_v)
                P.wait("tensor", s_v)
                for hf in range(2):
                    P.emit("tensor", lambda t, c=c, hf=hf: t.matmul(
                        p_m1[0:NQ, hf * 512:(hf + 1) * 512], wm_bf[:, c * NQ:(c + 1) * NQ],
                        w1t[:, hf * 512:(hf + 1) * 512], start=(c == 0), stop=(c == 31)))
                P.emit("tensor", lambda t, c=c: t.matmul(
                    p_bb[0:1, 1024:1024 + NQ], b1a_c[:, c:c + 1], wm_32[:, c * NQ:(c + 1) * NQ],
                    start=(c == 0), stop=(c == 31)))
                P.emit("tensor", lambda t, c=c: t.transpose(
                    p_tr[:, 0:NQ], wqo_s[0:NQ, c * 128:(c + 1) * 128], ident_bf[0:NQ, 0:NQ]), inc=s_t)
                P.wait("vector", s_t)
                P.emit("vector", lambda v, c=c: v.tensor_copy(wqoT[:, c * NQ:(c + 1) * NQ], p_tr[:, 0:NQ]), inc=s_v)
                P.barrier(("tensor", "vector", "gpsimd"))
            for c in range(32):
                P.emit("gpsimd", lambda g, c=c: g.dma_start(
                    w2blk[:, :], A(W["aW2"], c * 128 * D, [[D, 128], [1, D]])), inc=s_ld, incval=16)
                P.emit("sync", lambda s, c=c: s.dma_start(
                    w2b32[:, :], A(W["aW2"], c * 128 * D, [[D, 128], [1, D]])), inc=s_lh, incval=16)
                P.wait("tensor", s_ld)
                P.wait("tensor", s_lh)
                for hf in range(2):
                    P.emit("tensor", lambda t, c=c, hf=hf: t.matmul(
                        p_m2[0:NQ, hf * 512:(hf + 1) * 512], wqoT[:, c * NQ:(c + 1) * NQ],
                        w2blk[:, hf * 512:(hf + 1) * 512], start=(c == 0), stop=(c == 31)))
                for hf in range(2):
                    P.emit("tensor", lambda t, c=c, hf=hf: t.matmul(
                        p_bb[0:1, hf * 512:(hf + 1) * 512], bqo_c[:, c:c + 1],
                        w2b32[:, hf * 512:(hf + 1) * 512], start=(c == 0), stop=(c == 31)),
                        inc=(s_t if hf == 1 else None))
                P.barrier(("tensor", "gpsimd", "sync"))
            P.wait("vector", s_t)
            P.emit("vector", lambda v: v.tensor_copy(brow[0:1, 0:D], p_bb[0:1, 0:D]))
            P.emit("vector", lambda v: v.tensor_copy(brow[0:1, D:D + NQ], p_bb[0:1, 1024:1024 + NQ]))
            P.emit("vector", lambda v: v.tensor_copy(m1row[0:NQ, :], p_m1[0:NQ, :]))
            P.emit("vector", lambda v: v.tensor_copy(m2_bf[0:NQ, :], p_m2[0:NQ, :]), inc=s_v)
            ldpar(exrow[0:1, :], "ab2", [[D, 1], [1, D]])
            P.wait("vector", s_lh)
            P.wait("vector", s_v)
            P.emit("vector", lambda v: v.tensor_tensor(brow[0:1, 0:D], brow[0:1, 0:D], exrow[0:1, :], OP.add), inc=s_v)
            P.wait("sync", s_v)
            ldpar(exrow[0:1, 0:NQ], "abm", [[NQ, 1], [1, NQ]])
            P.wait("vector", s_lh)
            P.emit("vector", lambda v: v.tensor_tensor(brow[0:1, D:D + NQ], brow[0:1, D:D + NQ], exrow[0:1, 0:NQ], OP.add), inc=s_v)
            P.wait("tensor", s_v)
            P.barrier()
        with contextlib.ExitStack() as pps:
            p_tm = pps.enter_context(nc.psum_tensor("pp_tm", [128, NQ], F32))
            p_tmb = pps.enter_context(nc.psum_tensor("pp_tmb", [128, NQ], BF16))
            for c in range(NC):
                P.emit("tensor", lambda t, c=c: t.transpose(
                    p_tm[:, 0:1], brow[0:1, c * 128:(c + 1) * 128], ident_f[0:1, 0:1]), inc=s_t)
                P.wait("vector", s_t)
                P.emit("vector", lambda v, c=c: v.tensor_copy(bias2_c[:, c:c + 1], p_tm[:, 0:1]), inc=s_v)
                P.wait("tensor", s_v)
            P.emit("tensor", lambda t: t.transpose(p_tm[0:NQ, 0:1], brow[0:1, D:D + NQ], ident_f[0:1, 0:1]), inc=s_t)
            P.wait("vector", s_t)
            P.emit("vector", lambda v: v.tensor_copy(bias1_c[0:NQ, 0:1], p_tm[0:NQ, 0:1]), inc=s_v)
            P.wait("tensor", s_v)
            for c in range(NC):
                P.emit("tensor", lambda t, c=c: t.transpose(
                    p_tmb[:, 0:NQ], m1row[0:NQ, c * 128:(c + 1) * 128], ident_bf[0:NQ, 0:NQ]), inc=s_t)
                P.wait("vector", s_t)
                P.emit("vector", lambda v, c=c: v.tensor_copy(m1_bf[:, c * NQ:(c + 1) * NQ], p_tmb[:, 0:NQ]), inc=s_v)
                P.wait("tensor", s_v)
            P.barrier()

    if stage == "prep":
        tap("m1_bf", m1_bf, 128, NC * NQ, BF16)
        tap("m2_bf", m2_bf, NQ, D, BF16)
        tap("bias1_c", bias1_c, NQ, 1, F32)
        tap("bias2_c", bias2_c, 128, NC, F32)
        return finish_stage()

    def ln_fm(BUF, g_col, b_col, p_st, p_bc, sqtmp, rows):
        row_a, row_b, row_c = rows
        for qtr in range(4):
            for c in range(NC):
                P.emit("tensor", lambda t, qtr=qtr, c=c: t.matmul(
                    p_st[0:1, 0:512], ones_c[:, 0:1],
                    BUF[:, c * S + qtr * 512: c * S + qtr * 512 + 512],
                    start=(c == 0), stop=(c == NC - 1)), inc=(s_t if c == NC - 1 else None))
            P.wait("vector", s_t)
            P.emit("vector", lambda v, qtr=qtr: v.tensor_copy(
                row_a[0:1, qtr * 512:(qtr + 1) * 512], p_st[0:1, 0:512]), inc=s_v)
            sq_marks = {}
            for c in range(NC):
                half = c % 2
                P.wait("vector", s_t, sq_marks.get(c - 2, 0))
                P.emit("vector", lambda v, qtr=qtr, c=c, half=half: v.tensor_tensor(
                    sqtmp[:, half * 512:(half + 1) * 512],
                    BUF[:, c * S + qtr * 512: c * S + qtr * 512 + 512],
                    BUF[:, c * S + qtr * 512: c * S + qtr * 512 + 512], OP.mult), inc=s_v)
                P.wait("tensor", s_v)
                P.emit("tensor", lambda t, c=c, half=half: t.matmul(
                    p_st[0:1, 512:1024], ones_c[:, 0:1], sqtmp[:, half * 512:(half + 1) * 512],
                    start=(c == 0), stop=(c == NC - 1)), inc=s_t)
                sq_marks[c] = s_t.n
            P.wait("vector", s_t)
            P.emit("vector", lambda v, qtr=qtr: v.tensor_copy(
                row_b[0:1, qtr * 512:(qtr + 1) * 512], p_st[0:1, 512:1024]), inc=s_v)
        P.wait("vector", s_v)
        P.chain("vector", [
            lambda v: v.tensor_scalar(row_a[0:1, :], row_a[0:1, :], 1.0 / D, None, OP.mult),
            lambda v: v.tensor_scalar(row_b[0:1, :], row_b[0:1, :], 1.0 / D, None, OP.mult),
            lambda v: v.tensor_tensor(row_c[0:1, :], row_a[0:1, :], row_a[0:1, :], OP.mult),
            lambda v: v.tensor_tensor(row_b[0:1, :], row_b[0:1, :], row_c[0:1, :], OP.subtract),
            lambda v: v.tensor_scalar(row_b[0:1, :], row_b[0:1, :], LN_EPS, None, OP.add),
        ], s_v)
        P.wait("scalar", s_v)
        P.emit("scalar", lambda a: a.activation(row_c[0:1, :], row_b[0:1, :], AF.Sqrt), inc=s_a)
        P.wait("vector", s_a)
        P.chain("vector", [
            lambda v: v.reciprocal(row_b[0:1, :], row_c[0:1, :]),
            lambda v: v.tensor_tensor(row_c[0:1, :], row_c[0:1, :], row_b[0:1, :], OP.mult),
            lambda v: v.tensor_scalar(row_c[0:1, :], row_c[0:1, :], -1.0, 2.0, OP.mult, OP.add),
            lambda v: v.tensor_tensor(row_b[0:1, :], row_b[0:1, :], row_c[0:1, :], OP.mult),
            lambda v: v.tensor_tensor(row_a[0:1, :], row_a[0:1, :], row_b[0:1, :], OP.mult),
        ], s_v)
        for qtr in range(4):
            P.wait("tensor", s_v)
            P.emit("tensor", lambda t, qtr=qtr: t.matmul(
                p_bc[:, 0:512], ones_r[0:1, :], row_b[0:1, qtr * 512:(qtr + 1) * 512], start=True, stop=True))
            P.emit("tensor", lambda t, qtr=qtr: t.matmul(
                p_bc[:, 512:1024], ones_r[0:1, :], row_a[0:1, qtr * 512:(qtr + 1) * 512], start=True, stop=True),
                inc=s_t)
            P.wait("vector", s_t)
            for c in range(NC):
                sl = BUF[:, c * S + qtr * 512: c * S + qtr * 512 + 512]
                P.emit("vector", lambda v, sl=sl: v.tensor_tensor(sl, sl, p_bc[:, 0:512], OP.mult),
                       inc=(s_v if c == NC - 1 else None))
            P.wait("vector", s_v)
            for c in range(NC):
                sl = BUF[:, c * S + qtr * 512: c * S + qtr * 512 + 512]
                P.emit("vector", lambda v, sl=sl: v.tensor_tensor(sl, sl, p_bc[:, 512:1024], OP.subtract),
                       inc=(s_v if c == NC - 1 else None))
            if g_col is not None:
                P.wait("vector", s_v)
                for c in range(NC):
                    sl = BUF[:, c * S + qtr * 512: c * S + qtr * 512 + 512]
                    P.emit("vector", lambda v, sl=sl, c=c: v.tensor_scalar(
                        sl, sl, g_col[:, c:c + 1], b_col[:, c:c + 1], OP.mult, OP.add),
                        inc=(s_v if c == NC - 1 else None))
            P.barrier(("tensor", "vector"))

    def spill_tok(BUF, dram, dram_off, p_t1, p_t2, stg):
        for ti in range(NT):
            P.wait("vector", s_st, s_st.n)
            P.wait("tensor", s_v)
            for r in range(2):
                pb = p_t1 if r == 0 else p_t2
                for c in range(4):
                    P.emit("tensor", lambda t, ti=ti, r=r, c=c, pb=pb: t.transpose(
                        pb[:, c * 128:(c + 1) * 128],
                        BUF[:, (4 * r + c) * S + ti * 128: (4 * r + c) * S + ti * 128 + 128],
                        ident_f[:, :]), inc=(s_t if c == 3 else None))
                P.wait("vector", s_t)
                P.emit("vector", lambda v, r=r, pb=pb, stg=stg: v.tensor_copy(
                    stg[:, r * 512:(r + 1) * 512], pb[:, :]), inc=s_v)
            P.wait("sync", s_v)
            P.emit("sync", lambda s, ti=ti, dram=dram, dram_off=dram_off, stg=stg: s.dma_start(
                A(dram, dram_off + ti * 128 * D, [[D, 128], [128, NC], [1, 128]]),
                A(stg, 0, [[D, 128], [128, NC], [1, 128]])), inc=s_st, incval=16)
        P.barrier(("tensor", "vector", "sync"))

    for b in range(B_LOC):
        span_xnt = contextlib.ExitStack()
        open_spans.append(span_xnt)
        span_qkv = contextlib.ExitStack()
        open_spans.append(span_qkv)
        span_at = contextlib.ExitStack()
        open_spans.append(span_at)
        XNT = span_xnt.enter_context(nc.sbuf_tensor(f"XNT_{b}", [128, NC * S], BF16, side="right"))
        with contextlib.ExitStack() as ph:
            xtok = ph.enter_context(nc.sbuf_tensor(f"xtok_{b}", [128, NT * D], F32))
            xn32 = ph.enter_context(nc.sbuf_tensor(f"xn32_{b}", [128, D], F32))
            sqd = ph.enter_context(nc.sbuf_tensor(f"sqd_{b}", [128, 2 * D], F32))
            stg = ph.enter_context(nc.sbuf_tensor(f"stg_{b}", [128, D], F32))
            p_trf = ph.enter_context(nc.psum_tensor(f"p0_trf_{b}", [128, 512], F32))
            p_trf2 = ph.enter_context(nc.psum_tensor(f"p0_trf2_{b}", [128, 512], F32))

            P.emit("sync", lambda s, b=b: s.dma_start(
                xtok[:, :], A(x_in, b * S * D, [[D, 128], [128 * D, NT], [1, D]])),
                inc=s_lh, incval=16)
            P.barrier()
            sq_marks = {}
            for ti in range(NT):
                P.emit("vector", lambda v, ti=ti: v.reduce_sum(
                    st_a[:, ti:ti + 1], xtok[:, ti * D:(ti + 1) * D], axis=AXX), inc=s_v)
                if ti >= 2:
                    P.wait("scalar", s_a, sq_marks[ti - 2])
                P.emit("scalar", lambda a, ti=ti: a.activation(
                    sqd[:, (ti % 2) * D:(ti % 2 + 1) * D], xtok[:, ti * D:(ti + 1) * D], AF.Square,
                    accum_out=st_b[:, ti:ti + 1]), inc=s_a)
                sq_marks[ti] = s_a.n
            P.wait("vector", s_a)
            P.wait("vector", s_v)
            P.chain("vector", [
                lambda v: v.tensor_scalar(st_a[:, 0:NT], st_a[:, 0:NT], 1.0 / D, None, OP.mult),
                lambda v: v.tensor_scalar(st_b[:, 0:NT], st_b[:, 0:NT], 1.0 / D, None, OP.mult),
                lambda v: v.tensor_tensor(st_c[:, 0:NT], st_a[:, 0:NT], st_a[:, 0:NT], OP.mult),
                lambda v: v.tensor_tensor(st_b[:, 0:NT], st_b[:, 0:NT], st_c[:, 0:NT], OP.subtract),
                lambda v: v.tensor_scalar(st_b[:, 0:NT], st_b[:, 0:NT], LN_EPS, None, OP.add),
            ], s_v)
            P.wait("scalar", s_v)
            P.emit("scalar", lambda a: a.activation(st_c[:, 0:NT], st_b[:, 0:NT], AF.Sqrt), inc=s_a)
            P.wait("vector", s_a)
            P.chain("vector", [
                lambda v: v.reciprocal(st_b[:, 0:NT], st_c[:, 0:NT]),
                lambda v: v.tensor_tensor(st_c[:, 0:NT], st_c[:, 0:NT], st_b[:, 0:NT], OP.mult),
                lambda v: v.tensor_scalar(st_c[:, 0:NT], st_c[:, 0:NT], -1.0, 2.0, OP.mult, OP.add),
                lambda v: v.tensor_tensor(st_b[:, 0:NT], st_b[:, 0:NT], st_c[:, 0:NT], OP.mult),
            ], s_v)
            for ti in range(NT):
                P.wait("vector", s_v)
                P.emit("vector", lambda v, ti=ti: v.tensor_scalar(
                    xn32[:, :], xtok[:, ti * D:(ti + 1) * D], st_a[:, ti:ti + 1], st_b[:, ti:ti + 1],
                    OP.subtract, OP.mult), inc=s_v)
                P.wait("tensor", s_v)
                for r in range(2):
                    pb = p_trf if r == 0 else p_trf2
                    for c in range(4):
                        P.emit("tensor", lambda t, r=r, c=c, pb=pb: t.transpose(
                            pb[:, c * 128:(c + 1) * 128],
                            xn32[:, r * 512 + c * 128: r * 512 + (c + 1) * 128], ident_f[:, :]),
                            inc=(s_t if c == 3 else None))
                    P.wait("vector", s_t)
                    P.emit("vector", lambda v, r=r, pb=pb, stg=stg: v.tensor_copy(stg[:, r * 512:(r + 1) * 512], pb[:, :]))
                    P.emit("vector", lambda v, r=r, ti=ti, pb=pb, XNT=XNT: v.tensor_copy(
                        A(XNT, (4 * r) * S + ti * 128, [[NC * S, 128], [S, 4], [1, 128]]),
                        pb[:, :]), inc=s_v)
                P.wait("sync", s_v)
                P.emit("sync", lambda s, b=b, ti=ti, stg=stg: s.dma_start(
                    A(sc_xn32, b * D * S + ti * 128, [[S, 128], [128 * S, NC], [1, 128]]),
                    A(stg, 0, [[D, 128], [128, NC], [1, 128]])), inc=s_st, incval=16)
                P.barrier(("tensor", "vector", "sync"))
            P.emit("sync", lambda s, b=b, XNT=XNT: s.dma_start(
                A(sc_xnt, b * D * S, [[S, 128], [128 * S, NC], [1, S]]),
                A(XNT, 0, [[NC * S, 128], [S, NC], [1, S]])), inc=s_st, incval=16)
            P.wait("sync", s_st)
            P.barrier()

        if stage == "p0" and b == 0:
            tap("XNT", XNT, 128, NC * S, BF16)
            return finish_stage()

        if True:
            QT = span_qkv.enter_context(nc.sbuf_tensor(f"QT_{b}", [128, H * S], BF16))
            KT = span_qkv.enter_context(nc.sbuf_tensor(f"KT_{b}", [128, H * S], BF16))
            span_vt = contextlib.ExitStack()
            open_spans.append(span_vt)
            VTOK = span_vt.enter_context(nc.sbuf_tensor(f"VTOK_{b}", [128, NT * D], BF16, side="right"))
            qqh_t = span_qkv.enter_context(nc.sbuf_tensor(f"qqh_t_{b}", [128, NT * H * NQ], BF16))
            kqh_t = span_qkv.enter_context(nc.sbuf_tensor(f"kqh_t_{b}", [128, NT * H * NQ], BF16))
            with contextlib.ExitStack() as p1:
                wpro = p1.enter_context(nc.sbuf_tensor(f"wpro_{b}", [128, H * NC * 128], BF16))
                qq_t = p1.enter_context(nc.sbuf_tensor(f"qq_t_{b}", [128, NT * H * NQ], F32))
                kq_t = p1.enter_context(nc.sbuf_tensor(f"kq_t_{b}", [128, NT * H * NQ], F32))
                qn2 = p1.enter_context(nc.sbuf_tensor(f"qn2_{b}", [128, NT * H], F32))
                kn2 = p1.enter_context(nc.sbuf_tensor(f"kn2_{b}", [128, NT * H], F32))
                sq_s = p1.enter_context(nc.sbuf_tensor(f"sq_s_{b}", [128, H * NQ], F32))
                p_a = p1.enter_context(nc.psum_tensor(f"p1_a_{b}", [128, 512], F32))
                p_b = p1.enter_context(nc.psum_tensor(f"p1_b_{b}", [128, 512], F32))
                p_c = p1.enter_context(nc.psum_tensor(f"p1_c_{b}", [128, 512], F32))
                p_d = p1.enter_context(nc.psum_tensor(f"p1_d_{b}", [128, 512], F32))
                p_v = p1.enter_context(nc.psum_tensor(f"p1_v_{b}", [128, 1024], F32))
                p_q = p1.enter_context(nc.psum_tensor(f"p1_q_{b}", [128, H * NQ], F32))
                pbank = (p_a, p_b, p_c, p_d)

                for (wname, bcol, OUT, scale) in (("aWq", bq_c, QT, ISQHD), ("aWk", bk_c, KT, None)):
                    P.barrier(("tensor", "gpsimd", "vector"))
                    P.emit("gpsimd", lambda g, wname=wname: g.dma_start(
                        wpro[:, :], A(W[wname], 0, [[HD, 128], [D * HD, H], [128 * HD, NC], [1, 128]])),
                        inc=s_ld, incval=16)
                    P.wait("tensor", s_ld)
                    for h in range(H):
                        P.wait("tensor", s_v)
                        for tc in range(4):
                            for f in range(NC):
                                P.emit("tensor", lambda t, h=h, tc=tc, f=f: t.matmul(
                                    pbank[tc][:, :], wpro[:, (h * NC + f) * 128:(h * NC + f + 1) * 128],
                                    XNT[:, f * S + tc * 512: f * S + (tc + 1) * 512],
                                    start=(f == 0), stop=(f == NC - 1)),
                                    inc=(s_t if f == NC - 1 else None))
                        P.wait("vector", s_t)
                        for tc in range(4):
                            if scale is None:
                                P.emit("vector", lambda v, h=h, tc=tc, OUT=OUT, bcol=bcol: v.tensor_scalar(
                                    OUT[:, h * S + tc * 512: h * S + (tc + 1) * 512],
                                    pbank[tc][:, :], bcol[:, h:h + 1], None, OP.add),
                                    inc=(s_v if tc == 3 else None))
                            else:
                                P.emit("vector", lambda v, h=h, tc=tc, OUT=OUT, bcol=bcol, scale=scale: v.tensor_scalar(
                                    OUT[:, h * S + tc * 512: h * S + (tc + 1) * 512],
                                    pbank[tc][:, :], bcol[:, h:h + 1], scale, OP.add, OP.mult),
                                    inc=(s_v if tc == 3 else None))
                P.barrier(("tensor", "vector", "gpsimd"))
                for f in range(NC):
                    P.emit("gpsimd", lambda g, f=f: g.dma_start(
                        wpro[:, f * 1024:(f + 1) * 1024],
                        A(W["aWv"], f * 128 * HD, [[HD, 128], [D * HD, H], [1, 128]])),
                        inc=s_ld, incval=16)
                P.barrier(("tensor", "gpsimd", "vector"))
                for ti in range(NT):
                    P.wait("tensor", s_v)
                    for f in range(NC):
                        for hf in range(2):
                            P.emit("tensor", lambda t, ti=ti, f=f, hf=hf: t.matmul(
                                p_v[:, hf * 512:(hf + 1) * 512],
                                XNT[:, f * S + ti * 128: f * S + ti * 128 + 128],
                                wpro[:, f * 1024 + hf * 512: f * 1024 + (hf + 1) * 512],
                                start=(f == 0), stop=False))
                    for hf in range(2):
                        P.emit("tensor", lambda t, hf=hf: t.matmul(
                            p_v[:, hf * 512:(hf + 1) * 512], ones_r[0:1, :],
                            bv_r[0:1, hf * 512:(hf + 1) * 512], start=False, stop=True),
                            inc=(s_t if hf == 1 else None))
                    P.wait("vector", s_t)
                    P.emit("vector", lambda v, ti=ti, VTOK=VTOK: v.tensor_copy(
                        VTOK[:, ti * D:(ti + 1) * D], p_v[:, :]), inc=s_v)
                P.barrier(("tensor", "vector", "gpsimd"))
                for (QKT, bias_row, qq_buf) in ((QT, bc2q_rq, qq_t), (KT, bc2q_r, kq_t)):
                    for ti in range(NT):
                        P.wait("tensor", s_v)
                        for h in range(H):
                            P.emit("tensor", lambda t, ti=ti, h=h, QKT=QKT: t.matmul(
                                p_q[:, h * NQ:(h + 1) * NQ],
                                QKT[:, h * S + ti * 128: h * S + ti * 128 + 128],
                                wc2q_bf[:, h * NQ:(h + 1) * NQ], start=(h == 0), stop=False))
                        P.emit("tensor", lambda t, bias_row=bias_row: t.matmul(
                            p_q[:, :], ones_r[0:1, :], bias_row[0:1, :], start=False, stop=True), inc=s_t)
                        P.wait("vector", s_t)
                        P.emit("vector", lambda v, ti=ti, qq_buf=qq_buf: v.tensor_copy(
                            qq_buf[:, ti * H * NQ:(ti + 1) * H * NQ], p_q[:, :]), inc=s_v)
                P.barrier(("tensor", "vector"))
                for (qq_buf, n2_buf) in ((qq_t, qn2), (kq_t, kn2)):
                    for ti in range(NT):
                        P.wait("vector", s_v)
                        P.emit("vector", lambda v, ti=ti, qq_buf=qq_buf: v.tensor_tensor(
                            sq_s[:, :], qq_buf[:, ti * H * NQ:(ti + 1) * H * NQ],
                            qq_buf[:, ti * H * NQ:(ti + 1) * H * NQ], OP.mult), inc=s_v)
                        P.wait("vector", s_v)
                        P.emit("vector", lambda v, ti=ti, n2_buf=n2_buf: v.reduce_sum(
                            n2_buf[:, ti * H:(ti + 1) * H],
                            A(sq_s, 0, [[H * NQ, 128], [NQ, H], [1, NQ]]), axis=AXX), inc=s_v)
                for n2_buf in (qn2, kn2):
                    P.wait("scalar", s_v)
                    P.emit("scalar", lambda a, n2_buf=n2_buf: a.activation(
                        n2_buf[:, :], n2_buf[:, :], AF.Sqrt), inc=s_a)
                    P.wait("vector", s_a)
                    P.chain("vector", [
                        lambda v, n2_buf=n2_buf: v.tensor_scalar(n2_buf[:, :], n2_buf[:, :], 1e-12, None, OP.max),
                        lambda v, n2_buf=n2_buf: v.reciprocal(n2_buf[:, :], n2_buf[:, :]),
                    ], s_v)
                P.wait("vector", s_v)
                for (qq_buf, n2_buf, OUTT) in ((qq_t, qn2, qqh_t), (kq_t, kn2, kqh_t)):
                    for ti in range(NT):
                        for h in range(H):
                            P.emit("vector", lambda v, ti=ti, h=h, qq_buf=qq_buf, n2_buf=n2_buf, OUTT=OUTT: v.tensor_scalar(
                                OUTT[:, ti * H * NQ + h * NQ: ti * H * NQ + (h + 1) * NQ],
                                qq_buf[:, ti * H * NQ + h * NQ: ti * H * NQ + (h + 1) * NQ],
                                n2_buf[:, ti * H + h: ti * H + h + 1], None, OP.mult))
                P.emit("vector", lambda v: v.engine_nop(), inc=s_v)
                P.wait("sync", s_v)
                P.emit("sync", lambda s, b=b, VTOK=VTOK: s.dma_start(
                    A(sc_vtok, b * 128 * NT * D, [[NT * D, 128], [1, NT * D]]),
                    VTOK[:, :]), inc=s_st, incval=16)
                P.barrier()

            if stage == "p1" and b == 0:
                tap("QT", QT, 128, H * S, BF16)
                tap("KT", KT, 128, H * S, BF16)
                tap("VTOK", VTOK, 128, NT * D, BF16)
                tap("qqh_t", qqh_t, 128, NT * H * NQ, BF16)
                tap("kqh_t", kqh_t, 128, NT * H * NQ, BF16)
                return finish_stage()
        span_vt.close()
        span_xnt.close()

        attnT = span_at.enter_context(nc.sbuf_tensor(f"attnT_{b}", [128, H * S], BF16, side="right"))
        with contextlib.ExitStack() as p2:
            fidb = [p2.enter_context(nc.sbuf_tensor(f"fidb{par}_{b}", [128, S], F32)) for par in range(2)]
            sinb = [p2.enter_context(nc.sbuf_tensor(f"sinb{par}_{b}", [128, S], F32)) for par in range(2)]
            pbuf = [p2.enter_context(nc.sbuf_tensor(f"pbuf{par}_{b}", [128, S], BF16, side="right")) for par in range(4)]
            mcol = [p2.enter_context(nc.sbuf_tensor(f"mcol{par}_{b}", [128, 8], F32)) for par in range(4)]
            pTq = p2.enter_context(nc.sbuf_tensor(f"pTq_{b}", [128, NT * 256], BF16))
            vtk = [p2.enter_context(nc.sbuf_tensor(f"vtk{par}_{b}", [128, NT * 128], BF16)) for par in range(2)]
            p_da = p2.enter_context(nc.psum_tensor(f"p2_da_{b}", [128, 1024], F32))
            p_cw = [p2.enter_context(nc.psum_tensor(f"p2_cw{k}_{b}", [128, 512], F32)) for k in range(4)]
            p_pt = p2.enter_context(nc.psum_tensor(f"p2_pt_{b}", [128, 1024], BF16))
            p_pv = p2.enter_context(nc.psum_tensor(f"p2_pv_{b}", [128, 512], F32))

            mk = {"fh": [0, 0], "t": {}, "z": {}, "cw": {}, "sin": {}, "exp": {},
                  "esum": {}, "pn": {}, "ptd": {}, "fid": {}, "qkevac": 0,
                  "pvread": 0, "pvdone": 0}
            mk["pvh"] = [0, 0]

            def emit_pv(h, qp, hp, lvs, lv_mark):
                P.wait("tensor", s_pt, mk["ptd"][h * NT + 2 * qp + 1])
                P.wait("tensor", s_a, mk["pvdone"])
                P.wait("tensor", lvs, lv_mark)
                for kt in range(NT):
                    P.emit("tensor", lambda t, hp=hp, kt=kt: t.matmul(
                        p_pv[:, 0:256], vtk[hp][:, kt * 128:(kt + 1) * 128],
                        pTq[:, kt * 256:(kt + 1) * 256],
                        start=(kt == 0), stop=(kt == NT - 1)),
                        inc=(s_t if kt == NT - 1 else None))
                mk["pvread"] = s_t.n
                P.wait("scalar", s_t, mk["pvread"])
                P.emit("scalar", lambda a, h=h, qp=qp: a.activation(
                    attnT[:, h * S + qp * 256: h * S + (qp + 1) * 256],
                    p_pv[:, 0:256], AF.Copy), inc=s_a)
                mk["pvdone"] = s_a.n
                mk["pvh"][hp] = mk["pvread"]

            def emit_tail(h, qp, hp, lvs, lv_mark):
                for qt in (2 * qp, 2 * qp + 1):
                    p4, qi = qt % 4, qt % 2
                    tq = h * NT + qt
                    P.wait("vector", s_a, mk["exp"][tq])
                    P.emit("vector", lambda v, p4=p4: v.reciprocal(
                        mcol[p4][:, 5:6], mcol[p4][:, 4:5]), inc=s_v)
                    mk["esum"][tq] = s_v.n
                    P.wait("scalar", s_v, mk["esum"][tq])
                    P.emit("scalar", lambda a, p4=p4: a.activation(
                        pbuf[p4][:, :], pbuf[p4][:, :], AF.Copy,
                        scale=mcol[p4][:, 5:6]), inc=s_a)
                    mk["pn"][tq] = s_a.n
                    P.wait("sync", s_a, mk["pn"][tq])
                    P.wait("sync", s_t, mk["pvread"])
                    P.emit("sync", lambda s, p4=p4, qi=qi: s.dma_start_transpose(
                        A(pTq, qi * 128, [[NT * 256, 128], [256, NT], [1, 128]]),
                        A(pbuf[p4], 0, [[S, 128], [128, NT], [1, 128]])),
                        inc=s_pt, incval=16)
                    mk["ptd"][tq] = s_pt.n

            pend = [None]
            for h in range(H):
                hp = h % 2
                lvs = s_lv if hp == 0 else s_lv2
                P.wait("sync", s_t, mk["pvh"][hp])
                P.emit("sync", lambda s, b=b, h=h, hp=hp: s.dma_start(
                    A(vtk[hp], 0, [[NT * 128, 128], [128, NT], [1, 128]]),
                    A(sc_vtok, b * 128 * NT * D + h * 128, [[NT * D, 128], [D, NT], [1, 128]])),
                    inc=lvs, incval=16)
                lv_mark = lvs.n
                for (tok, TT) in ((qqh_t, qqT_h), (kqh_t, kqT_h)):
                    for batch in range(2):
                        P.wait("tensor", s_a, mk["qkevac"])
                        for j in range(8):
                            ti = batch * 8 + j
                            P.emit("tensor", lambda t, ti=ti, j=j, h=h, tok=tok: t.transpose(
                                p_pt[0:NQ, j * 128:(j + 1) * 128],
                                tok[:, ti * H * NQ + h * NQ: ti * H * NQ + (h + 1) * NQ],
                                ident_bf[:, :]), inc=(s_t if j == 7 else None))
                        P.wait("scalar", s_t)
                        P.emit("scalar", lambda a, batch=batch, TT=TT: a.activation(
                            TT[0:NQ, batch * 1024:(batch + 1) * 1024], p_pt[0:NQ, :],
                            AF.Copy), inc=s_a)
                        mk["qkevac"] = s_a.n
                P.wait("tensor", s_a, mk["qkevac"])
                for qp in range(NT // 2):
                    tiles = (2 * qp, 2 * qp + 1)
                    for qt in tiles:
                        par = qt % 2
                        tq = h * NT + qt
                        P.wait("vector", s_a, mk["exp"].get(tq - 2, 0))
                        for half in range(2):
                            P.wait("tensor", s_v, mk["fh"][1 - half])
                            for kc in (2 * half, 2 * half + 1):
                                P.emit("tensor", lambda t, h=h, qt=qt, kc=kc, half=half: t.matmul(
                                    p_da[:, (kc - 2 * half) * 512:(kc - 2 * half + 1) * 512],
                                    qqT_h[0:NQ, qt * 128:(qt + 1) * 128],
                                    kqT_h[0:NQ, kc * 512:(kc + 1) * 512], start=True, stop=True),
                                    inc=(s_t if kc % 2 == 1 else None))
                            dm = s_t.n
                            P.wait("vector", s_t, dm)
                            P.emit("vector", lambda v, par=par, half=half: v.tensor_scalar(
                                fidb[par][:, half * 1024:(half + 1) * 1024], p_da[:, :],
                                1.0, -1.0, OP.min, OP.max), inc=s_v)
                            mk["fh"][half] = s_v.n
                        mk["fid", tq] = s_v.n
                    if pend[0] is not None:
                        emit_tail(*pend[0])
                    for qt in tiles:
                        par = qt % 2
                        tq = h * NT + qt
                        P.wait("scalar", s_v, max(mk["fid", tq], mk["t"].get(tq - 2, 0)))
                        P.emit("scalar", lambda a, par=par: a.activation(
                            sinb[par][:, :], fidb[par][:, :], AF.Sin, scale=PI_LO), inc=s_a)
                        mk["sin"][tq] = s_a.n
                    if pend[0] is not None:
                        emit_pv(*pend[0])
                    pend[0] = (h, qp, hp, lvs, lv_mark)
                    for qt in tiles:
                        par = qt % 2
                        tq = h * NT + qt
                        for kc in range(4):
                            P.wait("tensor", s_v, mk["z"].get((kc, tq - 1), 0))
                            P.emit("tensor", lambda t, h=h, qt=qt, kc=kc: t.matmul(
                                p_cw[kc][:, :], QT[:, h * S + qt * 128: h * S + qt * 128 + 128],
                                KT[:, h * S + kc * 512: h * S + (kc + 1) * 512],
                                start=True, stop=True), inc=s_t)
                            mk["cw"][kc, tq] = s_t.n
                        P.wait("vector", s_a, mk["sin"][tq])
                        P.chain("vector", [
                            lambda v, par=par: v.scalar_tensor_tensor(
                                sinb[par][:, :], sinb[par][:, :], 0.1, fidb[par][:, :],
                                OP.mult, OP.mult),
                            lambda v, par=par: v.tensor_tensor(
                                fidb[par][:, :], fidb[par][:, :], sinb[par][:, :], OP.add),
                        ], s_v)
                        mk["t"][tq] = s_v.n
                        for kc in range(4):
                            P.wait("vector", s_t, mk["cw"][kc, tq])
                            P.emit("vector", lambda v, par=par, kc=kc: v.tensor_tensor(
                                fidb[par][:, kc * 512:(kc + 1) * 512],
                                fidb[par][:, kc * 512:(kc + 1) * 512],
                                p_cw[kc][:, :], OP.add), inc=s_v)
                            mk["z"][kc, tq] = s_v.n
                    for qt in tiles:
                        par, p4 = qt % 2, qt % 4
                        tq = h * NT + qt
                        P.wait("scalar", s_pt, mk["ptd"].get(tq - 4, 0))
                        P.wait("scalar", s_v, mk["z"][3, tq])
                        P.emit("scalar", lambda a, par=par, p4=p4: a.activation(
                            pbuf[p4][:, :], fidb[par][:, :], AF.Exp, scale=0.5,
                            accum_out=mcol[p4][:, 4:5]), inc=s_a)
                        mk["exp"][tq] = s_a.n
            emit_tail(*pend[0])
            emit_pv(*pend[0])
            P.barrier()
        span_qkv.close()

        if stage == "p2" and b == 0:
            tap("attnT", attnT, 128, H * S, BF16)
            return finish_stage()

        span_x1 = contextlib.ExitStack()
        open_spans.append(span_x1)
        X1IN = span_x1.enter_context(nc.sbuf_tensor(f"X1IN_{b}", [128, NC * S], F32))
        with contextlib.ExitStack() as p3:
            XN32 = p3.enter_context(nc.sbuf_tensor(f"XN32_{b}", [128, NC * S], F32))
            wo_sb = p3.enter_context(nc.sbuf_tensor(f"wo_sb_{b}", [128, NC * 1024], BF16))
            p_w1 = p3.enter_context(nc.psum_tensor(f"p3_w1_{b}", [128, 512], F32))
            p_w2 = p3.enter_context(nc.psum_tensor(f"p3_w2_{b}", [128, 512], F32))
            P.emit("sync", lambda s, b=b, XN32=XN32: s.dma_start(
                A(XN32, 0, [[NC * S, 128], [S, NC], [1, S]]),
                A(sc_xn32, b * D * S, [[S, 128], [128 * S, NC], [1, S]])), inc=s_lh, incval=16)
            P.emit("gpsimd", lambda g, wo_sb=wo_sb: g.dma_start(
                wo_sb[:, :], A(W["aWo"], 0, [[D, 128], [128 * D, NC], [1, D]])), inc=s_ld, incval=16)
            P.barrier()
            for dc in range(NC):
                for tc in range(4):
                    pw = p_w1 if (dc * 4 + tc) % 2 == 0 else p_w2
                    P.wait("tensor", s_v)
                    for f in range(NC):
                        P.emit("tensor", lambda t, dc=dc, tc=tc, f=f, pw=pw: t.matmul(
                            pw[:, :], wo_sb[:, f * 1024 + dc * 128: f * 1024 + (dc + 1) * 128],
                            attnT[:, f * S + tc * 512: f * S + (tc + 1) * 512],
                            start=(f == 0), stop=(f == NC - 1)), inc=(s_t if f == NC - 1 else None))
                    P.wait("vector", s_t)
                    P.emit("vector", lambda v, dc=dc, tc=tc, pw=pw, X1IN=X1IN, XN32=XN32: v.scalar_tensor_tensor(
                        X1IN[:, dc * S + tc * 512: dc * S + (tc + 1) * 512], pw[:, :],
                        bo_c[:, dc:dc + 1], XN32[:, dc * S + tc * 512: dc * S + (tc + 1) * 512],
                        OP.add, OP.add), inc=s_v)
            P.barrier()
        span_at.close()
        with contextlib.ExitStack() as p3b:
            p_st = p3b.enter_context(nc.psum_tensor(f"p3_st_{b}", [1, 1024], F32))
            p_bc = p3b.enter_context(nc.psum_tensor(f"p3_bc_{b}", [128, 1024], F32))
            sqt = p3b.enter_context(nc.sbuf_tensor(f"sqt_{b}", [128, 1024], F32))
            rows = tuple(p3b.enter_context(nc.sbuf_tensor(f"row3__{b}" + ch, [1, S], F32)) for ch in "abc")
            ln_fm(X1IN, g1_c, be1_c, p_st, p_bc, sqt, rows)
            P.barrier()

        if stage == "p3" and b == 0:
            tap("X1IN", X1IN, 128, NC * S, F32)
            return finish_stage()

        span_x1t = contextlib.ExitStack()
        open_spans.append(span_x1t)
        X1T = span_x1t.enter_context(nc.sbuf_tensor(f"X1T_{b}", [128, NC * S], BF16, side="right"))
        with contextlib.ExitStack() as p4:
            qm_s = p4.enter_context(nc.sbuf_tensor(f"qm_s_{b}", [NQ, S], F32))
            th_s = p4.enter_context(nc.sbuf_tensor(f"th_s_{b}", [NQ, S], F32))
            qe_b = p4.enter_context(nc.sbuf_tensor(f"qe_b_{b}", [NQ, S], BF16))
            p_qm = p4.enter_context(nc.psum_tensor(f"p4_qm_{b}", [NQ, 512], F32))
            p_f1 = p4.enter_context(nc.psum_tensor(f"p4_f1_{b}", [128, 512], F32))
            p_f2 = p4.enter_context(nc.psum_tensor(f"p4_f2_{b}", [128, 512], F32))
            for c in range(NC):
                P.emit("vector", lambda v, c=c, X1T=X1T, X1IN=X1IN: v.tensor_copy(
                    X1T[:, c * S:(c + 1) * S], X1IN[:, c * S:(c + 1) * S]), inc=s_v)
            P.barrier(("tensor", "vector"))
            for tc in range(4):
                for f in range(NC):
                    P.emit("tensor", lambda t, tc=tc, f=f, X1T=X1T: t.matmul(
                        p_qm[0:NQ, :], m1_bf[:, f * NQ:(f + 1) * NQ],
                        X1T[:, f * S + tc * 512: f * S + (tc + 1) * 512],
                        start=(f == 0), stop=(f == NC - 1)), inc=(s_t if f == NC - 1 else None))
                P.wait("vector", s_t)
                P.emit("vector", lambda v, tc=tc, qm_s=qm_s: v.tensor_scalar(
                    qm_s[0:NQ, tc * 512:(tc + 1) * 512], p_qm[0:NQ, :], bias1_c[0:NQ, 0:1], None, OP.add), inc=s_v)
                P.wait("tensor", s_v)
            P.wait("scalar", s_v)
            P.emit("scalar", lambda a, th_s=th_s, qm_s=qm_s: a.activation(th_s[0:NQ, :], qm_s[0:NQ, :], AF.Tanh), inc=s_a)
            P.wait("vector", s_a)
            P.wait("vector", s_v)
            P.emit("vector", lambda v, qm_s=qm_s: v.tensor_scalar(
                qm_s[0:NQ, :], qm_s[0:NQ, :], 8.0, -8.0, OP.min, OP.max), inc=s_v)
            P.wait("scalar", s_v)
            P.emit("scalar", lambda a, qm_s=qm_s: a.activation(qm_s[0:NQ, :], qm_s[0:NQ, :], AF.Sin, scale=PI8_LO), inc=s_a)
            P.wait("vector", s_a)
            P.chain("vector", [
                lambda v, qm_s=qm_s: v.tensor_tensor(qm_s[0:NQ, :], qm_s[0:NQ, :], qm_s[0:NQ, :], OP.mult),
                lambda v, qm_s=qm_s: v.tensor_scalar(qm_s[0:NQ, :], qm_s[0:NQ, :], -2.0, 1.0, OP.mult, OP.add),
                lambda v, qe_b=qe_b, th_s=th_s, qm_s=qm_s: v.tensor_tensor(
                    qe_b[0:NQ, :], th_s[0:NQ, :], qm_s[0:NQ, :], OP.mult),
            ], s_v)
            P.wait("tensor", s_v)
            for dc in range(NC):
                for tc in range(4):
                    pw = p_f1 if (dc * 4 + tc) % 2 == 0 else p_f2
                    P.emit("tensor", lambda t, dc=dc, tc=tc, pw=pw, qe_b=qe_b: t.matmul(
                        pw[:, :], m2_bf[0:NQ, dc * 128:(dc + 1) * 128],
                        qe_b[0:NQ, tc * 512:(tc + 1) * 512], start=True, stop=True), inc=s_t)
                    P.wait("vector", s_t)
                    P.emit("vector", lambda v, dc=dc, tc=tc, pw=pw, X1IN=X1IN: v.scalar_tensor_tensor(
                        X1IN[:, dc * S + tc * 512: dc * S + (tc + 1) * 512], pw[:, :],
                        bias2_c[:, dc:dc + 1], X1IN[:, dc * S + tc * 512: dc * S + (tc + 1) * 512],
                        OP.add, OP.add), inc=s_v)
                    P.wait("tensor", s_v)
            P.barrier(("tensor", "vector", "scalar"))
        span_x1t.close()
        with contextlib.ExitStack() as p4b:
            p_st = p4b.enter_context(nc.psum_tensor(f"p4_st_{b}", [1, 1024], F32))
            p_bc = p4b.enter_context(nc.psum_tensor(f"p4_bc_{b}", [128, 1024], F32))
            sqt = p4b.enter_context(nc.sbuf_tensor(f"sqt2_{b}", [128, 1024], F32))
            rows = tuple(p4b.enter_context(nc.sbuf_tensor(f"row4__{b}" + ch, [1, S], F32)) for ch in "abc")
            ln_fm(X1IN, g2_c, be2_c, p_st, p_bc, sqt, rows)
            P.barrier()

        if stage == "p4" and b == 0:
            tap("X1IN", X1IN, 128, NC * S, F32)
            return finish_stage()

        with contextlib.ExitStack() as p5:
            p_t1 = p5.enter_context(nc.psum_tensor(f"p5_t1_{b}", [128, 512], F32))
            p_t2 = p5.enter_context(nc.psum_tensor(f"p5_t2_{b}", [128, 512], F32))
            stg = p5.enter_context(nc.sbuf_tensor(f"stg5_{b}", [128, D], F32))
            spill_tok(X1IN, sc_aout, b * S * D, p_t1, p_t2, stg)
        span_x1.close()

        span_qf = contextlib.ExitStack()
        open_spans.append(span_qf)
        span_f = contextlib.ExitStack()
        open_spans.append(span_f)
        QfT = span_qf.enter_context(nc.sbuf_tensor(f"QfT_{b}", [128, NC * S], BF16))
        KfT = span_qf.enter_context(nc.sbuf_tensor(f"KfT_{b}", [128, NC * S], BF16))
        VfTOK = span_qf.enter_context(nc.sbuf_tensor(f"VfTOK_{b}", [128, NT * D], BF16))
        span_xnt2 = contextlib.ExitStack()
        open_spans.append(span_xnt2)
        XNT2 = span_xnt2.enter_context(nc.sbuf_tensor(f"XNT2_{b}", [128, NC * S], BF16, side="right"))
        with contextlib.ExitStack() as p6:
            wpro = p6.enter_context(nc.sbuf_tensor(f"wpro6_{b}", [128, NC * 1024], BF16))
            p_a = p6.enter_context(nc.psum_tensor(f"p6_a_{b}", [128, 512], F32))
            p_b = p6.enter_context(nc.psum_tensor(f"p6_b_{b}", [128, 512], F32))
            p_v = p6.enter_context(nc.psum_tensor(f"p6_v_{b}", [128, 1024], F32))
            P.emit("sync", lambda s, b=b, XNT2=XNT2: s.dma_start(
                A(XNT2, 0, [[NC * S, 128], [S, NC], [1, S]]),
                A(sc_xnt, b * D * S, [[S, 128], [128 * S, NC], [1, S]])), inc=s_lh, incval=16)
            P.barrier()
            for (wname, bcol, OUT, scale) in (("fWq", fbq_c, QfT, INV32), ("fWk", fbk_c, KfT, None)):
                P.barrier(("tensor", "gpsimd", "vector"))
                P.emit("gpsimd", lambda g, wname=wname, wpro=wpro: g.dma_start(
                    wpro[:, :], A(W[wname], 0, [[D, 128], [128 * D, NC], [1, D]])), inc=s_ld, incval=16)
                P.wait("tensor", s_ld)
                for dc in range(NC):
                    for tc in range(4):
                        pw = p_a if (dc * 4 + tc) % 2 == 0 else p_b
                        P.wait("tensor", s_v)
                        for f in range(NC):
                            P.emit("tensor", lambda t, dc=dc, tc=tc, f=f, pw=pw, wpro=wpro, XNT2=XNT2: t.matmul(
                                pw[:, :], wpro[:, f * 1024 + dc * 128: f * 1024 + (dc + 1) * 128],
                                XNT2[:, f * S + tc * 512: f * S + (tc + 1) * 512],
                                start=(f == 0), stop=(f == NC - 1)), inc=(s_t if f == NC - 1 else None))
                        P.wait("vector", s_t)
                        if scale is None:
                            P.emit("vector", lambda v, dc=dc, tc=tc, pw=pw, OUT=OUT, bcol=bcol: v.tensor_scalar(
                                OUT[:, dc * S + tc * 512: dc * S + (tc + 1) * 512],
                                pw[:, :], bcol[:, dc:dc + 1], None, OP.add), inc=s_v)
                        else:
                            P.emit("vector", lambda v, dc=dc, tc=tc, pw=pw, OUT=OUT, bcol=bcol, scale=scale: v.tensor_scalar(
                                OUT[:, dc * S + tc * 512: dc * S + (tc + 1) * 512],
                                pw[:, :], bcol[:, dc:dc + 1], scale, OP.add, OP.mult), inc=s_v)
            P.barrier(("tensor", "vector", "gpsimd"))
            P.emit("gpsimd", lambda g, wpro=wpro: g.dma_start(
                wpro[:, :], A(W["fWv"], 0, [[D, 128], [128 * D, NC], [1, D]])), inc=s_ld, incval=16)
            P.barrier(("tensor", "gpsimd", "vector"))
            for ti in range(NT):
                P.wait("tensor", s_v)
                for f in range(NC):
                    for hf in range(2):
                        P.emit("tensor", lambda t, ti=ti, f=f, hf=hf, wpro=wpro, XNT2=XNT2: t.matmul(
                            p_v[:, hf * 512:(hf + 1) * 512],
                            XNT2[:, f * S + ti * 128: f * S + ti * 128 + 128],
                            wpro[:, f * 1024 + hf * 512: f * 1024 + (hf + 1) * 512],
                            start=(f == 0), stop=False))
                for hf in range(2):
                    P.emit("tensor", lambda t, hf=hf: t.matmul(
                        p_v[:, hf * 512:(hf + 1) * 512], ones_r[0:1, :],
                        fbv_r[0:1, hf * 512:(hf + 1) * 512], start=False, stop=True),
                        inc=(s_t if hf == 1 else None))
                P.wait("vector", s_t)
                P.emit("vector", lambda v, ti=ti, VfTOK=VfTOK: v.tensor_copy(
                    VfTOK[:, ti * D:(ti + 1) * D], p_v[:, :]), inc=s_v)
            P.barrier()

        span_xnt2.close()
        afT = span_f.enter_context(nc.sbuf_tensor(f"afT_{b}", [128, NC * S], BF16, side="right"))
        with contextlib.ExitStack() as p6b:
            pbuf6 = [p6b.enter_context(nc.sbuf_tensor(f"pbuf6{i}_{b}", [128, S], BF16)) for i in range(2)]
            pTq = p6b.enter_context(nc.sbuf_tensor(f"pTq6_{b}", [128, NT * 512], BF16))
            mcol = p6b.enter_context(nc.sbuf_tensor(f"mcol6_{b}", [128, 8], F32))
            p_s2 = p6b.enter_context(nc.psum_tensor(f"p6_s2_{b}", [128, 2048], F32))
            p_pv1 = p6b.enter_context(nc.psum_tensor(f"p6_pv1_{b}", [128, 512], F32))
            p_pv2 = p6b.enter_context(nc.psum_tensor(f"p6_pv2_{b}", [128, 512], F32))
            mk6 = {"exp": {}, "esum": {}, "pn": {}, "ptd": {}}
            pv_read = [0]
            for qt in range(NT):
                quad, qi = qt // 4, qt % 4
                pb = pbuf6[qt % 2]
                mc = (qt % 2) * 4
                P.wait("tensor", s_a, mk6["exp"].get(qt - 1, 0))
                for kc in range(4):
                    for f in range(NC):
                        P.emit("tensor", lambda t, qt=qt, kc=kc, f=f, QfT=QfT, KfT=KfT: t.matmul(
                            p_s2[:, kc * 512:(kc + 1) * 512],
                            QfT[:, f * S + qt * 128: f * S + qt * 128 + 128],
                            KfT[:, f * S + kc * 512: f * S + (kc + 1) * 512],
                            start=(f == 0), stop=(f == NC - 1)),
                            inc=(s_t if (kc == 3 and f == NC - 1) else None))
                sc_mark = s_t.n
                P.wait("scalar", s_t, sc_mark)
                P.wait("scalar", s_pt, mk6["ptd"].get(qt - 2, 0))
                P.emit("scalar", lambda a, pb=pb, mc=mc: a.activation(
                    pb[:, :], p_s2[:, :], AF.Exp, scale=1.0,
                    accum_out=mcol[:, mc:mc + 1]), inc=s_a)
                mk6["exp"][qt] = s_a.n
                P.wait("vector", s_a, mk6["exp"][qt])
                P.emit("vector", lambda v, mc=mc: v.reciprocal(
                    mcol[:, mc + 1:mc + 2], mcol[:, mc:mc + 1]), inc=s_v)
                mk6["esum"][qt] = s_v.n
                P.wait("scalar", s_v, mk6["esum"][qt])
                P.emit("scalar", lambda a, pb=pb, mc=mc: a.activation(
                    pb[:, :], pb[:, :], AF.Copy, scale=mcol[:, mc + 1:mc + 2]), inc=s_a)
                mk6["pn"][qt] = s_a.n
                P.wait("sync", s_a, mk6["pn"][qt])
                P.wait("sync", s_t, pv_read[0])
                P.emit("sync", lambda s, pb=pb, qi=qi, pTq=pTq: s.dma_start_transpose(
                    A(pTq, qi * 128, [[NT * 512, 128], [512, NT], [1, 128]]),
                    A(pb, 0, [[S, 128], [128, NT], [1, 128]])), inc=s_pt, incval=16)
                mk6["ptd"][qt] = s_pt.n
                if qi == 3:
                    P.wait("tensor", s_pt, mk6["ptd"][qt])
                    for dc in range(NC):
                        pv = p_pv1 if dc % 2 == 0 else p_pv2
                        if dc >= 2:
                            P.wait("tensor", s_v)
                        for kt in range(NT):
                            P.emit("tensor", lambda t, dc=dc, kt=kt, pv=pv, VfTOK=VfTOK, pTq=pTq: t.matmul(
                                pv[:, :], VfTOK[:, kt * D + dc * 128: kt * D + (dc + 1) * 128],
                                pTq[:, kt * 512:(kt + 1) * 512],
                                start=(kt == 0), stop=(kt == NT - 1)), inc=(s_t if kt == NT - 1 else None))
                        P.wait("vector", s_t)
                        P.emit("vector", lambda v, dc=dc, quad=quad, pv=pv, afT=afT: v.tensor_copy(
                            afT[:, dc * S + quad * 512: dc * S + (quad + 1) * 512], pv[:, :]), inc=s_v)
                    pv_read[0] = s_t.n
                    P.wait("tensor", s_v)
            P.barrier()
        span_qf.close()

        with contextlib.ExitStack() as p6c:
            XN32 = p6c.enter_context(nc.sbuf_tensor(f"XN32b_{b}", [128, NC * S], F32))
            XFT = p6c.enter_context(nc.sbuf_tensor(f"XFT_{b}", [128, NC * S], BF16))
            wo2 = p6c.enter_context(nc.sbuf_tensor(f"wo2_{b}", [128, NC * 1024], BF16))
            xfrow = p6c.enter_context(nc.sbuf_tensor(f"xfrow_{b}", [128, S], F32))
            g_s = p6c.enter_context(nc.sbuf_tensor(f"g_s_{b}", [E, S], F32))
            gt_s = p6c.enter_context(nc.sbuf_tensor(f"gt_s_{b}", [128, E], F32))
            gcol = p6c.enter_context(nc.sbuf_tensor(f"gcol_{b}", [128, 4], F32))
            stg = p6c.enter_context(nc.sbuf_tensor(f"stg6_{b}", [128, D], F32))
            p_w1 = p6c.enter_context(nc.psum_tensor(f"p6_w1_{b}", [128, 512], F32))
            p_w2 = p6c.enter_context(nc.psum_tensor(f"p6_w2_{b}", [128, 512], F32))
            p_g = p6c.enter_context(nc.psum_tensor(f"p6_g_{b}", [E, 512], F32))
            p_gt = p6c.enter_context(nc.psum_tensor(f"p6_gt_{b}", [128, E], F32))
            p_t1 = p6c.enter_context(nc.psum_tensor(f"p6_t1_{b}", [128, 512], F32))
            p_t2 = p6c.enter_context(nc.psum_tensor(f"p6_t2_{b}", [128, 512], F32))
            P.emit("sync", lambda s, b=b, XN32=XN32: s.dma_start(
                A(XN32, 0, [[NC * S, 128], [S, NC], [1, S]]),
                A(sc_xn32, b * D * S, [[S, 128], [128 * S, NC], [1, S]])), inc=s_lh, incval=16)
            P.emit("gpsimd", lambda g, wo2=wo2: g.dma_start(
                wo2[:, :], A(W["fWo"], 0, [[D, 128], [128 * D, NC], [1, D]])), inc=s_ld, incval=16)
            P.barrier()
            stg_tok = p6c.enter_context(nc.sbuf_tensor(f"stg_tok_{b}", [128, S], F32))
            for dc in range(NC):
                for tc in range(4):
                    pw = p_w1 if tc % 2 == 0 else p_w2
                    for f in range(NC):
                        P.emit("tensor", lambda t, dc=dc, tc=tc, f=f, pw=pw, wo2=wo2, afT=afT: t.matmul(
                            pw[:, :], wo2[:, f * 1024 + dc * 128: f * 1024 + (dc + 1) * 128],
                            afT[:, f * S + tc * 512: f * S + (tc + 1) * 512],
                            start=(f == 0), stop=(f == NC - 1)), inc=(s_t if f == NC - 1 else None))
                    P.wait("vector", s_t)
                    P.chain("vector", [
                        lambda v, dc=dc, tc=tc, pw=pw, XN32=XN32, xfrow=xfrow: v.scalar_tensor_tensor(
                            xfrow[:, tc * 512:(tc + 1) * 512], pw[:, :], fbo_c[:, dc:dc + 1],
                            XN32[:, dc * S + tc * 512: dc * S + (tc + 1) * 512], OP.add, OP.add),
                        lambda v, dc=dc, tc=tc, xfrow=xfrow, XFT=XFT: v.tensor_copy(
                            XFT[:, dc * S + tc * 512: dc * S + (tc + 1) * 512],
                            xfrow[:, tc * 512:(tc + 1) * 512]),
                    ], s_v)
                    P.wait("tensor", s_v)
                P.wait("tensor", s_st, s_st.n)
                for g4 in range(4):
                    pt = p_t1 if g4 % 2 == 0 else p_t2
                    for j in range(4):
                        P.emit("tensor", lambda t, g4=g4, j=j, pt=pt, xfrow=xfrow: t.transpose(
                            pt[:, j * 128:(j + 1) * 128],
                            xfrow[:, (g4 * 4 + j) * 128:(g4 * 4 + j + 1) * 128], ident_f[:, :]),
                            inc=(s_t if j == 3 else None))
                    P.wait("vector", s_t)
                    P.emit("vector", lambda v, g4=g4, pt=pt, stg_tok=stg_tok: v.tensor_copy(
                        stg_tok[:, g4 * 512:(g4 + 1) * 512], pt[:, :]), inc=s_v)
                    P.wait("tensor", s_v)
                P.wait("sync", s_v)
                P.emit("sync", lambda s, b=b, dc=dc, stg_tok=stg_tok: s.dma_start(
                    A(sc_xf32, b * S * D + dc * 128, [[D, 128], [128 * D, NT], [1, 128]]),
                    A(stg_tok, 0, [[S, 128], [128, NT], [1, 128]])), inc=s_st, incval=16)
                P.barrier(("tensor", "vector", "sync"))
            P.emit("sync", lambda s, b=b, XFT=XFT: s.dma_start(
                A(sc_xft, b * D * S, [[S, 128], [128 * S, NC], [1, S]]),
                A(XFT, 0, [[NC * S, 128], [S, NC], [1, S]])), inc=s_st, incval=16)
            for tc in range(4):
                for f in range(NC):
                    P.emit("tensor", lambda t, tc=tc, f=f, XFT=XFT: t.matmul(
                        p_g[0:E, :], wg_bf[:, f * E:(f + 1) * E],
                        XFT[:, f * S + tc * 512: f * S + (tc + 1) * 512],
                        start=(f == 0), stop=(f == NC - 1)), inc=(s_t if f == NC - 1 else None))
                P.wait("vector", s_t)
                P.emit("vector", lambda v, tc=tc, g_s=g_s: v.tensor_scalar(
                    g_s[0:E, tc * 512:(tc + 1) * 512], p_g[0:E, :], fbg_c[0:E, 0:1], None, OP.add), inc=s_v)
                P.wait("tensor", s_v)
            for ti in range(NT):
                P.emit("tensor", lambda t, ti=ti, g_s=g_s: t.transpose(
                    p_gt[:, 0:E], g_s[0:E, ti * 128:(ti + 1) * 128], ident_f[0:E, 0:E]), inc=s_t)
                P.wait("vector", s_t)
                P.chain("vector", [
                    lambda v, gt_s=gt_s: v.tensor_copy(gt_s[:, :], p_gt[:, 0:E]),
                    lambda v, gt_s=gt_s, gcol=gcol: v.reduce_max(gcol[:, 0:1], gt_s[:, :], axis=AXX),
                    lambda v, gcol=gcol: v.tensor_scalar(gcol[:, 1:2], gcol[:, 0:1], -1.0, None, OP.mult),
                ], s_v)
                P.wait("scalar", s_v)
                P.emit("scalar", lambda a, gt_s=gt_s, gcol=gcol: a.activation(
                    gt_s[:, :], gt_s[:, :], AF.Exp, bias=gcol[:, 1:2], scale=1.0,
                    accum_out=gcol[:, 2:3]), inc=s_a)
                P.wait("vector", s_a)
                P.chain("vector", [
                    lambda v, gcol=gcol: v.reciprocal(gcol[:, 3:4], gcol[:, 2:3]),
                    lambda v, b=b, ti=ti, gt_s=gt_s, gcol=gcol: v.tensor_scalar(
                        gates_t[:, (b * NT + ti) * E:(b * NT + ti + 1) * E], gt_s[:, :],
                        gcol[:, 3:4], None, OP.mult),
                ], s_v)
                P.wait("tensor", s_v)
            P.wait("sync", s_st)
            P.barrier()
        span_f.close()
        P.barrier()

        if stage == "p6" and b == 0:
            tap("gates_t", gates_t, 128, NT * E, F32)
            return finish_stage()

    with contextlib.ExitStack() as p7:
        acc = p7.enter_context(nc.sbuf_tensor("acc", [128, NT * D], F32))
        b2_r = p7.enter_context(nc.sbuf_tensor("b2_r", [1, D], F32))
        w1h = p7.enter_context(nc.sbuf_tensor("w1h", [128, NC * 2048], BF16))
        w2h = p7.enter_context(nc.sbuf_tensor("w2h", [128, 16 * 1024], BF16))
        h_sb = p7.enter_context(nc.sbuf_tensor("h_sb", [128, 16 * 512], BF16))
        xfs = p7.enter_context(nc.sbuf_tensor("xfs", [128, NC * 512], BF16))
        xfs2 = p7.enter_context(nc.sbuf_tensor("xfs2", [128, NC * 512], BF16))
        aob = p7.enter_context(nc.sbuf_tensor("aob", [128, D], F32))
        aob2 = p7.enter_context(nc.sbuf_tensor("aob2", [128, D], F32))
        p_hb = [p7.enter_context(nc.psum_tensor(f"p7_h{i}", [128, 512], F32)) for i in range(4)]
        p_e1 = p7.enter_context(nc.psum_tensor("p7_e1", [128, 1024], F32))
        p_e2 = p7.enter_context(nc.psum_tensor("p7_e2", [128, 1024], F32))

        marks = {"w1q": [0, 0, 0, 0], "w2q": [0, 0, 0, 0], "xfs": [0, 0],
                 "relu": [0, 0, 0, 0], "stt": [0, 0], "b2": 0}
        ld_w1q = [0, 0, 0, 0]
        ld_w2q = [0, 0, 0, 0]
        for tb in range(B_LOC):
            P.wait("sync", s_st)
            P.emit("sync", lambda s, tb=tb, acc=acc: s.dma_start(
                acc[:, :], A(x_in, tb * S * D, [[D, 128], [128 * D, NT], [1, D]])),
                inc=s_lh, incval=16)
            P.barrier(("sync", "vector", "gpsimd"))
            for ti in range(NT):
                ab = aob if ti % 2 == 0 else aob2
                P.wait("sync", s_v)
                P.emit("sync", lambda s, tb=tb, ti=ti, ab=ab: s.dma_start(
                    ab[:, :], A(sc_aout, tb * S * D + ti * 128 * D, [[D, 128], [1, D]])),
                    inc=s_lh, incval=16)
                P.wait("vector", s_lh)
                P.emit("vector", lambda v, ti=ti, ab=ab, acc=acc: v.tensor_tensor(
                    acc[:, ti * D:(ti + 1) * D], acc[:, ti * D:(ti + 1) * D], ab[:, :], OP.add), inc=s_v)
                P.wait("sync", s_v)
                P.wait("sync", s_lh)
                P.emit("sync", lambda s, tb=tb, ti=ti, ab=ab: s.dma_start(
                    ab[:, :], A(sc_xf32, tb * S * D + ti * 128 * D, [[D, 128], [1, D]])),
                    inc=s_lh, incval=16)
                P.wait("vector", s_lh)
                P.emit("vector", lambda v, ti=ti, ab=ab, acc=acc: v.tensor_tensor(
                    acc[:, ti * D:(ti + 1) * D], acc[:, ti * D:(ti + 1) * D], ab[:, :], OP.add), inc=s_v)
            P.barrier(("sync", "vector", "gpsimd", "tensor", "scalar"))
            for e in range(E):
                P.wait("sync", s_t, marks["b2"])
                P.emit("sync", lambda s, e=e, b2_r=b2_r: s.dma_start(
                    b2_r[0:1, :], A(W["fb2"], e * D, [[D, 1], [1, D]])), inc=s_lb, incval=16)
                lb_mark = s_lb.n
                for dh in range(2):
                    for q in range(4):
                        P.wait("gpsimd", s_t, marks["w1q"][q])
                        P.emit("gpsimd", lambda g, e=e, dh=dh, q=q, w1h=w1h: g.dma_start(
                            A(w1h, q * 512, [[NC * 2048, 128], [2048, NC], [1, 512]]),
                            A(W["fW1"], e * D * FF + dh * 2048 + q * 512,
                              [[FF, 128], [128 * FF, NC], [1, 512]])), inc=s_ld, incval=16)
                        ld_w1q[q] = s_ld.n
                    for q in range(4):
                        P.wait("gpsimd", s_t, marks["w2q"][q])
                        P.emit("gpsimd", lambda g, e=e, dh=dh, q=q, w2h=w2h: g.dma_start(
                            A(w2h, q * 4 * 1024, [[16 * 1024, 128], [1024, 4], [1, 1024]]),
                            A(W["fW2"], e * FF * D + (dh * 2048 + q * 4 * 128) * D,
                              [[D, 128], [128 * D, 4], [1, D]])), inc=s_ld, incval=16)
                    ld_all = s_ld.n
                    for tc in range(4):
                        xf_cur = xfs if tc % 2 == 0 else xfs2
                        lxs = s_lx if tc % 2 == 0 else s_lx2
                        P.wait("sync", s_t, marks["xfs"][tc % 2])
                        P.emit("sync", lambda s, tb=tb, tc=tc, xf_cur=xf_cur: s.dma_start(
                            A(xf_cur, 0, [[NC * 512, 128], [512, NC], [1, 512]]),
                            A(sc_xft, tb * D * S + tc * 512, [[S, 128], [128 * S, NC], [1, 512]])),
                            inc=lxs, incval=16)
                        P.wait("tensor", lxs, lxs.n)
                        for hd in range(16):
                            ph_ = p_hb[hd % 4]
                            P.wait("tensor", s_a, marks["relu"][hd % 4])
                            if tc == 0 and hd == 0:
                                P.wait("tensor", s_ld, ld_all)
                            for f in range(NC):
                                P.emit("tensor", lambda t, hd=hd, f=f, ph_=ph_, w1h=w1h, xf_cur=xf_cur: t.matmul(
                                    ph_[:, :], w1h[:, f * 2048 + hd * 128: f * 2048 + (hd + 1) * 128],
                                    xf_cur[:, f * 512:(f + 1) * 512],
                                    start=(f == 0), stop=(f == NC - 1)), inc=(s_t if f == NC - 1 else None))
                            if tc == 3 and hd % 4 == 3:
                                marks["w1q"][hd // 4] = s_t.n
                            if hd == 15:
                                marks["xfs"][tc % 2] = s_t.n
                            P.wait("scalar", s_t)
                            P.emit("scalar", lambda a, e=e, dh=dh, hd=hd, ph_=ph_, h_sb=h_sb: a.activation(
                                h_sb[:, hd * 512:(hd + 1) * 512], ph_[:, :], AF.Relu,
                                bias=b1_c[:, e * 32 + dh * 16 + hd: e * 32 + dh * 16 + hd + 1]), inc=s_a)
                            marks["relu"][hd % 4] = s_a.n
                        relu_all = s_a.n
                        P.wait("tensor", s_a, relu_all)
                        for ti in range(4):
                            pe = p_e1 if ti % 2 == 0 else p_e2
                            P.wait("tensor", s_v, marks["stt"][ti % 2])
                            if dh == 0 and tc == 0 and ti == 0:
                                P.wait("tensor", s_lb, lb_mark)
                            for hd in range(16):
                                for hf in range(2):
                                    stop = (dh == 1 and hd == 15)
                                    P.emit("tensor", lambda t, hd=hd, hf=hf, ti=ti, pe=pe, h_sb=h_sb, w2h=w2h, stop=stop: t.matmul(
                                        pe[:, hf * 512:(hf + 1) * 512],
                                        h_sb[:, hd * 512 + ti * 128: hd * 512 + (ti + 1) * 128],
                                        w2h[:, hd * 1024 + hf * 512: hd * 1024 + (hf + 1) * 512],
                                        start=(hd == 0), stop=stop),
                                        inc=(s_t if (dh == 1 and hd == 15 and hf == 1) else None))
                                if tc == 3 and ti == 3 and hd % 4 == 3:
                                    pass
                            if dh == 0:
                                for hf in range(2):
                                    P.emit("tensor", lambda t, e=e, hf=hf, pe=pe: t.matmul(
                                        pe[:, hf * 512:(hf + 1) * 512], ones_r[0:1, :],
                                        b2_r[0:1, hf * 512:(hf + 1) * 512],
                                        start=False, stop=True), inc=(s_t if hf == 1 else None))
                            if tc == 3 and ti == 3:
                                for q in range(4):
                                    marks["w2q"][q] = s_t.n
                            if dh == 0 and tc == 3 and ti == 3:
                                marks["b2"] = s_t.n
                            eo_done = s_t.n
                            P.wait("vector", s_t, eo_done)
                            tix = tc * 4 + ti
                            P.emit("vector", lambda v, e=e, tb=tb, tix=tix, pe=pe, acc=acc: v.scalar_tensor_tensor(
                                acc[:, tix * D:(tix + 1) * D], pe[:, :],
                                gates_t[:, (tb * NT + tix) * E + e:(tb * NT + tix) * E + e + 1],
                                acc[:, tix * D:(tix + 1) * D], OP.mult, OP.add), inc=s_v)
                            marks["stt"][ti % 2] = s_v.n
            P.barrier(("tensor", "vector", "sync"))
            P.emit("sync", lambda s, tb=tb, acc=acc: s.dma_start(
                A(y_out, tb * S * D, [[D, 128], [128 * D, NT], [1, D]]),
                A(acc, 0, [[NT * D, 128], [D, NT], [1, D]])), inc=s_st, incval=16)
        P.wait("sync", s_st)
        P.barrier()

    stack.close()
    P.replay()
    P.close()
    return nc


def _shard_inputs(inputs):
    x = np.ascontiguousarray(inputs["L0_attn_x"], dtype=np.float32)
    ren = {
        "aWq": "L0_attn_Wq", "abq": "L0_attn_bq", "aWk": "L0_attn_Wk", "abk": "L0_attn_bk",
        "aWv": "L0_attn_Wv", "abv": "L0_attn_bv", "aWc2q": "L0_attn_Wc2q", "abc2q": "L0_attn_bc2q",
        "aWo": "L0_attn_Wo", "abo": "L0_attn_bo", "ag1": "L0_attn_g1", "abeta1": "L0_attn_beta1",
        "ag2": "L0_attn_g2", "abeta2": "L0_attn_beta2", "aW1": "L0_attn_W1", "ab1": "L0_attn_b1",
        "aWm": "L0_attn_Wm", "abm": "L0_attn_bm", "aWqo": "L0_attn_Wqo", "abqo": "L0_attn_bqo",
        "aW2": "L0_attn_W2", "ab2": "L0_attn_b2",
        "fWq": "L0_ffn_Wq", "fbq": "L0_ffn_bq", "fWk": "L0_ffn_Wk", "fbk": "L0_ffn_bk",
        "fWv": "L0_ffn_Wv", "fbv": "L0_ffn_bv", "fWo": "L0_ffn_Wo", "fbo": "L0_ffn_bo",
        "fWg": "L0_ffn_Wg", "fbg": "L0_ffn_bg", "fW1": "L0_ffn_W1", "fb1": "L0_ffn_b1",
        "fW2": "L0_ffn_W2", "fb2": "L0_ffn_b2",
    }
    w = {k: np.ascontiguousarray(inputs[src], dtype=np.float32) for k, src in ren.items()}
    in_maps = []
    for c in range(8):
        m = {"x": x[2 * c:2 * c + 2]}
        m.update(w)
        in_maps.append(m)
    return in_maps


def kernel(**inputs):
    in_maps = _shard_inputs(inputs)
    nc = build()
    res = run_bass_kernel_spmd(nc, in_maps, core_ids=list(range(8)))
    return np.concatenate([r["y"] for r in res.results], axis=0)


# version 14
# speedup vs baseline: 1.0252x; 1.0252x over previous
import contextlib
import types
import numpy as np
import concourse.bass as bass
import concourse.mybir as mybir
from concourse.bass_utils import run_bass_kernel_spmd

F32 = mybir.dt.float32
BF16 = mybir.dt.bfloat16
AF = mybir.ActivationFunctionType
OP = mybir.AluOpType
AXX = mybir.AxisListType.X

PI_LO = float(np.nextafter(np.float32(np.pi), np.float32(0)))
PI8_LO = float(np.nextafter(np.float32(np.pi / 8), np.float32(0)))
ISQHD = float(np.float32(1.0 / np.sqrt(128.0)))
INV32 = float(np.float32(1.0 / 32.0))
LN_EPS = 1e-5

B_LOC, S, D, H, HD, NQ, FF, E = 2, 2048, 1024, 8, 128, 8, 4096, 8
NT = S // 128
NC = D // 128
ENGS = ("tensor", "vector", "scalar", "gpsimd", "sync")


class Sem:
    __slots__ = ("h", "n", "name")

    def __init__(self, h, name):
        self.h, self.n, self.name = h, 0, name


class Prog:
    def __init__(self, nc):
        self.nc = nc
        self.q = {e: [] for e in ENGS}
        self._cms = []
        self.sems = {}

    def sem(self, name):
        if name not in self.sems:
            cm = self.nc.semaphore(name)
            self.sems[name] = Sem(cm.__enter__(), name)
            self._cms.append(cm)
        return self.sems[name]

    @staticmethod
    def _freeze(fn):
        if fn.__closure__:
            cells = tuple(types.CellType(c.cell_contents) for c in fn.__closure__)
            return types.FunctionType(fn.__code__, fn.__globals__, fn.__name__,
                                      fn.__defaults__, cells)
        return fn

    def emit(self, eng, fn, inc=None, incval=1):
        fn = self._freeze(fn)
        if inc is not None:
            inc.n += incval

        def op(h, fn=fn, inc=inc, incval=incval):
            r = fn(h)
            if inc is not None:
                r.then_inc(inc.h, incval)
        self.q[eng].append(op)

    def wait(self, eng, sm, v=None):
        v = sm.n if v is None else v
        self.q[eng].append(lambda h, sm=sm, v=v: h.wait_ge(sm.h, v))

    dma_sems = ()

    def barrier(self, engs=ENGS):
        bar = self.sem("bar_" + "".join(sorted(e[0] + e[1] for e in engs)))
        for e in engs:
            for sm in self.dma_sems:
                self.wait(e, sm)
            self.q[e].append(lambda h: h.drain())
            self.q[e].append(lambda h, b=bar: h.drain().then_inc(b.h, 1))
        bar.n += len(engs)
        tgt = bar.n
        for e in engs:
            self.q[e].append(lambda h, b=bar, t=tgt: h.wait_ge(b.h, t))

    def chain(self, eng, fns, sem, final_inc=True):
        for i, fn in enumerate(fns):
            if i > 0:
                self.wait(eng, sem)
            last = (i == len(fns) - 1)
            self.emit(eng, fn, inc=(sem if (not last or final_inc) else None))

    def replay(self):
        with self.nc.Block() as block:
            @block.tensor
            def _(t):
                for op in self.q["tensor"]:
                    op(t)

            @block.vector
            def _(v):
                for op in self.q["vector"]:
                    op(v)

            @block.scalar
            def _(a):
                for op in self.q["scalar"]:
                    op(a)

            @block.gpsimd
            def _(g):
                for op in self.q["gpsimd"]:
                    op(g)

            @block.sync
            def _(s):
                for op in self.q["sync"]:
                    op(s)

    def close(self):
        for cm in reversed(self._cms):
            cm.__exit__(None, None, None)


def A(handle, offset, dims):
    return bass.AP(handle, int(offset), [list(d) for d in dims])


def build(stage=None, taps=()):
    nc = bass.Bass("TRN2", target_bir_lowering=False)
    P = Prog(nc)
    stack = contextlib.ExitStack()

    def sbuf(name, shape, dtype):
        return stack.enter_context(nc.sbuf_tensor(name, shape, dtype))

    x_in = nc.dram_tensor("x", [B_LOC, S, D], F32, kind="ExternalInput")
    y_out = nc.dram_tensor("y", [B_LOC, S, D], F32, kind="ExternalOutput")
    wshape = {
        "aWq": [H, D, HD], "abq": [H, HD], "aWk": [H, D, HD], "abk": [H, HD],
        "aWv": [H, D, HD], "abv": [H, HD], "aWc2q": [H, HD, NQ], "abc2q": [H, NQ],
        "aWo": [D, D], "abo": [D], "ag1": [D], "abeta1": [D], "ag2": [D], "abeta2": [D],
        "aW1": [D, FF], "ab1": [FF], "aWm": [FF, NQ], "abm": [NQ],
        "aWqo": [NQ, FF], "abqo": [FF], "aW2": [FF, D], "ab2": [D],
        "fWq": [D, D], "fbq": [D], "fWk": [D, D], "fbk": [D],
        "fWv": [D, D], "fbv": [D], "fWo": [D, D], "fbo": [D],
        "fWg": [D, E], "fbg": [E], "fW1": [E, D, FF], "fb1": [E, FF],
        "fW2": [E, FF, D], "fb2": [E, D],
    }
    W = {k: nc.dram_tensor(k, v, F32, kind="ExternalInput") for k, v in wshape.items()}
    sc_xnt = nc.dram_tensor("sc_xnt", [B_LOC, D, S], BF16)
    sc_xn32 = nc.dram_tensor("sc_xn32", [B_LOC, D, S], F32)
    sc_aout = nc.dram_tensor("sc_aout", [B_LOC, S, D], F32)
    sc_xf32 = nc.dram_tensor("sc_xf32", [B_LOC, S, D], F32)
    sc_xft = nc.dram_tensor("sc_xft", [B_LOC, D, S], BF16)
    sc_vtok = nc.dram_tensor("sc_vtok", [B_LOC, 128, NT * D], BF16)

    s_ld = P.sem("ld")
    s_g = P.sem("g")
    s_lh = P.sem("lh")
    s_st = P.sem("st")
    s_t = P.sem("t")
    s_v = P.sem("v")
    s_a = P.sem("a")
    s_lx = P.sem("lx")
    s_lx2 = P.sem("lx2")
    s_lv = P.sem("lv")
    s_lv2 = P.sem("lv2")
    s_lb = P.sem("lb")
    s_pt = P.sem("pt")
    P.dma_sems = (s_ld, s_lh, s_st, s_lx, s_lx2, s_lv, s_lv2, s_lb, s_pt)

    tapset = set(taps)
    tap_req = []

    def tap(name, buf, rows, cols, dtype):
        if name in tapset:
            t = nc.dram_tensor("tap_" + name, [rows, cols], dtype, kind="ExternalOutput")
            tap_req.append((t, buf, rows, cols))

    open_spans = []

    def finish_stage():
        P.barrier()
        for (t, buf, rows, cols) in tap_req:
            P.emit("sync", lambda s, t=t, buf=buf, rows=rows, cols=cols: s.dma_start(
                A(t, 0, [[cols, rows], [1, cols]]),
                A(buf, 0, [[int(buf.shape[1]), rows], [1, cols]])), inc=s_st, incval=16)
        P.wait("sync", s_st)
        P.barrier()
        for sp in reversed(open_spans):
            sp.close()
        stack.close()
        P.replay()
        P.close()
        return nc

    ident_bf = sbuf("ident_bf", [128, 128], BF16)
    ident_f = sbuf("ident_f", [128, 128], F32)
    ones_c = sbuf("ones_c", [128, 1], F32)
    ones_r = sbuf("ones_r", [33, 128], F32)
    iota_r = sbuf("iota_r", [128, 128], F32)
    pidx = sbuf("pidx", [128, 1], F32)

    bq_c = sbuf("bq_c", [128, H], F32)
    bk_c = sbuf("bk_c", [128, H], F32)
    vb_r = sbuf("vb_r", [33, D], F32)
    bc2q_r = sbuf("bc2q_r", [1, H * NQ], F32)
    bc2q_rq = sbuf("bc2q_rq", [1, H * NQ], F32)
    bo_c = sbuf("bo_c", [128, NC], F32)
    g1_c = sbuf("g1_c", [128, NC], F32)
    be1_c = sbuf("be1_c", [128, NC], F32)
    g2_c = sbuf("g2_c", [128, NC], F32)
    be2_c = sbuf("be2_c", [128, NC], F32)
    fbq_c = sbuf("fbq_c", [128, NC], F32)
    fbk_c = sbuf("fbk_c", [128, NC], F32)
    fbo_c = sbuf("fbo_c", [128, NC], F32)
    fbg_c = sbuf("fbg_c", [E, 1], F32)
    b1_c = sbuf("b1_c", [128, E * 32], F32)
    m1_bf = sbuf("m1_bf", [128, NC * NQ], BF16)
    bias1_c = sbuf("bias1_c", [NQ, 1], F32)
    m2_bf = sbuf("m2_bf", [NQ, D], BF16)
    bias2_c = sbuf("bias2_c", [128, NC], F32)
    wc2q_bf = sbuf("wc2q_bf", [128, H * NQ], BF16)
    wg_bf = sbuf("wg_bf", [128, NC * E], BF16)
    gates_t = sbuf("gates_t", [128, B_LOC * NT * E], F32)

    st_a = sbuf("st_a", [128, 32], F32)
    st_b = sbuf("st_b", [128, 32], F32)
    st_c = sbuf("st_c", [128, 32], F32)

    P.emit("gpsimd", lambda g: g.iota(iota_r[:, :], [[1, 128]], channel_multiplier=0,
                                      allow_small_or_imprecise_dtypes=True))
    P.emit("gpsimd", lambda g: g.iota(pidx[:, :], [[1, 1]], channel_multiplier=1,
                                      allow_small_or_imprecise_dtypes=True))
    P.emit("gpsimd", lambda g: g.memset(ones_c[:, :], 1.0))
    P.emit("gpsimd", lambda g: g.memset(ones_r[:, :], 1.0), inc=s_g)

    def ldpar(dst, w, dims, eng="sync", off=0):
        sem = s_lh if eng == "sync" else s_ld
        P.emit(eng, lambda h, dst=dst, w=w, dims=dims, off=off: h.dma_start(
            dst, A(W[w], off, dims), allow_slow_non_contiguous=True), inc=sem, incval=16)

    ldpar(bo_c[:, :], "abo", [[1, 128], [128, NC]])
    ldpar(g1_c[:, :], "ag1", [[1, 128], [128, NC]])
    ldpar(be1_c[:, :], "abeta1", [[1, 128], [128, NC]])
    ldpar(g2_c[:, :], "ag2", [[1, 128], [128, NC]])
    ldpar(be2_c[:, :], "abeta2", [[1, 128], [128, NC]])
    ldpar(fbq_c[:, :], "fbq", [[1, 128], [128, NC]])
    ldpar(fbk_c[:, :], "fbk", [[1, 128], [128, NC]])
    ldpar(fbo_c[:, :], "fbo", [[1, 128], [128, NC]])
    ldpar(bq_c[:, :], "abq", [[1, 128], [128, H]])
    ldpar(bk_c[:, :], "abk", [[1, 128], [128, H]])
    ldpar(vb_r[0:1, :], "abv", [[D, 1], [1, D]])
    ldpar(bc2q_r[0:1, :], "abc2q", [[H * NQ, 1], [1, H * NQ]])
    ldpar(fbg_c[0:E, 0:1], "fbg", [[1, E], [1, 1]])
    ldpar(vb_r[32:33, :], "fbv", [[D, 1], [1, D]])
    ldpar(b1_c[:, :], "fb1", [[1, 128], [FF, E], [128, 32]])
    ldpar(wc2q_bf[:, :], "aWc2q", [[NQ, 128], [128 * NQ, H], [1, NQ]], eng="gpsimd")
    ldpar(wg_bf[:, :], "fWg", [[E, 128], [128 * E, NC], [1, E]], eng="gpsimd")
    P.barrier()
    P.emit("vector", lambda v: v.tensor_scalar(ident_bf[:, :], iota_r[:, :], pidx[:, 0:1], None, OP.is_equal))
    P.emit("vector", lambda v: v.tensor_scalar(ident_f[:, :], iota_r[:, :], pidx[:, 0:1], None, OP.is_equal))
    P.emit("vector", lambda v: v.tensor_scalar(bc2q_rq[0:1, :], bc2q_r[0:1, :], ISQHD, None, OP.mult), inc=s_v)
    P.barrier()

    with contextlib.ExitStack() as pp:
        w1blk = pp.enter_context(nc.sbuf_tensor("w1blk", [128, 1024], BF16))
        w1t = pp.enter_context(nc.sbuf_tensor("w1t", [128, 1024], BF16))
        wm_bf = pp.enter_context(nc.sbuf_tensor("wm_bf", [128, 32 * NQ], BF16))
        wm_32 = pp.enter_context(nc.sbuf_tensor("wm_32", [128, 32 * NQ], F32))
        b1a_c = pp.enter_context(nc.sbuf_tensor("b1a_c", [128, 32], F32))
        wqo_s = pp.enter_context(nc.sbuf_tensor("wqo_s", [NQ, FF], BF16))
        wqoT = pp.enter_context(nc.sbuf_tensor("wqoT", [128, 32 * NQ], BF16))
        bqo_c = pp.enter_context(nc.sbuf_tensor("bqo_c", [128, 32], F32))
        w2blk = pp.enter_context(nc.sbuf_tensor("w2blk", [128, 1024], BF16))
        w2b32 = pp.enter_context(nc.sbuf_tensor("w2b32", [128, 1024], F32))
        m1row = pp.enter_context(nc.sbuf_tensor("m1row", [NQ, D], BF16))
        brow = pp.enter_context(nc.sbuf_tensor("brow", [1, D + NQ], F32))
        exrow = pp.enter_context(nc.sbuf_tensor("exrow", [1, D], F32))
        with contextlib.ExitStack() as pps:
            p_tr = pps.enter_context(nc.psum_tensor("pp_tr", [128, 1024], BF16))
            p_m1 = pps.enter_context(nc.psum_tensor("pp_m1", [NQ, 1024], F32))
            p_m2 = pps.enter_context(nc.psum_tensor("pp_m2", [NQ, 1024], F32))
            p_bb = pps.enter_context(nc.psum_tensor("pp_bb", [1, 1536], F32))

            ldpar(wm_bf[:, :], "aWm", [[NQ, 128], [128 * NQ, 32], [1, NQ]], eng="gpsimd")
            ldpar(wm_32[:, :], "aWm", [[NQ, 128], [128 * NQ, 32], [1, NQ]])
            ldpar(b1a_c[:, :], "ab1", [[1, 128], [128, 32]])
            ldpar(wqo_s[0:NQ, :], "aWqo", [[FF, NQ], [1, FF]], eng="gpsimd")
            ldpar(bqo_c[:, :], "abqo", [[1, 128], [128, 32]])
            P.barrier()
            for c in range(32):
                P.emit("gpsimd", lambda g, c=c: g.dma_start(
                    w1blk[:, :], A(W["aW1"], c * 128, [[FF, 128], [128 * FF, NC], [1, 128]])),
                    inc=s_ld, incval=16)
                P.wait("tensor", s_ld)
                for f in range(NC):
                    P.emit("tensor", lambda t, f=f: t.transpose(
                        p_tr[:, f * 128:(f + 1) * 128], w1blk[:, f * 128:(f + 1) * 128], ident_bf[:, :]),
                        inc=(s_t if f == NC - 1 else None))
                P.wait("vector", s_t)
                P.emit("vector", lambda v: v.tensor_copy(w1t[:, :], p_tr[:, :]), inc=s_v)
                P.wait("tensor", s_v)
                for hf in range(2):
                    P.emit("tensor", lambda t, c=c, hf=hf: t.matmul(
                        p_m1[0:NQ, hf * 512:(hf + 1) * 512], wm_bf[:, c * NQ:(c + 1) * NQ],
                        w1t[:, hf * 512:(hf + 1) * 512], start=(c == 0), stop=(c == 31)))
                P.emit("tensor", lambda t, c=c: t.matmul(
                    p_bb[0:1, 1024:1024 + NQ], b1a_c[:, c:c + 1], wm_32[:, c * NQ:(c + 1) * NQ],
                    start=(c == 0), stop=(c == 31)))
                P.emit("tensor", lambda t, c=c: t.transpose(
                    p_tr[:, 0:NQ], wqo_s[0:NQ, c * 128:(c + 1) * 128], ident_bf[0:NQ, 0:NQ]), inc=s_t)
                P.wait("vector", s_t)
                P.emit("vector", lambda v, c=c: v.tensor_copy(wqoT[:, c * NQ:(c + 1) * NQ], p_tr[:, 0:NQ]), inc=s_v)
                P.barrier(("tensor", "vector", "gpsimd"))
            for c in range(32):
                P.emit("gpsimd", lambda g, c=c: g.dma_start(
                    w2blk[:, :], A(W["aW2"], c * 128 * D, [[D, 128], [1, D]])), inc=s_ld, incval=16)
                P.emit("sync", lambda s, c=c: s.dma_start(
                    w2b32[:, :], A(W["aW2"], c * 128 * D, [[D, 128], [1, D]])), inc=s_lh, incval=16)
                P.wait("tensor", s_ld)
                P.wait("tensor", s_lh)
                for hf in range(2):
                    P.emit("tensor", lambda t, c=c, hf=hf: t.matmul(
                        p_m2[0:NQ, hf * 512:(hf + 1) * 512], wqoT[:, c * NQ:(c + 1) * NQ],
                        w2blk[:, hf * 512:(hf + 1) * 512], start=(c == 0), stop=(c == 31)))
                for hf in range(2):
                    P.emit("tensor", lambda t, c=c, hf=hf: t.matmul(
                        p_bb[0:1, hf * 512:(hf + 1) * 512], bqo_c[:, c:c + 1],
                        w2b32[:, hf * 512:(hf + 1) * 512], start=(c == 0), stop=(c == 31)),
                        inc=(s_t if hf == 1 else None))
                P.barrier(("tensor", "gpsimd", "sync"))
            P.wait("vector", s_t)
            P.emit("vector", lambda v: v.tensor_copy(brow[0:1, 0:D], p_bb[0:1, 0:D]))
            P.emit("vector", lambda v: v.tensor_copy(brow[0:1, D:D + NQ], p_bb[0:1, 1024:1024 + NQ]))
            P.emit("vector", lambda v: v.tensor_copy(m1row[0:NQ, :], p_m1[0:NQ, :]))
            P.emit("vector", lambda v: v.tensor_copy(m2_bf[0:NQ, :], p_m2[0:NQ, :]), inc=s_v)
            ldpar(exrow[0:1, :], "ab2", [[D, 1], [1, D]])
            P.wait("vector", s_lh)
            P.wait("vector", s_v)
            P.emit("vector", lambda v: v.tensor_tensor(brow[0:1, 0:D], brow[0:1, 0:D], exrow[0:1, :], OP.add), inc=s_v)
            P.wait("sync", s_v)
            ldpar(exrow[0:1, 0:NQ], "abm", [[NQ, 1], [1, NQ]])
            P.wait("vector", s_lh)
            P.emit("vector", lambda v: v.tensor_tensor(brow[0:1, D:D + NQ], brow[0:1, D:D + NQ], exrow[0:1, 0:NQ], OP.add), inc=s_v)
            P.wait("tensor", s_v)
            P.barrier()
        with contextlib.ExitStack() as pps:
            p_tm = pps.enter_context(nc.psum_tensor("pp_tm", [128, NQ], F32))
            p_tmb = pps.enter_context(nc.psum_tensor("pp_tmb", [128, NQ], BF16))
            for c in range(NC):
                P.emit("tensor", lambda t, c=c: t.transpose(
                    p_tm[:, 0:1], brow[0:1, c * 128:(c + 1) * 128], ident_f[0:1, 0:1]), inc=s_t)
                P.wait("vector", s_t)
                P.emit("vector", lambda v, c=c: v.tensor_copy(bias2_c[:, c:c + 1], p_tm[:, 0:1]), inc=s_v)
                P.wait("tensor", s_v)
            P.emit("tensor", lambda t: t.transpose(p_tm[0:NQ, 0:1], brow[0:1, D:D + NQ], ident_f[0:1, 0:1]), inc=s_t)
            P.wait("vector", s_t)
            P.emit("vector", lambda v: v.tensor_copy(bias1_c[0:NQ, 0:1], p_tm[0:NQ, 0:1]), inc=s_v)
            P.wait("tensor", s_v)
            for c in range(NC):
                P.emit("tensor", lambda t, c=c: t.transpose(
                    p_tmb[:, 0:NQ], m1row[0:NQ, c * 128:(c + 1) * 128], ident_bf[0:NQ, 0:NQ]), inc=s_t)
                P.wait("vector", s_t)
                P.emit("vector", lambda v, c=c: v.tensor_copy(m1_bf[:, c * NQ:(c + 1) * NQ], p_tmb[:, 0:NQ]), inc=s_v)
                P.wait("tensor", s_v)
            P.barrier()

    if stage == "prep":
        tap("m1_bf", m1_bf, 128, NC * NQ, BF16)
        tap("m2_bf", m2_bf, NQ, D, BF16)
        tap("bias1_c", bias1_c, NQ, 1, F32)
        tap("bias2_c", bias2_c, 128, NC, F32)
        return finish_stage()

    def ln_fm(BUF, g_col, b_col, p_st, p_bc, sqtmp, rows):
        row_a, row_b, row_c = rows
        for qtr in range(4):
            for c in range(NC):
                P.emit("tensor", lambda t, qtr=qtr, c=c: t.matmul(
                    p_st[0:1, 0:512], ones_c[:, 0:1],
                    BUF[:, c * S + qtr * 512: c * S + qtr * 512 + 512],
                    start=(c == 0), stop=(c == NC - 1)), inc=(s_t if c == NC - 1 else None))
            P.wait("vector", s_t)
            P.emit("vector", lambda v, qtr=qtr: v.tensor_copy(
                row_a[0:1, qtr * 512:(qtr + 1) * 512], p_st[0:1, 0:512]), inc=s_v)
            sq_marks = {}
            for c in range(NC):
                half = c % 2
                P.wait("vector", s_t, sq_marks.get(c - 2, 0))
                P.emit("vector", lambda v, qtr=qtr, c=c, half=half: v.tensor_tensor(
                    sqtmp[:, half * 512:(half + 1) * 512],
                    BUF[:, c * S + qtr * 512: c * S + qtr * 512 + 512],
                    BUF[:, c * S + qtr * 512: c * S + qtr * 512 + 512], OP.mult), inc=s_v)
                P.wait("tensor", s_v)
                P.emit("tensor", lambda t, c=c, half=half: t.matmul(
                    p_st[0:1, 512:1024], ones_c[:, 0:1], sqtmp[:, half * 512:(half + 1) * 512],
                    start=(c == 0), stop=(c == NC - 1)), inc=s_t)
                sq_marks[c] = s_t.n
            P.wait("vector", s_t)
            P.emit("vector", lambda v, qtr=qtr: v.tensor_copy(
                row_b[0:1, qtr * 512:(qtr + 1) * 512], p_st[0:1, 512:1024]), inc=s_v)
        P.wait("vector", s_v)
        P.chain("vector", [
            lambda v: v.tensor_scalar(row_a[0:1, :], row_a[0:1, :], 1.0 / D, None, OP.mult),
            lambda v: v.tensor_scalar(row_b[0:1, :], row_b[0:1, :], 1.0 / D, None, OP.mult),
            lambda v: v.tensor_tensor(row_c[0:1, :], row_a[0:1, :], row_a[0:1, :], OP.mult),
            lambda v: v.tensor_tensor(row_b[0:1, :], row_b[0:1, :], row_c[0:1, :], OP.subtract),
            lambda v: v.tensor_scalar(row_b[0:1, :], row_b[0:1, :], LN_EPS, None, OP.add),
        ], s_v)
        P.wait("scalar", s_v)
        P.emit("scalar", lambda a: a.activation(row_c[0:1, :], row_b[0:1, :], AF.Sqrt), inc=s_a)
        P.wait("vector", s_a)
        P.chain("vector", [
            lambda v: v.reciprocal(row_b[0:1, :], row_c[0:1, :]),
            lambda v: v.tensor_tensor(row_c[0:1, :], row_c[0:1, :], row_b[0:1, :], OP.mult),
            lambda v: v.tensor_scalar(row_c[0:1, :], row_c[0:1, :], -1.0, 2.0, OP.mult, OP.add),
            lambda v: v.tensor_tensor(row_b[0:1, :], row_b[0:1, :], row_c[0:1, :], OP.mult),
            lambda v: v.tensor_tensor(row_a[0:1, :], row_a[0:1, :], row_b[0:1, :], OP.mult),
        ], s_v)
        for qtr in range(4):
            P.wait("tensor", s_v)
            P.emit("tensor", lambda t, qtr=qtr: t.matmul(
                p_bc[:, 0:512], ones_r[0:1, :], row_b[0:1, qtr * 512:(qtr + 1) * 512], start=True, stop=True))
            P.emit("tensor", lambda t, qtr=qtr: t.matmul(
                p_bc[:, 512:1024], ones_r[0:1, :], row_a[0:1, qtr * 512:(qtr + 1) * 512], start=True, stop=True),
                inc=s_t)
            P.wait("vector", s_t)
            for c in range(NC):
                sl = BUF[:, c * S + qtr * 512: c * S + qtr * 512 + 512]
                P.emit("vector", lambda v, sl=sl: v.tensor_tensor(sl, sl, p_bc[:, 0:512], OP.mult),
                       inc=(s_v if c == NC - 1 else None))
            P.wait("vector", s_v)
            for c in range(NC):
                sl = BUF[:, c * S + qtr * 512: c * S + qtr * 512 + 512]
                P.emit("vector", lambda v, sl=sl: v.tensor_tensor(sl, sl, p_bc[:, 512:1024], OP.subtract),
                       inc=(s_v if c == NC - 1 else None))
            if g_col is not None:
                P.wait("vector", s_v)
                for c in range(NC):
                    sl = BUF[:, c * S + qtr * 512: c * S + qtr * 512 + 512]
                    P.emit("vector", lambda v, sl=sl, c=c: v.tensor_scalar(
                        sl, sl, g_col[:, c:c + 1], b_col[:, c:c + 1], OP.mult, OP.add),
                        inc=(s_v if c == NC - 1 else None))
            P.barrier(("tensor", "vector"))

    def spill_tok(BUF, dram, dram_off, p_t1, p_t2, stg):
        for ti in range(NT):
            P.wait("vector", s_st, s_st.n)
            P.wait("tensor", s_v)
            for r in range(2):
                pb = p_t1 if r == 0 else p_t2
                for c in range(4):
                    P.emit("tensor", lambda t, ti=ti, r=r, c=c, pb=pb: t.transpose(
                        pb[:, c * 128:(c + 1) * 128],
                        BUF[:, (4 * r + c) * S + ti * 128: (4 * r + c) * S + ti * 128 + 128],
                        ident_f[:, :]), inc=(s_t if c == 3 else None))
                P.wait("vector", s_t)
                P.emit("vector", lambda v, r=r, pb=pb, stg=stg: v.tensor_copy(
                    stg[:, r * 512:(r + 1) * 512], pb[:, :]), inc=s_v)
            P.wait("sync", s_v)
            P.emit("sync", lambda s, ti=ti, dram=dram, dram_off=dram_off, stg=stg: s.dma_start(
                A(dram, dram_off + ti * 128 * D, [[D, 128], [128, NC], [1, 128]]),
                A(stg, 0, [[D, 128], [128, NC], [1, 128]])), inc=s_st, incval=16)
        P.barrier(("tensor", "vector", "sync"))

    for b in range(B_LOC):
        span_xnt = contextlib.ExitStack()
        open_spans.append(span_xnt)
        span_qkv = contextlib.ExitStack()
        open_spans.append(span_qkv)
        span_at = contextlib.ExitStack()
        open_spans.append(span_at)
        XNT = span_xnt.enter_context(nc.sbuf_tensor(f"XNT_{b}", [128, NC * S], BF16, side="right"))
        with contextlib.ExitStack() as ph:
            xtok = ph.enter_context(nc.sbuf_tensor(f"xtok_{b}", [128, NT * D], F32))
            xn32 = ph.enter_context(nc.sbuf_tensor(f"xn32_{b}", [128, D], F32))
            sqd = ph.enter_context(nc.sbuf_tensor(f"sqd_{b}", [128, 2 * D], F32))
            stg = ph.enter_context(nc.sbuf_tensor(f"stg_{b}", [128, D], F32))
            p_trf = ph.enter_context(nc.psum_tensor(f"p0_trf_{b}", [128, 512], F32))
            p_trf2 = ph.enter_context(nc.psum_tensor(f"p0_trf2_{b}", [128, 512], F32))

            P.emit("sync", lambda s, b=b: s.dma_start(
                xtok[:, :], A(x_in, b * S * D, [[D, 128], [128 * D, NT], [1, D]])),
                inc=s_lh, incval=16)
            P.barrier()
            sq_marks = {}
            for ti in range(NT):
                P.emit("vector", lambda v, ti=ti: v.reduce_sum(
                    st_a[:, ti:ti + 1], xtok[:, ti * D:(ti + 1) * D], axis=AXX), inc=s_v)
                if ti >= 2:
                    P.wait("scalar", s_a, sq_marks[ti - 2])
                P.emit("scalar", lambda a, ti=ti: a.activation(
                    sqd[:, (ti % 2) * D:(ti % 2 + 1) * D], xtok[:, ti * D:(ti + 1) * D], AF.Square,
                    accum_out=st_b[:, ti:ti + 1]), inc=s_a)
                sq_marks[ti] = s_a.n
            P.wait("vector", s_a)
            P.wait("vector", s_v)
            P.chain("vector", [
                lambda v: v.tensor_scalar(st_a[:, 0:NT], st_a[:, 0:NT], 1.0 / D, None, OP.mult),
                lambda v: v.tensor_scalar(st_b[:, 0:NT], st_b[:, 0:NT], 1.0 / D, None, OP.mult),
                lambda v: v.tensor_tensor(st_c[:, 0:NT], st_a[:, 0:NT], st_a[:, 0:NT], OP.mult),
                lambda v: v.tensor_tensor(st_b[:, 0:NT], st_b[:, 0:NT], st_c[:, 0:NT], OP.subtract),
                lambda v: v.tensor_scalar(st_b[:, 0:NT], st_b[:, 0:NT], LN_EPS, None, OP.add),
            ], s_v)
            P.wait("scalar", s_v)
            P.emit("scalar", lambda a: a.activation(st_c[:, 0:NT], st_b[:, 0:NT], AF.Sqrt), inc=s_a)
            P.wait("vector", s_a)
            P.chain("vector", [
                lambda v: v.reciprocal(st_b[:, 0:NT], st_c[:, 0:NT]),
                lambda v: v.tensor_tensor(st_c[:, 0:NT], st_c[:, 0:NT], st_b[:, 0:NT], OP.mult),
                lambda v: v.tensor_scalar(st_c[:, 0:NT], st_c[:, 0:NT], -1.0, 2.0, OP.mult, OP.add),
                lambda v: v.tensor_tensor(st_b[:, 0:NT], st_b[:, 0:NT], st_c[:, 0:NT], OP.mult),
            ], s_v)
            for ti in range(NT):
                P.wait("vector", s_v)
                P.emit("vector", lambda v, ti=ti: v.tensor_scalar(
                    xn32[:, :], xtok[:, ti * D:(ti + 1) * D], st_a[:, ti:ti + 1], st_b[:, ti:ti + 1],
                    OP.subtract, OP.mult), inc=s_v)
                P.wait("tensor", s_v)
                for r in range(2):
                    pb = p_trf if r == 0 else p_trf2
                    for c in range(4):
                        P.emit("tensor", lambda t, r=r, c=c, pb=pb: t.transpose(
                            pb[:, c * 128:(c + 1) * 128],
                            xn32[:, r * 512 + c * 128: r * 512 + (c + 1) * 128], ident_f[:, :]),
                            inc=(s_t if c == 3 else None))
                    P.wait("vector", s_t)
                    P.emit("vector", lambda v, r=r, pb=pb, stg=stg: v.tensor_copy(stg[:, r * 512:(r + 1) * 512], pb[:, :]))
                    P.emit("vector", lambda v, r=r, ti=ti, pb=pb, XNT=XNT: v.tensor_copy(
                        A(XNT, (4 * r) * S + ti * 128, [[NC * S, 128], [S, 4], [1, 128]]),
                        pb[:, :]), inc=s_v)
                P.wait("sync", s_v)
                P.emit("sync", lambda s, b=b, ti=ti, stg=stg: s.dma_start(
                    A(sc_xn32, b * D * S + ti * 128, [[S, 128], [128 * S, NC], [1, 128]]),
                    A(stg, 0, [[D, 128], [128, NC], [1, 128]])), inc=s_st, incval=16)
                P.barrier(("tensor", "vector", "sync"))
            P.emit("sync", lambda s, b=b, XNT=XNT: s.dma_start(
                A(sc_xnt, b * D * S, [[S, 128], [128 * S, NC], [1, S]]),
                A(XNT, 0, [[NC * S, 128], [S, NC], [1, S]])), inc=s_st, incval=16)
            P.wait("sync", s_st)
            P.barrier()

        if stage == "p0" and b == 0:
            tap("XNT", XNT, 128, NC * S, BF16)
            return finish_stage()

        if True:
            QT = span_qkv.enter_context(nc.sbuf_tensor(f"QT_{b}", [128, H * S], BF16))
            KT = span_qkv.enter_context(nc.sbuf_tensor(f"KT_{b}", [128, H * S], BF16))
            span_vt = contextlib.ExitStack()
            open_spans.append(span_vt)
            VTOK = span_vt.enter_context(nc.sbuf_tensor(f"VTOK_{b}", [128, NT * D], BF16, side="right"))
            qqh_t = span_qkv.enter_context(nc.sbuf_tensor(f"qqh_t_{b}", [128, NT * H * NQ], BF16))
            kqh_t = span_qkv.enter_context(nc.sbuf_tensor(f"kqh_t_{b}", [128, NT * H * NQ], BF16))
            qqT_h = span_qkv.enter_context(nc.sbuf_tensor(f"qqT_h_{b}", [NQ, S], BF16))
            kqT_h = span_qkv.enter_context(nc.sbuf_tensor(f"kqT_h_{b}", [NQ, S], BF16))
            with contextlib.ExitStack() as p1:
                wpro = p1.enter_context(nc.sbuf_tensor(f"wpro_{b}", [128, H * NC * 128], BF16))
                qq_t = p1.enter_context(nc.sbuf_tensor(f"qq_t_{b}", [128, NT * H * NQ], F32))
                kq_t = p1.enter_context(nc.sbuf_tensor(f"kq_t_{b}", [128, NT * H * NQ], F32))
                qn2 = p1.enter_context(nc.sbuf_tensor(f"qn2_{b}", [128, NT * H], F32))
                kn2 = p1.enter_context(nc.sbuf_tensor(f"kn2_{b}", [128, NT * H], F32))
                sq_s = p1.enter_context(nc.sbuf_tensor(f"sq_s_{b}", [128, H * NQ], F32))
                p_a = p1.enter_context(nc.psum_tensor(f"p1_a_{b}", [128, 512], F32))
                p_b = p1.enter_context(nc.psum_tensor(f"p1_b_{b}", [128, 512], F32))
                p_c = p1.enter_context(nc.psum_tensor(f"p1_c_{b}", [128, 512], F32))
                p_d = p1.enter_context(nc.psum_tensor(f"p1_d_{b}", [128, 512], F32))
                p_v = p1.enter_context(nc.psum_tensor(f"p1_v_{b}", [128, 1024], F32))
                p_q = p1.enter_context(nc.psum_tensor(f"p1_q_{b}", [128, H * NQ], F32))
                pbank = (p_a, p_b, p_c, p_d)

                for (wname, bcol, OUT, scale) in (("aWq", bq_c, QT, ISQHD), ("aWk", bk_c, KT, None)):
                    P.barrier(("tensor", "gpsimd", "vector"))
                    P.emit("gpsimd", lambda g, wname=wname: g.dma_start(
                        wpro[:, :], A(W[wname], 0, [[HD, 128], [D * HD, H], [128 * HD, NC], [1, 128]])),
                        inc=s_ld, incval=16)
                    P.wait("tensor", s_ld)
                    for h in range(H):
                        P.wait("tensor", s_v)
                        for tc in range(4):
                            for f in range(NC):
                                P.emit("tensor", lambda t, h=h, tc=tc, f=f: t.matmul(
                                    pbank[tc][:, :], wpro[:, (h * NC + f) * 128:(h * NC + f + 1) * 128],
                                    XNT[:, f * S + tc * 512: f * S + (tc + 1) * 512],
                                    start=(f == 0), stop=(f == NC - 1)),
                                    inc=(s_t if f == NC - 1 else None))
                        P.wait("vector", s_t)
                        for tc in range(4):
                            if scale is None:
                                P.emit("vector", lambda v, h=h, tc=tc, OUT=OUT, bcol=bcol: v.tensor_scalar(
                                    OUT[:, h * S + tc * 512: h * S + (tc + 1) * 512],
                                    pbank[tc][:, :], bcol[:, h:h + 1], None, OP.add),
                                    inc=(s_v if tc == 3 else None))
                            else:
                                P.emit("vector", lambda v, h=h, tc=tc, OUT=OUT, bcol=bcol, scale=scale: v.tensor_scalar(
                                    OUT[:, h * S + tc * 512: h * S + (tc + 1) * 512],
                                    pbank[tc][:, :], bcol[:, h:h + 1], scale, OP.add, OP.mult),
                                    inc=(s_v if tc == 3 else None))
                P.barrier(("tensor", "vector", "gpsimd"))
                for f in range(NC):
                    P.emit("gpsimd", lambda g, f=f: g.dma_start(
                        wpro[:, f * 1024:(f + 1) * 1024],
                        A(W["aWv"], f * 128 * HD, [[HD, 128], [D * HD, H], [1, 128]])),
                        inc=s_ld, incval=16)
                P.barrier(("tensor", "gpsimd", "vector"))
                for ti in range(NT):
                    P.wait("tensor", s_v)
                    for f in range(NC):
                        for hf in range(2):
                            P.emit("tensor", lambda t, ti=ti, f=f, hf=hf: t.matmul(
                                p_v[:, hf * 512:(hf + 1) * 512],
                                XNT[:, f * S + ti * 128: f * S + ti * 128 + 128],
                                wpro[:, f * 1024 + hf * 512: f * 1024 + (hf + 1) * 512],
                                start=(f == 0), stop=False))
                    for hf in range(2):
                        P.emit("tensor", lambda t, hf=hf: t.matmul(
                            p_v[:, hf * 512:(hf + 1) * 512], ones_r[0:1, :],
                            vb_r[0:1, hf * 512:(hf + 1) * 512], start=False, stop=True),
                            inc=(s_t if hf == 1 else None))
                    P.wait("vector", s_t)
                    P.emit("vector", lambda v, ti=ti, VTOK=VTOK: v.tensor_copy(
                        VTOK[:, ti * D:(ti + 1) * D], p_v[:, :]), inc=s_v)
                P.barrier(("tensor", "vector", "gpsimd"))
                for (QKT, bias_row, qq_buf) in ((QT, bc2q_rq, qq_t), (KT, bc2q_r, kq_t)):
                    for ti in range(NT):
                        P.wait("tensor", s_v)
                        for h in range(H):
                            P.emit("tensor", lambda t, ti=ti, h=h, QKT=QKT: t.matmul(
                                p_q[:, h * NQ:(h + 1) * NQ],
                                QKT[:, h * S + ti * 128: h * S + ti * 128 + 128],
                                wc2q_bf[:, h * NQ:(h + 1) * NQ], start=(h == 0), stop=False))
                        P.emit("tensor", lambda t, bias_row=bias_row: t.matmul(
                            p_q[:, :], ones_r[0:1, :], bias_row[0:1, :], start=False, stop=True), inc=s_t)
                        P.wait("vector", s_t)
                        P.emit("vector", lambda v, ti=ti, qq_buf=qq_buf: v.tensor_copy(
                            qq_buf[:, ti * H * NQ:(ti + 1) * H * NQ], p_q[:, :]), inc=s_v)
                P.barrier(("tensor", "vector"))
                for (qq_buf, n2_buf) in ((qq_t, qn2), (kq_t, kn2)):
                    for ti in range(NT):
                        P.wait("vector", s_v)
                        P.emit("vector", lambda v, ti=ti, qq_buf=qq_buf: v.tensor_tensor(
                            sq_s[:, :], qq_buf[:, ti * H * NQ:(ti + 1) * H * NQ],
                            qq_buf[:, ti * H * NQ:(ti + 1) * H * NQ], OP.mult), inc=s_v)
                        P.wait("vector", s_v)
                        P.emit("vector", lambda v, ti=ti, n2_buf=n2_buf: v.reduce_sum(
                            n2_buf[:, ti * H:(ti + 1) * H],
                            A(sq_s, 0, [[H * NQ, 128], [NQ, H], [1, NQ]]), axis=AXX), inc=s_v)
                for n2_buf in (qn2, kn2):
                    P.wait("scalar", s_v)
                    P.emit("scalar", lambda a, n2_buf=n2_buf: a.activation(
                        n2_buf[:, :], n2_buf[:, :], AF.Sqrt), inc=s_a)
                    P.wait("vector", s_a)
                    P.chain("vector", [
                        lambda v, n2_buf=n2_buf: v.tensor_scalar(n2_buf[:, :], n2_buf[:, :], 1e-12, None, OP.max),
                        lambda v, n2_buf=n2_buf: v.reciprocal(n2_buf[:, :], n2_buf[:, :]),
                    ], s_v)
                P.wait("vector", s_v)
                for (qq_buf, n2_buf, OUTT) in ((qq_t, qn2, qqh_t), (kq_t, kn2, kqh_t)):
                    for ti in range(NT):
                        for h in range(H):
                            P.emit("vector", lambda v, ti=ti, h=h, qq_buf=qq_buf, n2_buf=n2_buf, OUTT=OUTT: v.tensor_scalar(
                                OUTT[:, ti * H * NQ + h * NQ: ti * H * NQ + (h + 1) * NQ],
                                qq_buf[:, ti * H * NQ + h * NQ: ti * H * NQ + (h + 1) * NQ],
                                n2_buf[:, ti * H + h: ti * H + h + 1], None, OP.mult))
                P.emit("vector", lambda v: v.engine_nop(), inc=s_v)
                P.wait("sync", s_v)
                P.emit("sync", lambda s, b=b, VTOK=VTOK: s.dma_start(
                    A(sc_vtok, b * 128 * NT * D, [[NT * D, 128], [1, NT * D]]),
                    VTOK[:, :]), inc=s_st, incval=16)
                P.barrier()

            if stage == "p1" and b == 0:
                tap("QT", QT, 128, H * S, BF16)
                tap("KT", KT, 128, H * S, BF16)
                tap("VTOK", VTOK, 128, NT * D, BF16)
                tap("qqh_t", qqh_t, 128, NT * H * NQ, BF16)
                tap("kqh_t", kqh_t, 128, NT * H * NQ, BF16)
                return finish_stage()
        span_vt.close()
        span_xnt.close()

        attnT = span_at.enter_context(nc.sbuf_tensor(f"attnT_{b}", [128, H * S], BF16, side="right"))
        with contextlib.ExitStack() as p2:
            fidb = [p2.enter_context(nc.sbuf_tensor(f"fidb{par}_{b}", [128, S], F32)) for par in range(2)]
            sinb = [p2.enter_context(nc.sbuf_tensor(f"sinb{par}_{b}", [128, S], F32)) for par in range(2)]
            pbuf = [p2.enter_context(nc.sbuf_tensor(f"pbuf{par}_{b}", [128, S], BF16, side="right")) for par in range(4)]
            mcol = [p2.enter_context(nc.sbuf_tensor(f"mcol{par}_{b}", [128, 8], F32)) for par in range(4)]
            pTq = p2.enter_context(nc.sbuf_tensor(f"pTq_{b}", [128, NT * 256], BF16))
            vtk = [p2.enter_context(nc.sbuf_tensor(f"vtk{par}_{b}", [128, NT * 128], BF16)) for par in range(2)]
            p_da = p2.enter_context(nc.psum_tensor(f"p2_da_{b}", [128, 1024], F32))
            p_cw = [p2.enter_context(nc.psum_tensor(f"p2_cw{k}_{b}", [128, 512], F32)) for k in range(4)]
            p_pt = p2.enter_context(nc.psum_tensor(f"p2_pt_{b}", [128, 1024], BF16))
            p_pv = p2.enter_context(nc.psum_tensor(f"p2_pv_{b}", [128, 512], F32))

            mk = {"fh": [0, 0], "t": {}, "z": {}, "cw": {}, "sin": {}, "exp": {},
                  "esum": {}, "pn": {}, "ptd": {}, "fid": {}, "qkevac": 0,
                  "pvread": 0, "pvdone": 0}
            mk["pvh"] = [0, 0]

            def emit_pv(h, qp, hp, lvs, lv_mark):
                P.wait("tensor", s_pt, mk["ptd"][h * NT + 2 * qp + 1])
                P.wait("tensor", s_a, mk["pvdone"])
                P.wait("tensor", lvs, lv_mark)
                for kt in range(NT):
                    P.emit("tensor", lambda t, hp=hp, kt=kt: t.matmul(
                        p_pv[:, 0:256], vtk[hp][:, kt * 128:(kt + 1) * 128],
                        pTq[:, kt * 256:(kt + 1) * 256],
                        start=(kt == 0), stop=(kt == NT - 1)),
                        inc=(s_t if kt == NT - 1 else None))
                mk["pvread"] = s_t.n
                P.wait("scalar", s_t, mk["pvread"])
                P.emit("scalar", lambda a, h=h, qp=qp: a.activation(
                    attnT[:, h * S + qp * 256: h * S + (qp + 1) * 256],
                    p_pv[:, 0:256], AF.Copy), inc=s_a)
                mk["pvdone"] = s_a.n

            for h in range(H):
                hp = h % 2
                lvs = s_lv if hp == 0 else s_lv2
                P.wait("sync", s_t, mk["pvh"][hp])
                P.emit("sync", lambda s, b=b, h=h, hp=hp: s.dma_start(
                    A(vtk[hp], 0, [[NT * 128, 128], [128, NT], [1, 128]]),
                    A(sc_vtok, b * 128 * NT * D + h * 128, [[NT * D, 128], [D, NT], [1, 128]])),
                    inc=lvs, incval=16)
                lv_mark = lvs.n
                for (tok, TT) in ((qqh_t, qqT_h), (kqh_t, kqT_h)):
                    for batch in range(2):
                        P.wait("tensor", s_a, mk["qkevac"])
                        for j in range(8):
                            ti = batch * 8 + j
                            P.emit("tensor", lambda t, ti=ti, j=j, h=h, tok=tok: t.transpose(
                                p_pt[0:NQ, j * 128:(j + 1) * 128],
                                tok[:, ti * H * NQ + h * NQ: ti * H * NQ + (h + 1) * NQ],
                                ident_bf[:, :]), inc=(s_t if j == 7 else None))
                        P.wait("scalar", s_t)
                        P.emit("scalar", lambda a, batch=batch, TT=TT: a.activation(
                            TT[0:NQ, batch * 1024:(batch + 1) * 1024], p_pt[0:NQ, :],
                            AF.Copy), inc=s_a)
                        mk["qkevac"] = s_a.n
                P.wait("tensor", s_a, mk["qkevac"])
                for qp in range(NT // 2):
                    tiles = (2 * qp, 2 * qp + 1)
                    for qt in tiles:
                        par = qt % 2
                        tq = h * NT + qt
                        P.wait("vector", s_a, mk["exp"].get(tq - 2, 0))
                        for half in range(2):
                            P.wait("tensor", s_v, mk["fh"][1 - half])
                            for kc in (2 * half, 2 * half + 1):
                                P.emit("tensor", lambda t, h=h, qt=qt, kc=kc, half=half: t.matmul(
                                    p_da[:, (kc - 2 * half) * 512:(kc - 2 * half + 1) * 512],
                                    qqT_h[0:NQ, qt * 128:(qt + 1) * 128],
                                    kqT_h[0:NQ, kc * 512:(kc + 1) * 512], start=True, stop=True),
                                    inc=(s_t if kc % 2 == 1 else None))
                            dm = s_t.n
                            P.wait("vector", s_t, dm)
                            P.emit("vector", lambda v, par=par, half=half: v.tensor_scalar(
                                fidb[par][:, half * 1024:(half + 1) * 1024], p_da[:, :],
                                1.0, -1.0, OP.min, OP.max), inc=s_v)
                            mk["fh"][half] = s_v.n
                        mk["fid", tq] = s_v.n
                    for qt in tiles:
                        par = qt % 2
                        tq = h * NT + qt
                        P.wait("scalar", s_v, max(mk["fid", tq], mk["t"].get(tq - 2, 0)))
                        P.emit("scalar", lambda a, par=par: a.activation(
                            sinb[par][:, :], fidb[par][:, :], AF.Sin, scale=PI_LO), inc=s_a)
                        mk["sin"][tq] = s_a.n
                    if qp > 0:
                        emit_pv(h, qp - 1, hp, lvs, lv_mark)
                    for qt in tiles:
                        par = qt % 2
                        tq = h * NT + qt
                        for kc in range(4):
                            P.wait("tensor", s_v, mk["z"].get((kc, tq - 1), 0))
                            P.emit("tensor", lambda t, h=h, qt=qt, kc=kc: t.matmul(
                                p_cw[kc][:, :], QT[:, h * S + qt * 128: h * S + qt * 128 + 128],
                                KT[:, h * S + kc * 512: h * S + (kc + 1) * 512],
                                start=True, stop=True), inc=s_t)
                            mk["cw"][kc, tq] = s_t.n
                        P.wait("vector", s_a, mk["sin"][tq])
                        P.chain("vector", [
                            lambda v, par=par: v.scalar_tensor_tensor(
                                sinb[par][:, :], sinb[par][:, :], 0.1, fidb[par][:, :],
                                OP.mult, OP.mult),
                            lambda v, par=par: v.tensor_tensor(
                                fidb[par][:, :], fidb[par][:, :], sinb[par][:, :], OP.add),
                        ], s_v)
                        mk["t"][tq] = s_v.n
                        for kc in range(4):
                            P.wait("vector", s_t, mk["cw"][kc, tq])
                            P.emit("vector", lambda v, par=par, kc=kc: v.tensor_tensor(
                                fidb[par][:, kc * 512:(kc + 1) * 512],
                                fidb[par][:, kc * 512:(kc + 1) * 512],
                                p_cw[kc][:, :], OP.add), inc=s_v)
                            mk["z"][kc, tq] = s_v.n
                    for qt in tiles:
                        par, p4 = qt % 2, qt % 4
                        tq = h * NT + qt
                        P.wait("scalar", s_pt, mk["ptd"].get(tq - 4, 0))
                        P.wait("scalar", s_v, mk["z"][3, tq])
                        P.emit("scalar", lambda a, par=par, p4=p4: a.activation(
                            pbuf[p4][:, :], fidb[par][:, :], AF.Exp, scale=0.5,
                            accum_out=mcol[p4][:, 4:5]), inc=s_a)
                        mk["exp"][tq] = s_a.n
                    for qt in tiles:
                        p4 = qt % 4
                        tq = h * NT + qt
                        P.wait("vector", s_a, mk["exp"][tq])
                        P.emit("vector", lambda v, p4=p4: v.reciprocal(
                            mcol[p4][:, 5:6], mcol[p4][:, 4:5]), inc=s_v)
                        mk["esum"][tq] = s_v.n
                    for qt in tiles:
                        p4, qi = qt % 4, qt % 2
                        tq = h * NT + qt
                        P.wait("scalar", s_v, mk["esum"][tq])
                        P.emit("scalar", lambda a, p4=p4: a.activation(
                            pbuf[p4][:, :], pbuf[p4][:, :], AF.Copy,
                            scale=mcol[p4][:, 5:6]), inc=s_a)
                        mk["pn"][tq] = s_a.n
                        P.wait("sync", s_a, mk["pn"][tq])
                        P.wait("sync", s_t, mk["pvread"])
                        P.emit("sync", lambda s, p4=p4, qi=qi: s.dma_start_transpose(
                            A(pTq, qi * 128, [[NT * 256, 128], [256, NT], [1, 128]]),
                            A(pbuf[p4], 0, [[S, 128], [128, NT], [1, 128]])),
                            inc=s_pt, incval=16)
                        mk["ptd"][tq] = s_pt.n
                emit_pv(h, NT // 2 - 1, hp, lvs, lv_mark)
                mk["pvh"][hp] = mk["pvread"]
            P.barrier()
        span_qkv.close()

        if stage == "p2" and b == 0:
            tap("attnT", attnT, 128, H * S, BF16)
            return finish_stage()

        span_x1 = contextlib.ExitStack()
        open_spans.append(span_x1)
        X1IN = span_x1.enter_context(nc.sbuf_tensor(f"X1IN_{b}", [128, NC * S], F32))
        with contextlib.ExitStack() as p3:
            XN32 = p3.enter_context(nc.sbuf_tensor(f"XN32_{b}", [128, NC * S], F32))
            wo_sb = p3.enter_context(nc.sbuf_tensor(f"wo_sb_{b}", [128, NC * 1024], BF16))
            p_w1 = p3.enter_context(nc.psum_tensor(f"p3_w1_{b}", [128, 512], F32))
            p_w2 = p3.enter_context(nc.psum_tensor(f"p3_w2_{b}", [128, 512], F32))
            P.emit("sync", lambda s, b=b, XN32=XN32: s.dma_start(
                A(XN32, 0, [[NC * S, 128], [S, NC], [1, S]]),
                A(sc_xn32, b * D * S, [[S, 128], [128 * S, NC], [1, S]])), inc=s_lh, incval=16)
            P.emit("gpsimd", lambda g, wo_sb=wo_sb: g.dma_start(
                wo_sb[:, :], A(W["aWo"], 0, [[D, 128], [128 * D, NC], [1, D]])), inc=s_ld, incval=16)
            P.barrier()
            for dc in range(NC):
                for tc in range(4):
                    pw = p_w1 if (dc * 4 + tc) % 2 == 0 else p_w2
                    P.wait("tensor", s_v)
                    for f in range(NC):
                        P.emit("tensor", lambda t, dc=dc, tc=tc, f=f, pw=pw: t.matmul(
                            pw[:, :], wo_sb[:, f * 1024 + dc * 128: f * 1024 + (dc + 1) * 128],
                            attnT[:, f * S + tc * 512: f * S + (tc + 1) * 512],
                            start=(f == 0), stop=(f == NC - 1)), inc=(s_t if f == NC - 1 else None))
                    P.wait("vector", s_t)
                    P.emit("vector", lambda v, dc=dc, tc=tc, pw=pw, X1IN=X1IN, XN32=XN32: v.scalar_tensor_tensor(
                        X1IN[:, dc * S + tc * 512: dc * S + (tc + 1) * 512], pw[:, :],
                        bo_c[:, dc:dc + 1], XN32[:, dc * S + tc * 512: dc * S + (tc + 1) * 512],
                        OP.add, OP.add), inc=s_v)
            P.barrier()
        span_at.close()
        with contextlib.ExitStack() as p3b:
            p_st = p3b.enter_context(nc.psum_tensor(f"p3_st_{b}", [1, 1024], F32))
            p_bc = p3b.enter_context(nc.psum_tensor(f"p3_bc_{b}", [128, 1024], F32))
            sqt = p3b.enter_context(nc.sbuf_tensor(f"sqt_{b}", [128, 1024], F32))
            rows = tuple(p3b.enter_context(nc.sbuf_tensor(f"row3__{b}" + ch, [1, S], F32)) for ch in "abc")
            ln_fm(X1IN, g1_c, be1_c, p_st, p_bc, sqt, rows)
            P.barrier()

        if stage == "p3" and b == 0:
            tap("X1IN", X1IN, 128, NC * S, F32)
            return finish_stage()

        span_x1t = contextlib.ExitStack()
        open_spans.append(span_x1t)
        X1T = span_x1t.enter_context(nc.sbuf_tensor(f"X1T_{b}", [128, NC * S], BF16, side="right"))
        with contextlib.ExitStack() as p4:
            qm_s = p4.enter_context(nc.sbuf_tensor(f"qm_s_{b}", [NQ, S], F32))
            th_s = p4.enter_context(nc.sbuf_tensor(f"th_s_{b}", [NQ, S], F32))
            qe_b = p4.enter_context(nc.sbuf_tensor(f"qe_b_{b}", [NQ, S], BF16))
            p_qm = p4.enter_context(nc.psum_tensor(f"p4_qm_{b}", [NQ, 512], F32))
            p_f1 = p4.enter_context(nc.psum_tensor(f"p4_f1_{b}", [128, 512], F32))
            p_f2 = p4.enter_context(nc.psum_tensor(f"p4_f2_{b}", [128, 512], F32))
            for c in range(NC):
                P.emit("vector", lambda v, c=c, X1T=X1T, X1IN=X1IN: v.tensor_copy(
                    X1T[:, c * S:(c + 1) * S], X1IN[:, c * S:(c + 1) * S]), inc=s_v)
            P.barrier(("tensor", "vector"))
            for tc in range(4):
                for f in range(NC):
                    P.emit("tensor", lambda t, tc=tc, f=f, X1T=X1T: t.matmul(
                        p_qm[0:NQ, :], m1_bf[:, f * NQ:(f + 1) * NQ],
                        X1T[:, f * S + tc * 512: f * S + (tc + 1) * 512],
                        start=(f == 0), stop=(f == NC - 1)), inc=(s_t if f == NC - 1 else None))
                P.wait("vector", s_t)
                P.emit("vector", lambda v, tc=tc, qm_s=qm_s: v.tensor_scalar(
                    qm_s[0:NQ, tc * 512:(tc + 1) * 512], p_qm[0:NQ, :], bias1_c[0:NQ, 0:1], None, OP.add), inc=s_v)
                P.wait("tensor", s_v)
            P.wait("scalar", s_v)
            P.emit("scalar", lambda a, th_s=th_s, qm_s=qm_s: a.activation(th_s[0:NQ, :], qm_s[0:NQ, :], AF.Tanh), inc=s_a)
            P.wait("vector", s_a)
            P.wait("vector", s_v)
            P.emit("vector", lambda v, qm_s=qm_s: v.tensor_scalar(
                qm_s[0:NQ, :], qm_s[0:NQ, :], 8.0, -8.0, OP.min, OP.max), inc=s_v)
            P.wait("scalar", s_v)
            P.emit("scalar", lambda a, qm_s=qm_s: a.activation(qm_s[0:NQ, :], qm_s[0:NQ, :], AF.Sin, scale=PI8_LO), inc=s_a)
            P.wait("vector", s_a)
            P.chain("vector", [
                lambda v, qm_s=qm_s: v.tensor_tensor(qm_s[0:NQ, :], qm_s[0:NQ, :], qm_s[0:NQ, :], OP.mult),
                lambda v, qm_s=qm_s: v.tensor_scalar(qm_s[0:NQ, :], qm_s[0:NQ, :], -2.0, 1.0, OP.mult, OP.add),
                lambda v, qe_b=qe_b, th_s=th_s, qm_s=qm_s: v.tensor_tensor(
                    qe_b[0:NQ, :], th_s[0:NQ, :], qm_s[0:NQ, :], OP.mult),
            ], s_v)
            P.wait("tensor", s_v)
            for dc in range(NC):
                for tc in range(4):
                    pw = p_f1 if (dc * 4 + tc) % 2 == 0 else p_f2
                    P.emit("tensor", lambda t, dc=dc, tc=tc, pw=pw, qe_b=qe_b: t.matmul(
                        pw[:, :], m2_bf[0:NQ, dc * 128:(dc + 1) * 128],
                        qe_b[0:NQ, tc * 512:(tc + 1) * 512], start=True, stop=True), inc=s_t)
                    P.wait("vector", s_t)
                    P.emit("vector", lambda v, dc=dc, tc=tc, pw=pw, X1IN=X1IN: v.scalar_tensor_tensor(
                        X1IN[:, dc * S + tc * 512: dc * S + (tc + 1) * 512], pw[:, :],
                        bias2_c[:, dc:dc + 1], X1IN[:, dc * S + tc * 512: dc * S + (tc + 1) * 512],
                        OP.add, OP.add), inc=s_v)
                    P.wait("tensor", s_v)
            P.barrier(("tensor", "vector", "scalar"))
        span_x1t.close()
        with contextlib.ExitStack() as p4b:
            p_st = p4b.enter_context(nc.psum_tensor(f"p4_st_{b}", [1, 1024], F32))
            p_bc = p4b.enter_context(nc.psum_tensor(f"p4_bc_{b}", [128, 1024], F32))
            sqt = p4b.enter_context(nc.sbuf_tensor(f"sqt2_{b}", [128, 1024], F32))
            rows = tuple(p4b.enter_context(nc.sbuf_tensor(f"row4__{b}" + ch, [1, S], F32)) for ch in "abc")
            ln_fm(X1IN, g2_c, be2_c, p_st, p_bc, sqt, rows)
            P.barrier()

        if stage == "p4" and b == 0:
            tap("X1IN", X1IN, 128, NC * S, F32)
            return finish_stage()

        with contextlib.ExitStack() as p5:
            p_t1 = p5.enter_context(nc.psum_tensor(f"p5_t1_{b}", [128, 512], F32))
            p_t2 = p5.enter_context(nc.psum_tensor(f"p5_t2_{b}", [128, 512], F32))
            stg = p5.enter_context(nc.sbuf_tensor(f"stg5_{b}", [128, D], F32))
            spill_tok(X1IN, sc_aout, b * S * D, p_t1, p_t2, stg)
        span_x1.close()

        span_qf = contextlib.ExitStack()
        open_spans.append(span_qf)
        span_f = contextlib.ExitStack()
        open_spans.append(span_f)
        QfT = span_qf.enter_context(nc.sbuf_tensor(f"QfT_{b}", [128, NC * S], BF16))
        KfT = span_qf.enter_context(nc.sbuf_tensor(f"KfT_{b}", [128, NC * S], BF16))
        VfTOK = span_qf.enter_context(nc.sbuf_tensor(f"VfTOK_{b}", [128, NT * D], BF16))
        span_xnt2 = contextlib.ExitStack()
        open_spans.append(span_xnt2)
        XNT2 = span_xnt2.enter_context(nc.sbuf_tensor(f"XNT2_{b}", [128, NC * S], BF16, side="right"))
        with contextlib.ExitStack() as p6:
            wpro = p6.enter_context(nc.sbuf_tensor(f"wpro6_{b}", [128, NC * 1024], BF16))
            p_a = p6.enter_context(nc.psum_tensor(f"p6_a_{b}", [128, 512], F32))
            p_b = p6.enter_context(nc.psum_tensor(f"p6_b_{b}", [128, 512], F32))
            p_v = p6.enter_context(nc.psum_tensor(f"p6_v_{b}", [128, 1024], F32))
            P.emit("sync", lambda s, b=b, XNT2=XNT2: s.dma_start(
                A(XNT2, 0, [[NC * S, 128], [S, NC], [1, S]]),
                A(sc_xnt, b * D * S, [[S, 128], [128 * S, NC], [1, S]])), inc=s_lh, incval=16)
            P.barrier()
            for (wname, bcol, OUT, scale) in (("fWq", fbq_c, QfT, INV32), ("fWk", fbk_c, KfT, None)):
                P.barrier(("tensor", "gpsimd", "vector"))
                P.emit("gpsimd", lambda g, wname=wname, wpro=wpro: g.dma_start(
                    wpro[:, :], A(W[wname], 0, [[D, 128], [128 * D, NC], [1, D]])), inc=s_ld, incval=16)
                P.wait("tensor", s_ld)
                for dc in range(NC):
                    for tc in range(4):
                        pw = p_a if (dc * 4 + tc) % 2 == 0 else p_b
                        P.wait("tensor", s_v)
                        for f in range(NC):
                            P.emit("tensor", lambda t, dc=dc, tc=tc, f=f, pw=pw, wpro=wpro, XNT2=XNT2: t.matmul(
                                pw[:, :], wpro[:, f * 1024 + dc * 128: f * 1024 + (dc + 1) * 128],
                                XNT2[:, f * S + tc * 512: f * S + (tc + 1) * 512],
                                start=(f == 0), stop=(f == NC - 1)), inc=(s_t if f == NC - 1 else None))
                        P.wait("vector", s_t)
                        if scale is None:
                            P.emit("vector", lambda v, dc=dc, tc=tc, pw=pw, OUT=OUT, bcol=bcol: v.tensor_scalar(
                                OUT[:, dc * S + tc * 512: dc * S + (tc + 1) * 512],
                                pw[:, :], bcol[:, dc:dc + 1], None, OP.add), inc=s_v)
                        else:
                            P.emit("vector", lambda v, dc=dc, tc=tc, pw=pw, OUT=OUT, bcol=bcol, scale=scale: v.tensor_scalar(
                                OUT[:, dc * S + tc * 512: dc * S + (tc + 1) * 512],
                                pw[:, :], bcol[:, dc:dc + 1], scale, OP.add, OP.mult), inc=s_v)
            P.barrier(("tensor", "vector", "gpsimd"))
            P.emit("gpsimd", lambda g, wpro=wpro: g.dma_start(
                wpro[:, :], A(W["fWv"], 0, [[D, 128], [128 * D, NC], [1, D]])), inc=s_ld, incval=16)
            P.barrier(("tensor", "gpsimd", "vector"))
            for ti in range(NT):
                P.wait("tensor", s_v)
                for f in range(NC):
                    for hf in range(2):
                        P.emit("tensor", lambda t, ti=ti, f=f, hf=hf, wpro=wpro, XNT2=XNT2: t.matmul(
                            p_v[:, hf * 512:(hf + 1) * 512],
                            XNT2[:, f * S + ti * 128: f * S + ti * 128 + 128],
                            wpro[:, f * 1024 + hf * 512: f * 1024 + (hf + 1) * 512],
                            start=(f == 0), stop=False))
                for hf in range(2):
                    P.emit("tensor", lambda t, hf=hf: t.matmul(
                        p_v[:, hf * 512:(hf + 1) * 512], ones_r[32:33, :],
                        vb_r[32:33, hf * 512:(hf + 1) * 512], start=False, stop=True),
                        inc=(s_t if hf == 1 else None))
                P.wait("vector", s_t)
                P.emit("vector", lambda v, ti=ti, VfTOK=VfTOK: v.tensor_copy(
                    VfTOK[:, ti * D:(ti + 1) * D], p_v[:, :]), inc=s_v)
            P.barrier()

        span_xnt2.close()
        afT = span_f.enter_context(nc.sbuf_tensor(f"afT_{b}", [128, NC * S], BF16, side="right"))
        with contextlib.ExitStack() as p6b:
            pbuf6 = [p6b.enter_context(nc.sbuf_tensor(f"pbuf6{i}_{b}", [128, S], BF16)) for i in range(2)]
            pTq = p6b.enter_context(nc.sbuf_tensor(f"pTq6_{b}", [128, NT * 512], BF16))
            mcol = p6b.enter_context(nc.sbuf_tensor(f"mcol6_{b}", [128, 8], F32))
            p_s2 = p6b.enter_context(nc.psum_tensor(f"p6_s2_{b}", [128, 2048], F32))
            p_pv1 = p6b.enter_context(nc.psum_tensor(f"p6_pv1_{b}", [128, 512], F32))
            p_pv2 = p6b.enter_context(nc.psum_tensor(f"p6_pv2_{b}", [128, 512], F32))
            mk6 = {"exp": {}, "esum": {}, "pn": {}, "ptd": {}}
            pv_read = [0]
            for qt in range(NT):
                quad, qi = qt // 4, qt % 4
                pb = pbuf6[qt % 2]
                mc = (qt % 2) * 4
                P.wait("tensor", s_a, mk6["exp"].get(qt - 1, 0))
                for kc in range(4):
                    for f in range(NC):
                        P.emit("tensor", lambda t, qt=qt, kc=kc, f=f, QfT=QfT, KfT=KfT: t.matmul(
                            p_s2[:, kc * 512:(kc + 1) * 512],
                            QfT[:, f * S + qt * 128: f * S + qt * 128 + 128],
                            KfT[:, f * S + kc * 512: f * S + (kc + 1) * 512],
                            start=(f == 0), stop=(f == NC - 1)),
                            inc=(s_t if (kc == 3 and f == NC - 1) else None))
                sc_mark = s_t.n
                P.wait("scalar", s_t, sc_mark)
                P.wait("scalar", s_pt, mk6["ptd"].get(qt - 2, 0))
                P.emit("scalar", lambda a, pb=pb, mc=mc: a.activation(
                    pb[:, :], p_s2[:, :], AF.Exp, scale=1.0,
                    accum_out=mcol[:, mc:mc + 1]), inc=s_a)
                mk6["exp"][qt] = s_a.n
                P.wait("vector", s_a, mk6["exp"][qt])
                P.emit("vector", lambda v, mc=mc: v.reciprocal(
                    mcol[:, mc + 1:mc + 2], mcol[:, mc:mc + 1]), inc=s_v)
                mk6["esum"][qt] = s_v.n
                P.wait("scalar", s_v, mk6["esum"][qt])
                P.emit("scalar", lambda a, pb=pb, mc=mc: a.activation(
                    pb[:, :], pb[:, :], AF.Copy, scale=mcol[:, mc + 1:mc + 2]), inc=s_a)
                mk6["pn"][qt] = s_a.n
                P.wait("sync", s_a, mk6["pn"][qt])
                P.wait("sync", s_t, pv_read[0])
                P.emit("sync", lambda s, pb=pb, qi=qi, pTq=pTq: s.dma_start_transpose(
                    A(pTq, qi * 128, [[NT * 512, 128], [512, NT], [1, 128]]),
                    A(pb, 0, [[S, 128], [128, NT], [1, 128]])), inc=s_pt, incval=16)
                mk6["ptd"][qt] = s_pt.n
                if qi == 3:
                    P.wait("tensor", s_pt, mk6["ptd"][qt])
                    for dc in range(NC):
                        pv = p_pv1 if dc % 2 == 0 else p_pv2
                        if dc >= 2:
                            P.wait("tensor", s_v)
                        for kt in range(NT):
                            P.emit("tensor", lambda t, dc=dc, kt=kt, pv=pv, VfTOK=VfTOK, pTq=pTq: t.matmul(
                                pv[:, :], VfTOK[:, kt * D + dc * 128: kt * D + (dc + 1) * 128],
                                pTq[:, kt * 512:(kt + 1) * 512],
                                start=(kt == 0), stop=(kt == NT - 1)), inc=(s_t if kt == NT - 1 else None))
                        P.wait("vector", s_t)
                        P.emit("vector", lambda v, dc=dc, quad=quad, pv=pv, afT=afT: v.tensor_copy(
                            afT[:, dc * S + quad * 512: dc * S + (quad + 1) * 512], pv[:, :]), inc=s_v)
                    pv_read[0] = s_t.n
                    P.wait("tensor", s_v)
            P.barrier()
        span_qf.close()

        with contextlib.ExitStack() as p6c:
            XN32 = p6c.enter_context(nc.sbuf_tensor(f"XN32b_{b}", [128, NC * S], F32))
            XFT = p6c.enter_context(nc.sbuf_tensor(f"XFT_{b}", [128, NC * S], BF16))
            wo2 = p6c.enter_context(nc.sbuf_tensor(f"wo2_{b}", [128, NC * 1024], BF16))
            xfrow = p6c.enter_context(nc.sbuf_tensor(f"xfrow_{b}", [128, S], F32))
            g_s = p6c.enter_context(nc.sbuf_tensor(f"g_s_{b}", [E, S], F32))
            gt_s = p6c.enter_context(nc.sbuf_tensor(f"gt_s_{b}", [128, E], F32))
            gcol = p6c.enter_context(nc.sbuf_tensor(f"gcol_{b}", [128, 4], F32))
            stg = p6c.enter_context(nc.sbuf_tensor(f"stg6_{b}", [128, D], F32))
            p_w1 = p6c.enter_context(nc.psum_tensor(f"p6_w1_{b}", [128, 512], F32))
            p_w2 = p6c.enter_context(nc.psum_tensor(f"p6_w2_{b}", [128, 512], F32))
            p_g = p6c.enter_context(nc.psum_tensor(f"p6_g_{b}", [E, 512], F32))
            p_gt = p6c.enter_context(nc.psum_tensor(f"p6_gt_{b}", [128, E], F32))
            p_t1 = p6c.enter_context(nc.psum_tensor(f"p6_t1_{b}", [128, 512], F32))
            p_t2 = p6c.enter_context(nc.psum_tensor(f"p6_t2_{b}", [128, 512], F32))
            P.emit("sync", lambda s, b=b, XN32=XN32: s.dma_start(
                A(XN32, 0, [[NC * S, 128], [S, NC], [1, S]]),
                A(sc_xn32, b * D * S, [[S, 128], [128 * S, NC], [1, S]])), inc=s_lh, incval=16)
            P.emit("gpsimd", lambda g, wo2=wo2: g.dma_start(
                wo2[:, :], A(W["fWo"], 0, [[D, 128], [128 * D, NC], [1, D]])), inc=s_ld, incval=16)
            P.barrier()
            stg_tok = p6c.enter_context(nc.sbuf_tensor(f"stg_tok_{b}", [128, S], F32))
            for dc in range(NC):
                for tc in range(4):
                    pw = p_w1 if tc % 2 == 0 else p_w2
                    for f in range(NC):
                        P.emit("tensor", lambda t, dc=dc, tc=tc, f=f, pw=pw, wo2=wo2, afT=afT: t.matmul(
                            pw[:, :], wo2[:, f * 1024 + dc * 128: f * 1024 + (dc + 1) * 128],
                            afT[:, f * S + tc * 512: f * S + (tc + 1) * 512],
                            start=(f == 0), stop=(f == NC - 1)), inc=(s_t if f == NC - 1 else None))
                    P.wait("vector", s_t)
                    P.chain("vector", [
                        lambda v, dc=dc, tc=tc, pw=pw, XN32=XN32, xfrow=xfrow: v.scalar_tensor_tensor(
                            xfrow[:, tc * 512:(tc + 1) * 512], pw[:, :], fbo_c[:, dc:dc + 1],
                            XN32[:, dc * S + tc * 512: dc * S + (tc + 1) * 512], OP.add, OP.add),
                        lambda v, dc=dc, tc=tc, xfrow=xfrow, XFT=XFT: v.tensor_copy(
                            XFT[:, dc * S + tc * 512: dc * S + (tc + 1) * 512],
                            xfrow[:, tc * 512:(tc + 1) * 512]),
                    ], s_v)
                    P.wait("tensor", s_v)
                P.wait("tensor", s_st, s_st.n)
                for g4 in range(4):
                    pt = p_t1 if g4 % 2 == 0 else p_t2
                    for j in range(4):
                        P.emit("tensor", lambda t, g4=g4, j=j, pt=pt, xfrow=xfrow: t.transpose(
                            pt[:, j * 128:(j + 1) * 128],
                            xfrow[:, (g4 * 4 + j) * 128:(g4 * 4 + j + 1) * 128], ident_f[:, :]),
                            inc=(s_t if j == 3 else None))
                    P.wait("vector", s_t)
                    P.emit("vector", lambda v, g4=g4, pt=pt, stg_tok=stg_tok: v.tensor_copy(
                        stg_tok[:, g4 * 512:(g4 + 1) * 512], pt[:, :]), inc=s_v)
                    P.wait("tensor", s_v)
                P.wait("sync", s_v)
                P.emit("sync", lambda s, b=b, dc=dc, stg_tok=stg_tok: s.dma_start(
                    A(sc_xf32, b * S * D + dc * 128, [[D, 128], [128 * D, NT], [1, 128]]),
                    A(stg_tok, 0, [[S, 128], [128, NT], [1, 128]])), inc=s_st, incval=16)
                P.barrier(("tensor", "vector", "sync"))
            P.emit("sync", lambda s, b=b, XFT=XFT: s.dma_start(
                A(sc_xft, b * D * S, [[S, 128], [128 * S, NC], [1, S]]),
                A(XFT, 0, [[NC * S, 128], [S, NC], [1, S]])), inc=s_st, incval=16)
            for tc in range(4):
                for f in range(NC):
                    P.emit("tensor", lambda t, tc=tc, f=f, XFT=XFT: t.matmul(
                        p_g[0:E, :], wg_bf[:, f * E:(f + 1) * E],
                        XFT[:, f * S + tc * 512: f * S + (tc + 1) * 512],
                        start=(f == 0), stop=(f == NC - 1)), inc=(s_t if f == NC - 1 else None))
                P.wait("vector", s_t)
                P.emit("vector", lambda v, tc=tc, g_s=g_s: v.tensor_scalar(
                    g_s[0:E, tc * 512:(tc + 1) * 512], p_g[0:E, :], fbg_c[0:E, 0:1], None, OP.add), inc=s_v)
                P.wait("tensor", s_v)
            for ti in range(NT):
                P.emit("tensor", lambda t, ti=ti, g_s=g_s: t.transpose(
                    p_gt[:, 0:E], g_s[0:E, ti * 128:(ti + 1) * 128], ident_f[0:E, 0:E]), inc=s_t)
                P.wait("vector", s_t)
                P.chain("vector", [
                    lambda v, gt_s=gt_s: v.tensor_copy(gt_s[:, :], p_gt[:, 0:E]),
                    lambda v, gt_s=gt_s, gcol=gcol: v.reduce_max(gcol[:, 0:1], gt_s[:, :], axis=AXX),
                    lambda v, gcol=gcol: v.tensor_scalar(gcol[:, 1:2], gcol[:, 0:1], -1.0, None, OP.mult),
                ], s_v)
                P.wait("scalar", s_v)
                P.emit("scalar", lambda a, gt_s=gt_s, gcol=gcol: a.activation(
                    gt_s[:, :], gt_s[:, :], AF.Exp, bias=gcol[:, 1:2], scale=1.0,
                    accum_out=gcol[:, 2:3]), inc=s_a)
                P.wait("vector", s_a)
                P.chain("vector", [
                    lambda v, gcol=gcol: v.reciprocal(gcol[:, 3:4], gcol[:, 2:3]),
                    lambda v, b=b, ti=ti, gt_s=gt_s, gcol=gcol: v.tensor_scalar(
                        gates_t[:, (b * NT + ti) * E:(b * NT + ti + 1) * E], gt_s[:, :],
                        gcol[:, 3:4], None, OP.mult),
                ], s_v)
                P.wait("tensor", s_v)
            P.wait("sync", s_st)
            P.barrier()
        span_f.close()
        P.barrier()

        if stage == "p6" and b == 0:
            tap("gates_t", gates_t, 128, NT * E, F32)
            return finish_stage()

    with contextlib.ExitStack() as p7:
        acc = p7.enter_context(nc.sbuf_tensor("acc", [128, NT * D], F32))
        w1h = p7.enter_context(nc.sbuf_tensor("w1h", [128, NC * 2048], BF16))
        w2hb = [p7.enter_context(nc.sbuf_tensor(f"w2h{i}", [128, 16 * 1024], BF16)) for i in range(2)]
        h_sb = p7.enter_context(nc.sbuf_tensor("h_sb", [128, 16 * 512], BF16))
        xfs = p7.enter_context(nc.sbuf_tensor("xfs", [128, NC * 512], BF16))
        xfs2 = p7.enter_context(nc.sbuf_tensor("xfs2", [128, NC * 512], BF16))
        aob = p7.enter_context(nc.sbuf_tensor("aob", [128, D], F32))
        b2_r = aob
        p_hb = [p7.enter_context(nc.psum_tensor(f"p7_h{i}", [128, 512], F32)) for i in range(4)]
        p_e1 = p7.enter_context(nc.psum_tensor("p7_e1", [128, 1024], F32))
        p_e2 = p7.enter_context(nc.psum_tensor("p7_e2", [128, 1024], F32))

        marks = {"w1q": [0, 0, 0, 0], "w2rd": [0, 0], "xfs": [0, 0],
                 "relu": [0, 0, 0, 0], "stt": [0, 0], "b2": 0}
        ld_w1q = [0, 0, 0, 0]
        ld_w2q = [0, 0, 0, 0]
        for tb in range(B_LOC):
            P.wait("sync", s_st)
            P.emit("sync", lambda s, tb=tb, acc=acc: s.dma_start(
                acc[:, :], A(x_in, tb * S * D, [[D, 128], [128 * D, NT], [1, D]])),
                inc=s_lh, incval=16)
            P.barrier(("sync", "vector", "gpsimd"))
            for ti in range(NT):
                ab = aob
                P.wait("sync", s_v)
                P.emit("sync", lambda s, tb=tb, ti=ti, ab=ab: s.dma_start(
                    ab[:, :], A(sc_aout, tb * S * D + ti * 128 * D, [[D, 128], [1, D]])),
                    inc=s_lh, incval=16)
                P.wait("vector", s_lh)
                P.emit("vector", lambda v, ti=ti, ab=ab, acc=acc: v.tensor_tensor(
                    acc[:, ti * D:(ti + 1) * D], acc[:, ti * D:(ti + 1) * D], ab[:, :], OP.add), inc=s_v)
                P.wait("sync", s_v)
                P.wait("sync", s_lh)
                P.emit("sync", lambda s, tb=tb, ti=ti, ab=ab: s.dma_start(
                    ab[:, :], A(sc_xf32, tb * S * D + ti * 128 * D, [[D, 128], [1, D]])),
                    inc=s_lh, incval=16)
                P.wait("vector", s_lh)
                P.emit("vector", lambda v, ti=ti, ab=ab, acc=acc: v.tensor_tensor(
                    acc[:, ti * D:(ti + 1) * D], acc[:, ti * D:(ti + 1) * D], ab[:, :], OP.add), inc=s_v)
            P.barrier(("sync", "vector", "gpsimd", "tensor", "scalar"))
            for e in range(E):
                P.wait("sync", s_t, marks["b2"])
                P.emit("sync", lambda s, e=e, b2_r=b2_r: s.dma_start(
                    b2_r[0:1, :], A(W["fb2"], e * D, [[D, 1], [1, D]])), inc=s_lb, incval=16)
                lb_mark = s_lb.n
                for dh in range(2):
                    par2 = (tb * 16 + e * 2 + dh) % 2
                    w2h = w2hb[par2]
                    for q in range(4):
                        P.wait("gpsimd", s_t, marks["w1q"][q])
                        P.emit("gpsimd", lambda g, e=e, dh=dh, q=q, w1h=w1h: g.dma_start(
                            A(w1h, q * 512, [[NC * 2048, 128], [2048, NC], [1, 512]]),
                            A(W["fW1"], e * D * FF + dh * 2048 + q * 512,
                              [[FF, 128], [128 * FF, NC], [1, 512]])), inc=s_ld, incval=16)
                        ld_w1q[q] = s_ld.n
                    ld_w1 = s_ld.n
                    P.wait("gpsimd", s_t, marks["w2rd"][par2])
                    for q in range(4):
                        P.emit("gpsimd", lambda g, e=e, dh=dh, q=q, w2h=w2h: g.dma_start(
                            A(w2h, q * 4 * 1024, [[16 * 1024, 128], [1024, 4], [1, 1024]]),
                            A(W["fW2"], e * FF * D + (dh * 2048 + q * 4 * 128) * D,
                              [[D, 128], [128 * D, 4], [1, D]])), inc=s_ld, incval=16)
                    ld_w2 = s_ld.n
                    for tc in range(4):
                        xf_cur = xfs if tc % 2 == 0 else xfs2
                        lxs = s_lx if tc % 2 == 0 else s_lx2
                        P.wait("sync", s_t, marks["xfs"][tc % 2])
                        P.emit("sync", lambda s, tb=tb, tc=tc, xf_cur=xf_cur: s.dma_start(
                            A(xf_cur, 0, [[NC * 512, 128], [512, NC], [1, 512]]),
                            A(sc_xft, tb * D * S + tc * 512, [[S, 128], [128 * S, NC], [1, 512]])),
                            inc=lxs, incval=16)
                        P.wait("tensor", lxs, lxs.n)
                        for hd in range(16):
                            ph_ = p_hb[hd % 4]
                            P.wait("tensor", s_a, marks["relu"][hd % 4])
                            if tc == 0 and hd == 0:
                                P.wait("tensor", s_ld, ld_w1)
                            for f in range(NC):
                                P.emit("tensor", lambda t, hd=hd, f=f, ph_=ph_, w1h=w1h, xf_cur=xf_cur: t.matmul(
                                    ph_[:, :], w1h[:, f * 2048 + hd * 128: f * 2048 + (hd + 1) * 128],
                                    xf_cur[:, f * 512:(f + 1) * 512],
                                    start=(f == 0), stop=(f == NC - 1)), inc=(s_t if f == NC - 1 else None))
                            if tc == 3 and hd % 4 == 3:
                                marks["w1q"][hd // 4] = s_t.n
                            if hd == 15:
                                marks["xfs"][tc % 2] = s_t.n
                            P.wait("scalar", s_t)
                            P.emit("scalar", lambda a, e=e, dh=dh, hd=hd, ph_=ph_, h_sb=h_sb: a.activation(
                                h_sb[:, hd * 512:(hd + 1) * 512], ph_[:, :], AF.Relu,
                                bias=b1_c[:, e * 32 + dh * 16 + hd: e * 32 + dh * 16 + hd + 1]), inc=s_a)
                            marks["relu"][hd % 4] = s_a.n
                        relu_all = s_a.n
                        P.wait("tensor", s_a, relu_all)
                        for ti in range(4):
                            pe = p_e1 if ti % 2 == 0 else p_e2
                            P.wait("tensor", s_v, marks["stt"][ti % 2])
                            if tc == 0 and ti == 0:
                                P.wait("tensor", s_ld, ld_w2)
                            if dh == 0 and tc == 0 and ti == 0:
                                P.wait("tensor", s_lb, lb_mark)
                            for hd in range(16):
                                for hf in range(2):
                                    stop = (dh == 1 and hd == 15)
                                    P.emit("tensor", lambda t, hd=hd, hf=hf, ti=ti, pe=pe, h_sb=h_sb, w2h=w2h, stop=stop: t.matmul(
                                        pe[:, hf * 512:(hf + 1) * 512],
                                        h_sb[:, hd * 512 + ti * 128: hd * 512 + (ti + 1) * 128],
                                        w2h[:, hd * 1024 + hf * 512: hd * 1024 + (hf + 1) * 512],
                                        start=(hd == 0), stop=stop),
                                        inc=(s_t if (dh == 1 and hd == 15 and hf == 1) else None))
                                if tc == 3 and ti == 3 and hd % 4 == 3:
                                    pass
                            if dh == 0:
                                for hf in range(2):
                                    P.emit("tensor", lambda t, e=e, hf=hf, pe=pe: t.matmul(
                                        pe[:, hf * 512:(hf + 1) * 512], ones_r[0:1, :],
                                        b2_r[0:1, hf * 512:(hf + 1) * 512],
                                        start=False, stop=True), inc=(s_t if hf == 1 else None))
                            if tc == 3 and ti == 3:
                                marks["w2rd"][par2] = s_t.n
                            if dh == 0 and tc == 3 and ti == 3:
                                marks["b2"] = s_t.n
                            eo_done = s_t.n
                            P.wait("vector", s_t, eo_done)
                            tix = tc * 4 + ti
                            P.emit("vector", lambda v, e=e, tb=tb, tix=tix, pe=pe, acc=acc: v.scalar_tensor_tensor(
                                acc[:, tix * D:(tix + 1) * D], pe[:, :],
                                gates_t[:, (tb * NT + tix) * E + e:(tb * NT + tix) * E + e + 1],
                                acc[:, tix * D:(tix + 1) * D], OP.mult, OP.add), inc=s_v)
                            marks["stt"][ti % 2] = s_v.n
            P.barrier(("tensor", "vector", "sync"))
            P.emit("sync", lambda s, tb=tb, acc=acc: s.dma_start(
                A(y_out, tb * S * D, [[D, 128], [128 * D, NT], [1, D]]),
                A(acc, 0, [[NT * D, 128], [D, NT], [1, D]])), inc=s_st, incval=16)
        P.wait("sync", s_st)
        P.barrier()

    stack.close()
    P.replay()
    P.close()
    return nc


def _shard_inputs(inputs):
    x = np.ascontiguousarray(inputs["L0_attn_x"], dtype=np.float32)
    ren = {
        "aWq": "L0_attn_Wq", "abq": "L0_attn_bq", "aWk": "L0_attn_Wk", "abk": "L0_attn_bk",
        "aWv": "L0_attn_Wv", "abv": "L0_attn_bv", "aWc2q": "L0_attn_Wc2q", "abc2q": "L0_attn_bc2q",
        "aWo": "L0_attn_Wo", "abo": "L0_attn_bo", "ag1": "L0_attn_g1", "abeta1": "L0_attn_beta1",
        "ag2": "L0_attn_g2", "abeta2": "L0_attn_beta2", "aW1": "L0_attn_W1", "ab1": "L0_attn_b1",
        "aWm": "L0_attn_Wm", "abm": "L0_attn_bm", "aWqo": "L0_attn_Wqo", "abqo": "L0_attn_bqo",
        "aW2": "L0_attn_W2", "ab2": "L0_attn_b2",
        "fWq": "L0_ffn_Wq", "fbq": "L0_ffn_bq", "fWk": "L0_ffn_Wk", "fbk": "L0_ffn_bk",
        "fWv": "L0_ffn_Wv", "fbv": "L0_ffn_bv", "fWo": "L0_ffn_Wo", "fbo": "L0_ffn_bo",
        "fWg": "L0_ffn_Wg", "fbg": "L0_ffn_bg", "fW1": "L0_ffn_W1", "fb1": "L0_ffn_b1",
        "fW2": "L0_ffn_W2", "fb2": "L0_ffn_b2",
    }
    w = {k: np.ascontiguousarray(inputs[src], dtype=np.float32) for k, src in ren.items()}
    in_maps = []
    for c in range(8):
        m = {"x": x[2 * c:2 * c + 2]}
        m.update(w)
        in_maps.append(m)
    return in_maps


def kernel(**inputs):
    in_maps = _shard_inputs(inputs)
    nc = build()
    res = run_bass_kernel_spmd(nc, in_maps, core_ids=list(range(8)))
    return np.concatenate([r["y"] for r in res.results], axis=0)


# version 15
# speedup vs baseline: 1.0399x; 1.0143x over previous
import contextlib
import types
import numpy as np
import concourse.bass as bass
import concourse.mybir as mybir
from concourse.bass_utils import run_bass_kernel_spmd

F32 = mybir.dt.float32
BF16 = mybir.dt.bfloat16
AF = mybir.ActivationFunctionType
OP = mybir.AluOpType
AXX = mybir.AxisListType.X

PI_LO = float(np.nextafter(np.float32(np.pi), np.float32(0)))
PI8_LO = float(np.nextafter(np.float32(np.pi / 8), np.float32(0)))
ISQHD = float(np.float32(1.0 / np.sqrt(128.0)))
INV32 = float(np.float32(1.0 / 32.0))
LN_EPS = 1e-5

B_LOC, S, D, H, HD, NQ, FF, E = 2, 2048, 1024, 8, 128, 8, 4096, 8
NT = S // 128
NC = D // 128
ENGS = ("tensor", "vector", "scalar", "gpsimd", "sync")


class Sem:
    __slots__ = ("h", "n", "name")

    def __init__(self, h, name):
        self.h, self.n, self.name = h, 0, name


class Prog:
    def __init__(self, nc):
        self.nc = nc
        self.q = {e: [] for e in ENGS}
        self._cms = []
        self.sems = {}

    def sem(self, name):
        if name not in self.sems:
            cm = self.nc.semaphore(name)
            self.sems[name] = Sem(cm.__enter__(), name)
            self._cms.append(cm)
        return self.sems[name]

    @staticmethod
    def _freeze(fn):
        if fn.__closure__:
            cells = tuple(types.CellType(c.cell_contents) for c in fn.__closure__)
            return types.FunctionType(fn.__code__, fn.__globals__, fn.__name__,
                                      fn.__defaults__, cells)
        return fn

    def emit(self, eng, fn, inc=None, incval=1):
        fn = self._freeze(fn)
        if inc is not None:
            inc.n += incval

        def op(h, fn=fn, inc=inc, incval=incval):
            r = fn(h)
            if inc is not None:
                r.then_inc(inc.h, incval)
        self.q[eng].append(op)

    def wait(self, eng, sm, v=None):
        v = sm.n if v is None else v
        self.q[eng].append(lambda h, sm=sm, v=v: h.wait_ge(sm.h, v))

    dma_sems = ()

    def barrier(self, engs=ENGS):
        bar = self.sem("bar_" + "".join(sorted(e[0] + e[1] for e in engs)))
        for e in engs:
            for sm in self.dma_sems:
                self.wait(e, sm)
            self.q[e].append(lambda h: h.drain())
            self.q[e].append(lambda h, b=bar: h.drain().then_inc(b.h, 1))
        bar.n += len(engs)
        tgt = bar.n
        for e in engs:
            self.q[e].append(lambda h, b=bar, t=tgt: h.wait_ge(b.h, t))

    def chain(self, eng, fns, sem, final_inc=True):
        for i, fn in enumerate(fns):
            if i > 0:
                self.wait(eng, sem)
            last = (i == len(fns) - 1)
            self.emit(eng, fn, inc=(sem if (not last or final_inc) else None))

    def replay(self):
        with self.nc.Block() as block:
            @block.tensor
            def _(t):
                for op in self.q["tensor"]:
                    op(t)

            @block.vector
            def _(v):
                for op in self.q["vector"]:
                    op(v)

            @block.scalar
            def _(a):
                for op in self.q["scalar"]:
                    op(a)

            @block.gpsimd
            def _(g):
                for op in self.q["gpsimd"]:
                    op(g)

            @block.sync
            def _(s):
                for op in self.q["sync"]:
                    op(s)

    def close(self):
        for cm in reversed(self._cms):
            cm.__exit__(None, None, None)


def A(handle, offset, dims):
    return bass.AP(handle, int(offset), [list(d) for d in dims])


def build(stage=None, taps=()):
    nc = bass.Bass("TRN2", target_bir_lowering=False)
    P = Prog(nc)
    stack = contextlib.ExitStack()

    def sbuf(name, shape, dtype):
        return stack.enter_context(nc.sbuf_tensor(name, shape, dtype))

    x_in = nc.dram_tensor("x", [B_LOC, S, D], F32, kind="ExternalInput")
    y_out = nc.dram_tensor("y", [B_LOC, S, D], F32, kind="ExternalOutput")
    wshape = {
        "aWq": [H, D, HD], "abq": [H, HD], "aWk": [H, D, HD], "abk": [H, HD],
        "aWv": [H, D, HD], "abv": [H, HD], "aWc2q": [H, HD, NQ], "abc2q": [H, NQ],
        "aWo": [D, D], "abo": [D], "ag1": [D], "abeta1": [D], "ag2": [D], "abeta2": [D],
        "aW1": [D, FF], "ab1": [FF], "aWm": [FF, NQ], "abm": [NQ],
        "aWqo": [NQ, FF], "abqo": [FF], "aW2": [FF, D], "ab2": [D],
        "fWq": [D, D], "fbq": [D], "fWk": [D, D], "fbk": [D],
        "fWv": [D, D], "fbv": [D], "fWo": [D, D], "fbo": [D],
        "fWg": [D, E], "fbg": [E], "fW1": [E, D, FF], "fb1": [E, FF],
        "fW2": [E, FF, D], "fb2": [E, D],
    }
    W = {k: nc.dram_tensor(k, v, F32, kind="ExternalInput") for k, v in wshape.items()}
    sc_xnt = nc.dram_tensor("sc_xnt", [B_LOC, D, S], BF16)
    sc_xn32 = nc.dram_tensor("sc_xn32", [B_LOC, D, S], F32)
    sc_aout = nc.dram_tensor("sc_aout", [B_LOC, S, D], F32)
    sc_xf32 = nc.dram_tensor("sc_xf32", [B_LOC, S, D], F32)
    sc_xft = nc.dram_tensor("sc_xft", [B_LOC, D, S], BF16)
    sc_vtok = nc.dram_tensor("sc_vtok", [B_LOC, 128, NT * D], BF16)

    s_ld = P.sem("ld")
    s_g = P.sem("g")
    s_lh = P.sem("lh")
    s_st = P.sem("st")
    s_t = P.sem("t")
    s_v = P.sem("v")
    s_a = P.sem("a")
    s_lx = P.sem("lx")
    s_lx2 = P.sem("lx2")
    s_lv = P.sem("lv")
    s_lv2 = P.sem("lv2")
    s_lb = P.sem("lb")
    s_pt = P.sem("pt")
    P.dma_sems = (s_ld, s_lh, s_st, s_lx, s_lx2, s_lv, s_lv2, s_lb, s_pt)

    tapset = set(taps)
    tap_req = []

    def tap(name, buf, rows, cols, dtype):
        if name in tapset:
            t = nc.dram_tensor("tap_" + name, [rows, cols], dtype, kind="ExternalOutput")
            tap_req.append((t, buf, rows, cols))

    open_spans = []

    def finish_stage():
        P.barrier()
        for (t, buf, rows, cols) in tap_req:
            P.emit("sync", lambda s, t=t, buf=buf, rows=rows, cols=cols: s.dma_start(
                A(t, 0, [[cols, rows], [1, cols]]),
                A(buf, 0, [[int(buf.shape[1]), rows], [1, cols]])), inc=s_st, incval=16)
        P.wait("sync", s_st)
        P.barrier()
        for sp in reversed(open_spans):
            sp.close()
        stack.close()
        P.replay()
        P.close()
        return nc

    ident_bf = sbuf("ident_bf", [128, 128], BF16)
    ident_f = sbuf("ident_f", [128, 128], F32)
    ones_c = sbuf("ones_c", [128, 1], F32)
    ones_r = sbuf("ones_r", [65, 128], F32)
    iota_r = sbuf("iota_r", [128, 128], F32)
    pidx = sbuf("pidx", [128, 1], F32)

    bq_c = sbuf("bq_c", [128, H], F32)
    bk_c = sbuf("bk_c", [128, H], F32)
    vb_r = sbuf("vb_r", [65, D], F32)
    bc2q_r = sbuf("bc2q_r", [1, H * NQ], F32)
    bc2q_rq = sbuf("bc2q_rq", [1, H * NQ], F32)
    bo_c = sbuf("bo_c", [128, NC], F32)
    g1_c = sbuf("g1_c", [128, NC], F32)
    be1_c = sbuf("be1_c", [128, NC], F32)
    g2_c = sbuf("g2_c", [128, NC], F32)
    be2_c = sbuf("be2_c", [128, NC], F32)
    fbq_c = sbuf("fbq_c", [128, NC], F32)
    fbk_c = sbuf("fbk_c", [128, NC], F32)
    fbo_c = sbuf("fbo_c", [128, NC], F32)
    fbg_c = sbuf("fbg_c", [E, 1], F32)
    b1_c = sbuf("b1_c", [128, E * 32], F32)
    m1_bf = sbuf("m1_bf", [128, NC * NQ], BF16)
    bias1_c = sbuf("bias1_c", [NQ, 1], F32)
    m2_bf = sbuf("m2_bf", [NQ, D], BF16)
    bias2_c = sbuf("bias2_c", [128, NC], F32)
    wc2q_bf = sbuf("wc2q_bf", [128, H * NQ], BF16)
    wg_bf = sbuf("wg_bf", [128, NC * E], BF16)
    gates_t = sbuf("gates_t", [128, B_LOC * NT * E], F32)

    st_a = sbuf("st_a", [128, 32], F32)
    st_b = sbuf("st_b", [128, 32], F32)
    st_c = sbuf("st_c", [128, 32], F32)

    P.emit("gpsimd", lambda g: g.iota(iota_r[:, :], [[1, 128]], channel_multiplier=0,
                                      allow_small_or_imprecise_dtypes=True))
    P.emit("gpsimd", lambda g: g.iota(pidx[:, :], [[1, 1]], channel_multiplier=1,
                                      allow_small_or_imprecise_dtypes=True))
    P.emit("gpsimd", lambda g: g.memset(ones_c[:, :], 1.0))
    P.emit("gpsimd", lambda g: g.memset(ones_r[:, :], 1.0), inc=s_g)

    def ldpar(dst, w, dims, eng="sync", off=0):
        sem = s_lh if eng == "sync" else s_ld
        P.emit(eng, lambda h, dst=dst, w=w, dims=dims, off=off: h.dma_start(
            dst, A(W[w], off, dims), allow_slow_non_contiguous=True), inc=sem, incval=16)

    ldpar(bo_c[:, :], "abo", [[1, 128], [128, NC]])
    ldpar(g1_c[:, :], "ag1", [[1, 128], [128, NC]])
    ldpar(be1_c[:, :], "abeta1", [[1, 128], [128, NC]])
    ldpar(g2_c[:, :], "ag2", [[1, 128], [128, NC]])
    ldpar(be2_c[:, :], "abeta2", [[1, 128], [128, NC]])
    ldpar(fbq_c[:, :], "fbq", [[1, 128], [128, NC]])
    ldpar(fbk_c[:, :], "fbk", [[1, 128], [128, NC]])
    ldpar(fbo_c[:, :], "fbo", [[1, 128], [128, NC]])
    ldpar(bq_c[:, :], "abq", [[1, 128], [128, H]])
    ldpar(bk_c[:, :], "abk", [[1, 128], [128, H]])
    ldpar(vb_r[0:1, :], "abv", [[D, 1], [1, D]])
    ldpar(bc2q_r[0:1, :], "abc2q", [[H * NQ, 1], [1, H * NQ]])
    ldpar(fbg_c[0:E, 0:1], "fbg", [[1, E], [1, 1]])
    ldpar(vb_r[32:33, :], "fbv", [[D, 1], [1, D]])
    ldpar(b1_c[:, :], "fb1", [[1, 128], [FF, E], [128, 32]])
    ldpar(wc2q_bf[:, :], "aWc2q", [[NQ, 128], [128 * NQ, H], [1, NQ]], eng="gpsimd")
    ldpar(wg_bf[:, :], "fWg", [[E, 128], [128 * E, NC], [1, E]], eng="gpsimd")
    P.barrier()
    P.emit("vector", lambda v: v.tensor_scalar(ident_bf[:, :], iota_r[:, :], pidx[:, 0:1], None, OP.is_equal))
    P.emit("vector", lambda v: v.tensor_scalar(ident_f[:, :], iota_r[:, :], pidx[:, 0:1], None, OP.is_equal))
    P.emit("vector", lambda v: v.tensor_scalar(bc2q_rq[0:1, :], bc2q_r[0:1, :], ISQHD, None, OP.mult), inc=s_v)
    P.barrier()

    with contextlib.ExitStack() as pp:
        w1blk = pp.enter_context(nc.sbuf_tensor("w1blk", [128, 1024], BF16))
        w1t = pp.enter_context(nc.sbuf_tensor("w1t", [128, 1024], BF16))
        wm_bf = pp.enter_context(nc.sbuf_tensor("wm_bf", [128, 32 * NQ], BF16))
        wm_32 = pp.enter_context(nc.sbuf_tensor("wm_32", [128, 32 * NQ], F32))
        b1a_c = pp.enter_context(nc.sbuf_tensor("b1a_c", [128, 32], F32))
        wqo_s = pp.enter_context(nc.sbuf_tensor("wqo_s", [NQ, FF], BF16))
        wqoT = pp.enter_context(nc.sbuf_tensor("wqoT", [128, 32 * NQ], BF16))
        bqo_c = pp.enter_context(nc.sbuf_tensor("bqo_c", [128, 32], F32))
        w2blk = pp.enter_context(nc.sbuf_tensor("w2blk", [128, 1024], BF16))
        w2b32 = pp.enter_context(nc.sbuf_tensor("w2b32", [128, 1024], F32))
        m1row = pp.enter_context(nc.sbuf_tensor("m1row", [NQ, D], BF16))
        brow = pp.enter_context(nc.sbuf_tensor("brow", [1, D + NQ], F32))
        exrow = pp.enter_context(nc.sbuf_tensor("exrow", [1, D], F32))
        with contextlib.ExitStack() as pps:
            p_tr = pps.enter_context(nc.psum_tensor("pp_tr", [128, 1024], BF16))
            p_m1 = pps.enter_context(nc.psum_tensor("pp_m1", [NQ, 1024], F32))
            p_m2 = pps.enter_context(nc.psum_tensor("pp_m2", [NQ, 1024], F32))
            p_bb = pps.enter_context(nc.psum_tensor("pp_bb", [1, 1536], F32))

            ldpar(wm_bf[:, :], "aWm", [[NQ, 128], [128 * NQ, 32], [1, NQ]], eng="gpsimd")
            ldpar(wm_32[:, :], "aWm", [[NQ, 128], [128 * NQ, 32], [1, NQ]])
            ldpar(b1a_c[:, :], "ab1", [[1, 128], [128, 32]])
            ldpar(wqo_s[0:NQ, :], "aWqo", [[FF, NQ], [1, FF]], eng="gpsimd")
            ldpar(bqo_c[:, :], "abqo", [[1, 128], [128, 32]])
            P.barrier()
            for c in range(32):
                P.emit("gpsimd", lambda g, c=c: g.dma_start(
                    w1blk[:, :], A(W["aW1"], c * 128, [[FF, 128], [128 * FF, NC], [1, 128]])),
                    inc=s_ld, incval=16)
                P.wait("tensor", s_ld)
                for f in range(NC):
                    P.emit("tensor", lambda t, f=f: t.transpose(
                        p_tr[:, f * 128:(f + 1) * 128], w1blk[:, f * 128:(f + 1) * 128], ident_bf[:, :]),
                        inc=(s_t if f == NC - 1 else None))
                P.wait("vector", s_t)
                P.emit("vector", lambda v: v.tensor_copy(w1t[:, :], p_tr[:, :]), inc=s_v)
                P.wait("tensor", s_v)
                for hf in range(2):
                    P.emit("tensor", lambda t, c=c, hf=hf: t.matmul(
                        p_m1[0:NQ, hf * 512:(hf + 1) * 512], wm_bf[:, c * NQ:(c + 1) * NQ],
                        w1t[:, hf * 512:(hf + 1) * 512], start=(c == 0), stop=(c == 31)))
                P.emit("tensor", lambda t, c=c: t.matmul(
                    p_bb[0:1, 1024:1024 + NQ], b1a_c[:, c:c + 1], wm_32[:, c * NQ:(c + 1) * NQ],
                    start=(c == 0), stop=(c == 31)))
                P.emit("tensor", lambda t, c=c: t.transpose(
                    p_tr[:, 0:NQ], wqo_s[0:NQ, c * 128:(c + 1) * 128], ident_bf[0:NQ, 0:NQ]), inc=s_t)
                P.wait("vector", s_t)
                P.emit("vector", lambda v, c=c: v.tensor_copy(wqoT[:, c * NQ:(c + 1) * NQ], p_tr[:, 0:NQ]), inc=s_v)
                P.barrier(("tensor", "vector", "gpsimd"))
            for c in range(32):
                P.emit("gpsimd", lambda g, c=c: g.dma_start(
                    w2blk[:, :], A(W["aW2"], c * 128 * D, [[D, 128], [1, D]])), inc=s_ld, incval=16)
                P.emit("sync", lambda s, c=c: s.dma_start(
                    w2b32[:, :], A(W["aW2"], c * 128 * D, [[D, 128], [1, D]])), inc=s_lh, incval=16)
                P.wait("tensor", s_ld)
                P.wait("tensor", s_lh)
                for hf in range(2):
                    P.emit("tensor", lambda t, c=c, hf=hf: t.matmul(
                        p_m2[0:NQ, hf * 512:(hf + 1) * 512], wqoT[:, c * NQ:(c + 1) * NQ],
                        w2blk[:, hf * 512:(hf + 1) * 512], start=(c == 0), stop=(c == 31)))
                for hf in range(2):
                    P.emit("tensor", lambda t, c=c, hf=hf: t.matmul(
                        p_bb[0:1, hf * 512:(hf + 1) * 512], bqo_c[:, c:c + 1],
                        w2b32[:, hf * 512:(hf + 1) * 512], start=(c == 0), stop=(c == 31)),
                        inc=(s_t if hf == 1 else None))
                P.barrier(("tensor", "gpsimd", "sync"))
            P.wait("vector", s_t)
            P.emit("vector", lambda v: v.tensor_copy(brow[0:1, 0:D], p_bb[0:1, 0:D]))
            P.emit("vector", lambda v: v.tensor_copy(brow[0:1, D:D + NQ], p_bb[0:1, 1024:1024 + NQ]))
            P.emit("vector", lambda v: v.tensor_copy(m1row[0:NQ, :], p_m1[0:NQ, :]))
            P.emit("vector", lambda v: v.tensor_copy(m2_bf[0:NQ, :], p_m2[0:NQ, :]), inc=s_v)
            ldpar(exrow[0:1, :], "ab2", [[D, 1], [1, D]])
            P.wait("vector", s_lh)
            P.wait("vector", s_v)
            P.emit("vector", lambda v: v.tensor_tensor(brow[0:1, 0:D], brow[0:1, 0:D], exrow[0:1, :], OP.add), inc=s_v)
            P.wait("sync", s_v)
            ldpar(exrow[0:1, 0:NQ], "abm", [[NQ, 1], [1, NQ]])
            P.wait("vector", s_lh)
            P.emit("vector", lambda v: v.tensor_tensor(brow[0:1, D:D + NQ], brow[0:1, D:D + NQ], exrow[0:1, 0:NQ], OP.add), inc=s_v)
            P.wait("tensor", s_v)
            P.barrier()
        with contextlib.ExitStack() as pps:
            p_tm = pps.enter_context(nc.psum_tensor("pp_tm", [128, NQ], F32))
            p_tmb = pps.enter_context(nc.psum_tensor("pp_tmb", [128, NQ], BF16))
            for c in range(NC):
                P.emit("tensor", lambda t, c=c: t.transpose(
                    p_tm[:, 0:1], brow[0:1, c * 128:(c + 1) * 128], ident_f[0:1, 0:1]), inc=s_t)
                P.wait("vector", s_t)
                P.emit("vector", lambda v, c=c: v.tensor_copy(bias2_c[:, c:c + 1], p_tm[:, 0:1]), inc=s_v)
                P.wait("tensor", s_v)
            P.emit("tensor", lambda t: t.transpose(p_tm[0:NQ, 0:1], brow[0:1, D:D + NQ], ident_f[0:1, 0:1]), inc=s_t)
            P.wait("vector", s_t)
            P.emit("vector", lambda v: v.tensor_copy(bias1_c[0:NQ, 0:1], p_tm[0:NQ, 0:1]), inc=s_v)
            P.wait("tensor", s_v)
            for c in range(NC):
                P.emit("tensor", lambda t, c=c: t.transpose(
                    p_tmb[:, 0:NQ], m1row[0:NQ, c * 128:(c + 1) * 128], ident_bf[0:NQ, 0:NQ]), inc=s_t)
                P.wait("vector", s_t)
                P.emit("vector", lambda v, c=c: v.tensor_copy(m1_bf[:, c * NQ:(c + 1) * NQ], p_tmb[:, 0:NQ]), inc=s_v)
                P.wait("tensor", s_v)
            P.barrier()

    if stage == "prep":
        tap("m1_bf", m1_bf, 128, NC * NQ, BF16)
        tap("m2_bf", m2_bf, NQ, D, BF16)
        tap("bias1_c", bias1_c, NQ, 1, F32)
        tap("bias2_c", bias2_c, 128, NC, F32)
        return finish_stage()

    def ln_fm(BUF, g_col, b_col, p_st, p_bc, sqtmp, rows):
        row_a, row_b, row_c = rows
        for qtr in range(4):
            for c in range(NC):
                P.emit("tensor", lambda t, qtr=qtr, c=c: t.matmul(
                    p_st[0:1, 0:512], ones_c[:, 0:1],
                    BUF[:, c * S + qtr * 512: c * S + qtr * 512 + 512],
                    start=(c == 0), stop=(c == NC - 1)), inc=(s_t if c == NC - 1 else None))
            P.wait("vector", s_t)
            P.emit("vector", lambda v, qtr=qtr: v.tensor_copy(
                row_a[0:1, qtr * 512:(qtr + 1) * 512], p_st[0:1, 0:512]), inc=s_v)
            sq_marks = {}
            for c in range(NC):
                half = c % 2
                P.wait("vector", s_t, sq_marks.get(c - 2, 0))
                P.emit("vector", lambda v, qtr=qtr, c=c, half=half: v.tensor_tensor(
                    sqtmp[:, half * 512:(half + 1) * 512],
                    BUF[:, c * S + qtr * 512: c * S + qtr * 512 + 512],
                    BUF[:, c * S + qtr * 512: c * S + qtr * 512 + 512], OP.mult), inc=s_v)
                P.wait("tensor", s_v)
                P.emit("tensor", lambda t, c=c, half=half: t.matmul(
                    p_st[0:1, 512:1024], ones_c[:, 0:1], sqtmp[:, half * 512:(half + 1) * 512],
                    start=(c == 0), stop=(c == NC - 1)), inc=s_t)
                sq_marks[c] = s_t.n
            P.wait("vector", s_t)
            P.emit("vector", lambda v, qtr=qtr: v.tensor_copy(
                row_b[0:1, qtr * 512:(qtr + 1) * 512], p_st[0:1, 512:1024]), inc=s_v)
        P.wait("vector", s_v)
        P.chain("vector", [
            lambda v: v.tensor_scalar(row_a[0:1, :], row_a[0:1, :], 1.0 / D, None, OP.mult),
            lambda v: v.tensor_scalar(row_b[0:1, :], row_b[0:1, :], 1.0 / D, None, OP.mult),
            lambda v: v.tensor_tensor(row_c[0:1, :], row_a[0:1, :], row_a[0:1, :], OP.mult),
            lambda v: v.tensor_tensor(row_b[0:1, :], row_b[0:1, :], row_c[0:1, :], OP.subtract),
            lambda v: v.tensor_scalar(row_b[0:1, :], row_b[0:1, :], LN_EPS, None, OP.add),
        ], s_v)
        P.wait("scalar", s_v)
        P.emit("scalar", lambda a: a.activation(row_c[0:1, :], row_b[0:1, :], AF.Sqrt), inc=s_a)
        P.wait("vector", s_a)
        P.chain("vector", [
            lambda v: v.reciprocal(row_b[0:1, :], row_c[0:1, :]),
            lambda v: v.tensor_tensor(row_c[0:1, :], row_c[0:1, :], row_b[0:1, :], OP.mult),
            lambda v: v.tensor_scalar(row_c[0:1, :], row_c[0:1, :], -1.0, 2.0, OP.mult, OP.add),
            lambda v: v.tensor_tensor(row_b[0:1, :], row_b[0:1, :], row_c[0:1, :], OP.mult),
            lambda v: v.tensor_tensor(row_a[0:1, :], row_a[0:1, :], row_b[0:1, :], OP.mult),
        ], s_v)
        for qtr in range(4):
            P.wait("tensor", s_v)
            P.emit("tensor", lambda t, qtr=qtr: t.matmul(
                p_bc[:, 0:512], ones_r[0:1, :], row_b[0:1, qtr * 512:(qtr + 1) * 512], start=True, stop=True))
            P.emit("tensor", lambda t, qtr=qtr: t.matmul(
                p_bc[:, 512:1024], ones_r[0:1, :], row_a[0:1, qtr * 512:(qtr + 1) * 512], start=True, stop=True),
                inc=s_t)
            P.wait("vector", s_t)
            for c in range(NC):
                sl = BUF[:, c * S + qtr * 512: c * S + qtr * 512 + 512]
                P.emit("vector", lambda v, sl=sl: v.tensor_tensor(sl, sl, p_bc[:, 0:512], OP.mult),
                       inc=(s_v if c == NC - 1 else None))
            P.wait("vector", s_v)
            for c in range(NC):
                sl = BUF[:, c * S + qtr * 512: c * S + qtr * 512 + 512]
                P.emit("vector", lambda v, sl=sl: v.tensor_tensor(sl, sl, p_bc[:, 512:1024], OP.subtract),
                       inc=(s_v if c == NC - 1 else None))
            if g_col is not None:
                P.wait("vector", s_v)
                for c in range(NC):
                    sl = BUF[:, c * S + qtr * 512: c * S + qtr * 512 + 512]
                    P.emit("vector", lambda v, sl=sl, c=c: v.tensor_scalar(
                        sl, sl, g_col[:, c:c + 1], b_col[:, c:c + 1], OP.mult, OP.add),
                        inc=(s_v if c == NC - 1 else None))
            P.barrier(("tensor", "vector"))

    def spill_tok(BUF, dram, dram_off, p_t1, p_t2, stg):
        for ti in range(NT):
            P.wait("vector", s_st, s_st.n)
            P.wait("tensor", s_v)
            for r in range(2):
                pb = p_t1 if r == 0 else p_t2
                for c in range(4):
                    P.emit("tensor", lambda t, ti=ti, r=r, c=c, pb=pb: t.transpose(
                        pb[:, c * 128:(c + 1) * 128],
                        BUF[:, (4 * r + c) * S + ti * 128: (4 * r + c) * S + ti * 128 + 128],
                        ident_f[:, :]), inc=(s_t if c == 3 else None))
                P.wait("vector", s_t)
                P.emit("vector", lambda v, r=r, pb=pb, stg=stg: v.tensor_copy(
                    stg[:, r * 512:(r + 1) * 512], pb[:, :]), inc=s_v)
            P.wait("sync", s_v)
            P.emit("sync", lambda s, ti=ti, dram=dram, dram_off=dram_off, stg=stg: s.dma_start(
                A(dram, dram_off + ti * 128 * D, [[D, 128], [128, NC], [1, 128]]),
                A(stg, 0, [[D, 128], [128, NC], [1, 128]])), inc=s_st, incval=16)
        P.barrier(("tensor", "vector", "sync"))

    for b in range(B_LOC):
        span_xnt = contextlib.ExitStack()
        open_spans.append(span_xnt)
        span_qkv = contextlib.ExitStack()
        open_spans.append(span_qkv)
        span_at = contextlib.ExitStack()
        open_spans.append(span_at)
        XNT = span_xnt.enter_context(nc.sbuf_tensor(f"XNT_{b}", [128, NC * S], BF16, side="right"))
        with contextlib.ExitStack() as ph:
            xtok = ph.enter_context(nc.sbuf_tensor(f"xtok_{b}", [128, NT * D], F32))
            xn32 = ph.enter_context(nc.sbuf_tensor(f"xn32_{b}", [128, D], F32))
            sqd = ph.enter_context(nc.sbuf_tensor(f"sqd_{b}", [128, 2 * D], F32))
            stg = ph.enter_context(nc.sbuf_tensor(f"stg_{b}", [128, D], F32))
            p_trf = ph.enter_context(nc.psum_tensor(f"p0_trf_{b}", [128, 512], F32))
            p_trf2 = ph.enter_context(nc.psum_tensor(f"p0_trf2_{b}", [128, 512], F32))

            P.emit("sync", lambda s, b=b: s.dma_start(
                xtok[:, :], A(x_in, b * S * D, [[D, 128], [128 * D, NT], [1, D]])),
                inc=s_lh, incval=16)
            P.barrier()
            sq_marks = {}
            for ti in range(NT):
                P.emit("vector", lambda v, ti=ti: v.reduce_sum(
                    st_a[:, ti:ti + 1], xtok[:, ti * D:(ti + 1) * D], axis=AXX), inc=s_v)
                if ti >= 2:
                    P.wait("scalar", s_a, sq_marks[ti - 2])
                P.emit("scalar", lambda a, ti=ti: a.activation(
                    sqd[:, (ti % 2) * D:(ti % 2 + 1) * D], xtok[:, ti * D:(ti + 1) * D], AF.Square,
                    accum_out=st_b[:, ti:ti + 1]), inc=s_a)
                sq_marks[ti] = s_a.n
            P.wait("vector", s_a)
            P.wait("vector", s_v)
            P.chain("vector", [
                lambda v: v.tensor_scalar(st_a[:, 0:NT], st_a[:, 0:NT], 1.0 / D, None, OP.mult),
                lambda v: v.tensor_scalar(st_b[:, 0:NT], st_b[:, 0:NT], 1.0 / D, None, OP.mult),
                lambda v: v.tensor_tensor(st_c[:, 0:NT], st_a[:, 0:NT], st_a[:, 0:NT], OP.mult),
                lambda v: v.tensor_tensor(st_b[:, 0:NT], st_b[:, 0:NT], st_c[:, 0:NT], OP.subtract),
                lambda v: v.tensor_scalar(st_b[:, 0:NT], st_b[:, 0:NT], LN_EPS, None, OP.add),
            ], s_v)
            P.wait("scalar", s_v)
            P.emit("scalar", lambda a: a.activation(st_c[:, 0:NT], st_b[:, 0:NT], AF.Sqrt), inc=s_a)
            P.wait("vector", s_a)
            P.chain("vector", [
                lambda v: v.reciprocal(st_b[:, 0:NT], st_c[:, 0:NT]),
                lambda v: v.tensor_tensor(st_c[:, 0:NT], st_c[:, 0:NT], st_b[:, 0:NT], OP.mult),
                lambda v: v.tensor_scalar(st_c[:, 0:NT], st_c[:, 0:NT], -1.0, 2.0, OP.mult, OP.add),
                lambda v: v.tensor_tensor(st_b[:, 0:NT], st_b[:, 0:NT], st_c[:, 0:NT], OP.mult),
            ], s_v)
            for ti in range(NT):
                P.wait("vector", s_v)
                P.emit("vector", lambda v, ti=ti: v.tensor_scalar(
                    xn32[:, :], xtok[:, ti * D:(ti + 1) * D], st_a[:, ti:ti + 1], st_b[:, ti:ti + 1],
                    OP.subtract, OP.mult), inc=s_v)
                P.wait("tensor", s_v)
                for r in range(2):
                    pb = p_trf if r == 0 else p_trf2
                    for c in range(4):
                        P.emit("tensor", lambda t, r=r, c=c, pb=pb: t.transpose(
                            pb[:, c * 128:(c + 1) * 128],
                            xn32[:, r * 512 + c * 128: r * 512 + (c + 1) * 128], ident_f[:, :]),
                            inc=(s_t if c == 3 else None))
                    P.wait("vector", s_t)
                    P.emit("vector", lambda v, r=r, pb=pb, stg=stg: v.tensor_copy(stg[:, r * 512:(r + 1) * 512], pb[:, :]))
                    P.emit("vector", lambda v, r=r, ti=ti, pb=pb, XNT=XNT: v.tensor_copy(
                        A(XNT, (4 * r) * S + ti * 128, [[NC * S, 128], [S, 4], [1, 128]]),
                        pb[:, :]), inc=s_v)
                P.wait("sync", s_v)
                P.emit("sync", lambda s, b=b, ti=ti, stg=stg: s.dma_start(
                    A(sc_xn32, b * D * S + ti * 128, [[S, 128], [128 * S, NC], [1, 128]]),
                    A(stg, 0, [[D, 128], [128, NC], [1, 128]])), inc=s_st, incval=16)
                P.barrier(("tensor", "vector", "sync"))
            P.emit("sync", lambda s, b=b, XNT=XNT: s.dma_start(
                A(sc_xnt, b * D * S, [[S, 128], [128 * S, NC], [1, S]]),
                A(XNT, 0, [[NC * S, 128], [S, NC], [1, S]])), inc=s_st, incval=16)
            P.wait("sync", s_st)
            P.barrier()

        if stage == "p0" and b == 0:
            tap("XNT", XNT, 128, NC * S, BF16)
            return finish_stage()

        if True:
            QT = span_qkv.enter_context(nc.sbuf_tensor(f"QT_{b}", [128, H * S], BF16))
            KT = span_qkv.enter_context(nc.sbuf_tensor(f"KT_{b}", [128, H * S], BF16))
            span_vt = contextlib.ExitStack()
            open_spans.append(span_vt)
            VTOK = span_vt.enter_context(nc.sbuf_tensor(f"VTOK_{b}", [128, NT * D], BF16, side="right"))
            qqh_t = span_qkv.enter_context(nc.sbuf_tensor(f"qqh_t_{b}", [128, NT * H * NQ], BF16))
            kqh_t = span_qkv.enter_context(nc.sbuf_tensor(f"kqh_t_{b}", [128, NT * H * NQ], BF16))
            qqT_h = span_qkv.enter_context(nc.sbuf_tensor(f"qqT_h_{b}", [NQ, S], BF16))
            kqT_h = span_qkv.enter_context(nc.sbuf_tensor(f"kqT_h_{b}", [NQ, S], BF16))
            with contextlib.ExitStack() as p1:
                wpro = p1.enter_context(nc.sbuf_tensor(f"wpro_{b}", [128, H * NC * 128], BF16))
                qq_t = p1.enter_context(nc.sbuf_tensor(f"qq_t_{b}", [128, NT * H * NQ], F32))
                kq_t = p1.enter_context(nc.sbuf_tensor(f"kq_t_{b}", [128, NT * H * NQ], F32))
                qn2 = p1.enter_context(nc.sbuf_tensor(f"qn2_{b}", [128, NT * H], F32))
                kn2 = p1.enter_context(nc.sbuf_tensor(f"kn2_{b}", [128, NT * H], F32))
                sq_s = p1.enter_context(nc.sbuf_tensor(f"sq_s_{b}", [128, H * NQ], F32))
                p_a = p1.enter_context(nc.psum_tensor(f"p1_a_{b}", [128, 512], F32))
                p_b = p1.enter_context(nc.psum_tensor(f"p1_b_{b}", [128, 512], F32))
                p_c = p1.enter_context(nc.psum_tensor(f"p1_c_{b}", [128, 512], F32))
                p_d = p1.enter_context(nc.psum_tensor(f"p1_d_{b}", [128, 512], F32))
                p_v = p1.enter_context(nc.psum_tensor(f"p1_v_{b}", [128, 1024], F32))
                p_q = p1.enter_context(nc.psum_tensor(f"p1_q_{b}", [128, H * NQ], F32))
                pbank = (p_a, p_b, p_c, p_d)

                for (wname, bcol, OUT, scale) in (("aWq", bq_c, QT, ISQHD), ("aWk", bk_c, KT, None)):
                    P.barrier(("tensor", "gpsimd", "vector"))
                    P.emit("gpsimd", lambda g, wname=wname: g.dma_start(
                        wpro[:, :], A(W[wname], 0, [[HD, 128], [D * HD, H], [128 * HD, NC], [1, 128]])),
                        inc=s_ld, incval=16)
                    P.wait("tensor", s_ld)
                    for h in range(H):
                        P.wait("tensor", s_v)
                        for tc in range(4):
                            for f in range(NC):
                                P.emit("tensor", lambda t, h=h, tc=tc, f=f: t.matmul(
                                    pbank[tc][:, :], wpro[:, (h * NC + f) * 128:(h * NC + f + 1) * 128],
                                    XNT[:, f * S + tc * 512: f * S + (tc + 1) * 512],
                                    start=(f == 0), stop=(f == NC - 1)),
                                    inc=(s_t if f == NC - 1 else None))
                        P.wait("vector", s_t)
                        for tc in range(4):
                            if scale is None:
                                P.emit("vector", lambda v, h=h, tc=tc, OUT=OUT, bcol=bcol: v.tensor_scalar(
                                    OUT[:, h * S + tc * 512: h * S + (tc + 1) * 512],
                                    pbank[tc][:, :], bcol[:, h:h + 1], None, OP.add),
                                    inc=(s_v if tc == 3 else None))
                            else:
                                P.emit("vector", lambda v, h=h, tc=tc, OUT=OUT, bcol=bcol, scale=scale: v.tensor_scalar(
                                    OUT[:, h * S + tc * 512: h * S + (tc + 1) * 512],
                                    pbank[tc][:, :], bcol[:, h:h + 1], scale, OP.add, OP.mult),
                                    inc=(s_v if tc == 3 else None))
                P.barrier(("tensor", "vector", "gpsimd"))
                for f in range(NC):
                    P.emit("gpsimd", lambda g, f=f: g.dma_start(
                        wpro[:, f * 1024:(f + 1) * 1024],
                        A(W["aWv"], f * 128 * HD, [[HD, 128], [D * HD, H], [1, 128]])),
                        inc=s_ld, incval=16)
                P.barrier(("tensor", "gpsimd", "vector"))
                for ti in range(NT):
                    P.wait("tensor", s_v)
                    for f in range(NC):
                        for hf in range(2):
                            P.emit("tensor", lambda t, ti=ti, f=f, hf=hf: t.matmul(
                                p_v[:, hf * 512:(hf + 1) * 512],
                                XNT[:, f * S + ti * 128: f * S + ti * 128 + 128],
                                wpro[:, f * 1024 + hf * 512: f * 1024 + (hf + 1) * 512],
                                start=(f == 0), stop=False))
                    for hf in range(2):
                        P.emit("tensor", lambda t, hf=hf: t.matmul(
                            p_v[:, hf * 512:(hf + 1) * 512], ones_r[0:1, :],
                            vb_r[0:1, hf * 512:(hf + 1) * 512], start=False, stop=True),
                            inc=(s_t if hf == 1 else None))
                    P.wait("vector", s_t)
                    P.emit("vector", lambda v, ti=ti, VTOK=VTOK: v.tensor_copy(
                        VTOK[:, ti * D:(ti + 1) * D], p_v[:, :]), inc=s_v)
                P.barrier(("tensor", "vector", "gpsimd"))
                for (QKT, bias_row, qq_buf) in ((QT, bc2q_rq, qq_t), (KT, bc2q_r, kq_t)):
                    for ti in range(NT):
                        P.wait("tensor", s_v)
                        for h in range(H):
                            P.emit("tensor", lambda t, ti=ti, h=h, QKT=QKT: t.matmul(
                                p_q[:, h * NQ:(h + 1) * NQ],
                                QKT[:, h * S + ti * 128: h * S + ti * 128 + 128],
                                wc2q_bf[:, h * NQ:(h + 1) * NQ], start=(h == 0), stop=False))
                        P.emit("tensor", lambda t, bias_row=bias_row: t.matmul(
                            p_q[:, :], ones_r[0:1, :], bias_row[0:1, :], start=False, stop=True), inc=s_t)
                        P.wait("vector", s_t)
                        P.emit("vector", lambda v, ti=ti, qq_buf=qq_buf: v.tensor_copy(
                            qq_buf[:, ti * H * NQ:(ti + 1) * H * NQ], p_q[:, :]), inc=s_v)
                P.barrier(("tensor", "vector"))
                for (qq_buf, n2_buf) in ((qq_t, qn2), (kq_t, kn2)):
                    for ti in range(NT):
                        P.wait("vector", s_v)
                        P.emit("vector", lambda v, ti=ti, qq_buf=qq_buf: v.tensor_tensor(
                            sq_s[:, :], qq_buf[:, ti * H * NQ:(ti + 1) * H * NQ],
                            qq_buf[:, ti * H * NQ:(ti + 1) * H * NQ], OP.mult), inc=s_v)
                        P.wait("vector", s_v)
                        P.emit("vector", lambda v, ti=ti, n2_buf=n2_buf: v.reduce_sum(
                            n2_buf[:, ti * H:(ti + 1) * H],
                            A(sq_s, 0, [[H * NQ, 128], [NQ, H], [1, NQ]]), axis=AXX), inc=s_v)
                for n2_buf in (qn2, kn2):
                    P.wait("scalar", s_v)
                    P.emit("scalar", lambda a, n2_buf=n2_buf: a.activation(
                        n2_buf[:, :], n2_buf[:, :], AF.Sqrt), inc=s_a)
                    P.wait("vector", s_a)
                    P.chain("vector", [
                        lambda v, n2_buf=n2_buf: v.tensor_scalar(n2_buf[:, :], n2_buf[:, :], 1e-12, None, OP.max),
                        lambda v, n2_buf=n2_buf: v.reciprocal(n2_buf[:, :], n2_buf[:, :]),
                    ], s_v)
                P.wait("vector", s_v)
                for (qq_buf, n2_buf, OUTT) in ((qq_t, qn2, qqh_t), (kq_t, kn2, kqh_t)):
                    for ti in range(NT):
                        for h in range(H):
                            P.emit("vector", lambda v, ti=ti, h=h, qq_buf=qq_buf, n2_buf=n2_buf, OUTT=OUTT: v.tensor_scalar(
                                OUTT[:, ti * H * NQ + h * NQ: ti * H * NQ + (h + 1) * NQ],
                                qq_buf[:, ti * H * NQ + h * NQ: ti * H * NQ + (h + 1) * NQ],
                                n2_buf[:, ti * H + h: ti * H + h + 1], None, OP.mult))
                P.emit("vector", lambda v: v.engine_nop(), inc=s_v)
                P.wait("sync", s_v)
                P.emit("sync", lambda s, b=b, VTOK=VTOK: s.dma_start(
                    A(sc_vtok, b * 128 * NT * D, [[NT * D, 128], [1, NT * D]]),
                    VTOK[:, :]), inc=s_st, incval=16)
                P.barrier()

            if stage == "p1" and b == 0:
                tap("QT", QT, 128, H * S, BF16)
                tap("KT", KT, 128, H * S, BF16)
                tap("VTOK", VTOK, 128, NT * D, BF16)
                tap("qqh_t", qqh_t, 128, NT * H * NQ, BF16)
                tap("kqh_t", kqh_t, 128, NT * H * NQ, BF16)
                return finish_stage()
        span_vt.close()
        span_xnt.close()

        attnT = span_at.enter_context(nc.sbuf_tensor(f"attnT_{b}", [128, H * S], BF16, side="right"))
        with contextlib.ExitStack() as p2:
            fidb = [p2.enter_context(nc.sbuf_tensor(f"fidb{par}_{b}", [128, S], F32)) for par in range(2)]
            sinb = [p2.enter_context(nc.sbuf_tensor(f"sinb{par}_{b}", [128, S], F32)) for par in range(2)]
            pbuf = [p2.enter_context(nc.sbuf_tensor(f"pbuf{par}_{b}", [128, S], BF16, side="right")) for par in range(4)]
            mcol = [p2.enter_context(nc.sbuf_tensor(f"mcol{par}_{b}", [128, 8], F32)) for par in range(4)]
            pTq = p2.enter_context(nc.sbuf_tensor(f"pTq_{b}", [128, NT * 256], BF16))
            vtk = [p2.enter_context(nc.sbuf_tensor(f"vtk{par}_{b}", [128, NT * 128], BF16)) for par in range(2)]
            p_da = p2.enter_context(nc.psum_tensor(f"p2_da_{b}", [128, 1024], F32))
            p_cw = [p2.enter_context(nc.psum_tensor(f"p2_cw{k}_{b}", [128, 512], F32)) for k in range(4)]
            p_pt = p2.enter_context(nc.psum_tensor(f"p2_pt_{b}", [128, 1024], BF16))
            p_pv = p2.enter_context(nc.psum_tensor(f"p2_pv_{b}", [128, 512], F32))

            mk = {"fh": [0, 0], "t": {}, "z": {}, "cw": {}, "sin": {}, "exp": {},
                  "esum": {}, "pn": {}, "ptd": {}, "fid": {}, "qkevac": 0,
                  "pvread": 0, "pvdone": 0}
            mk["pvh"] = [0, 0]

            def emit_pv(h, qp, hp, lvs, lv_mark):
                P.wait("tensor", s_pt, mk["ptd"][h * NT + 2 * qp + 1])
                P.wait("tensor", s_a, mk["pvdone"])
                P.wait("tensor", lvs, lv_mark)
                for kt in range(NT):
                    P.emit("tensor", lambda t, hp=hp, kt=kt: t.matmul(
                        p_pv[:, 0:256], vtk[hp][:, kt * 128:(kt + 1) * 128],
                        pTq[:, kt * 256:(kt + 1) * 256],
                        start=(kt == 0), stop=(kt == NT - 1)),
                        inc=(s_t if kt == NT - 1 else None))
                mk["pvread"] = s_t.n
                P.wait("scalar", s_t, mk["pvread"])
                P.emit("scalar", lambda a, h=h, qp=qp: a.activation(
                    attnT[:, h * S + qp * 256: h * S + (qp + 1) * 256],
                    p_pv[:, 0:256], AF.Copy), inc=s_a)
                mk["pvdone"] = s_a.n

            for h in range(H):
                hp = h % 2
                lvs = s_lv if hp == 0 else s_lv2
                P.wait("sync", s_t, mk["pvh"][hp])
                P.emit("sync", lambda s, b=b, h=h, hp=hp: s.dma_start(
                    A(vtk[hp], 0, [[NT * 128, 128], [128, NT], [1, 128]]),
                    A(sc_vtok, b * 128 * NT * D + h * 128, [[NT * D, 128], [D, NT], [1, 128]])),
                    inc=lvs, incval=16)
                lv_mark = lvs.n
                for (tok, TT) in ((qqh_t, qqT_h), (kqh_t, kqT_h)):
                    for batch in range(2):
                        P.wait("tensor", s_a, mk["qkevac"])
                        for j in range(8):
                            ti = batch * 8 + j
                            P.emit("tensor", lambda t, ti=ti, j=j, h=h, tok=tok: t.transpose(
                                p_pt[0:NQ, j * 128:(j + 1) * 128],
                                tok[:, ti * H * NQ + h * NQ: ti * H * NQ + (h + 1) * NQ],
                                ident_bf[:, :]), inc=(s_t if j == 7 else None))
                        P.wait("scalar", s_t)
                        P.emit("scalar", lambda a, batch=batch, TT=TT: a.activation(
                            TT[0:NQ, batch * 1024:(batch + 1) * 1024], p_pt[0:NQ, :],
                            AF.Copy), inc=s_a)
                        mk["qkevac"] = s_a.n
                P.wait("tensor", s_a, mk["qkevac"])
                for qp in range(NT // 2):
                    tiles = (2 * qp, 2 * qp + 1)
                    for qt in tiles:
                        par = qt % 2
                        tq = h * NT + qt
                        P.wait("vector", s_a, mk["exp"].get(tq - 2, 0))
                        for half in range(2):
                            P.wait("tensor", s_v, mk["fh"][1 - half])
                            for kc in (2 * half, 2 * half + 1):
                                P.emit("tensor", lambda t, h=h, qt=qt, kc=kc, half=half: t.matmul(
                                    p_da[:, (kc - 2 * half) * 512:(kc - 2 * half + 1) * 512],
                                    qqT_h[0:NQ, qt * 128:(qt + 1) * 128],
                                    kqT_h[0:NQ, kc * 512:(kc + 1) * 512], start=True, stop=True),
                                    inc=(s_t if kc % 2 == 1 else None))
                            dm = s_t.n
                            P.wait("vector", s_t, dm)
                            P.emit("vector", lambda v, par=par, half=half: v.tensor_scalar(
                                fidb[par][:, half * 1024:(half + 1) * 1024], p_da[:, :],
                                1.0, -1.0, OP.min, OP.max), inc=s_v)
                            mk["fh"][half] = s_v.n
                        mk["fid", tq] = s_v.n
                    for qt in tiles:
                        par = qt % 2
                        tq = h * NT + qt
                        P.wait("scalar", s_v, max(mk["fid", tq], mk["t"].get(tq - 2, 0)))
                        P.emit("scalar", lambda a, par=par: a.activation(
                            sinb[par][:, :], fidb[par][:, :], AF.Sin, scale=PI_LO), inc=s_a)
                        mk["sin"][tq] = s_a.n
                    if qp > 0:
                        emit_pv(h, qp - 1, hp, lvs, lv_mark)
                    for qt in tiles:
                        par = qt % 2
                        tq = h * NT + qt
                        for kc in range(4):
                            P.wait("tensor", s_v, mk["z"].get((kc, tq - 1), 0))
                            P.emit("tensor", lambda t, h=h, qt=qt, kc=kc: t.matmul(
                                p_cw[kc][:, :], QT[:, h * S + qt * 128: h * S + qt * 128 + 128],
                                KT[:, h * S + kc * 512: h * S + (kc + 1) * 512],
                                start=True, stop=True), inc=s_t)
                            mk["cw"][kc, tq] = s_t.n
                        P.wait("vector", s_a, mk["sin"][tq])
                        P.chain("vector", [
                            lambda v, par=par: v.scalar_tensor_tensor(
                                sinb[par][:, :], sinb[par][:, :], 0.1, fidb[par][:, :],
                                OP.mult, OP.mult),
                            lambda v, par=par: v.tensor_tensor(
                                fidb[par][:, :], fidb[par][:, :], sinb[par][:, :], OP.add),
                        ], s_v)
                        mk["t"][tq] = s_v.n
                        for kc in range(4):
                            P.wait("vector", s_t, mk["cw"][kc, tq])
                            P.emit("vector", lambda v, par=par, kc=kc: v.tensor_tensor(
                                fidb[par][:, kc * 512:(kc + 1) * 512],
                                fidb[par][:, kc * 512:(kc + 1) * 512],
                                p_cw[kc][:, :], OP.add), inc=s_v)
                            mk["z"][kc, tq] = s_v.n
                    for qt in tiles:
                        par, p4 = qt % 2, qt % 4
                        tq = h * NT + qt
                        P.wait("scalar", s_pt, mk["ptd"].get(tq - 4, 0))
                        P.wait("scalar", s_v, mk["z"][3, tq])
                        P.emit("scalar", lambda a, par=par, p4=p4: a.activation(
                            pbuf[p4][:, :], fidb[par][:, :], AF.Exp, scale=0.5,
                            accum_out=mcol[p4][:, 4:5]), inc=s_a)
                        mk["exp"][tq] = s_a.n
                    for qt in tiles:
                        p4 = qt % 4
                        tq = h * NT + qt
                        P.wait("vector", s_a, mk["exp"][tq])
                        P.emit("vector", lambda v, p4=p4: v.reciprocal(
                            mcol[p4][:, 5:6], mcol[p4][:, 4:5]), inc=s_v)
                        mk["esum"][tq] = s_v.n
                    for qt in tiles:
                        p4, qi = qt % 4, qt % 2
                        tq = h * NT + qt
                        P.wait("scalar", s_v, mk["esum"][tq])
                        P.emit("scalar", lambda a, p4=p4: a.activation(
                            pbuf[p4][:, :], pbuf[p4][:, :], AF.Copy,
                            scale=mcol[p4][:, 5:6]), inc=s_a)
                        mk["pn"][tq] = s_a.n
                        P.wait("sync", s_a, mk["pn"][tq])
                        P.wait("sync", s_t, mk["pvread"])
                        P.emit("sync", lambda s, p4=p4, qi=qi: s.dma_start_transpose(
                            A(pTq, qi * 128, [[NT * 256, 128], [256, NT], [1, 128]]),
                            A(pbuf[p4], 0, [[S, 128], [128, NT], [1, 128]])),
                            inc=s_pt, incval=16)
                        mk["ptd"][tq] = s_pt.n
                emit_pv(h, NT // 2 - 1, hp, lvs, lv_mark)
                mk["pvh"][hp] = mk["pvread"]
            P.barrier()
        span_qkv.close()

        if stage == "p2" and b == 0:
            tap("attnT", attnT, 128, H * S, BF16)
            return finish_stage()

        span_x1 = contextlib.ExitStack()
        open_spans.append(span_x1)
        X1IN = span_x1.enter_context(nc.sbuf_tensor(f"X1IN_{b}", [128, NC * S], F32))
        with contextlib.ExitStack() as p3:
            XN32 = p3.enter_context(nc.sbuf_tensor(f"XN32_{b}", [128, NC * S], F32))
            wo_sb = p3.enter_context(nc.sbuf_tensor(f"wo_sb_{b}", [128, NC * 1024], BF16))
            p_w1 = p3.enter_context(nc.psum_tensor(f"p3_w1_{b}", [128, 512], F32))
            p_w2 = p3.enter_context(nc.psum_tensor(f"p3_w2_{b}", [128, 512], F32))
            P.emit("sync", lambda s, b=b, XN32=XN32: s.dma_start(
                A(XN32, 0, [[NC * S, 128], [S, NC], [1, S]]),
                A(sc_xn32, b * D * S, [[S, 128], [128 * S, NC], [1, S]])), inc=s_lh, incval=16)
            P.emit("gpsimd", lambda g, wo_sb=wo_sb: g.dma_start(
                wo_sb[:, :], A(W["aWo"], 0, [[D, 128], [128 * D, NC], [1, D]])), inc=s_ld, incval=16)
            P.barrier()
            for dc in range(NC):
                for tc in range(4):
                    pw = p_w1 if (dc * 4 + tc) % 2 == 0 else p_w2
                    P.wait("tensor", s_v)
                    for f in range(NC):
                        P.emit("tensor", lambda t, dc=dc, tc=tc, f=f, pw=pw: t.matmul(
                            pw[:, :], wo_sb[:, f * 1024 + dc * 128: f * 1024 + (dc + 1) * 128],
                            attnT[:, f * S + tc * 512: f * S + (tc + 1) * 512],
                            start=(f == 0), stop=(f == NC - 1)), inc=(s_t if f == NC - 1 else None))
                    P.wait("vector", s_t)
                    P.emit("vector", lambda v, dc=dc, tc=tc, pw=pw, X1IN=X1IN, XN32=XN32: v.scalar_tensor_tensor(
                        X1IN[:, dc * S + tc * 512: dc * S + (tc + 1) * 512], pw[:, :],
                        bo_c[:, dc:dc + 1], XN32[:, dc * S + tc * 512: dc * S + (tc + 1) * 512],
                        OP.add, OP.add), inc=s_v)
            P.barrier()
        span_at.close()
        with contextlib.ExitStack() as p3b:
            p_st = p3b.enter_context(nc.psum_tensor(f"p3_st_{b}", [1, 1024], F32))
            p_bc = p3b.enter_context(nc.psum_tensor(f"p3_bc_{b}", [128, 1024], F32))
            sqt = p3b.enter_context(nc.sbuf_tensor(f"sqt_{b}", [128, 1024], F32))
            rows = tuple(p3b.enter_context(nc.sbuf_tensor(f"row3__{b}" + ch, [1, S], F32)) for ch in "abc")
            ln_fm(X1IN, g1_c, be1_c, p_st, p_bc, sqt, rows)
            P.barrier()

        if stage == "p3" and b == 0:
            tap("X1IN", X1IN, 128, NC * S, F32)
            return finish_stage()

        span_x1t = contextlib.ExitStack()
        open_spans.append(span_x1t)
        X1T = span_x1t.enter_context(nc.sbuf_tensor(f"X1T_{b}", [128, NC * S], BF16, side="right"))
        with contextlib.ExitStack() as p4:
            qm_s = p4.enter_context(nc.sbuf_tensor(f"qm_s_{b}", [NQ, S], F32))
            th_s = p4.enter_context(nc.sbuf_tensor(f"th_s_{b}", [NQ, S], F32))
            qe_b = p4.enter_context(nc.sbuf_tensor(f"qe_b_{b}", [NQ, S], BF16))
            p_qm = p4.enter_context(nc.psum_tensor(f"p4_qm_{b}", [NQ, 512], F32))
            p_f1 = p4.enter_context(nc.psum_tensor(f"p4_f1_{b}", [128, 512], F32))
            p_f2 = p4.enter_context(nc.psum_tensor(f"p4_f2_{b}", [128, 512], F32))
            for c in range(NC):
                P.emit("vector", lambda v, c=c, X1T=X1T, X1IN=X1IN: v.tensor_copy(
                    X1T[:, c * S:(c + 1) * S], X1IN[:, c * S:(c + 1) * S]), inc=s_v)
            P.barrier(("tensor", "vector"))
            for tc in range(4):
                for f in range(NC):
                    P.emit("tensor", lambda t, tc=tc, f=f, X1T=X1T: t.matmul(
                        p_qm[0:NQ, :], m1_bf[:, f * NQ:(f + 1) * NQ],
                        X1T[:, f * S + tc * 512: f * S + (tc + 1) * 512],
                        start=(f == 0), stop=(f == NC - 1)), inc=(s_t if f == NC - 1 else None))
                P.wait("vector", s_t)
                P.emit("vector", lambda v, tc=tc, qm_s=qm_s: v.tensor_scalar(
                    qm_s[0:NQ, tc * 512:(tc + 1) * 512], p_qm[0:NQ, :], bias1_c[0:NQ, 0:1], None, OP.add), inc=s_v)
                P.wait("tensor", s_v)
            P.wait("scalar", s_v)
            P.emit("scalar", lambda a, th_s=th_s, qm_s=qm_s: a.activation(th_s[0:NQ, :], qm_s[0:NQ, :], AF.Tanh), inc=s_a)
            P.wait("vector", s_a)
            P.wait("vector", s_v)
            P.emit("vector", lambda v, qm_s=qm_s: v.tensor_scalar(
                qm_s[0:NQ, :], qm_s[0:NQ, :], 8.0, -8.0, OP.min, OP.max), inc=s_v)
            P.wait("scalar", s_v)
            P.emit("scalar", lambda a, qm_s=qm_s: a.activation(qm_s[0:NQ, :], qm_s[0:NQ, :], AF.Sin, scale=PI8_LO), inc=s_a)
            P.wait("vector", s_a)
            P.chain("vector", [
                lambda v, qm_s=qm_s: v.tensor_tensor(qm_s[0:NQ, :], qm_s[0:NQ, :], qm_s[0:NQ, :], OP.mult),
                lambda v, qm_s=qm_s: v.tensor_scalar(qm_s[0:NQ, :], qm_s[0:NQ, :], -2.0, 1.0, OP.mult, OP.add),
                lambda v, qe_b=qe_b, th_s=th_s, qm_s=qm_s: v.tensor_tensor(
                    qe_b[0:NQ, :], th_s[0:NQ, :], qm_s[0:NQ, :], OP.mult),
            ], s_v)
            P.wait("tensor", s_v)
            for dc in range(NC):
                for tc in range(4):
                    pw = p_f1 if (dc * 4 + tc) % 2 == 0 else p_f2
                    P.emit("tensor", lambda t, dc=dc, tc=tc, pw=pw, qe_b=qe_b: t.matmul(
                        pw[:, :], m2_bf[0:NQ, dc * 128:(dc + 1) * 128],
                        qe_b[0:NQ, tc * 512:(tc + 1) * 512], start=True, stop=True), inc=s_t)
                    P.wait("vector", s_t)
                    P.emit("vector", lambda v, dc=dc, tc=tc, pw=pw, X1IN=X1IN: v.scalar_tensor_tensor(
                        X1IN[:, dc * S + tc * 512: dc * S + (tc + 1) * 512], pw[:, :],
                        bias2_c[:, dc:dc + 1], X1IN[:, dc * S + tc * 512: dc * S + (tc + 1) * 512],
                        OP.add, OP.add), inc=s_v)
                    P.wait("tensor", s_v)
            P.barrier(("tensor", "vector", "scalar"))
        span_x1t.close()
        with contextlib.ExitStack() as p4b:
            p_st = p4b.enter_context(nc.psum_tensor(f"p4_st_{b}", [1, 1024], F32))
            p_bc = p4b.enter_context(nc.psum_tensor(f"p4_bc_{b}", [128, 1024], F32))
            sqt = p4b.enter_context(nc.sbuf_tensor(f"sqt2_{b}", [128, 1024], F32))
            rows = tuple(p4b.enter_context(nc.sbuf_tensor(f"row4__{b}" + ch, [1, S], F32)) for ch in "abc")
            ln_fm(X1IN, g2_c, be2_c, p_st, p_bc, sqt, rows)
            P.barrier()

        if stage == "p4" and b == 0:
            tap("X1IN", X1IN, 128, NC * S, F32)
            return finish_stage()

        with contextlib.ExitStack() as p5:
            p_t1 = p5.enter_context(nc.psum_tensor(f"p5_t1_{b}", [128, 512], F32))
            p_t2 = p5.enter_context(nc.psum_tensor(f"p5_t2_{b}", [128, 512], F32))
            stg = p5.enter_context(nc.sbuf_tensor(f"stg5_{b}", [128, D], F32))
            spill_tok(X1IN, sc_aout, b * S * D, p_t1, p_t2, stg)
        span_x1.close()

        span_qf = contextlib.ExitStack()
        open_spans.append(span_qf)
        span_f = contextlib.ExitStack()
        open_spans.append(span_f)
        QfT = span_qf.enter_context(nc.sbuf_tensor(f"QfT_{b}", [128, NC * S], BF16))
        KfT = span_qf.enter_context(nc.sbuf_tensor(f"KfT_{b}", [128, NC * S], BF16))
        VfTOK = span_qf.enter_context(nc.sbuf_tensor(f"VfTOK_{b}", [128, NT * D], BF16))
        span_xnt2 = contextlib.ExitStack()
        open_spans.append(span_xnt2)
        XNT2 = span_xnt2.enter_context(nc.sbuf_tensor(f"XNT2_{b}", [128, NC * S], BF16, side="right"))
        with contextlib.ExitStack() as p6:
            wpro = p6.enter_context(nc.sbuf_tensor(f"wpro6_{b}", [128, NC * 1024], BF16))
            p_a = p6.enter_context(nc.psum_tensor(f"p6_a_{b}", [128, 512], F32))
            p_b = p6.enter_context(nc.psum_tensor(f"p6_b_{b}", [128, 512], F32))
            p_v = p6.enter_context(nc.psum_tensor(f"p6_v_{b}", [128, 1024], F32))
            P.emit("sync", lambda s, b=b, XNT2=XNT2: s.dma_start(
                A(XNT2, 0, [[NC * S, 128], [S, NC], [1, S]]),
                A(sc_xnt, b * D * S, [[S, 128], [128 * S, NC], [1, S]])), inc=s_lh, incval=16)
            P.barrier()
            for (wname, bcol, OUT, scale) in (("fWq", fbq_c, QfT, INV32), ("fWk", fbk_c, KfT, None)):
                P.barrier(("tensor", "gpsimd", "vector"))
                P.emit("gpsimd", lambda g, wname=wname, wpro=wpro: g.dma_start(
                    wpro[:, :], A(W[wname], 0, [[D, 128], [128 * D, NC], [1, D]])), inc=s_ld, incval=16)
                P.wait("tensor", s_ld)
                for dc in range(NC):
                    for tc in range(4):
                        pw = p_a if (dc * 4 + tc) % 2 == 0 else p_b
                        P.wait("tensor", s_v)
                        for f in range(NC):
                            P.emit("tensor", lambda t, dc=dc, tc=tc, f=f, pw=pw, wpro=wpro, XNT2=XNT2: t.matmul(
                                pw[:, :], wpro[:, f * 1024 + dc * 128: f * 1024 + (dc + 1) * 128],
                                XNT2[:, f * S + tc * 512: f * S + (tc + 1) * 512],
                                start=(f == 0), stop=(f == NC - 1)), inc=(s_t if f == NC - 1 else None))
                        P.wait("vector", s_t)
                        if scale is None:
                            P.emit("vector", lambda v, dc=dc, tc=tc, pw=pw, OUT=OUT, bcol=bcol: v.tensor_scalar(
                                OUT[:, dc * S + tc * 512: dc * S + (tc + 1) * 512],
                                pw[:, :], bcol[:, dc:dc + 1], None, OP.add), inc=s_v)
                        else:
                            P.emit("vector", lambda v, dc=dc, tc=tc, pw=pw, OUT=OUT, bcol=bcol, scale=scale: v.tensor_scalar(
                                OUT[:, dc * S + tc * 512: dc * S + (tc + 1) * 512],
                                pw[:, :], bcol[:, dc:dc + 1], scale, OP.add, OP.mult), inc=s_v)
            P.barrier(("tensor", "vector", "gpsimd"))
            P.emit("gpsimd", lambda g, wpro=wpro: g.dma_start(
                wpro[:, :], A(W["fWv"], 0, [[D, 128], [128 * D, NC], [1, D]])), inc=s_ld, incval=16)
            P.barrier(("tensor", "gpsimd", "vector"))
            for ti in range(NT):
                P.wait("tensor", s_v)
                for f in range(NC):
                    for hf in range(2):
                        P.emit("tensor", lambda t, ti=ti, f=f, hf=hf, wpro=wpro, XNT2=XNT2: t.matmul(
                            p_v[:, hf * 512:(hf + 1) * 512],
                            XNT2[:, f * S + ti * 128: f * S + ti * 128 + 128],
                            wpro[:, f * 1024 + hf * 512: f * 1024 + (hf + 1) * 512],
                            start=(f == 0), stop=False))
                for hf in range(2):
                    P.emit("tensor", lambda t, hf=hf: t.matmul(
                        p_v[:, hf * 512:(hf + 1) * 512], ones_r[32:33, :],
                        vb_r[32:33, hf * 512:(hf + 1) * 512], start=False, stop=True),
                        inc=(s_t if hf == 1 else None))
                P.wait("vector", s_t)
                P.emit("vector", lambda v, ti=ti, VfTOK=VfTOK: v.tensor_copy(
                    VfTOK[:, ti * D:(ti + 1) * D], p_v[:, :]), inc=s_v)
            P.barrier()

        span_xnt2.close()
        afT = span_f.enter_context(nc.sbuf_tensor(f"afT_{b}", [128, NC * S], BF16, side="right"))
        with contextlib.ExitStack() as p6b:
            pbuf6 = [p6b.enter_context(nc.sbuf_tensor(f"pbuf6{i}_{b}", [128, S], BF16)) for i in range(2)]
            pTq = p6b.enter_context(nc.sbuf_tensor(f"pTq6_{b}", [128, NT * 512], BF16))
            mcol = p6b.enter_context(nc.sbuf_tensor(f"mcol6_{b}", [128, 8], F32))
            p_s2 = p6b.enter_context(nc.psum_tensor(f"p6_s2_{b}", [128, 2048], F32))
            p_pv1 = p6b.enter_context(nc.psum_tensor(f"p6_pv1_{b}", [128, 512], F32))
            p_pv2 = p6b.enter_context(nc.psum_tensor(f"p6_pv2_{b}", [128, 512], F32))
            mk6 = {"exp": {}, "esum": {}, "pn": {}, "ptd": {}}
            pv_read = [0]
            for qt in range(NT):
                quad, qi = qt // 4, qt % 4
                pb = pbuf6[qt % 2]
                mc = (qt % 2) * 4
                P.wait("tensor", s_a, mk6["exp"].get(qt - 1, 0))
                for kc in range(4):
                    for f in range(NC):
                        P.emit("tensor", lambda t, qt=qt, kc=kc, f=f, QfT=QfT, KfT=KfT: t.matmul(
                            p_s2[:, kc * 512:(kc + 1) * 512],
                            QfT[:, f * S + qt * 128: f * S + qt * 128 + 128],
                            KfT[:, f * S + kc * 512: f * S + (kc + 1) * 512],
                            start=(f == 0), stop=(f == NC - 1)),
                            inc=(s_t if (kc == 3 and f == NC - 1) else None))
                sc_mark = s_t.n
                P.wait("scalar", s_t, sc_mark)
                P.wait("scalar", s_pt, mk6["ptd"].get(qt - 2, 0))
                P.emit("scalar", lambda a, pb=pb, mc=mc: a.activation(
                    pb[:, :], p_s2[:, :], AF.Exp, scale=1.0,
                    accum_out=mcol[:, mc:mc + 1]), inc=s_a)
                mk6["exp"][qt] = s_a.n
                P.wait("vector", s_a, mk6["exp"][qt])
                P.emit("vector", lambda v, mc=mc: v.reciprocal(
                    mcol[:, mc + 1:mc + 2], mcol[:, mc:mc + 1]), inc=s_v)
                mk6["esum"][qt] = s_v.n
                P.wait("scalar", s_v, mk6["esum"][qt])
                P.emit("scalar", lambda a, pb=pb, mc=mc: a.activation(
                    pb[:, :], pb[:, :], AF.Copy, scale=mcol[:, mc + 1:mc + 2]), inc=s_a)
                mk6["pn"][qt] = s_a.n
                P.wait("sync", s_a, mk6["pn"][qt])
                P.wait("sync", s_t, pv_read[0])
                P.emit("sync", lambda s, pb=pb, qi=qi, pTq=pTq: s.dma_start_transpose(
                    A(pTq, qi * 128, [[NT * 512, 128], [512, NT], [1, 128]]),
                    A(pb, 0, [[S, 128], [128, NT], [1, 128]])), inc=s_pt, incval=16)
                mk6["ptd"][qt] = s_pt.n
                if qi == 3:
                    P.wait("tensor", s_pt, mk6["ptd"][qt])
                    for dc in range(NC):
                        pv = p_pv1 if dc % 2 == 0 else p_pv2
                        if dc >= 2:
                            P.wait("tensor", s_v)
                        for kt in range(NT):
                            P.emit("tensor", lambda t, dc=dc, kt=kt, pv=pv, VfTOK=VfTOK, pTq=pTq: t.matmul(
                                pv[:, :], VfTOK[:, kt * D + dc * 128: kt * D + (dc + 1) * 128],
                                pTq[:, kt * 512:(kt + 1) * 512],
                                start=(kt == 0), stop=(kt == NT - 1)), inc=(s_t if kt == NT - 1 else None))
                        P.wait("vector", s_t)
                        P.emit("vector", lambda v, dc=dc, quad=quad, pv=pv, afT=afT: v.tensor_copy(
                            afT[:, dc * S + quad * 512: dc * S + (quad + 1) * 512], pv[:, :]), inc=s_v)
                    pv_read[0] = s_t.n
                    P.wait("tensor", s_v)
            P.barrier()
        span_qf.close()

        with contextlib.ExitStack() as p6c:
            XN32 = p6c.enter_context(nc.sbuf_tensor(f"XN32b_{b}", [128, NC * S], F32))
            XFT = p6c.enter_context(nc.sbuf_tensor(f"XFT_{b}", [128, NC * S], BF16))
            wo2 = p6c.enter_context(nc.sbuf_tensor(f"wo2_{b}", [128, NC * 1024], BF16))
            xfrow = p6c.enter_context(nc.sbuf_tensor(f"xfrow_{b}", [128, S], F32))
            g_s = p6c.enter_context(nc.sbuf_tensor(f"g_s_{b}", [E, S], F32))
            gt_s = p6c.enter_context(nc.sbuf_tensor(f"gt_s_{b}", [128, E], F32))
            gcol = p6c.enter_context(nc.sbuf_tensor(f"gcol_{b}", [128, 4], F32))
            stg = p6c.enter_context(nc.sbuf_tensor(f"stg6_{b}", [128, D], F32))
            p_w1 = p6c.enter_context(nc.psum_tensor(f"p6_w1_{b}", [128, 512], F32))
            p_w2 = p6c.enter_context(nc.psum_tensor(f"p6_w2_{b}", [128, 512], F32))
            p_g = p6c.enter_context(nc.psum_tensor(f"p6_g_{b}", [E, 512], F32))
            p_gt = p6c.enter_context(nc.psum_tensor(f"p6_gt_{b}", [128, E], F32))
            p_t1 = p6c.enter_context(nc.psum_tensor(f"p6_t1_{b}", [128, 512], F32))
            p_t2 = p6c.enter_context(nc.psum_tensor(f"p6_t2_{b}", [128, 512], F32))
            P.emit("sync", lambda s, b=b, XN32=XN32: s.dma_start(
                A(XN32, 0, [[NC * S, 128], [S, NC], [1, S]]),
                A(sc_xn32, b * D * S, [[S, 128], [128 * S, NC], [1, S]])), inc=s_lh, incval=16)
            P.emit("gpsimd", lambda g, wo2=wo2: g.dma_start(
                wo2[:, :], A(W["fWo"], 0, [[D, 128], [128 * D, NC], [1, D]])), inc=s_ld, incval=16)
            P.barrier()
            stg_tok = p6c.enter_context(nc.sbuf_tensor(f"stg_tok_{b}", [128, S], F32))
            for dc in range(NC):
                for tc in range(4):
                    pw = p_w1 if tc % 2 == 0 else p_w2
                    for f in range(NC):
                        P.emit("tensor", lambda t, dc=dc, tc=tc, f=f, pw=pw, wo2=wo2, afT=afT: t.matmul(
                            pw[:, :], wo2[:, f * 1024 + dc * 128: f * 1024 + (dc + 1) * 128],
                            afT[:, f * S + tc * 512: f * S + (tc + 1) * 512],
                            start=(f == 0), stop=(f == NC - 1)), inc=(s_t if f == NC - 1 else None))
                    P.wait("vector", s_t)
                    P.chain("vector", [
                        lambda v, dc=dc, tc=tc, pw=pw, XN32=XN32, xfrow=xfrow: v.scalar_tensor_tensor(
                            xfrow[:, tc * 512:(tc + 1) * 512], pw[:, :], fbo_c[:, dc:dc + 1],
                            XN32[:, dc * S + tc * 512: dc * S + (tc + 1) * 512], OP.add, OP.add),
                        lambda v, dc=dc, tc=tc, xfrow=xfrow, XFT=XFT: v.tensor_copy(
                            XFT[:, dc * S + tc * 512: dc * S + (tc + 1) * 512],
                            xfrow[:, tc * 512:(tc + 1) * 512]),
                    ], s_v)
                    P.wait("tensor", s_v)
                P.wait("tensor", s_st, s_st.n)
                for g4 in range(4):
                    pt = p_t1 if g4 % 2 == 0 else p_t2
                    for j in range(4):
                        P.emit("tensor", lambda t, g4=g4, j=j, pt=pt, xfrow=xfrow: t.transpose(
                            pt[:, j * 128:(j + 1) * 128],
                            xfrow[:, (g4 * 4 + j) * 128:(g4 * 4 + j + 1) * 128], ident_f[:, :]),
                            inc=(s_t if j == 3 else None))
                    P.wait("vector", s_t)
                    P.emit("vector", lambda v, g4=g4, pt=pt, stg_tok=stg_tok: v.tensor_copy(
                        stg_tok[:, g4 * 512:(g4 + 1) * 512], pt[:, :]), inc=s_v)
                    P.wait("tensor", s_v)
                P.wait("sync", s_v)
                P.emit("sync", lambda s, b=b, dc=dc, stg_tok=stg_tok: s.dma_start(
                    A(sc_xf32, b * S * D + dc * 128, [[D, 128], [128 * D, NT], [1, 128]]),
                    A(stg_tok, 0, [[S, 128], [128, NT], [1, 128]])), inc=s_st, incval=16)
                P.barrier(("tensor", "vector", "sync"))
            P.emit("sync", lambda s, b=b, XFT=XFT: s.dma_start(
                A(sc_xft, b * D * S, [[S, 128], [128 * S, NC], [1, S]]),
                A(XFT, 0, [[NC * S, 128], [S, NC], [1, S]])), inc=s_st, incval=16)
            for tc in range(4):
                for f in range(NC):
                    P.emit("tensor", lambda t, tc=tc, f=f, XFT=XFT: t.matmul(
                        p_g[0:E, :], wg_bf[:, f * E:(f + 1) * E],
                        XFT[:, f * S + tc * 512: f * S + (tc + 1) * 512],
                        start=(f == 0), stop=(f == NC - 1)), inc=(s_t if f == NC - 1 else None))
                P.wait("vector", s_t)
                P.emit("vector", lambda v, tc=tc, g_s=g_s: v.tensor_scalar(
                    g_s[0:E, tc * 512:(tc + 1) * 512], p_g[0:E, :], fbg_c[0:E, 0:1], None, OP.add), inc=s_v)
                P.wait("tensor", s_v)
            for ti in range(NT):
                P.emit("tensor", lambda t, ti=ti, g_s=g_s: t.transpose(
                    p_gt[:, 0:E], g_s[0:E, ti * 128:(ti + 1) * 128], ident_f[0:E, 0:E]), inc=s_t)
                P.wait("vector", s_t)
                P.chain("vector", [
                    lambda v, gt_s=gt_s: v.tensor_copy(gt_s[:, :], p_gt[:, 0:E]),
                    lambda v, gt_s=gt_s, gcol=gcol: v.reduce_max(gcol[:, 0:1], gt_s[:, :], axis=AXX),
                    lambda v, gcol=gcol: v.tensor_scalar(gcol[:, 1:2], gcol[:, 0:1], -1.0, None, OP.mult),
                ], s_v)
                P.wait("scalar", s_v)
                P.emit("scalar", lambda a, gt_s=gt_s, gcol=gcol: a.activation(
                    gt_s[:, :], gt_s[:, :], AF.Exp, bias=gcol[:, 1:2], scale=1.0,
                    accum_out=gcol[:, 2:3]), inc=s_a)
                P.wait("vector", s_a)
                P.chain("vector", [
                    lambda v, gcol=gcol: v.reciprocal(gcol[:, 3:4], gcol[:, 2:3]),
                    lambda v, b=b, ti=ti, gt_s=gt_s, gcol=gcol: v.tensor_scalar(
                        gates_t[:, (b * NT + ti) * E:(b * NT + ti + 1) * E], gt_s[:, :],
                        gcol[:, 3:4], None, OP.mult),
                ], s_v)
                P.wait("tensor", s_v)
            P.wait("sync", s_st)
            P.barrier()
        span_f.close()
        P.barrier()

        if stage == "p6" and b == 0:
            tap("gates_t", gates_t, 128, NT * E, F32)
            return finish_stage()

    with contextlib.ExitStack() as p7:
        acc = p7.enter_context(nc.sbuf_tensor("acc", [128, NT * D], F32))
        w1h = p7.enter_context(nc.sbuf_tensor("w1h", [128, NC * 2048], BF16))
        w2hb = [p7.enter_context(nc.sbuf_tensor(f"w2h{i}", [128, 16 * 1024], BF16)) for i in range(2)]
        h_sb = p7.enter_context(nc.sbuf_tensor("h_sb", [128, 16 * 512], BF16))
        xfs = p7.enter_context(nc.sbuf_tensor("xfs", [128, NC * 512], BF16))
        xfs2 = p7.enter_context(nc.sbuf_tensor("xfs2", [128, NC * 512], BF16))
        aob = p7.enter_context(nc.sbuf_tensor("aob", [128, D], F32))
        p_hb = [p7.enter_context(nc.psum_tensor(f"p7_h{i}", [128, 512], F32)) for i in range(4)]
        p_e1 = p7.enter_context(nc.psum_tensor("p7_e1", [128, 1024], F32))
        p_e2 = p7.enter_context(nc.psum_tensor("p7_e2", [128, 1024], F32))

        marks = {"w1q": [0, 0, 0, 0], "w2rd": [0, 0], "xfs": [0, 0],
                 "relu": [0, 0, 0, 0], "stt": [0, 0], "b2": 0}
        ld_w1q = [0, 0, 0, 0]
        ld_w2q = [0, 0, 0, 0]
        for tb in range(B_LOC):
            P.wait("sync", s_st)
            init_mk = [0] * NT
            for e in range(E):
                P.wait("sync", s_t, marks["b2"])
                P.emit("sync", lambda s, e=e, vb_r=vb_r: s.dma_start(
                    vb_r[64:65, :], A(W["fb2"], e * D, [[D, 1], [1, D]])), inc=s_lb, incval=16)
                lb_mark = s_lb.n
                for dh in range(2):
                    par2 = (tb * 16 + e * 2 + dh) % 2
                    w2h = w2hb[par2]
                    for q in range(4):
                        P.wait("gpsimd", s_t, marks["w1q"][q])
                        P.emit("gpsimd", lambda g, e=e, dh=dh, q=q, w1h=w1h: g.dma_start(
                            A(w1h, q * 512, [[NC * 2048, 128], [2048, NC], [1, 512]]),
                            A(W["fW1"], e * D * FF + dh * 2048 + q * 512,
                              [[FF, 128], [128 * FF, NC], [1, 512]])), inc=s_ld, incval=16)
                        ld_w1q[q] = s_ld.n
                    ld_w1 = s_ld.n
                    P.wait("gpsimd", s_t, marks["w2rd"][par2])
                    for q in range(4):
                        P.emit("gpsimd", lambda g, e=e, dh=dh, q=q, w2h=w2h: g.dma_start(
                            A(w2h, q * 4 * 1024, [[16 * 1024, 128], [1024, 4], [1, 1024]]),
                            A(W["fW2"], e * FF * D + (dh * 2048 + q * 4 * 128) * D,
                              [[D, 128], [128 * D, 4], [1, D]])), inc=s_ld, incval=16)
                    ld_w2 = s_ld.n
                    for tc in range(4):
                        xf_cur = xfs if tc % 2 == 0 else xfs2
                        lxs = s_lx if tc % 2 == 0 else s_lx2
                        P.wait("sync", s_t, marks["xfs"][tc % 2])
                        P.emit("sync", lambda s, tb=tb, tc=tc, xf_cur=xf_cur: s.dma_start(
                            A(xf_cur, 0, [[NC * 512, 128], [512, NC], [1, 512]]),
                            A(sc_xft, tb * D * S + tc * 512, [[S, 128], [128 * S, NC], [1, 512]])),
                            inc=lxs, incval=16)
                        P.wait("tensor", lxs, lxs.n)
                        if e == 0 and dh == 0:
                            P.emit("sync", lambda s, tb=tb, tc=tc, acc=acc: s.dma_start(
                                A(acc, tc * 4 * D, [[NT * D, 128], [D, 4], [1, D]]),
                                A(x_in, tb * S * D + tc * 4 * 128 * D,
                                  [[D, 128], [128 * D, 4], [1, D]])), inc=s_lh, incval=16)
                            for ti4 in range(4):
                                tix = tc * 4 + ti4
                                for src_t in (sc_aout, sc_xf32):
                                    P.wait("sync", s_v)
                                    P.emit("sync", lambda s, tb=tb, tix=tix, src_t=src_t: s.dma_start(
                                        aob[:, :], A(src_t, tb * S * D + tix * 128 * D,
                                                     [[D, 128], [1, D]])), inc=s_lh, incval=16)
                                    P.wait("vector", s_lh)
                                    P.emit("vector", lambda v, tix=tix: v.tensor_tensor(
                                        acc[:, tix * D:(tix + 1) * D],
                                        acc[:, tix * D:(tix + 1) * D], aob[:, :], OP.add), inc=s_v)
                                init_mk[tix] = s_v.n
                        for hd in range(16):
                            ph_ = p_hb[hd % 4]
                            P.wait("tensor", s_a, marks["relu"][hd % 4])
                            if tc == 0 and hd == 0:
                                P.wait("tensor", s_ld, ld_w1)
                            for f in range(NC):
                                P.emit("tensor", lambda t, hd=hd, f=f, ph_=ph_, w1h=w1h, xf_cur=xf_cur: t.matmul(
                                    ph_[:, :], w1h[:, f * 2048 + hd * 128: f * 2048 + (hd + 1) * 128],
                                    xf_cur[:, f * 512:(f + 1) * 512],
                                    start=(f == 0), stop=(f == NC - 1)), inc=(s_t if f == NC - 1 else None))
                            if tc == 3 and hd % 4 == 3:
                                marks["w1q"][hd // 4] = s_t.n
                            if hd == 15:
                                marks["xfs"][tc % 2] = s_t.n
                            P.wait("scalar", s_t)
                            P.emit("scalar", lambda a, e=e, dh=dh, hd=hd, ph_=ph_, h_sb=h_sb: a.activation(
                                h_sb[:, hd * 512:(hd + 1) * 512], ph_[:, :], AF.Relu,
                                bias=b1_c[:, e * 32 + dh * 16 + hd: e * 32 + dh * 16 + hd + 1]), inc=s_a)
                            marks["relu"][hd % 4] = s_a.n
                        relu_all = s_a.n
                        P.wait("tensor", s_a, relu_all)
                        for ti in range(4):
                            pe = p_e1 if ti % 2 == 0 else p_e2
                            P.wait("tensor", s_v, marks["stt"][ti % 2])
                            if tc == 0 and ti == 0:
                                P.wait("tensor", s_ld, ld_w2)
                            if dh == 0 and tc == 0 and ti == 0:
                                P.wait("tensor", s_lb, lb_mark)
                            if e == 0 and dh == 0:
                                P.wait("vector", s_v, init_mk[tc * 4 + ti])
                            for hd in range(16):
                                for hf in range(2):
                                    stop = (dh == 1 and hd == 15)
                                    P.emit("tensor", lambda t, hd=hd, hf=hf, ti=ti, pe=pe, h_sb=h_sb, w2h=w2h, stop=stop: t.matmul(
                                        pe[:, hf * 512:(hf + 1) * 512],
                                        h_sb[:, hd * 512 + ti * 128: hd * 512 + (ti + 1) * 128],
                                        w2h[:, hd * 1024 + hf * 512: hd * 1024 + (hf + 1) * 512],
                                        start=(hd == 0), stop=stop),
                                        inc=(s_t if (dh == 1 and hd == 15 and hf == 1) else None))
                                if tc == 3 and ti == 3 and hd % 4 == 3:
                                    pass
                            if dh == 0:
                                for hf in range(2):
                                    P.emit("tensor", lambda t, e=e, hf=hf, pe=pe: t.matmul(
                                        pe[:, hf * 512:(hf + 1) * 512], ones_r[64:65, :],
                                        vb_r[64:65, hf * 512:(hf + 1) * 512],
                                        start=False, stop=True), inc=(s_t if hf == 1 else None))
                            if tc == 3 and ti == 3:
                                marks["w2rd"][par2] = s_t.n
                            if dh == 0 and tc == 3 and ti == 3:
                                marks["b2"] = s_t.n
                            eo_done = s_t.n
                            P.wait("vector", s_t, eo_done)
                            tix = tc * 4 + ti
                            P.emit("vector", lambda v, e=e, tb=tb, tix=tix, pe=pe, acc=acc: v.scalar_tensor_tensor(
                                acc[:, tix * D:(tix + 1) * D], pe[:, :],
                                gates_t[:, (tb * NT + tix) * E + e:(tb * NT + tix) * E + e + 1],
                                acc[:, tix * D:(tix + 1) * D], OP.mult, OP.add), inc=s_v)
                            marks["stt"][ti % 2] = s_v.n
            P.barrier(("tensor", "vector", "sync"))
            P.emit("sync", lambda s, tb=tb, acc=acc: s.dma_start(
                A(y_out, tb * S * D, [[D, 128], [128 * D, NT], [1, D]]),
                A(acc, 0, [[NT * D, 128], [D, NT], [1, D]])), inc=s_st, incval=16)
        P.wait("sync", s_st)
        P.barrier()

    stack.close()
    P.replay()
    P.close()
    return nc


def _shard_inputs(inputs):
    x = np.ascontiguousarray(inputs["L0_attn_x"], dtype=np.float32)
    ren = {
        "aWq": "L0_attn_Wq", "abq": "L0_attn_bq", "aWk": "L0_attn_Wk", "abk": "L0_attn_bk",
        "aWv": "L0_attn_Wv", "abv": "L0_attn_bv", "aWc2q": "L0_attn_Wc2q", "abc2q": "L0_attn_bc2q",
        "aWo": "L0_attn_Wo", "abo": "L0_attn_bo", "ag1": "L0_attn_g1", "abeta1": "L0_attn_beta1",
        "ag2": "L0_attn_g2", "abeta2": "L0_attn_beta2", "aW1": "L0_attn_W1", "ab1": "L0_attn_b1",
        "aWm": "L0_attn_Wm", "abm": "L0_attn_bm", "aWqo": "L0_attn_Wqo", "abqo": "L0_attn_bqo",
        "aW2": "L0_attn_W2", "ab2": "L0_attn_b2",
        "fWq": "L0_ffn_Wq", "fbq": "L0_ffn_bq", "fWk": "L0_ffn_Wk", "fbk": "L0_ffn_bk",
        "fWv": "L0_ffn_Wv", "fbv": "L0_ffn_bv", "fWo": "L0_ffn_Wo", "fbo": "L0_ffn_bo",
        "fWg": "L0_ffn_Wg", "fbg": "L0_ffn_bg", "fW1": "L0_ffn_W1", "fb1": "L0_ffn_b1",
        "fW2": "L0_ffn_W2", "fb2": "L0_ffn_b2",
    }
    w = {k: np.ascontiguousarray(inputs[src], dtype=np.float32) for k, src in ren.items()}
    in_maps = []
    for c in range(8):
        m = {"x": x[2 * c:2 * c + 2]}
        m.update(w)
        in_maps.append(m)
    return in_maps


def kernel(**inputs):
    in_maps = _shard_inputs(inputs)
    nc = build()
    res = run_bass_kernel_spmd(nc, in_maps, core_ids=list(range(8)))
    return np.concatenate([r["y"] for r in res.results], axis=0)


# version 16
# speedup vs baseline: 1.1077x; 1.0652x over previous
import contextlib
import types
import numpy as np
import concourse.bass as bass
import concourse.mybir as mybir
from concourse.bass_utils import run_bass_kernel_spmd

F32 = mybir.dt.float32
BF16 = mybir.dt.bfloat16
AF = mybir.ActivationFunctionType
OP = mybir.AluOpType
AXX = mybir.AxisListType.X

PI_LO = float(np.nextafter(np.float32(np.pi), np.float32(0)))
PI8_LO = float(np.nextafter(np.float32(np.pi / 8), np.float32(0)))
ISQHD = float(np.float32(1.0 / np.sqrt(128.0)))
INV32 = float(np.float32(1.0 / 32.0))
LN_EPS = 1e-5

B_LOC, S, D, H, HD, NQ, FF, E = 2, 2048, 1024, 8, 128, 8, 4096, 8
NT = S // 128
NC = D // 128
ENGS = ("tensor", "vector", "scalar", "gpsimd", "sync")


class Sem:
    __slots__ = ("h", "n", "name")

    def __init__(self, h, name):
        self.h, self.n, self.name = h, 0, name


class Prog:
    def __init__(self, nc):
        self.nc = nc
        self.q = {e: [] for e in ENGS}
        self._cms = []
        self.sems = {}

    def sem(self, name):
        if name not in self.sems:
            cm = self.nc.semaphore(name)
            self.sems[name] = Sem(cm.__enter__(), name)
            self._cms.append(cm)
        return self.sems[name]

    @staticmethod
    def _freeze(fn):
        if fn.__closure__:
            cells = tuple(types.CellType(c.cell_contents) for c in fn.__closure__)
            return types.FunctionType(fn.__code__, fn.__globals__, fn.__name__,
                                      fn.__defaults__, cells)
        return fn

    def emit(self, eng, fn, inc=None, incval=1):
        fn = self._freeze(fn)
        if inc is not None:
            inc.n += incval

        def op(h, fn=fn, inc=inc, incval=incval):
            r = fn(h)
            if inc is not None:
                r.then_inc(inc.h, incval)
        self.q[eng].append(op)

    def wait(self, eng, sm, v=None):
        v = sm.n if v is None else v
        self.q[eng].append(lambda h, sm=sm, v=v: h.wait_ge(sm.h, v))

    dma_sems = ()

    def barrier(self, engs=ENGS):
        bar = self.sem("bar_" + "".join(sorted(e[0] + e[1] for e in engs)))
        for e in engs:
            for sm in self.dma_sems:
                self.wait(e, sm)
            self.q[e].append(lambda h: h.drain())
            self.q[e].append(lambda h, b=bar: h.drain().then_inc(b.h, 1))
        bar.n += len(engs)
        tgt = bar.n
        for e in engs:
            self.q[e].append(lambda h, b=bar, t=tgt: h.wait_ge(b.h, t))

    def chain(self, eng, fns, sem, final_inc=True):
        for i, fn in enumerate(fns):
            if i > 0:
                self.wait(eng, sem)
            last = (i == len(fns) - 1)
            self.emit(eng, fn, inc=(sem if (not last or final_inc) else None))

    def replay(self):
        with self.nc.Block() as block:
            @block.tensor
            def _(t):
                for op in self.q["tensor"]:
                    op(t)

            @block.vector
            def _(v):
                for op in self.q["vector"]:
                    op(v)

            @block.scalar
            def _(a):
                for op in self.q["scalar"]:
                    op(a)

            @block.gpsimd
            def _(g):
                for op in self.q["gpsimd"]:
                    op(g)

            @block.sync
            def _(s):
                for op in self.q["sync"]:
                    op(s)

    def close(self):
        for cm in reversed(self._cms):
            cm.__exit__(None, None, None)


def A(handle, offset, dims):
    return bass.AP(handle, int(offset), [list(d) for d in dims])


def build(stage=None, taps=()):
    nc = bass.Bass("TRN2", target_bir_lowering=False)
    P = Prog(nc)
    stack = contextlib.ExitStack()

    def sbuf(name, shape, dtype):
        return stack.enter_context(nc.sbuf_tensor(name, shape, dtype))

    x_in = nc.dram_tensor("x", [B_LOC, S, D], F32, kind="ExternalInput")
    y_out = nc.dram_tensor("y", [B_LOC, S, D], F32, kind="ExternalOutput")
    wshape = {
        "aWq": [H, D, HD], "abq": [H, HD], "aWk": [H, D, HD], "abk": [H, HD],
        "aWv": [H, D, HD], "abv": [H, HD], "aWc2q": [H, HD, NQ], "abc2q": [H, NQ],
        "aWo": [D, D], "abo": [D], "ag1": [D], "abeta1": [D], "ag2": [D], "abeta2": [D],
        "aW1": [D, FF], "ab1": [FF], "aWm": [FF, NQ], "abm": [NQ],
        "aWqo": [NQ, FF], "abqo": [FF], "aW2": [FF, D], "ab2": [D],
        "fWq": [D, D], "fbq": [D], "fWk": [D, D], "fbk": [D],
        "fWv": [D, D], "fbv": [D], "fWo": [D, D], "fbo": [D],
        "fWg": [D, E], "fbg": [E], "fW1": [E, D, FF], "fb1": [E, FF],
        "fW2": [E, FF, D], "fb2": [E, D],
    }
    W = {k: nc.dram_tensor(k, v, F32, kind="ExternalInput") for k, v in wshape.items()}
    sc_xnt = nc.dram_tensor("sc_xnt", [B_LOC, D, S], BF16)
    sc_xn32 = nc.dram_tensor("sc_xn32", [B_LOC, D, S], F32)
    sc_aout = nc.dram_tensor("sc_aout", [B_LOC, S, D], F32)
    sc_xf32 = nc.dram_tensor("sc_xf32", [B_LOC, S, D], F32)
    sc_xft = nc.dram_tensor("sc_xft", [B_LOC, D, S], BF16)
    sc_vtok = nc.dram_tensor("sc_vtok", [B_LOC, 128, NT * D], BF16)

    s_ld = P.sem("ld")
    s_g = P.sem("g")
    s_lh = P.sem("lh")
    s_st = P.sem("st")
    s_t = P.sem("t")
    s_v = P.sem("v")
    s_a = P.sem("a")
    s_lx = P.sem("lx")
    s_lx2 = P.sem("lx2")
    s_lv = P.sem("lv")
    s_lv2 = P.sem("lv2")
    s_lb = P.sem("lb")
    s_pt = P.sem("pt")
    s_ld2 = P.sem("ld2")
    P.dma_sems = (s_ld, s_lh, s_st, s_lx, s_lx2, s_lv, s_lv2, s_lb, s_pt, s_ld2)

    tapset = set(taps)
    tap_req = []

    def tap(name, buf, rows, cols, dtype):
        if name in tapset:
            t = nc.dram_tensor("tap_" + name, [rows, cols], dtype, kind="ExternalOutput")
            tap_req.append((t, buf, rows, cols))

    open_spans = []

    def finish_stage():
        P.barrier()
        for (t, buf, rows, cols) in tap_req:
            P.emit("sync", lambda s, t=t, buf=buf, rows=rows, cols=cols: s.dma_start(
                A(t, 0, [[cols, rows], [1, cols]]),
                A(buf, 0, [[int(buf.shape[1]), rows], [1, cols]])), inc=s_st, incval=16)
        P.wait("sync", s_st)
        P.barrier()
        for sp in reversed(open_spans):
            sp.close()
        stack.close()
        P.replay()
        P.close()
        return nc

    ident_bf = sbuf("ident_bf", [128, 128], BF16)
    ident_f = sbuf("ident_f", [128, 128], F32)
    ones_c = sbuf("ones_c", [128, 1], F32)
    ones_r = sbuf("ones_r", [65, 128], F32)
    iota_r = sbuf("iota_r", [128, 128], F32)
    pidx = sbuf("pidx", [128, 1], F32)

    bq_c = sbuf("bq_c", [128, H], F32)
    bk_c = sbuf("bk_c", [128, H], F32)
    vb_r = sbuf("vb_r", [65, D], F32)
    bc2q_r = sbuf("bc2q_r", [1, H * NQ], F32)
    bc2q_rq = sbuf("bc2q_rq", [1, H * NQ], F32)
    bo_c = sbuf("bo_c", [128, NC], F32)
    g1_c = sbuf("g1_c", [128, NC], F32)
    be1_c = sbuf("be1_c", [128, NC], F32)
    g2_c = sbuf("g2_c", [128, NC], F32)
    be2_c = sbuf("be2_c", [128, NC], F32)
    fbq_c = sbuf("fbq_c", [128, NC], F32)
    fbk_c = sbuf("fbk_c", [128, NC], F32)
    fbo_c = sbuf("fbo_c", [128, NC], F32)
    fbg_c = sbuf("fbg_c", [E, 1], F32)
    b1_c = sbuf("b1_c", [128, E * 32], F32)
    m1_bf = sbuf("m1_bf", [128, NC * NQ], BF16)
    bias1_c = sbuf("bias1_c", [NQ, 1], F32)
    m2_bf = sbuf("m2_bf", [NQ, D], BF16)
    bias2_c = sbuf("bias2_c", [128, NC], F32)
    wc2q_bf = sbuf("wc2q_bf", [128, H * NQ], BF16)
    wg_bf = sbuf("wg_bf", [128, NC * E], BF16)
    gates_t = sbuf("gates_t", [128, B_LOC * NT * E], F32)

    st_a = sbuf("st_a", [128, 32], F32)
    st_b = sbuf("st_b", [128, 32], F32)
    st_c = sbuf("st_c", [128, 32], F32)

    P.emit("gpsimd", lambda g: g.iota(iota_r[:, :], [[1, 128]], channel_multiplier=0,
                                      allow_small_or_imprecise_dtypes=True))
    P.emit("gpsimd", lambda g: g.iota(pidx[:, :], [[1, 1]], channel_multiplier=1,
                                      allow_small_or_imprecise_dtypes=True))
    P.emit("gpsimd", lambda g: g.memset(ones_c[:, :], 1.0))
    P.emit("gpsimd", lambda g: g.memset(ones_r[:, :], 1.0), inc=s_g)

    def ldpar(dst, w, dims, eng="sync", off=0):
        sem = s_lh if eng == "sync" else s_ld
        P.emit(eng, lambda h, dst=dst, w=w, dims=dims, off=off: h.dma_start(
            dst, A(W[w], off, dims), allow_slow_non_contiguous=True), inc=sem, incval=16)

    ldpar(bo_c[:, :], "abo", [[1, 128], [128, NC]])
    ldpar(g1_c[:, :], "ag1", [[1, 128], [128, NC]])
    ldpar(be1_c[:, :], "abeta1", [[1, 128], [128, NC]])
    ldpar(g2_c[:, :], "ag2", [[1, 128], [128, NC]])
    ldpar(be2_c[:, :], "abeta2", [[1, 128], [128, NC]])
    ldpar(fbq_c[:, :], "fbq", [[1, 128], [128, NC]])
    ldpar(fbk_c[:, :], "fbk", [[1, 128], [128, NC]])
    ldpar(fbo_c[:, :], "fbo", [[1, 128], [128, NC]])
    ldpar(bq_c[:, :], "abq", [[1, 128], [128, H]])
    ldpar(bk_c[:, :], "abk", [[1, 128], [128, H]])
    ldpar(vb_r[0:1, :], "abv", [[D, 1], [1, D]])
    ldpar(bc2q_r[0:1, :], "abc2q", [[H * NQ, 1], [1, H * NQ]])
    ldpar(fbg_c[0:E, 0:1], "fbg", [[1, E], [1, 1]])
    ldpar(vb_r[32:33, :], "fbv", [[D, 1], [1, D]])
    ldpar(b1_c[:, :], "fb1", [[1, 128], [FF, E], [128, 32]])
    ldpar(wc2q_bf[:, :], "aWc2q", [[NQ, 128], [128 * NQ, H], [1, NQ]], eng="gpsimd")
    ldpar(wg_bf[:, :], "fWg", [[E, 128], [128 * E, NC], [1, E]], eng="gpsimd")
    P.barrier()
    P.emit("vector", lambda v: v.tensor_scalar(ident_bf[:, :], iota_r[:, :], pidx[:, 0:1], None, OP.is_equal))
    P.emit("vector", lambda v: v.tensor_scalar(ident_f[:, :], iota_r[:, :], pidx[:, 0:1], None, OP.is_equal))
    P.emit("vector", lambda v: v.tensor_scalar(bc2q_rq[0:1, :], bc2q_r[0:1, :], ISQHD, None, OP.mult), inc=s_v)
    P.barrier()

    with contextlib.ExitStack() as pp:
        w1blk = pp.enter_context(nc.sbuf_tensor("w1blk", [128, 1024], BF16))
        w1t = pp.enter_context(nc.sbuf_tensor("w1t", [128, 1024], BF16))
        wm_bf = pp.enter_context(nc.sbuf_tensor("wm_bf", [128, 32 * NQ], BF16))
        wm_32 = pp.enter_context(nc.sbuf_tensor("wm_32", [128, 32 * NQ], F32))
        b1a_c = pp.enter_context(nc.sbuf_tensor("b1a_c", [128, 32], F32))
        wqo_s = pp.enter_context(nc.sbuf_tensor("wqo_s", [NQ, FF], BF16))
        wqoT = pp.enter_context(nc.sbuf_tensor("wqoT", [128, 32 * NQ], BF16))
        bqo_c = pp.enter_context(nc.sbuf_tensor("bqo_c", [128, 32], F32))
        w2blk = pp.enter_context(nc.sbuf_tensor("w2blk", [128, 1024], BF16))
        w2b32 = pp.enter_context(nc.sbuf_tensor("w2b32", [128, 1024], F32))
        m1row = pp.enter_context(nc.sbuf_tensor("m1row", [NQ, D], BF16))
        brow = pp.enter_context(nc.sbuf_tensor("brow", [1, D + NQ], F32))
        exrow = pp.enter_context(nc.sbuf_tensor("exrow", [1, D], F32))
        with contextlib.ExitStack() as pps:
            p_tr = pps.enter_context(nc.psum_tensor("pp_tr", [128, 1024], BF16))
            p_m1 = pps.enter_context(nc.psum_tensor("pp_m1", [NQ, 1024], F32))
            p_m2 = pps.enter_context(nc.psum_tensor("pp_m2", [NQ, 1024], F32))
            p_bb = pps.enter_context(nc.psum_tensor("pp_bb", [1, 1536], F32))

            ldpar(wm_bf[:, :], "aWm", [[NQ, 128], [128 * NQ, 32], [1, NQ]], eng="gpsimd")
            ldpar(wm_32[:, :], "aWm", [[NQ, 128], [128 * NQ, 32], [1, NQ]])
            ldpar(b1a_c[:, :], "ab1", [[1, 128], [128, 32]])
            ldpar(wqo_s[0:NQ, :], "aWqo", [[FF, NQ], [1, FF]], eng="gpsimd")
            ldpar(bqo_c[:, :], "abqo", [[1, 128], [128, 32]])
            P.barrier()
            for c in range(32):
                P.emit("gpsimd", lambda g, c=c: g.dma_start(
                    w1blk[:, :], A(W["aW1"], c * 128, [[FF, 128], [128 * FF, NC], [1, 128]])),
                    inc=s_ld, incval=16)
                P.wait("tensor", s_ld)
                for f in range(NC):
                    P.emit("tensor", lambda t, f=f: t.transpose(
                        p_tr[:, f * 128:(f + 1) * 128], w1blk[:, f * 128:(f + 1) * 128], ident_bf[:, :]),
                        inc=(s_t if f == NC - 1 else None))
                P.wait("vector", s_t)
                P.emit("vector", lambda v: v.tensor_copy(w1t[:, :], p_tr[:, :]), inc=s_v)
                P.wait("tensor", s_v)
                for hf in range(2):
                    P.emit("tensor", lambda t, c=c, hf=hf: t.matmul(
                        p_m1[0:NQ, hf * 512:(hf + 1) * 512], wm_bf[:, c * NQ:(c + 1) * NQ],
                        w1t[:, hf * 512:(hf + 1) * 512], start=(c == 0), stop=(c == 31)))
                P.emit("tensor", lambda t, c=c: t.matmul(
                    p_bb[0:1, 1024:1024 + NQ], b1a_c[:, c:c + 1], wm_32[:, c * NQ:(c + 1) * NQ],
                    start=(c == 0), stop=(c == 31)))
                P.emit("tensor", lambda t, c=c: t.transpose(
                    p_tr[:, 0:NQ], wqo_s[0:NQ, c * 128:(c + 1) * 128], ident_bf[0:NQ, 0:NQ]), inc=s_t)
                P.wait("vector", s_t)
                P.emit("vector", lambda v, c=c: v.tensor_copy(wqoT[:, c * NQ:(c + 1) * NQ], p_tr[:, 0:NQ]), inc=s_v)
                P.barrier(("tensor", "vector", "gpsimd"))
            for c in range(32):
                P.emit("gpsimd", lambda g, c=c: g.dma_start(
                    w2blk[:, :], A(W["aW2"], c * 128 * D, [[D, 128], [1, D]])), inc=s_ld, incval=16)
                P.emit("sync", lambda s, c=c: s.dma_start(
                    w2b32[:, :], A(W["aW2"], c * 128 * D, [[D, 128], [1, D]])), inc=s_lh, incval=16)
                P.wait("tensor", s_ld)
                P.wait("tensor", s_lh)
                for hf in range(2):
                    P.emit("tensor", lambda t, c=c, hf=hf: t.matmul(
                        p_m2[0:NQ, hf * 512:(hf + 1) * 512], wqoT[:, c * NQ:(c + 1) * NQ],
                        w2blk[:, hf * 512:(hf + 1) * 512], start=(c == 0), stop=(c == 31)))
                for hf in range(2):
                    P.emit("tensor", lambda t, c=c, hf=hf: t.matmul(
                        p_bb[0:1, hf * 512:(hf + 1) * 512], bqo_c[:, c:c + 1],
                        w2b32[:, hf * 512:(hf + 1) * 512], start=(c == 0), stop=(c == 31)),
                        inc=(s_t if hf == 1 else None))
                P.barrier(("tensor", "gpsimd", "sync"))
            P.wait("vector", s_t)
            P.emit("vector", lambda v: v.tensor_copy(brow[0:1, 0:D], p_bb[0:1, 0:D]))
            P.emit("vector", lambda v: v.tensor_copy(brow[0:1, D:D + NQ], p_bb[0:1, 1024:1024 + NQ]))
            P.emit("vector", lambda v: v.tensor_copy(m1row[0:NQ, :], p_m1[0:NQ, :]))
            P.emit("vector", lambda v: v.tensor_copy(m2_bf[0:NQ, :], p_m2[0:NQ, :]), inc=s_v)
            ldpar(exrow[0:1, :], "ab2", [[D, 1], [1, D]])
            P.wait("vector", s_lh)
            P.wait("vector", s_v)
            P.emit("vector", lambda v: v.tensor_tensor(brow[0:1, 0:D], brow[0:1, 0:D], exrow[0:1, :], OP.add), inc=s_v)
            P.wait("sync", s_v)
            ldpar(exrow[0:1, 0:NQ], "abm", [[NQ, 1], [1, NQ]])
            P.wait("vector", s_lh)
            P.emit("vector", lambda v: v.tensor_tensor(brow[0:1, D:D + NQ], brow[0:1, D:D + NQ], exrow[0:1, 0:NQ], OP.add), inc=s_v)
            P.wait("tensor", s_v)
            P.barrier()
        with contextlib.ExitStack() as pps:
            p_tm = pps.enter_context(nc.psum_tensor("pp_tm", [128, NQ], F32))
            p_tmb = pps.enter_context(nc.psum_tensor("pp_tmb", [128, NQ], BF16))
            for c in range(NC):
                P.emit("tensor", lambda t, c=c: t.transpose(
                    p_tm[:, 0:1], brow[0:1, c * 128:(c + 1) * 128], ident_f[0:1, 0:1]), inc=s_t)
                P.wait("vector", s_t)
                P.emit("vector", lambda v, c=c: v.tensor_copy(bias2_c[:, c:c + 1], p_tm[:, 0:1]), inc=s_v)
                P.wait("tensor", s_v)
            P.emit("tensor", lambda t: t.transpose(p_tm[0:NQ, 0:1], brow[0:1, D:D + NQ], ident_f[0:1, 0:1]), inc=s_t)
            P.wait("vector", s_t)
            P.emit("vector", lambda v: v.tensor_copy(bias1_c[0:NQ, 0:1], p_tm[0:NQ, 0:1]), inc=s_v)
            P.wait("tensor", s_v)
            for c in range(NC):
                P.emit("tensor", lambda t, c=c: t.transpose(
                    p_tmb[:, 0:NQ], m1row[0:NQ, c * 128:(c + 1) * 128], ident_bf[0:NQ, 0:NQ]), inc=s_t)
                P.wait("vector", s_t)
                P.emit("vector", lambda v, c=c: v.tensor_copy(m1_bf[:, c * NQ:(c + 1) * NQ], p_tmb[:, 0:NQ]), inc=s_v)
                P.wait("tensor", s_v)
            P.barrier()

    if stage == "prep":
        tap("m1_bf", m1_bf, 128, NC * NQ, BF16)
        tap("m2_bf", m2_bf, NQ, D, BF16)
        tap("bias1_c", bias1_c, NQ, 1, F32)
        tap("bias2_c", bias2_c, 128, NC, F32)
        return finish_stage()

    def ln_fm(BUF, g_col, b_col, p_st, p_bc, sqtmp, rows):
        row_a, row_b, row_c = rows
        for qtr in range(4):
            for c in range(NC):
                P.emit("tensor", lambda t, qtr=qtr, c=c: t.matmul(
                    p_st[0:1, 0:512], ones_c[:, 0:1],
                    BUF[:, c * S + qtr * 512: c * S + qtr * 512 + 512],
                    start=(c == 0), stop=(c == NC - 1)), inc=(s_t if c == NC - 1 else None))
            P.wait("vector", s_t)
            P.emit("vector", lambda v, qtr=qtr: v.tensor_copy(
                row_a[0:1, qtr * 512:(qtr + 1) * 512], p_st[0:1, 0:512]), inc=s_v)
            sq_marks = {}
            for c in range(NC):
                half = c % 2
                P.wait("vector", s_t, sq_marks.get(c - 2, 0))
                P.emit("vector", lambda v, qtr=qtr, c=c, half=half: v.tensor_tensor(
                    sqtmp[:, half * 512:(half + 1) * 512],
                    BUF[:, c * S + qtr * 512: c * S + qtr * 512 + 512],
                    BUF[:, c * S + qtr * 512: c * S + qtr * 512 + 512], OP.mult), inc=s_v)
                P.wait("tensor", s_v)
                P.emit("tensor", lambda t, c=c, half=half: t.matmul(
                    p_st[0:1, 512:1024], ones_c[:, 0:1], sqtmp[:, half * 512:(half + 1) * 512],
                    start=(c == 0), stop=(c == NC - 1)), inc=s_t)
                sq_marks[c] = s_t.n
            P.wait("vector", s_t)
            P.emit("vector", lambda v, qtr=qtr: v.tensor_copy(
                row_b[0:1, qtr * 512:(qtr + 1) * 512], p_st[0:1, 512:1024]), inc=s_v)
        P.wait("vector", s_v)
        P.chain("vector", [
            lambda v: v.tensor_scalar(row_a[0:1, :], row_a[0:1, :], 1.0 / D, None, OP.mult),
            lambda v: v.tensor_scalar(row_b[0:1, :], row_b[0:1, :], 1.0 / D, None, OP.mult),
            lambda v: v.tensor_tensor(row_c[0:1, :], row_a[0:1, :], row_a[0:1, :], OP.mult),
            lambda v: v.tensor_tensor(row_b[0:1, :], row_b[0:1, :], row_c[0:1, :], OP.subtract),
            lambda v: v.tensor_scalar(row_b[0:1, :], row_b[0:1, :], LN_EPS, None, OP.add),
        ], s_v)
        P.wait("scalar", s_v)
        P.emit("scalar", lambda a: a.activation(row_c[0:1, :], row_b[0:1, :], AF.Sqrt), inc=s_a)
        P.wait("vector", s_a)
        P.chain("vector", [
            lambda v: v.reciprocal(row_b[0:1, :], row_c[0:1, :]),
            lambda v: v.tensor_tensor(row_c[0:1, :], row_c[0:1, :], row_b[0:1, :], OP.mult),
            lambda v: v.tensor_scalar(row_c[0:1, :], row_c[0:1, :], -1.0, 2.0, OP.mult, OP.add),
            lambda v: v.tensor_tensor(row_b[0:1, :], row_b[0:1, :], row_c[0:1, :], OP.mult),
            lambda v: v.tensor_tensor(row_a[0:1, :], row_a[0:1, :], row_b[0:1, :], OP.mult),
        ], s_v)
        for qtr in range(4):
            P.wait("tensor", s_v)
            P.emit("tensor", lambda t, qtr=qtr: t.matmul(
                p_bc[:, 0:512], ones_r[0:1, :], row_b[0:1, qtr * 512:(qtr + 1) * 512], start=True, stop=True))
            P.emit("tensor", lambda t, qtr=qtr: t.matmul(
                p_bc[:, 512:1024], ones_r[0:1, :], row_a[0:1, qtr * 512:(qtr + 1) * 512], start=True, stop=True),
                inc=s_t)
            P.wait("vector", s_t)
            for c in range(NC):
                sl = BUF[:, c * S + qtr * 512: c * S + qtr * 512 + 512]
                P.emit("vector", lambda v, sl=sl: v.tensor_tensor(sl, sl, p_bc[:, 0:512], OP.mult),
                       inc=(s_v if c == NC - 1 else None))
            P.wait("vector", s_v)
            for c in range(NC):
                sl = BUF[:, c * S + qtr * 512: c * S + qtr * 512 + 512]
                P.emit("vector", lambda v, sl=sl: v.tensor_tensor(sl, sl, p_bc[:, 512:1024], OP.subtract),
                       inc=(s_v if c == NC - 1 else None))
            if g_col is not None:
                P.wait("vector", s_v)
                for c in range(NC):
                    sl = BUF[:, c * S + qtr * 512: c * S + qtr * 512 + 512]
                    P.emit("vector", lambda v, sl=sl, c=c: v.tensor_scalar(
                        sl, sl, g_col[:, c:c + 1], b_col[:, c:c + 1], OP.mult, OP.add),
                        inc=(s_v if c == NC - 1 else None))
            P.barrier(("tensor", "vector"))

    def spill_tok(BUF, dram, dram_off, p_t1, p_t2, stg):
        for ti in range(NT):
            P.wait("vector", s_st, s_st.n)
            P.wait("tensor", s_v)
            for r in range(2):
                pb = p_t1 if r == 0 else p_t2
                for c in range(4):
                    P.emit("tensor", lambda t, ti=ti, r=r, c=c, pb=pb: t.transpose(
                        pb[:, c * 128:(c + 1) * 128],
                        BUF[:, (4 * r + c) * S + ti * 128: (4 * r + c) * S + ti * 128 + 128],
                        ident_f[:, :]), inc=(s_t if c == 3 else None))
                P.wait("vector", s_t)
                P.emit("vector", lambda v, r=r, pb=pb, stg=stg: v.tensor_copy(
                    stg[:, r * 512:(r + 1) * 512], pb[:, :]), inc=s_v)
            P.wait("sync", s_v)
            P.emit("sync", lambda s, ti=ti, dram=dram, dram_off=dram_off, stg=stg: s.dma_start(
                A(dram, dram_off + ti * 128 * D, [[D, 128], [128, NC], [1, 128]]),
                A(stg, 0, [[D, 128], [128, NC], [1, 128]])), inc=s_st, incval=16)
        P.barrier(("tensor", "vector", "sync"))

    for b in range(B_LOC):
        span_xnt = contextlib.ExitStack()
        open_spans.append(span_xnt)
        span_qkv = contextlib.ExitStack()
        open_spans.append(span_qkv)
        span_at = contextlib.ExitStack()
        open_spans.append(span_at)
        XNT = span_xnt.enter_context(nc.sbuf_tensor(f"XNT_{b}", [128, NC * S], BF16, side="right"))
        with contextlib.ExitStack() as ph:
            xtok = ph.enter_context(nc.sbuf_tensor(f"xtok_{b}", [128, NT * D], F32))
            xn32 = ph.enter_context(nc.sbuf_tensor(f"xn32_{b}", [128, D], F32))
            sqd = ph.enter_context(nc.sbuf_tensor(f"sqd_{b}", [128, 2 * D], F32))
            stg = ph.enter_context(nc.sbuf_tensor(f"stg_{b}", [128, D], F32))
            p_trf = ph.enter_context(nc.psum_tensor(f"p0_trf_{b}", [128, 512], F32))
            p_trf2 = ph.enter_context(nc.psum_tensor(f"p0_trf2_{b}", [128, 512], F32))

            P.emit("sync", lambda s, b=b: s.dma_start(
                xtok[:, :], A(x_in, b * S * D, [[D, 128], [128 * D, NT], [1, D]])),
                inc=s_lh, incval=16)
            P.barrier()
            sq_marks = {}
            for ti in range(NT):
                P.emit("vector", lambda v, ti=ti: v.reduce_sum(
                    st_a[:, ti:ti + 1], xtok[:, ti * D:(ti + 1) * D], axis=AXX), inc=s_v)
                if ti >= 2:
                    P.wait("scalar", s_a, sq_marks[ti - 2])
                P.emit("scalar", lambda a, ti=ti: a.activation(
                    sqd[:, (ti % 2) * D:(ti % 2 + 1) * D], xtok[:, ti * D:(ti + 1) * D], AF.Square,
                    accum_out=st_b[:, ti:ti + 1]), inc=s_a)
                sq_marks[ti] = s_a.n
            P.wait("vector", s_a)
            P.wait("vector", s_v)
            P.chain("vector", [
                lambda v: v.tensor_scalar(st_a[:, 0:NT], st_a[:, 0:NT], 1.0 / D, None, OP.mult),
                lambda v: v.tensor_scalar(st_b[:, 0:NT], st_b[:, 0:NT], 1.0 / D, None, OP.mult),
                lambda v: v.tensor_tensor(st_c[:, 0:NT], st_a[:, 0:NT], st_a[:, 0:NT], OP.mult),
                lambda v: v.tensor_tensor(st_b[:, 0:NT], st_b[:, 0:NT], st_c[:, 0:NT], OP.subtract),
                lambda v: v.tensor_scalar(st_b[:, 0:NT], st_b[:, 0:NT], LN_EPS, None, OP.add),
            ], s_v)
            P.wait("scalar", s_v)
            P.emit("scalar", lambda a: a.activation(st_c[:, 0:NT], st_b[:, 0:NT], AF.Sqrt), inc=s_a)
            P.wait("vector", s_a)
            P.chain("vector", [
                lambda v: v.reciprocal(st_b[:, 0:NT], st_c[:, 0:NT]),
                lambda v: v.tensor_tensor(st_c[:, 0:NT], st_c[:, 0:NT], st_b[:, 0:NT], OP.mult),
                lambda v: v.tensor_scalar(st_c[:, 0:NT], st_c[:, 0:NT], -1.0, 2.0, OP.mult, OP.add),
                lambda v: v.tensor_tensor(st_b[:, 0:NT], st_b[:, 0:NT], st_c[:, 0:NT], OP.mult),
            ], s_v)
            for ti in range(NT):
                P.wait("vector", s_v)
                P.emit("vector", lambda v, ti=ti: v.tensor_scalar(
                    xn32[:, :], xtok[:, ti * D:(ti + 1) * D], st_a[:, ti:ti + 1], st_b[:, ti:ti + 1],
                    OP.subtract, OP.mult), inc=s_v)
                P.wait("tensor", s_v)
                for r in range(2):
                    pb = p_trf if r == 0 else p_trf2
                    for c in range(4):
                        P.emit("tensor", lambda t, r=r, c=c, pb=pb: t.transpose(
                            pb[:, c * 128:(c + 1) * 128],
                            xn32[:, r * 512 + c * 128: r * 512 + (c + 1) * 128], ident_f[:, :]),
                            inc=(s_t if c == 3 else None))
                    P.wait("vector", s_t)
                    P.emit("vector", lambda v, r=r, pb=pb, stg=stg: v.tensor_copy(stg[:, r * 512:(r + 1) * 512], pb[:, :]))
                    P.emit("vector", lambda v, r=r, ti=ti, pb=pb, XNT=XNT: v.tensor_copy(
                        A(XNT, (4 * r) * S + ti * 128, [[NC * S, 128], [S, 4], [1, 128]]),
                        pb[:, :]), inc=s_v)
                P.wait("sync", s_v)
                P.emit("sync", lambda s, b=b, ti=ti, stg=stg: s.dma_start(
                    A(sc_xn32, b * D * S + ti * 128, [[S, 128], [128 * S, NC], [1, 128]]),
                    A(stg, 0, [[D, 128], [128, NC], [1, 128]])), inc=s_st, incval=16)
                P.barrier(("tensor", "vector", "sync"))
            P.emit("sync", lambda s, b=b, XNT=XNT: s.dma_start(
                A(sc_xnt, b * D * S, [[S, 128], [128 * S, NC], [1, S]]),
                A(XNT, 0, [[NC * S, 128], [S, NC], [1, S]])), inc=s_st, incval=16)
            P.wait("sync", s_st)
            P.barrier()

        if stage == "p0" and b == 0:
            tap("XNT", XNT, 128, NC * S, BF16)
            return finish_stage()

        if True:
            QT = span_qkv.enter_context(nc.sbuf_tensor(f"QT_{b}", [128, H * S], BF16))
            KT = span_qkv.enter_context(nc.sbuf_tensor(f"KT_{b}", [128, H * S], BF16))
            span_vt = contextlib.ExitStack()
            open_spans.append(span_vt)
            VTOK = span_vt.enter_context(nc.sbuf_tensor(f"VTOK_{b}", [128, NT * D], BF16, side="right"))
            qqh_t = span_qkv.enter_context(nc.sbuf_tensor(f"qqh_t_{b}", [128, NT * H * NQ], BF16))
            kqh_t = span_qkv.enter_context(nc.sbuf_tensor(f"kqh_t_{b}", [128, NT * H * NQ], BF16))
            qqT_h = span_qkv.enter_context(nc.sbuf_tensor(f"qqT_h_{b}", [NQ, S], BF16))
            kqT_h = span_qkv.enter_context(nc.sbuf_tensor(f"kqT_h_{b}", [NQ, S], BF16))
            with contextlib.ExitStack() as p1:
                wpro = p1.enter_context(nc.sbuf_tensor(f"wpro_{b}", [128, H * NC * 128], BF16))
                qq_t = p1.enter_context(nc.sbuf_tensor(f"qq_t_{b}", [128, NT * H * NQ], F32))
                kq_t = p1.enter_context(nc.sbuf_tensor(f"kq_t_{b}", [128, NT * H * NQ], F32))
                qn2 = p1.enter_context(nc.sbuf_tensor(f"qn2_{b}", [128, NT * H], F32))
                kn2 = p1.enter_context(nc.sbuf_tensor(f"kn2_{b}", [128, NT * H], F32))
                sq_s = p1.enter_context(nc.sbuf_tensor(f"sq_s_{b}", [128, H * NQ], F32))
                p_a = p1.enter_context(nc.psum_tensor(f"p1_a_{b}", [128, 512], F32))
                p_b = p1.enter_context(nc.psum_tensor(f"p1_b_{b}", [128, 512], F32))
                p_c = p1.enter_context(nc.psum_tensor(f"p1_c_{b}", [128, 512], F32))
                p_d = p1.enter_context(nc.psum_tensor(f"p1_d_{b}", [128, 512], F32))
                p_v = p1.enter_context(nc.psum_tensor(f"p1_v_{b}", [128, 1024], F32))
                p_q = p1.enter_context(nc.psum_tensor(f"p1_q_{b}", [128, H * NQ], F32))
                pbank = (p_a, p_b, p_c, p_d)

                for (wname, bcol, OUT, scale) in (("aWq", bq_c, QT, ISQHD), ("aWk", bk_c, KT, None)):
                    P.barrier(("tensor", "gpsimd", "vector"))
                    P.emit("gpsimd", lambda g, wname=wname: g.dma_start(
                        wpro[:, :], A(W[wname], 0, [[HD, 128], [D * HD, H], [128 * HD, NC], [1, 128]])),
                        inc=s_ld, incval=16)
                    P.wait("tensor", s_ld)
                    for h in range(H):
                        P.wait("tensor", s_v)
                        for tc in range(4):
                            for f in range(NC):
                                P.emit("tensor", lambda t, h=h, tc=tc, f=f: t.matmul(
                                    pbank[tc][:, :], wpro[:, (h * NC + f) * 128:(h * NC + f + 1) * 128],
                                    XNT[:, f * S + tc * 512: f * S + (tc + 1) * 512],
                                    start=(f == 0), stop=(f == NC - 1)),
                                    inc=(s_t if f == NC - 1 else None))
                        P.wait("vector", s_t)
                        for tc in range(4):
                            if scale is None:
                                P.emit("vector", lambda v, h=h, tc=tc, OUT=OUT, bcol=bcol: v.tensor_scalar(
                                    OUT[:, h * S + tc * 512: h * S + (tc + 1) * 512],
                                    pbank[tc][:, :], bcol[:, h:h + 1], None, OP.add),
                                    inc=(s_v if tc == 3 else None))
                            else:
                                P.emit("vector", lambda v, h=h, tc=tc, OUT=OUT, bcol=bcol, scale=scale: v.tensor_scalar(
                                    OUT[:, h * S + tc * 512: h * S + (tc + 1) * 512],
                                    pbank[tc][:, :], bcol[:, h:h + 1], scale, OP.add, OP.mult),
                                    inc=(s_v if tc == 3 else None))
                P.barrier(("tensor", "vector", "gpsimd"))
                for f in range(NC):
                    P.emit("gpsimd", lambda g, f=f: g.dma_start(
                        wpro[:, f * 1024:(f + 1) * 1024],
                        A(W["aWv"], f * 128 * HD, [[HD, 128], [D * HD, H], [1, 128]])),
                        inc=s_ld, incval=16)
                P.barrier(("tensor", "gpsimd", "vector"))
                for ti in range(NT):
                    P.wait("tensor", s_v)
                    for f in range(NC):
                        for hf in range(2):
                            P.emit("tensor", lambda t, ti=ti, f=f, hf=hf: t.matmul(
                                p_v[:, hf * 512:(hf + 1) * 512],
                                XNT[:, f * S + ti * 128: f * S + ti * 128 + 128],
                                wpro[:, f * 1024 + hf * 512: f * 1024 + (hf + 1) * 512],
                                start=(f == 0), stop=False))
                    for hf in range(2):
                        P.emit("tensor", lambda t, hf=hf: t.matmul(
                            p_v[:, hf * 512:(hf + 1) * 512], ones_r[0:1, :],
                            vb_r[0:1, hf * 512:(hf + 1) * 512], start=False, stop=True),
                            inc=(s_t if hf == 1 else None))
                    P.wait("vector", s_t)
                    P.emit("vector", lambda v, ti=ti, VTOK=VTOK: v.tensor_copy(
                        VTOK[:, ti * D:(ti + 1) * D], p_v[:, :]), inc=s_v)
                P.barrier(("tensor", "vector", "gpsimd"))
                for (QKT, bias_row, qq_buf) in ((QT, bc2q_rq, qq_t), (KT, bc2q_r, kq_t)):
                    for ti in range(NT):
                        P.wait("tensor", s_v)
                        for h in range(H):
                            P.emit("tensor", lambda t, ti=ti, h=h, QKT=QKT: t.matmul(
                                p_q[:, h * NQ:(h + 1) * NQ],
                                QKT[:, h * S + ti * 128: h * S + ti * 128 + 128],
                                wc2q_bf[:, h * NQ:(h + 1) * NQ], start=(h == 0), stop=False))
                        P.emit("tensor", lambda t, bias_row=bias_row: t.matmul(
                            p_q[:, :], ones_r[0:1, :], bias_row[0:1, :], start=False, stop=True), inc=s_t)
                        P.wait("vector", s_t)
                        P.emit("vector", lambda v, ti=ti, qq_buf=qq_buf: v.tensor_copy(
                            qq_buf[:, ti * H * NQ:(ti + 1) * H * NQ], p_q[:, :]), inc=s_v)
                P.barrier(("tensor", "vector"))
                for (qq_buf, n2_buf) in ((qq_t, qn2), (kq_t, kn2)):
                    for ti in range(NT):
                        P.wait("vector", s_v)
                        P.emit("vector", lambda v, ti=ti, qq_buf=qq_buf: v.tensor_tensor(
                            sq_s[:, :], qq_buf[:, ti * H * NQ:(ti + 1) * H * NQ],
                            qq_buf[:, ti * H * NQ:(ti + 1) * H * NQ], OP.mult), inc=s_v)
                        P.wait("vector", s_v)
                        P.emit("vector", lambda v, ti=ti, n2_buf=n2_buf: v.reduce_sum(
                            n2_buf[:, ti * H:(ti + 1) * H],
                            A(sq_s, 0, [[H * NQ, 128], [NQ, H], [1, NQ]]), axis=AXX), inc=s_v)
                for n2_buf in (qn2, kn2):
                    P.wait("scalar", s_v)
                    P.emit("scalar", lambda a, n2_buf=n2_buf: a.activation(
                        n2_buf[:, :], n2_buf[:, :], AF.Sqrt), inc=s_a)
                    P.wait("vector", s_a)
                    P.chain("vector", [
                        lambda v, n2_buf=n2_buf: v.tensor_scalar(n2_buf[:, :], n2_buf[:, :], 1e-12, None, OP.max),
                        lambda v, n2_buf=n2_buf: v.reciprocal(n2_buf[:, :], n2_buf[:, :]),
                    ], s_v)
                P.wait("vector", s_v)
                for (qq_buf, n2_buf, OUTT) in ((qq_t, qn2, qqh_t), (kq_t, kn2, kqh_t)):
                    for ti in range(NT):
                        for h in range(H):
                            P.emit("vector", lambda v, ti=ti, h=h, qq_buf=qq_buf, n2_buf=n2_buf, OUTT=OUTT: v.tensor_scalar(
                                OUTT[:, ti * H * NQ + h * NQ: ti * H * NQ + (h + 1) * NQ],
                                qq_buf[:, ti * H * NQ + h * NQ: ti * H * NQ + (h + 1) * NQ],
                                n2_buf[:, ti * H + h: ti * H + h + 1], None, OP.mult))
                P.emit("vector", lambda v: v.engine_nop(), inc=s_v)
                P.wait("sync", s_v)
                P.emit("sync", lambda s, b=b, VTOK=VTOK: s.dma_start(
                    A(sc_vtok, b * 128 * NT * D, [[NT * D, 128], [1, NT * D]]),
                    VTOK[:, :]), inc=s_st, incval=16)
                P.barrier()

            if stage == "p1" and b == 0:
                tap("QT", QT, 128, H * S, BF16)
                tap("KT", KT, 128, H * S, BF16)
                tap("VTOK", VTOK, 128, NT * D, BF16)
                tap("qqh_t", qqh_t, 128, NT * H * NQ, BF16)
                tap("kqh_t", kqh_t, 128, NT * H * NQ, BF16)
                return finish_stage()
        span_vt.close()
        span_xnt.close()

        attnT = span_at.enter_context(nc.sbuf_tensor(f"attnT_{b}", [128, H * S], BF16, side="right"))
        with contextlib.ExitStack() as p2:
            fidb = [p2.enter_context(nc.sbuf_tensor(f"fidb{par}_{b}", [128, S], F32)) for par in range(2)]
            sinb = [p2.enter_context(nc.sbuf_tensor(f"sinb{par}_{b}", [128, S], F32)) for par in range(2)]
            pbuf = [p2.enter_context(nc.sbuf_tensor(f"pbuf{par}_{b}", [128, S], BF16, side="right")) for par in range(4)]
            mcol = [p2.enter_context(nc.sbuf_tensor(f"mcol{par}_{b}", [128, 8], F32)) for par in range(4)]
            pTq = p2.enter_context(nc.sbuf_tensor(f"pTq_{b}", [128, NT * 256], BF16))
            vtk = [p2.enter_context(nc.sbuf_tensor(f"vtk{par}_{b}", [128, NT * 128], BF16)) for par in range(2)]
            p_da = p2.enter_context(nc.psum_tensor(f"p2_da_{b}", [128, 1024], F32))
            p_cw = [p2.enter_context(nc.psum_tensor(f"p2_cw{k}_{b}", [128, 512], F32)) for k in range(4)]
            p_pt = p2.enter_context(nc.psum_tensor(f"p2_pt_{b}", [128, 1024], BF16))
            p_pv = p2.enter_context(nc.psum_tensor(f"p2_pv_{b}", [128, 512], F32))

            mk = {"fh": [0, 0], "t": {}, "z": {}, "cw": {}, "sin": {}, "exp": {},
                  "esum": {}, "pn": {}, "ptd": {}, "fid": {}, "qkevac": 0,
                  "pvread": 0, "pvdone": 0}
            mk["pvh"] = [0, 0]

            def emit_pv(h, qp, hp, lvs, lv_mark):
                P.wait("tensor", s_pt, mk["ptd"][h * NT + 2 * qp + 1])
                P.wait("tensor", s_a, mk["pvdone"])
                P.wait("tensor", lvs, lv_mark)
                for kt in range(NT):
                    P.emit("tensor", lambda t, hp=hp, kt=kt: t.matmul(
                        p_pv[:, 0:256], vtk[hp][:, kt * 128:(kt + 1) * 128],
                        pTq[:, kt * 256:(kt + 1) * 256],
                        start=(kt == 0), stop=(kt == NT - 1)),
                        inc=(s_t if kt == NT - 1 else None))
                mk["pvread"] = s_t.n
                P.wait("scalar", s_t, mk["pvread"])
                P.emit("scalar", lambda a, h=h, qp=qp: a.activation(
                    attnT[:, h * S + qp * 256: h * S + (qp + 1) * 256],
                    p_pv[:, 0:256], AF.Copy), inc=s_a)
                mk["pvdone"] = s_a.n

            for h in range(H):
                hp = h % 2
                lvs = s_lv if hp == 0 else s_lv2
                P.wait("sync", s_t, mk["pvh"][hp])
                P.emit("sync", lambda s, b=b, h=h, hp=hp: s.dma_start(
                    A(vtk[hp], 0, [[NT * 128, 128], [128, NT], [1, 128]]),
                    A(sc_vtok, b * 128 * NT * D + h * 128, [[NT * D, 128], [D, NT], [1, 128]])),
                    inc=lvs, incval=16)
                lv_mark = lvs.n
                for (tok, TT) in ((qqh_t, qqT_h), (kqh_t, kqT_h)):
                    for batch in range(2):
                        P.wait("tensor", s_a, mk["qkevac"])
                        for j in range(8):
                            ti = batch * 8 + j
                            P.emit("tensor", lambda t, ti=ti, j=j, h=h, tok=tok: t.transpose(
                                p_pt[0:NQ, j * 128:(j + 1) * 128],
                                tok[:, ti * H * NQ + h * NQ: ti * H * NQ + (h + 1) * NQ],
                                ident_bf[:, :]), inc=(s_t if j == 7 else None))
                        P.wait("scalar", s_t)
                        P.emit("scalar", lambda a, batch=batch, TT=TT: a.activation(
                            TT[0:NQ, batch * 1024:(batch + 1) * 1024], p_pt[0:NQ, :],
                            AF.Copy), inc=s_a)
                        mk["qkevac"] = s_a.n
                P.wait("tensor", s_a, mk["qkevac"])
                for qp in range(NT // 2):
                    tiles = (2 * qp, 2 * qp + 1)
                    for qt in tiles:
                        par = qt % 2
                        tq = h * NT + qt
                        P.wait("vector", s_a, mk["exp"].get(tq - 2, 0))
                        for half in range(2):
                            P.wait("tensor", s_v, mk["fh"][1 - half])
                            for kc in (2 * half, 2 * half + 1):
                                P.emit("tensor", lambda t, h=h, qt=qt, kc=kc, half=half: t.matmul(
                                    p_da[:, (kc - 2 * half) * 512:(kc - 2 * half + 1) * 512],
                                    qqT_h[0:NQ, qt * 128:(qt + 1) * 128],
                                    kqT_h[0:NQ, kc * 512:(kc + 1) * 512], start=True, stop=True),
                                    inc=(s_t if kc % 2 == 1 else None))
                            dm = s_t.n
                            P.wait("vector", s_t, dm)
                            P.emit("vector", lambda v, par=par, half=half: v.tensor_scalar(
                                fidb[par][:, half * 1024:(half + 1) * 1024], p_da[:, :],
                                1.0, -1.0, OP.min, OP.max), inc=s_v)
                            mk["fh"][half] = s_v.n
                        mk["fid", tq] = s_v.n
                    for qt in tiles:
                        par = qt % 2
                        tq = h * NT + qt
                        P.wait("scalar", s_v, max(mk["fid", tq], mk["t"].get(tq - 2, 0)))
                        P.emit("scalar", lambda a, par=par: a.activation(
                            sinb[par][:, :], fidb[par][:, :], AF.Sin, scale=PI_LO), inc=s_a)
                        mk["sin"][tq] = s_a.n
                    if qp > 0:
                        emit_pv(h, qp - 1, hp, lvs, lv_mark)
                    for qt in tiles:
                        par = qt % 2
                        tq = h * NT + qt
                        for kc in range(4):
                            P.wait("tensor", s_v, mk["z"].get((kc, tq - 1), 0))
                            P.emit("tensor", lambda t, h=h, qt=qt, kc=kc: t.matmul(
                                p_cw[kc][:, :], QT[:, h * S + qt * 128: h * S + qt * 128 + 128],
                                KT[:, h * S + kc * 512: h * S + (kc + 1) * 512],
                                start=True, stop=True), inc=s_t)
                            mk["cw"][kc, tq] = s_t.n
                        P.wait("vector", s_a, mk["sin"][tq])
                        P.chain("vector", [
                            lambda v, par=par: v.scalar_tensor_tensor(
                                sinb[par][:, :], sinb[par][:, :], 0.1, fidb[par][:, :],
                                OP.mult, OP.mult),
                            lambda v, par=par: v.tensor_tensor(
                                fidb[par][:, :], fidb[par][:, :], sinb[par][:, :], OP.add),
                        ], s_v)
                        mk["t"][tq] = s_v.n
                        for kc in range(4):
                            P.wait("vector", s_t, mk["cw"][kc, tq])
                            P.emit("vector", lambda v, par=par, kc=kc: v.tensor_tensor(
                                fidb[par][:, kc * 512:(kc + 1) * 512],
                                fidb[par][:, kc * 512:(kc + 1) * 512],
                                p_cw[kc][:, :], OP.add), inc=s_v)
                            mk["z"][kc, tq] = s_v.n
                    for qt in tiles:
                        par, p4 = qt % 2, qt % 4
                        tq = h * NT + qt
                        P.wait("scalar", s_pt, mk["ptd"].get(tq - 4, 0))
                        P.wait("scalar", s_v, mk["z"][3, tq])
                        P.emit("scalar", lambda a, par=par, p4=p4: a.activation(
                            pbuf[p4][:, :], fidb[par][:, :], AF.Exp, scale=0.5,
                            accum_out=mcol[p4][:, 4:5]), inc=s_a)
                        mk["exp"][tq] = s_a.n
                    for qt in tiles:
                        p4 = qt % 4
                        tq = h * NT + qt
                        P.wait("vector", s_a, mk["exp"][tq])
                        P.emit("vector", lambda v, p4=p4: v.reciprocal(
                            mcol[p4][:, 5:6], mcol[p4][:, 4:5]), inc=s_v)
                        mk["esum"][tq] = s_v.n
                    for qt in tiles:
                        p4, qi = qt % 4, qt % 2
                        tq = h * NT + qt
                        P.wait("scalar", s_v, mk["esum"][tq])
                        P.emit("scalar", lambda a, p4=p4: a.activation(
                            pbuf[p4][:, :], pbuf[p4][:, :], AF.Copy,
                            scale=mcol[p4][:, 5:6]), inc=s_a)
                        mk["pn"][tq] = s_a.n
                        P.wait("sync", s_a, mk["pn"][tq])
                        P.wait("sync", s_t, mk["pvread"])
                        P.emit("sync", lambda s, p4=p4, qi=qi: s.dma_start_transpose(
                            A(pTq, qi * 128, [[NT * 256, 128], [256, NT], [1, 128]]),
                            A(pbuf[p4], 0, [[S, 128], [128, NT], [1, 128]])),
                            inc=s_pt, incval=16)
                        mk["ptd"][tq] = s_pt.n
                emit_pv(h, NT // 2 - 1, hp, lvs, lv_mark)
                mk["pvh"][hp] = mk["pvread"]
            P.barrier()
        span_qkv.close()

        if stage == "p2" and b == 0:
            tap("attnT", attnT, 128, H * S, BF16)
            return finish_stage()

        span_x1 = contextlib.ExitStack()
        open_spans.append(span_x1)
        X1IN = span_x1.enter_context(nc.sbuf_tensor(f"X1IN_{b}", [128, NC * S], F32))
        with contextlib.ExitStack() as p3:
            XN32 = p3.enter_context(nc.sbuf_tensor(f"XN32_{b}", [128, NC * S], F32))
            wo_sb = p3.enter_context(nc.sbuf_tensor(f"wo_sb_{b}", [128, NC * 1024], BF16))
            p_w1 = p3.enter_context(nc.psum_tensor(f"p3_w1_{b}", [128, 512], F32))
            p_w2 = p3.enter_context(nc.psum_tensor(f"p3_w2_{b}", [128, 512], F32))
            P.emit("sync", lambda s, b=b, XN32=XN32: s.dma_start(
                A(XN32, 0, [[NC * S, 128], [S, NC], [1, S]]),
                A(sc_xn32, b * D * S, [[S, 128], [128 * S, NC], [1, S]])), inc=s_lh, incval=16)
            P.emit("gpsimd", lambda g, wo_sb=wo_sb: g.dma_start(
                wo_sb[:, :], A(W["aWo"], 0, [[D, 128], [128 * D, NC], [1, D]])), inc=s_ld, incval=16)
            P.barrier()
            for dc in range(NC):
                for tc in range(4):
                    pw = p_w1 if (dc * 4 + tc) % 2 == 0 else p_w2
                    P.wait("tensor", s_v)
                    for f in range(NC):
                        P.emit("tensor", lambda t, dc=dc, tc=tc, f=f, pw=pw: t.matmul(
                            pw[:, :], wo_sb[:, f * 1024 + dc * 128: f * 1024 + (dc + 1) * 128],
                            attnT[:, f * S + tc * 512: f * S + (tc + 1) * 512],
                            start=(f == 0), stop=(f == NC - 1)), inc=(s_t if f == NC - 1 else None))
                    P.wait("vector", s_t)
                    P.emit("vector", lambda v, dc=dc, tc=tc, pw=pw, X1IN=X1IN, XN32=XN32: v.scalar_tensor_tensor(
                        X1IN[:, dc * S + tc * 512: dc * S + (tc + 1) * 512], pw[:, :],
                        bo_c[:, dc:dc + 1], XN32[:, dc * S + tc * 512: dc * S + (tc + 1) * 512],
                        OP.add, OP.add), inc=s_v)
            P.barrier()
        span_at.close()
        with contextlib.ExitStack() as p3b:
            p_st = p3b.enter_context(nc.psum_tensor(f"p3_st_{b}", [1, 1024], F32))
            p_bc = p3b.enter_context(nc.psum_tensor(f"p3_bc_{b}", [128, 1024], F32))
            sqt = p3b.enter_context(nc.sbuf_tensor(f"sqt_{b}", [128, 1024], F32))
            rows = tuple(p3b.enter_context(nc.sbuf_tensor(f"row3__{b}" + ch, [1, S], F32)) for ch in "abc")
            ln_fm(X1IN, g1_c, be1_c, p_st, p_bc, sqt, rows)
            P.barrier()

        if stage == "p3" and b == 0:
            tap("X1IN", X1IN, 128, NC * S, F32)
            return finish_stage()

        span_x1t = contextlib.ExitStack()
        open_spans.append(span_x1t)
        X1T = span_x1t.enter_context(nc.sbuf_tensor(f"X1T_{b}", [128, NC * S], BF16, side="right"))
        with contextlib.ExitStack() as p4:
            qm_s = p4.enter_context(nc.sbuf_tensor(f"qm_s_{b}", [NQ, S], F32))
            th_s = p4.enter_context(nc.sbuf_tensor(f"th_s_{b}", [NQ, S], F32))
            qe_b = p4.enter_context(nc.sbuf_tensor(f"qe_b_{b}", [NQ, S], BF16))
            p_qm = p4.enter_context(nc.psum_tensor(f"p4_qm_{b}", [NQ, 512], F32))
            p_f1 = p4.enter_context(nc.psum_tensor(f"p4_f1_{b}", [128, 512], F32))
            p_f2 = p4.enter_context(nc.psum_tensor(f"p4_f2_{b}", [128, 512], F32))
            for c in range(NC):
                P.emit("vector", lambda v, c=c, X1T=X1T, X1IN=X1IN: v.tensor_copy(
                    X1T[:, c * S:(c + 1) * S], X1IN[:, c * S:(c + 1) * S]), inc=s_v)
            P.barrier(("tensor", "vector"))
            for tc in range(4):
                for f in range(NC):
                    P.emit("tensor", lambda t, tc=tc, f=f, X1T=X1T: t.matmul(
                        p_qm[0:NQ, :], m1_bf[:, f * NQ:(f + 1) * NQ],
                        X1T[:, f * S + tc * 512: f * S + (tc + 1) * 512],
                        start=(f == 0), stop=(f == NC - 1)), inc=(s_t if f == NC - 1 else None))
                P.wait("vector", s_t)
                P.emit("vector", lambda v, tc=tc, qm_s=qm_s: v.tensor_scalar(
                    qm_s[0:NQ, tc * 512:(tc + 1) * 512], p_qm[0:NQ, :], bias1_c[0:NQ, 0:1], None, OP.add), inc=s_v)
                P.wait("tensor", s_v)
            P.wait("scalar", s_v)
            P.emit("scalar", lambda a, th_s=th_s, qm_s=qm_s: a.activation(th_s[0:NQ, :], qm_s[0:NQ, :], AF.Tanh), inc=s_a)
            P.wait("vector", s_a)
            P.wait("vector", s_v)
            P.emit("vector", lambda v, qm_s=qm_s: v.tensor_scalar(
                qm_s[0:NQ, :], qm_s[0:NQ, :], 8.0, -8.0, OP.min, OP.max), inc=s_v)
            P.wait("scalar", s_v)
            P.emit("scalar", lambda a, qm_s=qm_s: a.activation(qm_s[0:NQ, :], qm_s[0:NQ, :], AF.Sin, scale=PI8_LO), inc=s_a)
            P.wait("vector", s_a)
            P.chain("vector", [
                lambda v, qm_s=qm_s: v.tensor_tensor(qm_s[0:NQ, :], qm_s[0:NQ, :], qm_s[0:NQ, :], OP.mult),
                lambda v, qm_s=qm_s: v.tensor_scalar(qm_s[0:NQ, :], qm_s[0:NQ, :], -2.0, 1.0, OP.mult, OP.add),
                lambda v, qe_b=qe_b, th_s=th_s, qm_s=qm_s: v.tensor_tensor(
                    qe_b[0:NQ, :], th_s[0:NQ, :], qm_s[0:NQ, :], OP.mult),
            ], s_v)
            P.wait("tensor", s_v)
            for dc in range(NC):
                for tc in range(4):
                    pw = p_f1 if (dc * 4 + tc) % 2 == 0 else p_f2
                    P.emit("tensor", lambda t, dc=dc, tc=tc, pw=pw, qe_b=qe_b: t.matmul(
                        pw[:, :], m2_bf[0:NQ, dc * 128:(dc + 1) * 128],
                        qe_b[0:NQ, tc * 512:(tc + 1) * 512], start=True, stop=True), inc=s_t)
                    P.wait("vector", s_t)
                    P.emit("vector", lambda v, dc=dc, tc=tc, pw=pw, X1IN=X1IN: v.scalar_tensor_tensor(
                        X1IN[:, dc * S + tc * 512: dc * S + (tc + 1) * 512], pw[:, :],
                        bias2_c[:, dc:dc + 1], X1IN[:, dc * S + tc * 512: dc * S + (tc + 1) * 512],
                        OP.add, OP.add), inc=s_v)
                    P.wait("tensor", s_v)
            P.barrier(("tensor", "vector", "scalar"))
        span_x1t.close()
        with contextlib.ExitStack() as p4b:
            p_st = p4b.enter_context(nc.psum_tensor(f"p4_st_{b}", [1, 1024], F32))
            p_bc = p4b.enter_context(nc.psum_tensor(f"p4_bc_{b}", [128, 1024], F32))
            sqt = p4b.enter_context(nc.sbuf_tensor(f"sqt2_{b}", [128, 1024], F32))
            rows = tuple(p4b.enter_context(nc.sbuf_tensor(f"row4__{b}" + ch, [1, S], F32)) for ch in "abc")
            ln_fm(X1IN, g2_c, be2_c, p_st, p_bc, sqt, rows)
            P.barrier()

        if stage == "p4" and b == 0:
            tap("X1IN", X1IN, 128, NC * S, F32)
            return finish_stage()

        with contextlib.ExitStack() as p5:
            p_t1 = p5.enter_context(nc.psum_tensor(f"p5_t1_{b}", [128, 512], F32))
            p_t2 = p5.enter_context(nc.psum_tensor(f"p5_t2_{b}", [128, 512], F32))
            stg = p5.enter_context(nc.sbuf_tensor(f"stg5_{b}", [128, D], F32))
            spill_tok(X1IN, sc_aout, b * S * D, p_t1, p_t2, stg)
        span_x1.close()

        span_qf = contextlib.ExitStack()
        open_spans.append(span_qf)
        span_f = contextlib.ExitStack()
        open_spans.append(span_f)
        QfT = span_qf.enter_context(nc.sbuf_tensor(f"QfT_{b}", [128, NC * S], BF16))
        KfT = span_qf.enter_context(nc.sbuf_tensor(f"KfT_{b}", [128, NC * S], BF16))
        VfTOK = span_qf.enter_context(nc.sbuf_tensor(f"VfTOK_{b}", [128, NT * D], BF16))
        span_xnt2 = contextlib.ExitStack()
        open_spans.append(span_xnt2)
        XNT2 = span_xnt2.enter_context(nc.sbuf_tensor(f"XNT2_{b}", [128, NC * S], BF16, side="right"))
        with contextlib.ExitStack() as p6:
            wprob = [p6.enter_context(nc.sbuf_tensor(f"wpro6{i}_{b}", [128, NC * 1024], BF16)) for i in range(2)]
            p_a = p6.enter_context(nc.psum_tensor(f"p6_a_{b}", [128, 512], F32))
            p_b = p6.enter_context(nc.psum_tensor(f"p6_b_{b}", [128, 512], F32))
            p_v = p6.enter_context(nc.psum_tensor(f"p6_v_{b}", [128, 1024], F32))
            P.emit("sync", lambda s, b=b, XNT2=XNT2: s.dma_start(
                A(XNT2, 0, [[NC * S, 128], [S, NC], [1, S]]),
                A(sc_xnt, b * D * S, [[S, 128], [128 * S, NC], [1, S]])), inc=s_lh, incval=16)
            P.barrier()
            ld_sems = (s_ld, s_ld2, s_ld)
            ld_w = {}
            rd_w = {}
            for wi, wname in enumerate(("fWq", "fWk")):
                P.emit("gpsimd", lambda g, wname=wname, wpro=wprob[wi % 2]: g.dma_start(
                    wpro[:, :], A(W[wname], 0, [[D, 128], [128 * D, NC], [1, D]])),
                    inc=ld_sems[wi], incval=16)
                ld_w[wi] = ld_sems[wi].n
            for wi, (bcol, OUT, scale) in enumerate(((fbq_c, QfT, INV32), (fbk_c, KfT, None))):
                wpro = wprob[wi % 2]
                P.wait("tensor", ld_sems[wi], ld_w[wi])
                for dc in range(NC):
                    for tc in range(4):
                        pw = p_a if (dc * 4 + tc) % 2 == 0 else p_b
                        P.wait("tensor", s_v)
                        for f in range(NC):
                            P.emit("tensor", lambda t, dc=dc, tc=tc, f=f, pw=pw, wpro=wpro, XNT2=XNT2: t.matmul(
                                pw[:, :], wpro[:, f * 1024 + dc * 128: f * 1024 + (dc + 1) * 128],
                                XNT2[:, f * S + tc * 512: f * S + (tc + 1) * 512],
                                start=(f == 0), stop=(f == NC - 1)), inc=(s_t if f == NC - 1 else None))
                        P.wait("vector", s_t)
                        if scale is None:
                            P.emit("vector", lambda v, dc=dc, tc=tc, pw=pw, OUT=OUT, bcol=bcol: v.tensor_scalar(
                                OUT[:, dc * S + tc * 512: dc * S + (tc + 1) * 512],
                                pw[:, :], bcol[:, dc:dc + 1], None, OP.add), inc=s_v)
                        else:
                            P.emit("vector", lambda v, dc=dc, tc=tc, pw=pw, OUT=OUT, bcol=bcol, scale=scale: v.tensor_scalar(
                                OUT[:, dc * S + tc * 512: dc * S + (tc + 1) * 512],
                                pw[:, :], bcol[:, dc:dc + 1], scale, OP.add, OP.mult), inc=s_v)
                rd_w[wi] = s_t.n
                if wi == 0:
                    P.wait("gpsimd", s_t, rd_w[0])
                    P.emit("gpsimd", lambda g: g.dma_start(
                        wprob[0][:, :], A(W["fWv"], 0, [[D, 128], [128 * D, NC], [1, D]])),
                        inc=s_ld, incval=16)
                    ld_w[2] = s_ld.n
            wpro = wprob[0]
            P.wait("tensor", s_ld, ld_w[2])
            for ti in range(NT):
                P.wait("tensor", s_v)
                for f in range(NC):
                    for hf in range(2):
                        P.emit("tensor", lambda t, ti=ti, f=f, hf=hf, wpro=wpro, XNT2=XNT2: t.matmul(
                            p_v[:, hf * 512:(hf + 1) * 512],
                            XNT2[:, f * S + ti * 128: f * S + ti * 128 + 128],
                            wpro[:, f * 1024 + hf * 512: f * 1024 + (hf + 1) * 512],
                            start=(f == 0), stop=False))
                for hf in range(2):
                    P.emit("tensor", lambda t, hf=hf: t.matmul(
                        p_v[:, hf * 512:(hf + 1) * 512], ones_r[32:33, :],
                        vb_r[32:33, hf * 512:(hf + 1) * 512], start=False, stop=True),
                        inc=(s_t if hf == 1 else None))
                P.wait("vector", s_t)
                P.emit("vector", lambda v, ti=ti, VfTOK=VfTOK: v.tensor_copy(
                    VfTOK[:, ti * D:(ti + 1) * D], p_v[:, :]), inc=s_v)
            P.barrier()

        span_xnt2.close()
        afT = span_f.enter_context(nc.sbuf_tensor(f"afT_{b}", [128, NC * S], BF16, side="right"))
        with contextlib.ExitStack() as p6b:
            pbuf6 = [p6b.enter_context(nc.sbuf_tensor(f"pbuf6{i}_{b}", [128, S], BF16)) for i in range(2)]
            pTq = p6b.enter_context(nc.sbuf_tensor(f"pTq6_{b}", [128, NT * 512], BF16))
            mcol = p6b.enter_context(nc.sbuf_tensor(f"mcol6_{b}", [128, 8], F32))
            p_s2 = p6b.enter_context(nc.psum_tensor(f"p6_s2_{b}", [128, 2048], F32))
            p_pv1 = p6b.enter_context(nc.psum_tensor(f"p6_pv1_{b}", [128, 512], F32))
            p_pv2 = p6b.enter_context(nc.psum_tensor(f"p6_pv2_{b}", [128, 512], F32))
            mk6 = {"exp": {}, "esum": {}, "pn": {}, "ptd": {}}
            pv_read = [0]
            for qt in range(NT):
                quad, qi = qt // 4, qt % 4
                pb = pbuf6[qt % 2]
                mc = (qt % 2) * 4
                P.wait("tensor", s_a, mk6["exp"].get(qt - 1, 0))
                for kc in range(4):
                    for f in range(NC):
                        P.emit("tensor", lambda t, qt=qt, kc=kc, f=f, QfT=QfT, KfT=KfT: t.matmul(
                            p_s2[:, kc * 512:(kc + 1) * 512],
                            QfT[:, f * S + qt * 128: f * S + qt * 128 + 128],
                            KfT[:, f * S + kc * 512: f * S + (kc + 1) * 512],
                            start=(f == 0), stop=(f == NC - 1)),
                            inc=(s_t if (kc == 3 and f == NC - 1) else None))
                sc_mark = s_t.n
                P.wait("scalar", s_t, sc_mark)
                P.wait("scalar", s_pt, mk6["ptd"].get(qt - 2, 0))
                P.emit("scalar", lambda a, pb=pb, mc=mc: a.activation(
                    pb[:, :], p_s2[:, :], AF.Exp, scale=1.0,
                    accum_out=mcol[:, mc:mc + 1]), inc=s_a)
                mk6["exp"][qt] = s_a.n
                P.wait("vector", s_a, mk6["exp"][qt])
                P.emit("vector", lambda v, mc=mc: v.reciprocal(
                    mcol[:, mc + 1:mc + 2], mcol[:, mc:mc + 1]), inc=s_v)
                mk6["esum"][qt] = s_v.n
                P.wait("scalar", s_v, mk6["esum"][qt])
                P.emit("scalar", lambda a, pb=pb, mc=mc: a.activation(
                    pb[:, :], pb[:, :], AF.Copy, scale=mcol[:, mc + 1:mc + 2]), inc=s_a)
                mk6["pn"][qt] = s_a.n
                P.wait("sync", s_a, mk6["pn"][qt])
                P.wait("sync", s_t, pv_read[0])
                P.emit("sync", lambda s, pb=pb, qi=qi, pTq=pTq: s.dma_start_transpose(
                    A(pTq, qi * 128, [[NT * 512, 128], [512, NT], [1, 128]]),
                    A(pb, 0, [[S, 128], [128, NT], [1, 128]])), inc=s_pt, incval=16)
                mk6["ptd"][qt] = s_pt.n
                if qi == 3:
                    P.wait("tensor", s_pt, mk6["ptd"][qt])
                    for dc in range(NC):
                        pv = p_pv1 if dc % 2 == 0 else p_pv2
                        if dc >= 2:
                            P.wait("tensor", s_v)
                        for kt in range(NT):
                            P.emit("tensor", lambda t, dc=dc, kt=kt, pv=pv, VfTOK=VfTOK, pTq=pTq: t.matmul(
                                pv[:, :], VfTOK[:, kt * D + dc * 128: kt * D + (dc + 1) * 128],
                                pTq[:, kt * 512:(kt + 1) * 512],
                                start=(kt == 0), stop=(kt == NT - 1)), inc=(s_t if kt == NT - 1 else None))
                        P.wait("vector", s_t)
                        P.emit("vector", lambda v, dc=dc, quad=quad, pv=pv, afT=afT: v.tensor_copy(
                            afT[:, dc * S + quad * 512: dc * S + (quad + 1) * 512], pv[:, :]), inc=s_v)
                    pv_read[0] = s_t.n
                    P.wait("tensor", s_v)
            P.barrier()
        span_qf.close()

        with contextlib.ExitStack() as p6c:
            XN32 = p6c.enter_context(nc.sbuf_tensor(f"XN32b_{b}", [128, NC * S], F32))
            XFT = p6c.enter_context(nc.sbuf_tensor(f"XFT_{b}", [128, NC * S], BF16))
            wo2 = p6c.enter_context(nc.sbuf_tensor(f"wo2_{b}", [128, NC * 1024], BF16))
            xfrow = p6c.enter_context(nc.sbuf_tensor(f"xfrow_{b}", [128, S], F32))
            g_s = p6c.enter_context(nc.sbuf_tensor(f"g_s_{b}", [E, S], F32))
            gt_s = p6c.enter_context(nc.sbuf_tensor(f"gt_s_{b}", [128, E], F32))
            gcol = p6c.enter_context(nc.sbuf_tensor(f"gcol_{b}", [128, 4], F32))
            stg = p6c.enter_context(nc.sbuf_tensor(f"stg6_{b}", [128, D], F32))
            p_w1 = p6c.enter_context(nc.psum_tensor(f"p6_w1_{b}", [128, 512], F32))
            p_w2 = p6c.enter_context(nc.psum_tensor(f"p6_w2_{b}", [128, 512], F32))
            p_g = p6c.enter_context(nc.psum_tensor(f"p6_g_{b}", [E, 512], F32))
            p_gt = p6c.enter_context(nc.psum_tensor(f"p6_gt_{b}", [128, E], F32))
            p_t1 = p6c.enter_context(nc.psum_tensor(f"p6_t1_{b}", [128, 512], F32))
            p_t2 = p6c.enter_context(nc.psum_tensor(f"p6_t2_{b}", [128, 512], F32))
            P.emit("sync", lambda s, b=b, XN32=XN32: s.dma_start(
                A(XN32, 0, [[NC * S, 128], [S, NC], [1, S]]),
                A(sc_xn32, b * D * S, [[S, 128], [128 * S, NC], [1, S]])), inc=s_lh, incval=16)
            P.emit("gpsimd", lambda g, wo2=wo2: g.dma_start(
                wo2[:, :], A(W["fWo"], 0, [[D, 128], [128 * D, NC], [1, D]])), inc=s_ld, incval=16)
            P.barrier()
            stg_tok = p6c.enter_context(nc.sbuf_tensor(f"stg_tok_{b}", [128, S], F32))
            for dc in range(NC):
                for tc in range(4):
                    pw = p_w1 if tc % 2 == 0 else p_w2
                    for f in range(NC):
                        P.emit("tensor", lambda t, dc=dc, tc=tc, f=f, pw=pw, wo2=wo2, afT=afT: t.matmul(
                            pw[:, :], wo2[:, f * 1024 + dc * 128: f * 1024 + (dc + 1) * 128],
                            afT[:, f * S + tc * 512: f * S + (tc + 1) * 512],
                            start=(f == 0), stop=(f == NC - 1)), inc=(s_t if f == NC - 1 else None))
                    P.wait("vector", s_t)
                    P.chain("vector", [
                        lambda v, dc=dc, tc=tc, pw=pw, XN32=XN32, xfrow=xfrow: v.scalar_tensor_tensor(
                            xfrow[:, tc * 512:(tc + 1) * 512], pw[:, :], fbo_c[:, dc:dc + 1],
                            XN32[:, dc * S + tc * 512: dc * S + (tc + 1) * 512], OP.add, OP.add),
                        lambda v, dc=dc, tc=tc, xfrow=xfrow, XFT=XFT: v.tensor_copy(
                            XFT[:, dc * S + tc * 512: dc * S + (tc + 1) * 512],
                            xfrow[:, tc * 512:(tc + 1) * 512]),
                    ], s_v)
                    P.wait("tensor", s_v)
                P.wait("tensor", s_st, s_st.n)
                for g4 in range(4):
                    pt = p_t1 if g4 % 2 == 0 else p_t2
                    for j in range(4):
                        P.emit("tensor", lambda t, g4=g4, j=j, pt=pt, xfrow=xfrow: t.transpose(
                            pt[:, j * 128:(j + 1) * 128],
                            xfrow[:, (g4 * 4 + j) * 128:(g4 * 4 + j + 1) * 128], ident_f[:, :]),
                            inc=(s_t if j == 3 else None))
                    P.wait("vector", s_t)
                    P.emit("vector", lambda v, g4=g4, pt=pt, stg_tok=stg_tok: v.tensor_copy(
                        stg_tok[:, g4 * 512:(g4 + 1) * 512], pt[:, :]), inc=s_v)
                    P.wait("tensor", s_v)
                P.wait("sync", s_v)
                P.emit("sync", lambda s, b=b, dc=dc, stg_tok=stg_tok: s.dma_start(
                    A(sc_xf32, b * S * D + dc * 128, [[D, 128], [128 * D, NT], [1, 128]]),
                    A(stg_tok, 0, [[S, 128], [128, NT], [1, 128]])), inc=s_st, incval=16)
                P.barrier(("tensor", "vector", "sync"))
            P.emit("sync", lambda s, b=b, XFT=XFT: s.dma_start(
                A(sc_xft, b * D * S, [[S, 128], [128 * S, NC], [1, S]]),
                A(XFT, 0, [[NC * S, 128], [S, NC], [1, S]])), inc=s_st, incval=16)
            for tc in range(4):
                for f in range(NC):
                    P.emit("tensor", lambda t, tc=tc, f=f, XFT=XFT: t.matmul(
                        p_g[0:E, :], wg_bf[:, f * E:(f + 1) * E],
                        XFT[:, f * S + tc * 512: f * S + (tc + 1) * 512],
                        start=(f == 0), stop=(f == NC - 1)), inc=(s_t if f == NC - 1 else None))
                P.wait("vector", s_t)
                P.emit("vector", lambda v, tc=tc, g_s=g_s: v.tensor_scalar(
                    g_s[0:E, tc * 512:(tc + 1) * 512], p_g[0:E, :], fbg_c[0:E, 0:1], None, OP.add), inc=s_v)
                P.wait("tensor", s_v)
            for ti in range(NT):
                P.emit("tensor", lambda t, ti=ti, g_s=g_s: t.transpose(
                    p_gt[:, 0:E], g_s[0:E, ti * 128:(ti + 1) * 128], ident_f[0:E, 0:E]), inc=s_t)
                P.wait("vector", s_t)
                P.chain("vector", [
                    lambda v, gt_s=gt_s: v.tensor_copy(gt_s[:, :], p_gt[:, 0:E]),
                    lambda v, gt_s=gt_s, gcol=gcol: v.reduce_max(gcol[:, 0:1], gt_s[:, :], axis=AXX),
                    lambda v, gcol=gcol: v.tensor_scalar(gcol[:, 1:2], gcol[:, 0:1], -1.0, None, OP.mult),
                ], s_v)
                P.wait("scalar", s_v)
                P.emit("scalar", lambda a, gt_s=gt_s, gcol=gcol: a.activation(
                    gt_s[:, :], gt_s[:, :], AF.Exp, bias=gcol[:, 1:2], scale=1.0,
                    accum_out=gcol[:, 2:3]), inc=s_a)
                P.wait("vector", s_a)
                P.chain("vector", [
                    lambda v, gcol=gcol: v.reciprocal(gcol[:, 3:4], gcol[:, 2:3]),
                    lambda v, b=b, ti=ti, gt_s=gt_s, gcol=gcol: v.tensor_scalar(
                        gates_t[:, (b * NT + ti) * E:(b * NT + ti + 1) * E], gt_s[:, :],
                        gcol[:, 3:4], None, OP.mult),
                ], s_v)
                P.wait("tensor", s_v)
            P.wait("sync", s_st)
            P.barrier()
        span_f.close()
        P.barrier()

        if stage == "p6" and b == 0:
            tap("gates_t", gates_t, 128, NT * E, F32)
            return finish_stage()

    with contextlib.ExitStack() as p7:
        acc = p7.enter_context(nc.sbuf_tensor("acc", [128, NT * D], F32))
        w1h = p7.enter_context(nc.sbuf_tensor("w1h", [128, NC * 2048], BF16))
        w2hb = [p7.enter_context(nc.sbuf_tensor(f"w2h{i}", [128, 16 * 1024], BF16)) for i in range(2)]
        h_sb = p7.enter_context(nc.sbuf_tensor("h_sb", [128, 16 * 512], BF16))
        xfs = p7.enter_context(nc.sbuf_tensor("xfs", [128, NC * 512], BF16))
        xfs2 = p7.enter_context(nc.sbuf_tensor("xfs2", [128, NC * 512], BF16))
        aob = p7.enter_context(nc.sbuf_tensor("aob", [128, D], F32))
        p_hb = [p7.enter_context(nc.psum_tensor(f"p7_h{i}", [128, 512], F32)) for i in range(4)]
        p_e1 = p7.enter_context(nc.psum_tensor("p7_e1", [128, 1024], F32))
        p_e2 = p7.enter_context(nc.psum_tensor("p7_e2", [128, 1024], F32))

        marks = {"w1q": [0, 0, 0, 0], "w2rd": [0, 0], "xfs": [0, 0],
                 "relu": [0, 0, 0, 0], "stt": [0, 0], "b2": 0}
        ld_w1q = [0, 0, 0, 0]
        ld_w2q = [0, 0, 0, 0]
        for tb in range(B_LOC):
            P.wait("sync", s_st)
            init_mk = [0] * NT
            for e in range(E):
                P.wait("sync", s_t, marks["b2"])
                P.emit("sync", lambda s, e=e, vb_r=vb_r: s.dma_start(
                    vb_r[64:65, :], A(W["fb2"], e * D, [[D, 1], [1, D]])), inc=s_lb, incval=16)
                lb_mark = s_lb.n
                for dh in range(2):
                    par2 = (tb * 16 + e * 2 + dh) % 2
                    w2h = w2hb[par2]
                    for q in range(4):
                        P.wait("gpsimd", s_t, marks["w1q"][q])
                        P.emit("gpsimd", lambda g, e=e, dh=dh, q=q, w1h=w1h: g.dma_start(
                            A(w1h, q * 512, [[NC * 2048, 128], [2048, NC], [1, 512]]),
                            A(W["fW1"], e * D * FF + dh * 2048 + q * 512,
                              [[FF, 128], [128 * FF, NC], [1, 512]])), inc=s_ld, incval=16)
                        ld_w1q[q] = s_ld.n
                    ld_w1 = s_ld.n
                    P.wait("gpsimd", s_t, marks["w2rd"][par2])
                    for q in range(4):
                        P.emit("gpsimd", lambda g, e=e, dh=dh, q=q, w2h=w2h: g.dma_start(
                            A(w2h, q * 4 * 1024, [[16 * 1024, 128], [1024, 4], [1, 1024]]),
                            A(W["fW2"], e * FF * D + (dh * 2048 + q * 4 * 128) * D,
                              [[D, 128], [128 * D, 4], [1, D]])), inc=s_ld, incval=16)
                    ld_w2 = s_ld.n
                    for tc in range(4):
                        xf_cur = xfs if tc % 2 == 0 else xfs2
                        lxs = s_lx if tc % 2 == 0 else s_lx2
                        P.wait("sync", s_t, marks["xfs"][tc % 2])
                        P.emit("sync", lambda s, tb=tb, tc=tc, xf_cur=xf_cur: s.dma_start(
                            A(xf_cur, 0, [[NC * 512, 128], [512, NC], [1, 512]]),
                            A(sc_xft, tb * D * S + tc * 512, [[S, 128], [128 * S, NC], [1, 512]])),
                            inc=lxs, incval=16)
                        P.wait("tensor", lxs, lxs.n)
                        if e == 0 and dh == 0:
                            P.emit("sync", lambda s, tb=tb, tc=tc, acc=acc: s.dma_start(
                                A(acc, tc * 4 * D, [[NT * D, 128], [D, 4], [1, D]]),
                                A(x_in, tb * S * D + tc * 4 * 128 * D,
                                  [[D, 128], [128 * D, 4], [1, D]])), inc=s_lh, incval=16)
                            for ti4 in range(4):
                                tix = tc * 4 + ti4
                                for src_t in (sc_aout, sc_xf32):
                                    P.wait("sync", s_v)
                                    P.emit("sync", lambda s, tb=tb, tix=tix, src_t=src_t: s.dma_start(
                                        aob[:, :], A(src_t, tb * S * D + tix * 128 * D,
                                                     [[D, 128], [1, D]])), inc=s_lh, incval=16)
                                    P.wait("vector", s_lh)
                                    P.emit("vector", lambda v, tix=tix: v.tensor_tensor(
                                        acc[:, tix * D:(tix + 1) * D],
                                        acc[:, tix * D:(tix + 1) * D], aob[:, :], OP.add), inc=s_v)
                                init_mk[tix] = s_v.n
                        for hd in range(16):
                            ph_ = p_hb[hd % 4]
                            P.wait("tensor", s_a, marks["relu"][hd % 4])
                            if tc == 0 and hd == 0:
                                P.wait("tensor", s_ld, ld_w1)
                            for f in range(NC):
                                P.emit("tensor", lambda t, hd=hd, f=f, ph_=ph_, w1h=w1h, xf_cur=xf_cur: t.matmul(
                                    ph_[:, :], w1h[:, f * 2048 + hd * 128: f * 2048 + (hd + 1) * 128],
                                    xf_cur[:, f * 512:(f + 1) * 512],
                                    start=(f == 0), stop=(f == NC - 1)), inc=(s_t if f == NC - 1 else None))
                            if tc == 3 and hd % 4 == 3:
                                marks["w1q"][hd // 4] = s_t.n
                            if hd == 15:
                                marks["xfs"][tc % 2] = s_t.n
                            P.wait("scalar", s_t)
                            P.emit("scalar", lambda a, e=e, dh=dh, hd=hd, ph_=ph_, h_sb=h_sb: a.activation(
                                h_sb[:, hd * 512:(hd + 1) * 512], ph_[:, :], AF.Relu,
                                bias=b1_c[:, e * 32 + dh * 16 + hd: e * 32 + dh * 16 + hd + 1]), inc=s_a)
                            marks["relu"][hd % 4] = s_a.n
                        relu_all = s_a.n
                        P.wait("tensor", s_a, relu_all)
                        for ti in range(4):
                            pe = p_e1 if ti % 2 == 0 else p_e2
                            P.wait("tensor", s_v, marks["stt"][ti % 2])
                            if tc == 0 and ti == 0:
                                P.wait("tensor", s_ld, ld_w2)
                            if dh == 0 and tc == 0 and ti == 0:
                                P.wait("tensor", s_lb, lb_mark)
                            if e == 0 and dh == 0:
                                P.wait("vector", s_v, init_mk[tc * 4 + ti])
                            for hd in range(16):
                                for hf in range(2):
                                    stop = (dh == 1 and hd == 15)
                                    P.emit("tensor", lambda t, hd=hd, hf=hf, ti=ti, pe=pe, h_sb=h_sb, w2h=w2h, stop=stop: t.matmul(
                                        pe[:, hf * 512:(hf + 1) * 512],
                                        h_sb[:, hd * 512 + ti * 128: hd * 512 + (ti + 1) * 128],
                                        w2h[:, hd * 1024 + hf * 512: hd * 1024 + (hf + 1) * 512],
                                        start=(hd == 0), stop=stop),
                                        inc=(s_t if (dh == 1 and hd == 15 and hf == 1) else None))
                                if tc == 3 and ti == 3 and hd % 4 == 3:
                                    pass
                            if dh == 0:
                                for hf in range(2):
                                    P.emit("tensor", lambda t, e=e, hf=hf, pe=pe: t.matmul(
                                        pe[:, hf * 512:(hf + 1) * 512], ones_r[64:65, :],
                                        vb_r[64:65, hf * 512:(hf + 1) * 512],
                                        start=False, stop=True), inc=(s_t if hf == 1 else None))
                            if tc == 3 and ti == 3:
                                marks["w2rd"][par2] = s_t.n
                            if dh == 0 and tc == 3 and ti == 3:
                                marks["b2"] = s_t.n
                            eo_done = s_t.n
                            P.wait("vector", s_t, eo_done)
                            tix = tc * 4 + ti
                            P.emit("vector", lambda v, e=e, tb=tb, tix=tix, pe=pe, acc=acc: v.scalar_tensor_tensor(
                                acc[:, tix * D:(tix + 1) * D], pe[:, :],
                                gates_t[:, (tb * NT + tix) * E + e:(tb * NT + tix) * E + e + 1],
                                acc[:, tix * D:(tix + 1) * D], OP.mult, OP.add), inc=s_v)
                            marks["stt"][ti % 2] = s_v.n
            P.barrier(("tensor", "vector", "sync"))
            P.emit("sync", lambda s, tb=tb, acc=acc: s.dma_start(
                A(y_out, tb * S * D, [[D, 128], [128 * D, NT], [1, D]]),
                A(acc, 0, [[NT * D, 128], [D, NT], [1, D]])), inc=s_st, incval=16)
        P.wait("sync", s_st)
        P.barrier()

    stack.close()
    P.replay()
    P.close()
    return nc


def _shard_inputs(inputs):
    x = np.ascontiguousarray(inputs["L0_attn_x"], dtype=np.float32)
    ren = {
        "aWq": "L0_attn_Wq", "abq": "L0_attn_bq", "aWk": "L0_attn_Wk", "abk": "L0_attn_bk",
        "aWv": "L0_attn_Wv", "abv": "L0_attn_bv", "aWc2q": "L0_attn_Wc2q", "abc2q": "L0_attn_bc2q",
        "aWo": "L0_attn_Wo", "abo": "L0_attn_bo", "ag1": "L0_attn_g1", "abeta1": "L0_attn_beta1",
        "ag2": "L0_attn_g2", "abeta2": "L0_attn_beta2", "aW1": "L0_attn_W1", "ab1": "L0_attn_b1",
        "aWm": "L0_attn_Wm", "abm": "L0_attn_bm", "aWqo": "L0_attn_Wqo", "abqo": "L0_attn_bqo",
        "aW2": "L0_attn_W2", "ab2": "L0_attn_b2",
        "fWq": "L0_ffn_Wq", "fbq": "L0_ffn_bq", "fWk": "L0_ffn_Wk", "fbk": "L0_ffn_bk",
        "fWv": "L0_ffn_Wv", "fbv": "L0_ffn_bv", "fWo": "L0_ffn_Wo", "fbo": "L0_ffn_bo",
        "fWg": "L0_ffn_Wg", "fbg": "L0_ffn_bg", "fW1": "L0_ffn_W1", "fb1": "L0_ffn_b1",
        "fW2": "L0_ffn_W2", "fb2": "L0_ffn_b2",
    }
    w = {k: np.ascontiguousarray(inputs[src], dtype=np.float32) for k, src in ren.items()}
    in_maps = []
    for c in range(8):
        m = {"x": x[2 * c:2 * c + 2]}
        m.update(w)
        in_maps.append(m)
    return in_maps


def kernel(**inputs):
    in_maps = _shard_inputs(inputs)
    nc = build()
    res = run_bass_kernel_spmd(nc, in_maps, core_ids=list(range(8)))
    return np.concatenate([r["y"] for r in res.results], axis=0)


# version 17
# speedup vs baseline: 1.1205x; 1.0115x over previous
import contextlib
import types
import numpy as np
import concourse.bass as bass
import concourse.mybir as mybir
from concourse.bass_utils import run_bass_kernel_spmd

F32 = mybir.dt.float32
BF16 = mybir.dt.bfloat16
AF = mybir.ActivationFunctionType
OP = mybir.AluOpType
AXX = mybir.AxisListType.X

PI_LO = float(np.nextafter(np.float32(np.pi), np.float32(0)))
PI8_LO = float(np.nextafter(np.float32(np.pi / 8), np.float32(0)))
ISQHD = float(np.float32(1.0 / np.sqrt(128.0)))
INV32 = float(np.float32(1.0 / 32.0))
LN_EPS = 1e-5

B_LOC, S, D, H, HD, NQ, FF, E = 2, 2048, 1024, 8, 128, 8, 4096, 8
NT = S // 128
NC = D // 128
ENGS = ("tensor", "vector", "scalar", "gpsimd", "sync")


class Sem:
    __slots__ = ("h", "n", "name")

    def __init__(self, h, name):
        self.h, self.n, self.name = h, 0, name


class Prog:
    def __init__(self, nc):
        self.nc = nc
        self.q = {e: [] for e in ENGS}
        self._cms = []
        self.sems = {}

    def sem(self, name):
        if name not in self.sems:
            cm = self.nc.semaphore(name)
            self.sems[name] = Sem(cm.__enter__(), name)
            self._cms.append(cm)
        return self.sems[name]

    @staticmethod
    def _freeze(fn):
        if fn.__closure__:
            cells = tuple(types.CellType(c.cell_contents) for c in fn.__closure__)
            return types.FunctionType(fn.__code__, fn.__globals__, fn.__name__,
                                      fn.__defaults__, cells)
        return fn

    def emit(self, eng, fn, inc=None, incval=1):
        fn = self._freeze(fn)
        if inc is not None:
            inc.n += incval

        def op(h, fn=fn, inc=inc, incval=incval):
            r = fn(h)
            if inc is not None:
                r.then_inc(inc.h, incval)
        self.q[eng].append(op)

    def wait(self, eng, sm, v=None):
        v = sm.n if v is None else v
        self.q[eng].append(lambda h, sm=sm, v=v: h.wait_ge(sm.h, v))

    dma_sems = ()

    def barrier(self, engs=ENGS):
        bar = self.sem("bar_" + "".join(sorted(e[0] + e[1] for e in engs)))
        for e in engs:
            for sm in self.dma_sems:
                self.wait(e, sm)
            self.q[e].append(lambda h: h.drain())
            self.q[e].append(lambda h, b=bar: h.drain().then_inc(b.h, 1))
        bar.n += len(engs)
        tgt = bar.n
        for e in engs:
            self.q[e].append(lambda h, b=bar, t=tgt: h.wait_ge(b.h, t))

    def chain(self, eng, fns, sem, final_inc=True):
        for i, fn in enumerate(fns):
            if i > 0:
                self.wait(eng, sem)
            last = (i == len(fns) - 1)
            self.emit(eng, fn, inc=(sem if (not last or final_inc) else None))

    def replay(self):
        with self.nc.Block() as block:
            @block.tensor
            def _(t):
                for op in self.q["tensor"]:
                    op(t)

            @block.vector
            def _(v):
                for op in self.q["vector"]:
                    op(v)

            @block.scalar
            def _(a):
                for op in self.q["scalar"]:
                    op(a)

            @block.gpsimd
            def _(g):
                for op in self.q["gpsimd"]:
                    op(g)

            @block.sync
            def _(s):
                for op in self.q["sync"]:
                    op(s)

    def close(self):
        for cm in reversed(self._cms):
            cm.__exit__(None, None, None)


def A(handle, offset, dims):
    return bass.AP(handle, int(offset), [list(d) for d in dims])


def build(stage=None, taps=()):
    nc = bass.Bass("TRN2", target_bir_lowering=False)
    P = Prog(nc)
    stack = contextlib.ExitStack()

    def sbuf(name, shape, dtype):
        return stack.enter_context(nc.sbuf_tensor(name, shape, dtype))

    x_in = nc.dram_tensor("x", [B_LOC, S, D], F32, kind="ExternalInput")
    y_out = nc.dram_tensor("y", [B_LOC, S, D], F32, kind="ExternalOutput")
    wshape = {
        "aWq": [H, D, HD], "abq": [H, HD], "aWk": [H, D, HD], "abk": [H, HD],
        "aWv": [H, D, HD], "abv": [H, HD], "aWc2q": [H, HD, NQ], "abc2q": [H, NQ],
        "aWo": [D, D], "abo": [D], "ag1": [D], "abeta1": [D], "ag2": [D], "abeta2": [D],
        "aW1": [D, FF], "ab1": [FF], "aWm": [FF, NQ], "abm": [NQ],
        "aWqo": [NQ, FF], "abqo": [FF], "aW2": [FF, D], "ab2": [D],
        "fWq": [D, D], "fbq": [D], "fWk": [D, D], "fbk": [D],
        "fWv": [D, D], "fbv": [D], "fWo": [D, D], "fbo": [D],
        "fWg": [D, E], "fbg": [E], "fW1": [E, D, FF], "fb1": [E, FF],
        "fW2": [E, FF, D], "fb2": [E, D],
    }
    W = {k: nc.dram_tensor(k, v, F32, kind="ExternalInput") for k, v in wshape.items()}
    sc_xnt = nc.dram_tensor("sc_xnt", [B_LOC, D, S], BF16)
    sc_xn32 = nc.dram_tensor("sc_xn32", [B_LOC, D, S], F32)
    sc_aout = nc.dram_tensor("sc_aout", [B_LOC, S, D], F32)
    sc_xf32 = nc.dram_tensor("sc_xf32", [B_LOC, S, D], F32)
    sc_xft = nc.dram_tensor("sc_xft", [B_LOC, D, S], BF16)
    sc_vtok = nc.dram_tensor("sc_vtok", [B_LOC, 128, NT * D], BF16)

    s_ld = P.sem("ld")
    s_g = P.sem("g")
    s_lh = P.sem("lh")
    s_st = P.sem("st")
    s_t = P.sem("t")
    s_v = P.sem("v")
    s_a = P.sem("a")
    s_lx = P.sem("lx")
    s_lx2 = P.sem("lx2")
    s_lv = P.sem("lv")
    s_lv2 = P.sem("lv2")
    s_lb = P.sem("lb")
    s_ptb = [P.sem(f"pt{i}") for i in range(4)]
    s_ld2 = P.sem("ld2")
    s_xc = P.sem("xc")
    P.dma_sems = (s_ld, s_lh, s_st, s_lx, s_lx2, s_lv, s_lv2, s_lb, s_ld2, s_xc) + tuple(s_ptb)

    tapset = set(taps)
    tap_req = []

    def tap(name, buf, rows, cols, dtype):
        if name in tapset:
            t = nc.dram_tensor("tap_" + name, [rows, cols], dtype, kind="ExternalOutput")
            tap_req.append((t, buf, rows, cols))

    open_spans = []

    def finish_stage():
        P.barrier()
        for (t, buf, rows, cols) in tap_req:
            P.emit("sync", lambda s, t=t, buf=buf, rows=rows, cols=cols: s.dma_start(
                A(t, 0, [[cols, rows], [1, cols]]),
                A(buf, 0, [[int(buf.shape[1]), rows], [1, cols]])), inc=s_st, incval=16)
        P.wait("sync", s_st)
        P.barrier()
        for sp in reversed(open_spans):
            sp.close()
        stack.close()
        P.replay()
        P.close()
        return nc

    ident_bf = sbuf("ident_bf", [128, 128], BF16)
    ident_f = sbuf("ident_f", [128, 128], F32)
    ones_c = sbuf("ones_c", [128, 1], F32)
    ones_r = sbuf("ones_r", [65, 128], F32)
    iota_r = sbuf("iota_r", [128, 128], F32)
    pidx = sbuf("pidx", [128, 1], F32)

    bq_c = sbuf("bq_c", [128, H], F32)
    bk_c = sbuf("bk_c", [128, H], F32)
    vb_r = sbuf("vb_r", [65, D], F32)
    bc2q_r = sbuf("bc2q_r", [1, H * NQ], F32)
    bc2q_rq = sbuf("bc2q_rq", [1, H * NQ], F32)
    bo_c = sbuf("bo_c", [128, NC], F32)
    g1_c = sbuf("g1_c", [128, NC], F32)
    be1_c = sbuf("be1_c", [128, NC], F32)
    g2_c = sbuf("g2_c", [128, NC], F32)
    be2_c = sbuf("be2_c", [128, NC], F32)
    fbq_c = sbuf("fbq_c", [128, NC], F32)
    fbk_c = sbuf("fbk_c", [128, NC], F32)
    fbo_c = sbuf("fbo_c", [128, NC], F32)
    fbg_c = sbuf("fbg_c", [E, 1], F32)
    b1_c = sbuf("b1_c", [128, E * 32], F32)
    m1_bf = sbuf("m1_bf", [128, NC * NQ], BF16)
    bias1_c = sbuf("bias1_c", [NQ, 1], F32)
    m2_bf = sbuf("m2_bf", [NQ, D], BF16)
    bias2_c = sbuf("bias2_c", [128, NC], F32)
    wc2q_bf = sbuf("wc2q_bf", [128, H * NQ], BF16)
    wg_bf = sbuf("wg_bf", [128, NC * E], BF16)
    gates_t = sbuf("gates_t", [128, B_LOC * NT * E], F32)

    st_a = sbuf("st_a", [128, 32], F32)
    st_b = sbuf("st_b", [128, 32], F32)
    st_c = sbuf("st_c", [128, 32], F32)

    P.emit("gpsimd", lambda g: g.iota(iota_r[:, :], [[1, 128]], channel_multiplier=0,
                                      allow_small_or_imprecise_dtypes=True))
    P.emit("gpsimd", lambda g: g.iota(pidx[:, :], [[1, 1]], channel_multiplier=1,
                                      allow_small_or_imprecise_dtypes=True))
    P.emit("gpsimd", lambda g: g.memset(ones_c[:, :], 1.0))
    P.emit("gpsimd", lambda g: g.memset(ones_r[:, :], 1.0), inc=s_g)

    def ldpar(dst, w, dims, eng="sync", off=0):
        sem = s_lh if eng == "sync" else s_ld
        P.emit(eng, lambda h, dst=dst, w=w, dims=dims, off=off: h.dma_start(
            dst, A(W[w], off, dims), allow_slow_non_contiguous=True), inc=sem, incval=16)

    ldpar(bo_c[:, :], "abo", [[1, 128], [128, NC]])
    ldpar(g1_c[:, :], "ag1", [[1, 128], [128, NC]])
    ldpar(be1_c[:, :], "abeta1", [[1, 128], [128, NC]])
    ldpar(g2_c[:, :], "ag2", [[1, 128], [128, NC]])
    ldpar(be2_c[:, :], "abeta2", [[1, 128], [128, NC]])
    ldpar(fbq_c[:, :], "fbq", [[1, 128], [128, NC]])
    ldpar(fbk_c[:, :], "fbk", [[1, 128], [128, NC]])
    ldpar(fbo_c[:, :], "fbo", [[1, 128], [128, NC]])
    ldpar(bq_c[:, :], "abq", [[1, 128], [128, H]])
    ldpar(bk_c[:, :], "abk", [[1, 128], [128, H]])
    ldpar(vb_r[0:1, :], "abv", [[D, 1], [1, D]])
    ldpar(bc2q_r[0:1, :], "abc2q", [[H * NQ, 1], [1, H * NQ]])
    ldpar(fbg_c[0:E, 0:1], "fbg", [[1, E], [1, 1]])
    ldpar(vb_r[32:33, :], "fbv", [[D, 1], [1, D]])
    ldpar(b1_c[:, :], "fb1", [[1, 128], [FF, E], [128, 32]])
    ldpar(wc2q_bf[:, :], "aWc2q", [[NQ, 128], [128 * NQ, H], [1, NQ]], eng="gpsimd")
    ldpar(wg_bf[:, :], "fWg", [[E, 128], [128 * E, NC], [1, E]], eng="gpsimd")
    P.barrier()
    P.emit("vector", lambda v: v.tensor_scalar(ident_bf[:, :], iota_r[:, :], pidx[:, 0:1], None, OP.is_equal))
    P.emit("vector", lambda v: v.tensor_scalar(ident_f[:, :], iota_r[:, :], pidx[:, 0:1], None, OP.is_equal))
    P.emit("vector", lambda v: v.tensor_scalar(bc2q_rq[0:1, :], bc2q_r[0:1, :], ISQHD, None, OP.mult), inc=s_v)
    P.barrier()

    with contextlib.ExitStack() as pp:
        w1blk = pp.enter_context(nc.sbuf_tensor("w1blk", [128, 1024], BF16))
        w1t = pp.enter_context(nc.sbuf_tensor("w1t", [128, 1024], BF16))
        wm_bf = pp.enter_context(nc.sbuf_tensor("wm_bf", [128, 32 * NQ], BF16))
        wm_32 = pp.enter_context(nc.sbuf_tensor("wm_32", [128, 32 * NQ], F32))
        b1a_c = pp.enter_context(nc.sbuf_tensor("b1a_c", [128, 32], F32))
        wqo_s = pp.enter_context(nc.sbuf_tensor("wqo_s", [NQ, FF], BF16))
        wqoT = pp.enter_context(nc.sbuf_tensor("wqoT", [128, 32 * NQ], BF16))
        bqo_c = pp.enter_context(nc.sbuf_tensor("bqo_c", [128, 32], F32))
        w2blk = pp.enter_context(nc.sbuf_tensor("w2blk", [128, 1024], BF16))
        w2b32 = pp.enter_context(nc.sbuf_tensor("w2b32", [128, 1024], F32))
        m1row = pp.enter_context(nc.sbuf_tensor("m1row", [NQ, D], BF16))
        brow = pp.enter_context(nc.sbuf_tensor("brow", [1, D + NQ], F32))
        exrow = pp.enter_context(nc.sbuf_tensor("exrow", [1, D], F32))
        with contextlib.ExitStack() as pps:
            p_tr = pps.enter_context(nc.psum_tensor("pp_tr", [128, 1024], BF16))
            p_m1 = pps.enter_context(nc.psum_tensor("pp_m1", [NQ, 1024], F32))
            p_m2 = pps.enter_context(nc.psum_tensor("pp_m2", [NQ, 1024], F32))
            p_bb = pps.enter_context(nc.psum_tensor("pp_bb", [1, 1536], F32))

            ldpar(wm_bf[:, :], "aWm", [[NQ, 128], [128 * NQ, 32], [1, NQ]], eng="gpsimd")
            ldpar(wm_32[:, :], "aWm", [[NQ, 128], [128 * NQ, 32], [1, NQ]])
            ldpar(b1a_c[:, :], "ab1", [[1, 128], [128, 32]])
            ldpar(wqo_s[0:NQ, :], "aWqo", [[FF, NQ], [1, FF]], eng="gpsimd")
            ldpar(bqo_c[:, :], "abqo", [[1, 128], [128, 32]])
            P.barrier()
            for c in range(32):
                P.emit("gpsimd", lambda g, c=c: g.dma_start(
                    w1blk[:, :], A(W["aW1"], c * 128, [[FF, 128], [128 * FF, NC], [1, 128]])),
                    inc=s_ld, incval=16)
                P.wait("tensor", s_ld)
                for f in range(NC):
                    P.emit("tensor", lambda t, f=f: t.transpose(
                        p_tr[:, f * 128:(f + 1) * 128], w1blk[:, f * 128:(f + 1) * 128], ident_bf[:, :]),
                        inc=(s_t if f == NC - 1 else None))
                P.wait("vector", s_t)
                P.emit("vector", lambda v: v.tensor_copy(w1t[:, :], p_tr[:, :]), inc=s_v)
                P.wait("tensor", s_v)
                for hf in range(2):
                    P.emit("tensor", lambda t, c=c, hf=hf: t.matmul(
                        p_m1[0:NQ, hf * 512:(hf + 1) * 512], wm_bf[:, c * NQ:(c + 1) * NQ],
                        w1t[:, hf * 512:(hf + 1) * 512], start=(c == 0), stop=(c == 31)))
                P.emit("tensor", lambda t, c=c: t.matmul(
                    p_bb[0:1, 1024:1024 + NQ], b1a_c[:, c:c + 1], wm_32[:, c * NQ:(c + 1) * NQ],
                    start=(c == 0), stop=(c == 31)))
                P.emit("tensor", lambda t, c=c: t.transpose(
                    p_tr[:, 0:NQ], wqo_s[0:NQ, c * 128:(c + 1) * 128], ident_bf[0:NQ, 0:NQ]), inc=s_t)
                P.wait("vector", s_t)
                P.emit("vector", lambda v, c=c: v.tensor_copy(wqoT[:, c * NQ:(c + 1) * NQ], p_tr[:, 0:NQ]), inc=s_v)
                P.barrier(("tensor", "vector", "gpsimd"))
            for c in range(32):
                P.emit("gpsimd", lambda g, c=c: g.dma_start(
                    w2blk[:, :], A(W["aW2"], c * 128 * D, [[D, 128], [1, D]])), inc=s_ld, incval=16)
                P.emit("sync", lambda s, c=c: s.dma_start(
                    w2b32[:, :], A(W["aW2"], c * 128 * D, [[D, 128], [1, D]])), inc=s_lh, incval=16)
                P.wait("tensor", s_ld)
                P.wait("tensor", s_lh)
                for hf in range(2):
                    P.emit("tensor", lambda t, c=c, hf=hf: t.matmul(
                        p_m2[0:NQ, hf * 512:(hf + 1) * 512], wqoT[:, c * NQ:(c + 1) * NQ],
                        w2blk[:, hf * 512:(hf + 1) * 512], start=(c == 0), stop=(c == 31)))
                for hf in range(2):
                    P.emit("tensor", lambda t, c=c, hf=hf: t.matmul(
                        p_bb[0:1, hf * 512:(hf + 1) * 512], bqo_c[:, c:c + 1],
                        w2b32[:, hf * 512:(hf + 1) * 512], start=(c == 0), stop=(c == 31)),
                        inc=(s_t if hf == 1 else None))
                P.barrier(("tensor", "gpsimd", "sync"))
            P.wait("vector", s_t)
            P.emit("vector", lambda v: v.tensor_copy(brow[0:1, 0:D], p_bb[0:1, 0:D]))
            P.emit("vector", lambda v: v.tensor_copy(brow[0:1, D:D + NQ], p_bb[0:1, 1024:1024 + NQ]))
            P.emit("vector", lambda v: v.tensor_copy(m1row[0:NQ, :], p_m1[0:NQ, :]))
            P.emit("vector", lambda v: v.tensor_copy(m2_bf[0:NQ, :], p_m2[0:NQ, :]), inc=s_v)
            ldpar(exrow[0:1, :], "ab2", [[D, 1], [1, D]])
            P.wait("vector", s_lh)
            P.wait("vector", s_v)
            P.emit("vector", lambda v: v.tensor_tensor(brow[0:1, 0:D], brow[0:1, 0:D], exrow[0:1, :], OP.add), inc=s_v)
            P.wait("sync", s_v)
            ldpar(exrow[0:1, 0:NQ], "abm", [[NQ, 1], [1, NQ]])
            P.wait("vector", s_lh)
            P.emit("vector", lambda v: v.tensor_tensor(brow[0:1, D:D + NQ], brow[0:1, D:D + NQ], exrow[0:1, 0:NQ], OP.add), inc=s_v)
            P.wait("tensor", s_v)
            P.barrier()
        with contextlib.ExitStack() as pps:
            p_tm = pps.enter_context(nc.psum_tensor("pp_tm", [128, NQ], F32))
            p_tmb = pps.enter_context(nc.psum_tensor("pp_tmb", [128, NQ], BF16))
            for c in range(NC):
                P.emit("tensor", lambda t, c=c: t.transpose(
                    p_tm[:, 0:1], brow[0:1, c * 128:(c + 1) * 128], ident_f[0:1, 0:1]), inc=s_t)
                P.wait("vector", s_t)
                P.emit("vector", lambda v, c=c: v.tensor_copy(bias2_c[:, c:c + 1], p_tm[:, 0:1]), inc=s_v)
                P.wait("tensor", s_v)
            P.emit("tensor", lambda t: t.transpose(p_tm[0:NQ, 0:1], brow[0:1, D:D + NQ], ident_f[0:1, 0:1]), inc=s_t)
            P.wait("vector", s_t)
            P.emit("vector", lambda v: v.tensor_copy(bias1_c[0:NQ, 0:1], p_tm[0:NQ, 0:1]), inc=s_v)
            P.wait("tensor", s_v)
            for c in range(NC):
                P.emit("tensor", lambda t, c=c: t.transpose(
                    p_tmb[:, 0:NQ], m1row[0:NQ, c * 128:(c + 1) * 128], ident_bf[0:NQ, 0:NQ]), inc=s_t)
                P.wait("vector", s_t)
                P.emit("vector", lambda v, c=c: v.tensor_copy(m1_bf[:, c * NQ:(c + 1) * NQ], p_tmb[:, 0:NQ]), inc=s_v)
                P.wait("tensor", s_v)
            P.barrier()

    if stage == "prep":
        tap("m1_bf", m1_bf, 128, NC * NQ, BF16)
        tap("m2_bf", m2_bf, NQ, D, BF16)
        tap("bias1_c", bias1_c, NQ, 1, F32)
        tap("bias2_c", bias2_c, 128, NC, F32)
        return finish_stage()

    def ln_fm(BUF, g_col, b_col, p_st, p_bc, sqtmp, rows):
        row_a, row_b, row_c = rows
        for qtr in range(4):
            for c in range(NC):
                P.emit("tensor", lambda t, qtr=qtr, c=c: t.matmul(
                    p_st[0:1, 0:512], ones_c[:, 0:1],
                    BUF[:, c * S + qtr * 512: c * S + qtr * 512 + 512],
                    start=(c == 0), stop=(c == NC - 1)), inc=(s_t if c == NC - 1 else None))
            P.wait("vector", s_t)
            P.emit("vector", lambda v, qtr=qtr: v.tensor_copy(
                row_a[0:1, qtr * 512:(qtr + 1) * 512], p_st[0:1, 0:512]), inc=s_v)
            sq_marks = {}
            for c in range(NC):
                half = c % 2
                P.wait("vector", s_t, sq_marks.get(c - 2, 0))
                P.emit("vector", lambda v, qtr=qtr, c=c, half=half: v.tensor_tensor(
                    sqtmp[:, half * 512:(half + 1) * 512],
                    BUF[:, c * S + qtr * 512: c * S + qtr * 512 + 512],
                    BUF[:, c * S + qtr * 512: c * S + qtr * 512 + 512], OP.mult), inc=s_v)
                P.wait("tensor", s_v)
                P.emit("tensor", lambda t, c=c, half=half: t.matmul(
                    p_st[0:1, 512:1024], ones_c[:, 0:1], sqtmp[:, half * 512:(half + 1) * 512],
                    start=(c == 0), stop=(c == NC - 1)), inc=s_t)
                sq_marks[c] = s_t.n
            P.wait("vector", s_t)
            P.emit("vector", lambda v, qtr=qtr: v.tensor_copy(
                row_b[0:1, qtr * 512:(qtr + 1) * 512], p_st[0:1, 512:1024]), inc=s_v)
        P.wait("vector", s_v)
        P.chain("vector", [
            lambda v: v.tensor_scalar(row_a[0:1, :], row_a[0:1, :], 1.0 / D, None, OP.mult),
            lambda v: v.tensor_scalar(row_b[0:1, :], row_b[0:1, :], 1.0 / D, None, OP.mult),
            lambda v: v.tensor_tensor(row_c[0:1, :], row_a[0:1, :], row_a[0:1, :], OP.mult),
            lambda v: v.tensor_tensor(row_b[0:1, :], row_b[0:1, :], row_c[0:1, :], OP.subtract),
            lambda v: v.tensor_scalar(row_b[0:1, :], row_b[0:1, :], LN_EPS, None, OP.add),
        ], s_v)
        P.wait("scalar", s_v)
        P.emit("scalar", lambda a: a.activation(row_c[0:1, :], row_b[0:1, :], AF.Sqrt), inc=s_a)
        P.wait("vector", s_a)
        P.chain("vector", [
            lambda v: v.reciprocal(row_b[0:1, :], row_c[0:1, :]),
            lambda v: v.tensor_tensor(row_c[0:1, :], row_c[0:1, :], row_b[0:1, :], OP.mult),
            lambda v: v.tensor_scalar(row_c[0:1, :], row_c[0:1, :], -1.0, 2.0, OP.mult, OP.add),
            lambda v: v.tensor_tensor(row_b[0:1, :], row_b[0:1, :], row_c[0:1, :], OP.mult),
            lambda v: v.tensor_tensor(row_a[0:1, :], row_a[0:1, :], row_b[0:1, :], OP.mult),
        ], s_v)
        for qtr in range(4):
            P.wait("tensor", s_v)
            P.emit("tensor", lambda t, qtr=qtr: t.matmul(
                p_bc[:, 0:512], ones_r[0:1, :], row_b[0:1, qtr * 512:(qtr + 1) * 512], start=True, stop=True))
            P.emit("tensor", lambda t, qtr=qtr: t.matmul(
                p_bc[:, 512:1024], ones_r[0:1, :], row_a[0:1, qtr * 512:(qtr + 1) * 512], start=True, stop=True),
                inc=s_t)
            P.wait("vector", s_t)
            for c in range(NC):
                sl = BUF[:, c * S + qtr * 512: c * S + qtr * 512 + 512]
                P.emit("vector", lambda v, sl=sl: v.tensor_tensor(sl, sl, p_bc[:, 0:512], OP.mult),
                       inc=(s_v if c == NC - 1 else None))
            P.wait("vector", s_v)
            for c in range(NC):
                sl = BUF[:, c * S + qtr * 512: c * S + qtr * 512 + 512]
                P.emit("vector", lambda v, sl=sl: v.tensor_tensor(sl, sl, p_bc[:, 512:1024], OP.subtract),
                       inc=(s_v if c == NC - 1 else None))
            if g_col is not None:
                P.wait("vector", s_v)
                for c in range(NC):
                    sl = BUF[:, c * S + qtr * 512: c * S + qtr * 512 + 512]
                    P.emit("vector", lambda v, sl=sl, c=c: v.tensor_scalar(
                        sl, sl, g_col[:, c:c + 1], b_col[:, c:c + 1], OP.mult, OP.add),
                        inc=(s_v if c == NC - 1 else None))
            P.barrier(("tensor", "vector"))

    def spill_tok(BUF, dram, dram_off, p_t1, p_t2, stg):
        for ti in range(NT):
            P.wait("vector", s_st, s_st.n)
            P.wait("tensor", s_v)
            for r in range(2):
                pb = p_t1 if r == 0 else p_t2
                for c in range(4):
                    P.emit("tensor", lambda t, ti=ti, r=r, c=c, pb=pb: t.transpose(
                        pb[:, c * 128:(c + 1) * 128],
                        BUF[:, (4 * r + c) * S + ti * 128: (4 * r + c) * S + ti * 128 + 128],
                        ident_f[:, :]), inc=(s_t if c == 3 else None))
                P.wait("vector", s_t)
                P.emit("vector", lambda v, r=r, pb=pb, stg=stg: v.tensor_copy(
                    stg[:, r * 512:(r + 1) * 512], pb[:, :]), inc=s_v)
            P.wait("sync", s_v)
            P.emit("sync", lambda s, ti=ti, dram=dram, dram_off=dram_off, stg=stg: s.dma_start(
                A(dram, dram_off + ti * 128 * D, [[D, 128], [128, NC], [1, 128]]),
                A(stg, 0, [[D, 128], [128, NC], [1, 128]])), inc=s_st, incval=16)
        P.barrier(("tensor", "vector", "sync"))

    for b in range(B_LOC):
        span_xnt = contextlib.ExitStack()
        open_spans.append(span_xnt)
        span_qkv = contextlib.ExitStack()
        open_spans.append(span_qkv)
        span_at = contextlib.ExitStack()
        open_spans.append(span_at)
        XNT = span_xnt.enter_context(nc.sbuf_tensor(f"XNT_{b}", [128, NC * S], BF16, side="right"))
        with contextlib.ExitStack() as ph:
            xtok = ph.enter_context(nc.sbuf_tensor(f"xtok_{b}", [128, NT * D], F32))
            xn32 = ph.enter_context(nc.sbuf_tensor(f"xn32_{b}", [128, D], F32))
            sqd = ph.enter_context(nc.sbuf_tensor(f"sqd_{b}", [128, 2 * D], F32))
            stg = ph.enter_context(nc.sbuf_tensor(f"stg_{b}", [128, D], F32))
            p_trf = ph.enter_context(nc.psum_tensor(f"p0_trf_{b}", [128, 512], F32))
            p_trf2 = ph.enter_context(nc.psum_tensor(f"p0_trf2_{b}", [128, 512], F32))

            P.emit("sync", lambda s, b=b: s.dma_start(
                xtok[:, :], A(x_in, b * S * D, [[D, 128], [128 * D, NT], [1, D]])),
                inc=s_lh, incval=16)
            P.barrier()
            sq_marks = {}
            for ti in range(NT):
                P.emit("vector", lambda v, ti=ti: v.reduce_sum(
                    st_a[:, ti:ti + 1], xtok[:, ti * D:(ti + 1) * D], axis=AXX), inc=s_v)
                if ti >= 2:
                    P.wait("scalar", s_a, sq_marks[ti - 2])
                P.emit("scalar", lambda a, ti=ti: a.activation(
                    sqd[:, (ti % 2) * D:(ti % 2 + 1) * D], xtok[:, ti * D:(ti + 1) * D], AF.Square,
                    accum_out=st_b[:, ti:ti + 1]), inc=s_a)
                sq_marks[ti] = s_a.n
            P.wait("vector", s_a)
            P.wait("vector", s_v)
            P.chain("vector", [
                lambda v: v.tensor_scalar(st_a[:, 0:NT], st_a[:, 0:NT], 1.0 / D, None, OP.mult),
                lambda v: v.tensor_scalar(st_b[:, 0:NT], st_b[:, 0:NT], 1.0 / D, None, OP.mult),
                lambda v: v.tensor_tensor(st_c[:, 0:NT], st_a[:, 0:NT], st_a[:, 0:NT], OP.mult),
                lambda v: v.tensor_tensor(st_b[:, 0:NT], st_b[:, 0:NT], st_c[:, 0:NT], OP.subtract),
                lambda v: v.tensor_scalar(st_b[:, 0:NT], st_b[:, 0:NT], LN_EPS, None, OP.add),
            ], s_v)
            P.wait("scalar", s_v)
            P.emit("scalar", lambda a: a.activation(st_c[:, 0:NT], st_b[:, 0:NT], AF.Sqrt), inc=s_a)
            P.wait("vector", s_a)
            P.chain("vector", [
                lambda v: v.reciprocal(st_b[:, 0:NT], st_c[:, 0:NT]),
                lambda v: v.tensor_tensor(st_c[:, 0:NT], st_c[:, 0:NT], st_b[:, 0:NT], OP.mult),
                lambda v: v.tensor_scalar(st_c[:, 0:NT], st_c[:, 0:NT], -1.0, 2.0, OP.mult, OP.add),
                lambda v: v.tensor_tensor(st_b[:, 0:NT], st_b[:, 0:NT], st_c[:, 0:NT], OP.mult),
            ], s_v)
            for ti in range(NT):
                P.wait("vector", s_v)
                P.emit("vector", lambda v, ti=ti: v.tensor_scalar(
                    xn32[:, :], xtok[:, ti * D:(ti + 1) * D], st_a[:, ti:ti + 1], st_b[:, ti:ti + 1],
                    OP.subtract, OP.mult), inc=s_v)
                P.wait("tensor", s_v)
                for r in range(2):
                    pb = p_trf if r == 0 else p_trf2
                    for c in range(4):
                        P.emit("tensor", lambda t, r=r, c=c, pb=pb: t.transpose(
                            pb[:, c * 128:(c + 1) * 128],
                            xn32[:, r * 512 + c * 128: r * 512 + (c + 1) * 128], ident_f[:, :]),
                            inc=(s_t if c == 3 else None))
                    P.wait("vector", s_t)
                    P.emit("vector", lambda v, r=r, pb=pb, stg=stg: v.tensor_copy(stg[:, r * 512:(r + 1) * 512], pb[:, :]))
                    P.emit("vector", lambda v, r=r, ti=ti, pb=pb, XNT=XNT: v.tensor_copy(
                        A(XNT, (4 * r) * S + ti * 128, [[NC * S, 128], [S, 4], [1, 128]]),
                        pb[:, :]), inc=s_v)
                P.wait("sync", s_v)
                P.emit("sync", lambda s, b=b, ti=ti, stg=stg: s.dma_start(
                    A(sc_xn32, b * D * S + ti * 128, [[S, 128], [128 * S, NC], [1, 128]]),
                    A(stg, 0, [[D, 128], [128, NC], [1, 128]])), inc=s_st, incval=16)
                P.barrier(("tensor", "vector", "sync"))
            P.emit("sync", lambda s, b=b, XNT=XNT: s.dma_start(
                A(sc_xnt, b * D * S, [[S, 128], [128 * S, NC], [1, S]]),
                A(XNT, 0, [[NC * S, 128], [S, NC], [1, S]])), inc=s_st, incval=16)
            P.wait("sync", s_st)
            P.barrier()

        if stage == "p0" and b == 0:
            tap("XNT", XNT, 128, NC * S, BF16)
            return finish_stage()

        if True:
            QT = span_qkv.enter_context(nc.sbuf_tensor(f"QT_{b}", [128, H * S], BF16))
            KT = span_qkv.enter_context(nc.sbuf_tensor(f"KT_{b}", [128, H * S], BF16))
            span_vt = contextlib.ExitStack()
            open_spans.append(span_vt)
            VTOK = span_vt.enter_context(nc.sbuf_tensor(f"VTOK_{b}", [128, NT * D], BF16, side="right"))
            qqh_t = span_qkv.enter_context(nc.sbuf_tensor(f"qqh_t_{b}", [128, NT * H * NQ], BF16))
            kqh_t = span_qkv.enter_context(nc.sbuf_tensor(f"kqh_t_{b}", [128, NT * H * NQ], BF16))
            qqT_h = span_qkv.enter_context(nc.sbuf_tensor(f"qqT_h_{b}", [NQ, S], BF16))
            kqT_h = span_qkv.enter_context(nc.sbuf_tensor(f"kqT_h_{b}", [NQ, S], BF16))
            with contextlib.ExitStack() as p1:
                wpro = p1.enter_context(nc.sbuf_tensor(f"wpro_{b}", [128, H * NC * 128], BF16))
                qq_t = p1.enter_context(nc.sbuf_tensor(f"qq_t_{b}", [128, NT * H * NQ], F32))
                kq_t = p1.enter_context(nc.sbuf_tensor(f"kq_t_{b}", [128, NT * H * NQ], F32))
                qn2 = p1.enter_context(nc.sbuf_tensor(f"qn2_{b}", [128, NT * H], F32))
                kn2 = p1.enter_context(nc.sbuf_tensor(f"kn2_{b}", [128, NT * H], F32))
                sq_s = p1.enter_context(nc.sbuf_tensor(f"sq_s_{b}", [128, H * NQ], F32))
                p_a = p1.enter_context(nc.psum_tensor(f"p1_a_{b}", [128, 512], F32))
                p_b = p1.enter_context(nc.psum_tensor(f"p1_b_{b}", [128, 512], F32))
                p_c = p1.enter_context(nc.psum_tensor(f"p1_c_{b}", [128, 512], F32))
                p_d = p1.enter_context(nc.psum_tensor(f"p1_d_{b}", [128, 512], F32))
                p_v = p1.enter_context(nc.psum_tensor(f"p1_v_{b}", [128, 1024], F32))
                p_q = p1.enter_context(nc.psum_tensor(f"p1_q_{b}", [128, H * NQ], F32))
                pbank = (p_a, p_b, p_c, p_d)

                for (wname, bcol, OUT, scale) in (("aWq", bq_c, QT, ISQHD), ("aWk", bk_c, KT, None)):
                    P.barrier(("tensor", "gpsimd", "vector"))
                    P.emit("gpsimd", lambda g, wname=wname: g.dma_start(
                        wpro[:, :], A(W[wname], 0, [[HD, 128], [D * HD, H], [128 * HD, NC], [1, 128]])),
                        inc=s_ld, incval=16)
                    P.wait("tensor", s_ld)
                    for h in range(H):
                        P.wait("tensor", s_v)
                        for tc in range(4):
                            for f in range(NC):
                                P.emit("tensor", lambda t, h=h, tc=tc, f=f: t.matmul(
                                    pbank[tc][:, :], wpro[:, (h * NC + f) * 128:(h * NC + f + 1) * 128],
                                    XNT[:, f * S + tc * 512: f * S + (tc + 1) * 512],
                                    start=(f == 0), stop=(f == NC - 1)),
                                    inc=(s_t if f == NC - 1 else None))
                        P.wait("vector", s_t)
                        for tc in range(4):
                            if scale is None:
                                P.emit("vector", lambda v, h=h, tc=tc, OUT=OUT, bcol=bcol: v.tensor_scalar(
                                    OUT[:, h * S + tc * 512: h * S + (tc + 1) * 512],
                                    pbank[tc][:, :], bcol[:, h:h + 1], None, OP.add),
                                    inc=(s_v if tc == 3 else None))
                            else:
                                P.emit("vector", lambda v, h=h, tc=tc, OUT=OUT, bcol=bcol, scale=scale: v.tensor_scalar(
                                    OUT[:, h * S + tc * 512: h * S + (tc + 1) * 512],
                                    pbank[tc][:, :], bcol[:, h:h + 1], scale, OP.add, OP.mult),
                                    inc=(s_v if tc == 3 else None))
                P.barrier(("tensor", "vector", "gpsimd"))
                for f in range(NC):
                    P.emit("gpsimd", lambda g, f=f: g.dma_start(
                        wpro[:, f * 1024:(f + 1) * 1024],
                        A(W["aWv"], f * 128 * HD, [[HD, 128], [D * HD, H], [1, 128]])),
                        inc=s_ld, incval=16)
                P.barrier(("tensor", "gpsimd", "vector"))
                for ti in range(NT):
                    P.wait("tensor", s_v)
                    for f in range(NC):
                        for hf in range(2):
                            P.emit("tensor", lambda t, ti=ti, f=f, hf=hf: t.matmul(
                                p_v[:, hf * 512:(hf + 1) * 512],
                                XNT[:, f * S + ti * 128: f * S + ti * 128 + 128],
                                wpro[:, f * 1024 + hf * 512: f * 1024 + (hf + 1) * 512],
                                start=(f == 0), stop=False))
                    for hf in range(2):
                        P.emit("tensor", lambda t, hf=hf: t.matmul(
                            p_v[:, hf * 512:(hf + 1) * 512], ones_r[0:1, :],
                            vb_r[0:1, hf * 512:(hf + 1) * 512], start=False, stop=True),
                            inc=(s_t if hf == 1 else None))
                    P.wait("vector", s_t)
                    P.emit("vector", lambda v, ti=ti, VTOK=VTOK: v.tensor_copy(
                        VTOK[:, ti * D:(ti + 1) * D], p_v[:, :]), inc=s_v)
                P.barrier(("tensor", "vector", "gpsimd"))
                for (QKT, bias_row, qq_buf) in ((QT, bc2q_rq, qq_t), (KT, bc2q_r, kq_t)):
                    for ti in range(NT):
                        P.wait("tensor", s_v)
                        for h in range(H):
                            P.emit("tensor", lambda t, ti=ti, h=h, QKT=QKT: t.matmul(
                                p_q[:, h * NQ:(h + 1) * NQ],
                                QKT[:, h * S + ti * 128: h * S + ti * 128 + 128],
                                wc2q_bf[:, h * NQ:(h + 1) * NQ], start=(h == 0), stop=False))
                        P.emit("tensor", lambda t, bias_row=bias_row: t.matmul(
                            p_q[:, :], ones_r[0:1, :], bias_row[0:1, :], start=False, stop=True), inc=s_t)
                        P.wait("vector", s_t)
                        P.emit("vector", lambda v, ti=ti, qq_buf=qq_buf: v.tensor_copy(
                            qq_buf[:, ti * H * NQ:(ti + 1) * H * NQ], p_q[:, :]), inc=s_v)
                P.barrier(("tensor", "vector"))
                for (qq_buf, n2_buf) in ((qq_t, qn2), (kq_t, kn2)):
                    for ti in range(NT):
                        P.wait("vector", s_v)
                        P.emit("vector", lambda v, ti=ti, qq_buf=qq_buf: v.tensor_tensor(
                            sq_s[:, :], qq_buf[:, ti * H * NQ:(ti + 1) * H * NQ],
                            qq_buf[:, ti * H * NQ:(ti + 1) * H * NQ], OP.mult), inc=s_v)
                        P.wait("vector", s_v)
                        P.emit("vector", lambda v, ti=ti, n2_buf=n2_buf: v.reduce_sum(
                            n2_buf[:, ti * H:(ti + 1) * H],
                            A(sq_s, 0, [[H * NQ, 128], [NQ, H], [1, NQ]]), axis=AXX), inc=s_v)
                for n2_buf in (qn2, kn2):
                    P.wait("scalar", s_v)
                    P.emit("scalar", lambda a, n2_buf=n2_buf: a.activation(
                        n2_buf[:, :], n2_buf[:, :], AF.Sqrt), inc=s_a)
                    P.wait("vector", s_a)
                    P.chain("vector", [
                        lambda v, n2_buf=n2_buf: v.tensor_scalar(n2_buf[:, :], n2_buf[:, :], 1e-12, None, OP.max),
                        lambda v, n2_buf=n2_buf: v.reciprocal(n2_buf[:, :], n2_buf[:, :]),
                    ], s_v)
                P.wait("vector", s_v)
                for (qq_buf, n2_buf, OUTT) in ((qq_t, qn2, qqh_t), (kq_t, kn2, kqh_t)):
                    for ti in range(NT):
                        for h in range(H):
                            P.emit("vector", lambda v, ti=ti, h=h, qq_buf=qq_buf, n2_buf=n2_buf, OUTT=OUTT: v.tensor_scalar(
                                OUTT[:, ti * H * NQ + h * NQ: ti * H * NQ + (h + 1) * NQ],
                                qq_buf[:, ti * H * NQ + h * NQ: ti * H * NQ + (h + 1) * NQ],
                                n2_buf[:, ti * H + h: ti * H + h + 1], None, OP.mult))
                P.emit("vector", lambda v: v.engine_nop(), inc=s_v)
                P.wait("sync", s_v)
                P.emit("sync", lambda s, b=b, VTOK=VTOK: s.dma_start(
                    A(sc_vtok, b * 128 * NT * D, [[NT * D, 128], [1, NT * D]]),
                    VTOK[:, :]), inc=s_st, incval=16)
                P.barrier()

            if stage == "p1" and b == 0:
                tap("QT", QT, 128, H * S, BF16)
                tap("KT", KT, 128, H * S, BF16)
                tap("VTOK", VTOK, 128, NT * D, BF16)
                tap("qqh_t", qqh_t, 128, NT * H * NQ, BF16)
                tap("kqh_t", kqh_t, 128, NT * H * NQ, BF16)
                return finish_stage()
        span_vt.close()
        span_xnt.close()

        attnT = span_at.enter_context(nc.sbuf_tensor(f"attnT_{b}", [128, H * S], BF16, side="right"))
        with contextlib.ExitStack() as p2:
            fidb = [p2.enter_context(nc.sbuf_tensor(f"fidb{par}_{b}", [128, S], F32)) for par in range(2)]
            sinb = [p2.enter_context(nc.sbuf_tensor(f"sinb{par}_{b}", [128, S], F32)) for par in range(2)]
            pbuf = [p2.enter_context(nc.sbuf_tensor(f"pbuf{par}_{b}", [128, S], BF16, side="right")) for par in range(4)]
            mcol = [p2.enter_context(nc.sbuf_tensor(f"mcol{par}_{b}", [128, 8], F32)) for par in range(4)]
            pTq = p2.enter_context(nc.sbuf_tensor(f"pTq_{b}", [128, NT * 256], BF16))
            vtk = [p2.enter_context(nc.sbuf_tensor(f"vtk{par}_{b}", [128, NT * 128], BF16)) for par in range(2)]
            p_da = p2.enter_context(nc.psum_tensor(f"p2_da_{b}", [128, 1024], F32))
            p_cw = [p2.enter_context(nc.psum_tensor(f"p2_cw{k}_{b}", [128, 512], F32)) for k in range(4)]
            p_pt = p2.enter_context(nc.psum_tensor(f"p2_pt_{b}", [128, 1024], BF16))
            p_pv = p2.enter_context(nc.psum_tensor(f"p2_pv_{b}", [128, 512], F32))

            mk = {"fh": [0, 0], "t": {}, "z": {}, "cw": {}, "sin": {}, "exp": {},
                  "esum": {}, "pn": {}, "ptd": {}, "fid": {}, "qkevac": 0,
                  "pvread": 0, "pvdone": 0}
            mk["pvh"] = [0, 0]

            def emit_pv(h, qp, hp, lvs, lv_mark):
                for qt in (2 * qp, 2 * qp + 1):
                    P.wait("tensor", s_ptb[qt % 4], mk["ptd"][h * NT + qt])
                P.wait("tensor", s_a, mk["pvdone"])
                P.wait("tensor", lvs, lv_mark)
                for kt in range(NT):
                    P.emit("tensor", lambda t, hp=hp, kt=kt: t.matmul(
                        p_pv[:, 0:256], vtk[hp][:, kt * 128:(kt + 1) * 128],
                        pTq[:, kt * 256:(kt + 1) * 256],
                        start=(kt == 0), stop=(kt == NT - 1)),
                        inc=(s_t if kt == NT - 1 else None))
                mk["pvread"] = s_t.n
                P.wait("scalar", s_t, mk["pvread"])
                P.emit("scalar", lambda a, h=h, qp=qp: a.activation(
                    attnT[:, h * S + qp * 256: h * S + (qp + 1) * 256],
                    p_pv[:, 0:256], AF.Copy), inc=s_a)
                mk["pvdone"] = s_a.n

            for h in range(H):
                hp = h % 2
                lvs = s_lv if hp == 0 else s_lv2
                P.wait("sync", s_t, mk["pvh"][hp])
                P.emit("sync", lambda s, b=b, h=h, hp=hp: s.dma_start(
                    A(vtk[hp], 0, [[NT * 128, 128], [128, NT], [1, 128]]),
                    A(sc_vtok, b * 128 * NT * D + h * 128, [[NT * D, 128], [D, NT], [1, 128]])),
                    inc=lvs, incval=16)
                lv_mark = lvs.n
                for (tok, TT) in ((qqh_t, qqT_h), (kqh_t, kqT_h)):
                    for batch in range(2):
                        P.wait("tensor", s_a, mk["qkevac"])
                        for j in range(8):
                            ti = batch * 8 + j
                            P.emit("tensor", lambda t, ti=ti, j=j, h=h, tok=tok: t.transpose(
                                p_pt[0:NQ, j * 128:(j + 1) * 128],
                                tok[:, ti * H * NQ + h * NQ: ti * H * NQ + (h + 1) * NQ],
                                ident_bf[:, :]), inc=(s_t if j == 7 else None))
                        P.wait("scalar", s_t)
                        P.emit("scalar", lambda a, batch=batch, TT=TT: a.activation(
                            TT[0:NQ, batch * 1024:(batch + 1) * 1024], p_pt[0:NQ, :],
                            AF.Copy), inc=s_a)
                        mk["qkevac"] = s_a.n
                P.wait("tensor", s_a, mk["qkevac"])
                for qp in range(NT // 2):
                    tiles = (2 * qp, 2 * qp + 1)
                    for qt in tiles:
                        par = qt % 2
                        tq = h * NT + qt
                        P.wait("vector", s_a, mk["exp"].get(tq - 2, 0))
                        for half in range(2):
                            P.wait("tensor", s_v, mk["fh"][1 - half])
                            for kc in (2 * half, 2 * half + 1):
                                P.emit("tensor", lambda t, h=h, qt=qt, kc=kc, half=half: t.matmul(
                                    p_da[:, (kc - 2 * half) * 512:(kc - 2 * half + 1) * 512],
                                    qqT_h[0:NQ, qt * 128:(qt + 1) * 128],
                                    kqT_h[0:NQ, kc * 512:(kc + 1) * 512], start=True, stop=True),
                                    inc=(s_t if kc % 2 == 1 else None))
                            dm = s_t.n
                            P.wait("vector", s_t, dm)
                            P.emit("vector", lambda v, par=par, half=half: v.tensor_scalar(
                                fidb[par][:, half * 1024:(half + 1) * 1024], p_da[:, :],
                                1.0, -1.0, OP.min, OP.max), inc=s_v)
                            mk["fh"][half] = s_v.n
                        mk["fid", tq] = s_v.n
                    for qt in tiles:
                        par = qt % 2
                        tq = h * NT + qt
                        P.wait("scalar", s_v, max(mk["fid", tq], mk["t"].get(tq - 2, 0)))
                        P.emit("scalar", lambda a, par=par: a.activation(
                            sinb[par][:, :], fidb[par][:, :], AF.Sin, scale=PI_LO), inc=s_a)
                        mk["sin"][tq] = s_a.n
                    if qp > 0:
                        emit_pv(h, qp - 1, hp, lvs, lv_mark)
                    for qt in tiles:
                        par = qt % 2
                        tq = h * NT + qt
                        for kc in range(4):
                            P.wait("tensor", s_v, mk["z"].get((kc, tq - 1), 0))
                            P.emit("tensor", lambda t, h=h, qt=qt, kc=kc: t.matmul(
                                p_cw[kc][:, :], QT[:, h * S + qt * 128: h * S + qt * 128 + 128],
                                KT[:, h * S + kc * 512: h * S + (kc + 1) * 512],
                                start=True, stop=True), inc=s_t)
                            mk["cw"][kc, tq] = s_t.n
                        P.wait("vector", s_a, mk["sin"][tq])
                        P.chain("vector", [
                            lambda v, par=par: v.scalar_tensor_tensor(
                                sinb[par][:, :], sinb[par][:, :], 0.1, fidb[par][:, :],
                                OP.mult, OP.mult),
                            lambda v, par=par: v.tensor_tensor(
                                fidb[par][:, :], fidb[par][:, :], sinb[par][:, :], OP.add),
                        ], s_v)
                        mk["t"][tq] = s_v.n
                        P.wait("vector", s_v, mk["t"][tq])
                        for kc in range(4):
                            P.wait("vector", s_t, mk["cw"][kc, tq])
                            P.emit("vector", lambda v, par=par, kc=kc: v.tensor_tensor(
                                fidb[par][:, kc * 512:(kc + 1) * 512],
                                fidb[par][:, kc * 512:(kc + 1) * 512],
                                p_cw[kc][:, :], OP.add), inc=s_v)
                            mk["z"][kc, tq] = s_v.n
                    for qt in tiles:
                        par, p4 = qt % 2, qt % 4
                        tq = h * NT + qt
                        P.wait("scalar", s_ptb[tq % 4], mk["ptd"].get(tq - 4, 0))
                        P.wait("scalar", s_v, mk["z"][3, tq])
                        P.emit("scalar", lambda a, par=par, p4=p4: a.activation(
                            pbuf[p4][:, :], fidb[par][:, :], AF.Exp, scale=0.5,
                            accum_out=mcol[p4][:, 4:5]), inc=s_a)
                        mk["exp"][tq] = s_a.n
                    for qt in tiles:
                        p4 = qt % 4
                        tq = h * NT + qt
                        P.wait("vector", s_a, mk["exp"][tq])
                        P.emit("vector", lambda v, p4=p4: v.reciprocal(
                            mcol[p4][:, 5:6], mcol[p4][:, 4:5]), inc=s_v)
                        mk["esum"][tq] = s_v.n
                    for qt in tiles:
                        p4, qi = qt % 4, qt % 2
                        tq = h * NT + qt
                        P.wait("scalar", s_v, mk["esum"][tq])
                        P.emit("scalar", lambda a, p4=p4: a.activation(
                            pbuf[p4][:, :], pbuf[p4][:, :], AF.Copy,
                            scale=mcol[p4][:, 5:6]), inc=s_a)
                        mk["pn"][tq] = s_a.n
                        P.wait("sync", s_a, mk["pn"][tq])
                        P.wait("sync", s_t, mk["pvread"])
                        P.emit("sync", lambda s, p4=p4, qi=qi: s.dma_start_transpose(
                            A(pTq, qi * 128, [[NT * 256, 128], [256, NT], [1, 128]]),
                            A(pbuf[p4], 0, [[S, 128], [128, NT], [1, 128]])),
                            inc=s_ptb[tq % 4], incval=16)
                        mk["ptd"][tq] = s_ptb[tq % 4].n
                emit_pv(h, NT // 2 - 1, hp, lvs, lv_mark)
                mk["pvh"][hp] = mk["pvread"]
            P.barrier()
        span_qkv.close()

        if stage == "p2" and b == 0:
            tap("attnT", attnT, 128, H * S, BF16)
            return finish_stage()

        span_x1 = contextlib.ExitStack()
        open_spans.append(span_x1)
        X1IN = span_x1.enter_context(nc.sbuf_tensor(f"X1IN_{b}", [128, NC * S], F32))
        with contextlib.ExitStack() as p3:
            XN32 = p3.enter_context(nc.sbuf_tensor(f"XN32_{b}", [128, NC * S], F32))
            wo_sb = p3.enter_context(nc.sbuf_tensor(f"wo_sb_{b}", [128, NC * 1024], BF16))
            p_w1 = p3.enter_context(nc.psum_tensor(f"p3_w1_{b}", [128, 512], F32))
            p_w2 = p3.enter_context(nc.psum_tensor(f"p3_w2_{b}", [128, 512], F32))
            P.emit("sync", lambda s, b=b, XN32=XN32: s.dma_start(
                A(XN32, 0, [[NC * S, 128], [S, NC], [1, S]]),
                A(sc_xn32, b * D * S, [[S, 128], [128 * S, NC], [1, S]])), inc=s_lh, incval=16)
            P.emit("gpsimd", lambda g, wo_sb=wo_sb: g.dma_start(
                wo_sb[:, :], A(W["aWo"], 0, [[D, 128], [128 * D, NC], [1, D]])), inc=s_ld, incval=16)
            P.barrier()
            for dc in range(NC):
                for tc in range(4):
                    pw = p_w1 if (dc * 4 + tc) % 2 == 0 else p_w2
                    P.wait("tensor", s_v)
                    for f in range(NC):
                        P.emit("tensor", lambda t, dc=dc, tc=tc, f=f, pw=pw: t.matmul(
                            pw[:, :], wo_sb[:, f * 1024 + dc * 128: f * 1024 + (dc + 1) * 128],
                            attnT[:, f * S + tc * 512: f * S + (tc + 1) * 512],
                            start=(f == 0), stop=(f == NC - 1)), inc=(s_t if f == NC - 1 else None))
                    P.wait("vector", s_t)
                    P.emit("vector", lambda v, dc=dc, tc=tc, pw=pw, X1IN=X1IN, XN32=XN32: v.scalar_tensor_tensor(
                        X1IN[:, dc * S + tc * 512: dc * S + (tc + 1) * 512], pw[:, :],
                        bo_c[:, dc:dc + 1], XN32[:, dc * S + tc * 512: dc * S + (tc + 1) * 512],
                        OP.add, OP.add), inc=s_v)
            P.barrier()
        span_at.close()
        with contextlib.ExitStack() as p3b:
            p_st = p3b.enter_context(nc.psum_tensor(f"p3_st_{b}", [1, 1024], F32))
            p_bc = p3b.enter_context(nc.psum_tensor(f"p3_bc_{b}", [128, 1024], F32))
            sqt = p3b.enter_context(nc.sbuf_tensor(f"sqt_{b}", [128, 1024], F32))
            rows = tuple(p3b.enter_context(nc.sbuf_tensor(f"row3__{b}" + ch, [1, S], F32)) for ch in "abc")
            ln_fm(X1IN, g1_c, be1_c, p_st, p_bc, sqt, rows)
            P.barrier()

        if stage == "p3" and b == 0:
            tap("X1IN", X1IN, 128, NC * S, F32)
            return finish_stage()

        span_x1t = contextlib.ExitStack()
        open_spans.append(span_x1t)
        X1T = span_x1t.enter_context(nc.sbuf_tensor(f"X1T_{b}", [128, NC * S], BF16, side="right"))
        with contextlib.ExitStack() as p4:
            qm_s = p4.enter_context(nc.sbuf_tensor(f"qm_s_{b}", [NQ, S], F32))
            th_s = p4.enter_context(nc.sbuf_tensor(f"th_s_{b}", [NQ, S], F32))
            qe_b = p4.enter_context(nc.sbuf_tensor(f"qe_b_{b}", [NQ, S], BF16))
            p_qm = p4.enter_context(nc.psum_tensor(f"p4_qm_{b}", [NQ, 512], F32))
            p_f1 = p4.enter_context(nc.psum_tensor(f"p4_f1_{b}", [128, 512], F32))
            p_f2 = p4.enter_context(nc.psum_tensor(f"p4_f2_{b}", [128, 512], F32))
            for c in range(NC):
                P.emit("vector", lambda v, c=c, X1T=X1T, X1IN=X1IN: v.tensor_copy(
                    X1T[:, c * S:(c + 1) * S], X1IN[:, c * S:(c + 1) * S]), inc=s_v)
            P.barrier(("tensor", "vector"))
            for tc in range(4):
                for f in range(NC):
                    P.emit("tensor", lambda t, tc=tc, f=f, X1T=X1T: t.matmul(
                        p_qm[0:NQ, :], m1_bf[:, f * NQ:(f + 1) * NQ],
                        X1T[:, f * S + tc * 512: f * S + (tc + 1) * 512],
                        start=(f == 0), stop=(f == NC - 1)), inc=(s_t if f == NC - 1 else None))
                P.wait("vector", s_t)
                P.emit("vector", lambda v, tc=tc, qm_s=qm_s: v.tensor_scalar(
                    qm_s[0:NQ, tc * 512:(tc + 1) * 512], p_qm[0:NQ, :], bias1_c[0:NQ, 0:1], None, OP.add), inc=s_v)
                P.wait("tensor", s_v)
            P.wait("scalar", s_v)
            P.emit("scalar", lambda a, th_s=th_s, qm_s=qm_s: a.activation(th_s[0:NQ, :], qm_s[0:NQ, :], AF.Tanh), inc=s_a)
            P.wait("vector", s_a)
            P.wait("vector", s_v)
            P.emit("vector", lambda v, qm_s=qm_s: v.tensor_scalar(
                qm_s[0:NQ, :], qm_s[0:NQ, :], 8.0, -8.0, OP.min, OP.max), inc=s_v)
            P.wait("scalar", s_v)
            P.emit("scalar", lambda a, qm_s=qm_s: a.activation(qm_s[0:NQ, :], qm_s[0:NQ, :], AF.Sin, scale=PI8_LO), inc=s_a)
            P.wait("vector", s_a)
            P.chain("vector", [
                lambda v, qm_s=qm_s: v.tensor_tensor(qm_s[0:NQ, :], qm_s[0:NQ, :], qm_s[0:NQ, :], OP.mult),
                lambda v, qm_s=qm_s: v.tensor_scalar(qm_s[0:NQ, :], qm_s[0:NQ, :], -2.0, 1.0, OP.mult, OP.add),
                lambda v, qe_b=qe_b, th_s=th_s, qm_s=qm_s: v.tensor_tensor(
                    qe_b[0:NQ, :], th_s[0:NQ, :], qm_s[0:NQ, :], OP.mult),
            ], s_v)
            P.wait("tensor", s_v)
            for dc in range(NC):
                for tc in range(4):
                    pw = p_f1 if (dc * 4 + tc) % 2 == 0 else p_f2
                    P.emit("tensor", lambda t, dc=dc, tc=tc, pw=pw, qe_b=qe_b: t.matmul(
                        pw[:, :], m2_bf[0:NQ, dc * 128:(dc + 1) * 128],
                        qe_b[0:NQ, tc * 512:(tc + 1) * 512], start=True, stop=True), inc=s_t)
                    P.wait("vector", s_t)
                    P.emit("vector", lambda v, dc=dc, tc=tc, pw=pw, X1IN=X1IN: v.scalar_tensor_tensor(
                        X1IN[:, dc * S + tc * 512: dc * S + (tc + 1) * 512], pw[:, :],
                        bias2_c[:, dc:dc + 1], X1IN[:, dc * S + tc * 512: dc * S + (tc + 1) * 512],
                        OP.add, OP.add), inc=s_v)
                    P.wait("tensor", s_v)
            P.barrier(("tensor", "vector", "scalar"))
        span_x1t.close()
        with contextlib.ExitStack() as p4b:
            p_st = p4b.enter_context(nc.psum_tensor(f"p4_st_{b}", [1, 1024], F32))
            p_bc = p4b.enter_context(nc.psum_tensor(f"p4_bc_{b}", [128, 1024], F32))
            sqt = p4b.enter_context(nc.sbuf_tensor(f"sqt2_{b}", [128, 1024], F32))
            rows = tuple(p4b.enter_context(nc.sbuf_tensor(f"row4__{b}" + ch, [1, S], F32)) for ch in "abc")
            ln_fm(X1IN, g2_c, be2_c, p_st, p_bc, sqt, rows)
            P.barrier()

        if stage == "p4" and b == 0:
            tap("X1IN", X1IN, 128, NC * S, F32)
            return finish_stage()

        with contextlib.ExitStack() as p5:
            p_t1 = p5.enter_context(nc.psum_tensor(f"p5_t1_{b}", [128, 512], F32))
            p_t2 = p5.enter_context(nc.psum_tensor(f"p5_t2_{b}", [128, 512], F32))
            stg = p5.enter_context(nc.sbuf_tensor(f"stg5_{b}", [128, D], F32))
            spill_tok(X1IN, sc_aout, b * S * D, p_t1, p_t2, stg)
        span_x1.close()

        span_qf = contextlib.ExitStack()
        open_spans.append(span_qf)
        span_f = contextlib.ExitStack()
        open_spans.append(span_f)
        QfT = span_qf.enter_context(nc.sbuf_tensor(f"QfT_{b}", [128, NC * S], BF16))
        KfT = span_qf.enter_context(nc.sbuf_tensor(f"KfT_{b}", [128, NC * S], BF16))
        VfTOK = span_qf.enter_context(nc.sbuf_tensor(f"VfTOK_{b}", [128, NT * D], BF16))
        span_xnt2 = contextlib.ExitStack()
        open_spans.append(span_xnt2)
        XNT2 = span_xnt2.enter_context(nc.sbuf_tensor(f"XNT2_{b}", [128, NC * S], BF16, side="right"))
        with contextlib.ExitStack() as p6:
            wprob = [p6.enter_context(nc.sbuf_tensor(f"wpro6{i}_{b}", [128, NC * 1024], BF16)) for i in range(2)]
            p_a = p6.enter_context(nc.psum_tensor(f"p6_a_{b}", [128, 512], F32))
            p_b = p6.enter_context(nc.psum_tensor(f"p6_b_{b}", [128, 512], F32))
            p_v = p6.enter_context(nc.psum_tensor(f"p6_v_{b}", [128, 1024], F32))
            P.emit("sync", lambda s, b=b, XNT2=XNT2: s.dma_start(
                A(XNT2, 0, [[NC * S, 128], [S, NC], [1, S]]),
                A(sc_xnt, b * D * S, [[S, 128], [128 * S, NC], [1, S]])), inc=s_lh, incval=16)
            P.barrier()
            ld_sems = (s_ld, s_ld2, s_ld)
            ld_w = {}
            rd_w = {}
            for wi, wname in enumerate(("fWq", "fWk")):
                P.emit("gpsimd", lambda g, wname=wname, wpro=wprob[wi % 2]: g.dma_start(
                    wpro[:, :], A(W[wname], 0, [[D, 128], [128 * D, NC], [1, D]])),
                    inc=ld_sems[wi], incval=16)
                ld_w[wi] = ld_sems[wi].n
            for wi, (bcol, OUT, scale) in enumerate(((fbq_c, QfT, INV32), (fbk_c, KfT, None))):
                wpro = wprob[wi % 2]
                P.wait("tensor", ld_sems[wi], ld_w[wi])
                for dc in range(NC):
                    for tc in range(4):
                        pw = p_a if (dc * 4 + tc) % 2 == 0 else p_b
                        P.wait("tensor", s_v)
                        for f in range(NC):
                            P.emit("tensor", lambda t, dc=dc, tc=tc, f=f, pw=pw, wpro=wpro, XNT2=XNT2: t.matmul(
                                pw[:, :], wpro[:, f * 1024 + dc * 128: f * 1024 + (dc + 1) * 128],
                                XNT2[:, f * S + tc * 512: f * S + (tc + 1) * 512],
                                start=(f == 0), stop=(f == NC - 1)), inc=(s_t if f == NC - 1 else None))
                        P.wait("vector", s_t)
                        if scale is None:
                            P.emit("vector", lambda v, dc=dc, tc=tc, pw=pw, OUT=OUT, bcol=bcol: v.tensor_scalar(
                                OUT[:, dc * S + tc * 512: dc * S + (tc + 1) * 512],
                                pw[:, :], bcol[:, dc:dc + 1], None, OP.add), inc=s_v)
                        else:
                            P.emit("vector", lambda v, dc=dc, tc=tc, pw=pw, OUT=OUT, bcol=bcol, scale=scale: v.tensor_scalar(
                                OUT[:, dc * S + tc * 512: dc * S + (tc + 1) * 512],
                                pw[:, :], bcol[:, dc:dc + 1], scale, OP.add, OP.mult), inc=s_v)
                rd_w[wi] = s_t.n
                if wi == 0:
                    P.wait("gpsimd", s_t, rd_w[0])
                    P.emit("gpsimd", lambda g: g.dma_start(
                        wprob[0][:, :], A(W["fWv"], 0, [[D, 128], [128 * D, NC], [1, D]])),
                        inc=s_ld, incval=16)
                    ld_w[2] = s_ld.n
            wpro = wprob[0]
            P.wait("tensor", s_ld, ld_w[2])
            for ti in range(NT):
                P.wait("tensor", s_v)
                for f in range(NC):
                    for hf in range(2):
                        P.emit("tensor", lambda t, ti=ti, f=f, hf=hf, wpro=wpro, XNT2=XNT2: t.matmul(
                            p_v[:, hf * 512:(hf + 1) * 512],
                            XNT2[:, f * S + ti * 128: f * S + ti * 128 + 128],
                            wpro[:, f * 1024 + hf * 512: f * 1024 + (hf + 1) * 512],
                            start=(f == 0), stop=False))
                for hf in range(2):
                    P.emit("tensor", lambda t, hf=hf: t.matmul(
                        p_v[:, hf * 512:(hf + 1) * 512], ones_r[32:33, :],
                        vb_r[32:33, hf * 512:(hf + 1) * 512], start=False, stop=True),
                        inc=(s_t if hf == 1 else None))
                P.wait("vector", s_t)
                P.emit("vector", lambda v, ti=ti, VfTOK=VfTOK: v.tensor_copy(
                    VfTOK[:, ti * D:(ti + 1) * D], p_v[:, :]), inc=s_v)
            P.barrier()

        span_xnt2.close()
        afT = span_f.enter_context(nc.sbuf_tensor(f"afT_{b}", [128, NC * S], BF16, side="right"))
        with contextlib.ExitStack() as p6b:
            pbuf6 = [p6b.enter_context(nc.sbuf_tensor(f"pbuf6{i}_{b}", [128, S], BF16)) for i in range(2)]
            pTq = p6b.enter_context(nc.sbuf_tensor(f"pTq6_{b}", [128, NT * 512], BF16))
            mcol = p6b.enter_context(nc.sbuf_tensor(f"mcol6_{b}", [128, 8], F32))
            p_s2 = p6b.enter_context(nc.psum_tensor(f"p6_s2_{b}", [128, 2048], F32))
            p_pv1 = p6b.enter_context(nc.psum_tensor(f"p6_pv1_{b}", [128, 512], F32))
            p_pv2 = p6b.enter_context(nc.psum_tensor(f"p6_pv2_{b}", [128, 512], F32))
            mk6 = {"exp": {}, "esum": {}, "pn": {}, "ptd": {}}
            pv_read = [0]
            for qt in range(NT):
                quad, qi = qt // 4, qt % 4
                pb = pbuf6[qt % 2]
                mc = (qt % 2) * 4
                P.wait("tensor", s_a, mk6["exp"].get(qt - 1, 0))
                for kc in range(4):
                    for f in range(NC):
                        P.emit("tensor", lambda t, qt=qt, kc=kc, f=f, QfT=QfT, KfT=KfT: t.matmul(
                            p_s2[:, kc * 512:(kc + 1) * 512],
                            QfT[:, f * S + qt * 128: f * S + qt * 128 + 128],
                            KfT[:, f * S + kc * 512: f * S + (kc + 1) * 512],
                            start=(f == 0), stop=(f == NC - 1)),
                            inc=(s_t if (kc == 3 and f == NC - 1) else None))
                sc_mark = s_t.n
                P.wait("scalar", s_t, sc_mark)
                P.wait("scalar", s_ptb[qt % 2], mk6["ptd"].get(qt - 2, 0))
                P.emit("scalar", lambda a, pb=pb, mc=mc: a.activation(
                    pb[:, :], p_s2[:, :], AF.Exp, scale=1.0,
                    accum_out=mcol[:, mc:mc + 1]), inc=s_a)
                mk6["exp"][qt] = s_a.n
                P.wait("vector", s_a, mk6["exp"][qt])
                P.emit("vector", lambda v, mc=mc: v.reciprocal(
                    mcol[:, mc + 1:mc + 2], mcol[:, mc:mc + 1]), inc=s_v)
                mk6["esum"][qt] = s_v.n
                P.wait("scalar", s_v, mk6["esum"][qt])
                P.emit("scalar", lambda a, pb=pb, mc=mc: a.activation(
                    pb[:, :], pb[:, :], AF.Copy, scale=mcol[:, mc + 1:mc + 2]), inc=s_a)
                mk6["pn"][qt] = s_a.n
                P.wait("sync", s_a, mk6["pn"][qt])
                P.wait("sync", s_t, pv_read[0])
                P.emit("sync", lambda s, pb=pb, qi=qi, pTq=pTq: s.dma_start_transpose(
                    A(pTq, qi * 128, [[NT * 512, 128], [512, NT], [1, 128]]),
                    A(pb, 0, [[S, 128], [128, NT], [1, 128]])), inc=s_ptb[qt % 2], incval=16)
                mk6["ptd"][qt] = s_ptb[qt % 2].n
                if qi == 3:
                    P.wait("tensor", s_ptb[0], mk6["ptd"][qt - 1])
                    P.wait("tensor", s_ptb[1], mk6["ptd"][qt])
                    for dc in range(NC):
                        pv = p_pv1 if dc % 2 == 0 else p_pv2
                        if dc >= 2:
                            P.wait("tensor", s_v)
                        for kt in range(NT):
                            P.emit("tensor", lambda t, dc=dc, kt=kt, pv=pv, VfTOK=VfTOK, pTq=pTq: t.matmul(
                                pv[:, :], VfTOK[:, kt * D + dc * 128: kt * D + (dc + 1) * 128],
                                pTq[:, kt * 512:(kt + 1) * 512],
                                start=(kt == 0), stop=(kt == NT - 1)), inc=(s_t if kt == NT - 1 else None))
                        P.wait("vector", s_t)
                        P.emit("vector", lambda v, dc=dc, quad=quad, pv=pv, afT=afT: v.tensor_copy(
                            afT[:, dc * S + quad * 512: dc * S + (quad + 1) * 512], pv[:, :]), inc=s_v)
                    pv_read[0] = s_t.n
                    P.wait("tensor", s_v)
            P.barrier()
        span_qf.close()

        with contextlib.ExitStack() as p6c:
            XN32 = p6c.enter_context(nc.sbuf_tensor(f"XN32b_{b}", [128, NC * S], F32))
            XFT = p6c.enter_context(nc.sbuf_tensor(f"XFT_{b}", [128, NC * S], BF16))
            wo2 = p6c.enter_context(nc.sbuf_tensor(f"wo2_{b}", [128, NC * 1024], BF16))
            xfrow = p6c.enter_context(nc.sbuf_tensor(f"xfrow_{b}", [128, S], F32))
            g_s = p6c.enter_context(nc.sbuf_tensor(f"g_s_{b}", [E, S], F32))
            gt_s = p6c.enter_context(nc.sbuf_tensor(f"gt_s_{b}", [128, E], F32))
            gcol = p6c.enter_context(nc.sbuf_tensor(f"gcol_{b}", [128, 4], F32))
            stg = p6c.enter_context(nc.sbuf_tensor(f"stg6_{b}", [128, D], F32))
            p_w1 = p6c.enter_context(nc.psum_tensor(f"p6_w1_{b}", [128, 512], F32))
            p_w2 = p6c.enter_context(nc.psum_tensor(f"p6_w2_{b}", [128, 512], F32))
            p_g = p6c.enter_context(nc.psum_tensor(f"p6_g_{b}", [E, 512], F32))
            p_gt = p6c.enter_context(nc.psum_tensor(f"p6_gt_{b}", [128, E], F32))
            p_t1 = p6c.enter_context(nc.psum_tensor(f"p6_t1_{b}", [128, 512], F32))
            p_t2 = p6c.enter_context(nc.psum_tensor(f"p6_t2_{b}", [128, 512], F32))
            P.emit("sync", lambda s, b=b, XN32=XN32: s.dma_start(
                A(XN32, 0, [[NC * S, 128], [S, NC], [1, S]]),
                A(sc_xn32, b * D * S, [[S, 128], [128 * S, NC], [1, S]])), inc=s_lh, incval=16)
            P.emit("gpsimd", lambda g, wo2=wo2: g.dma_start(
                wo2[:, :], A(W["fWo"], 0, [[D, 128], [128 * D, NC], [1, D]])), inc=s_ld, incval=16)
            P.barrier()
            stg_tok = p6c.enter_context(nc.sbuf_tensor(f"stg_tok_{b}", [128, S], F32))
            for dc in range(NC):
                for tc in range(4):
                    pw = p_w1 if tc % 2 == 0 else p_w2
                    for f in range(NC):
                        P.emit("tensor", lambda t, dc=dc, tc=tc, f=f, pw=pw, wo2=wo2, afT=afT: t.matmul(
                            pw[:, :], wo2[:, f * 1024 + dc * 128: f * 1024 + (dc + 1) * 128],
                            afT[:, f * S + tc * 512: f * S + (tc + 1) * 512],
                            start=(f == 0), stop=(f == NC - 1)), inc=(s_t if f == NC - 1 else None))
                    P.wait("vector", s_t)
                    P.chain("vector", [
                        lambda v, dc=dc, tc=tc, pw=pw, XN32=XN32, xfrow=xfrow: v.scalar_tensor_tensor(
                            xfrow[:, tc * 512:(tc + 1) * 512], pw[:, :], fbo_c[:, dc:dc + 1],
                            XN32[:, dc * S + tc * 512: dc * S + (tc + 1) * 512], OP.add, OP.add),
                        lambda v, dc=dc, tc=tc, xfrow=xfrow, XFT=XFT: v.tensor_copy(
                            XFT[:, dc * S + tc * 512: dc * S + (tc + 1) * 512],
                            xfrow[:, tc * 512:(tc + 1) * 512]),
                    ], s_v)
                    P.wait("tensor", s_v)
                P.wait("tensor", s_st, s_st.n)
                for g4 in range(4):
                    pt = p_t1 if g4 % 2 == 0 else p_t2
                    for j in range(4):
                        P.emit("tensor", lambda t, g4=g4, j=j, pt=pt, xfrow=xfrow: t.transpose(
                            pt[:, j * 128:(j + 1) * 128],
                            xfrow[:, (g4 * 4 + j) * 128:(g4 * 4 + j + 1) * 128], ident_f[:, :]),
                            inc=(s_t if j == 3 else None))
                    P.wait("vector", s_t)
                    P.emit("vector", lambda v, g4=g4, pt=pt, stg_tok=stg_tok: v.tensor_copy(
                        stg_tok[:, g4 * 512:(g4 + 1) * 512], pt[:, :]), inc=s_v)
                    P.wait("tensor", s_v)
                P.wait("sync", s_v)
                P.emit("sync", lambda s, b=b, dc=dc, stg_tok=stg_tok: s.dma_start(
                    A(sc_xf32, b * S * D + dc * 128, [[D, 128], [128 * D, NT], [1, 128]]),
                    A(stg_tok, 0, [[S, 128], [128, NT], [1, 128]])), inc=s_st, incval=16)
                P.barrier(("tensor", "vector", "sync"))
            P.emit("sync", lambda s, b=b, XFT=XFT: s.dma_start(
                A(sc_xft, b * D * S, [[S, 128], [128 * S, NC], [1, S]]),
                A(XFT, 0, [[NC * S, 128], [S, NC], [1, S]])), inc=s_st, incval=16)
            for tc in range(4):
                for f in range(NC):
                    P.emit("tensor", lambda t, tc=tc, f=f, XFT=XFT: t.matmul(
                        p_g[0:E, :], wg_bf[:, f * E:(f + 1) * E],
                        XFT[:, f * S + tc * 512: f * S + (tc + 1) * 512],
                        start=(f == 0), stop=(f == NC - 1)), inc=(s_t if f == NC - 1 else None))
                P.wait("vector", s_t)
                P.emit("vector", lambda v, tc=tc, g_s=g_s: v.tensor_scalar(
                    g_s[0:E, tc * 512:(tc + 1) * 512], p_g[0:E, :], fbg_c[0:E, 0:1], None, OP.add), inc=s_v)
                P.wait("tensor", s_v)
            for ti in range(NT):
                P.emit("tensor", lambda t, ti=ti, g_s=g_s: t.transpose(
                    p_gt[:, 0:E], g_s[0:E, ti * 128:(ti + 1) * 128], ident_f[0:E, 0:E]), inc=s_t)
                P.wait("vector", s_t)
                P.chain("vector", [
                    lambda v, gt_s=gt_s: v.tensor_copy(gt_s[:, :], p_gt[:, 0:E]),
                    lambda v, gt_s=gt_s, gcol=gcol: v.reduce_max(gcol[:, 0:1], gt_s[:, :], axis=AXX),
                    lambda v, gcol=gcol: v.tensor_scalar(gcol[:, 1:2], gcol[:, 0:1], -1.0, None, OP.mult),
                ], s_v)
                P.wait("scalar", s_v)
                P.emit("scalar", lambda a, gt_s=gt_s, gcol=gcol: a.activation(
                    gt_s[:, :], gt_s[:, :], AF.Exp, bias=gcol[:, 1:2], scale=1.0,
                    accum_out=gcol[:, 2:3]), inc=s_a)
                P.wait("vector", s_a)
                P.chain("vector", [
                    lambda v, gcol=gcol: v.reciprocal(gcol[:, 3:4], gcol[:, 2:3]),
                    lambda v, b=b, ti=ti, gt_s=gt_s, gcol=gcol: v.tensor_scalar(
                        gates_t[:, (b * NT + ti) * E:(b * NT + ti + 1) * E], gt_s[:, :],
                        gcol[:, 3:4], None, OP.mult),
                ], s_v)
                P.wait("tensor", s_v)
            P.wait("sync", s_st)
            P.barrier()
        span_f.close()
        P.barrier()

        if stage == "p6" and b == 0:
            tap("gates_t", gates_t, 128, NT * E, F32)
            return finish_stage()

    with contextlib.ExitStack() as p7:
        acc = p7.enter_context(nc.sbuf_tensor("acc", [128, NT * D], F32))
        w1h = p7.enter_context(nc.sbuf_tensor("w1h", [128, NC * 2048], BF16))
        w2hb = [p7.enter_context(nc.sbuf_tensor(f"w2h{i}", [128, 16 * 1024], BF16)) for i in range(2)]
        h_sb = p7.enter_context(nc.sbuf_tensor("h_sb", [128, 16 * 512], BF16))
        xfs = p7.enter_context(nc.sbuf_tensor("xfs", [128, NC * 512], BF16))
        xfs2 = p7.enter_context(nc.sbuf_tensor("xfs2", [128, NC * 512], BF16))
        aob = p7.enter_context(nc.sbuf_tensor("aob", [128, D], F32))
        p_hb = [p7.enter_context(nc.psum_tensor(f"p7_h{i}", [128, 512], F32)) for i in range(4)]
        p_e1 = p7.enter_context(nc.psum_tensor("p7_e1", [128, 1024], F32))
        p_e2 = p7.enter_context(nc.psum_tensor("p7_e2", [128, 1024], F32))

        marks = {"w1q": [0, 0, 0, 0], "w2rd": [0, 0], "xfs": [0, 0],
                 "relu": [0, 0, 0, 0], "stt": [0, 0], "b2": 0}
        ld_w1q = [0, 0, 0, 0]
        ld_w2q = [0, 0, 0, 0]
        for tb in range(B_LOC):
            P.wait("sync", s_st)
            init_mk = [0] * NT
            for e in range(E):
                P.wait("sync", s_t, marks["b2"])
                P.emit("sync", lambda s, e=e, vb_r=vb_r: s.dma_start(
                    vb_r[64:65, :], A(W["fb2"], e * D, [[D, 1], [1, D]])), inc=s_lb, incval=16)
                lb_mark = s_lb.n
                for dh in range(2):
                    par2 = (tb * 16 + e * 2 + dh) % 2
                    w2h = w2hb[par2]
                    P.wait("gpsimd", s_ld, s_ld.n)
                    for q in range(4):
                        P.wait("gpsimd", s_t, marks["w1q"][q])
                        P.emit("gpsimd", lambda g, e=e, dh=dh, q=q, w1h=w1h: g.dma_start(
                            A(w1h, q * 512, [[NC * 2048, 128], [2048, NC], [1, 512]]),
                            A(W["fW1"], e * D * FF + dh * 2048 + q * 512,
                              [[FF, 128], [128 * FF, NC], [1, 512]])), inc=s_ld, incval=16)
                        ld_w1q[q] = s_ld.n
                    ld_w1 = s_ld.n
                    P.wait("gpsimd", s_t, marks["w2rd"][par2])
                    P.wait("gpsimd", s_ld, s_ld.n)
                    for q in range(4):
                        P.emit("gpsimd", lambda g, e=e, dh=dh, q=q, w2h=w2h: g.dma_start(
                            A(w2h, q * 4 * 1024, [[16 * 1024, 128], [1024, 4], [1, 1024]]),
                            A(W["fW2"], e * FF * D + (dh * 2048 + q * 4 * 128) * D,
                              [[D, 128], [128 * D, 4], [1, D]])), inc=s_ld, incval=16)
                    ld_w2 = s_ld.n
                    for tc in range(4):
                        xf_cur = xfs if tc % 2 == 0 else xfs2
                        lxs = s_lx if tc % 2 == 0 else s_lx2
                        P.wait("sync", s_t, marks["xfs"][tc % 2])
                        P.emit("sync", lambda s, tb=tb, tc=tc, xf_cur=xf_cur: s.dma_start(
                            A(xf_cur, 0, [[NC * 512, 128], [512, NC], [1, 512]]),
                            A(sc_xft, tb * D * S + tc * 512, [[S, 128], [128 * S, NC], [1, 512]])),
                            inc=lxs, incval=16)
                        P.wait("tensor", lxs, lxs.n)
                        if e == 0 and dh == 0:
                            P.wait("sync", s_xc, s_xc.n)
                            P.emit("sync", lambda s, tb=tb, tc=tc, acc=acc: s.dma_start(
                                A(acc, tc * 4 * D, [[NT * D, 128], [D, 4], [1, D]]),
                                A(x_in, tb * S * D + tc * 4 * 128 * D,
                                  [[D, 128], [128 * D, 4], [1, D]])), inc=s_xc, incval=16)
                            xc_mark = s_xc.n
                            for ti4 in range(4):
                                tix = tc * 4 + ti4
                                for src_t in (sc_aout, sc_xf32):
                                    P.wait("sync", s_v)
                                    P.emit("sync", lambda s, tb=tb, tix=tix, src_t=src_t: s.dma_start(
                                        aob[:, :], A(src_t, tb * S * D + tix * 128 * D,
                                                     [[D, 128], [1, D]])), inc=s_lh, incval=16)
                                    P.wait("vector", s_xc, xc_mark)
                                    P.wait("vector", s_lh)
                                    P.emit("vector", lambda v, tix=tix: v.tensor_tensor(
                                        acc[:, tix * D:(tix + 1) * D],
                                        acc[:, tix * D:(tix + 1) * D], aob[:, :], OP.add), inc=s_v)
                                init_mk[tix] = s_v.n
                        for hd in range(16):
                            ph_ = p_hb[hd % 4]
                            P.wait("tensor", s_a, marks["relu"][hd % 4])
                            if tc == 0 and hd == 0:
                                P.wait("tensor", s_ld, ld_w1)
                            for f in range(NC):
                                P.emit("tensor", lambda t, hd=hd, f=f, ph_=ph_, w1h=w1h, xf_cur=xf_cur: t.matmul(
                                    ph_[:, :], w1h[:, f * 2048 + hd * 128: f * 2048 + (hd + 1) * 128],
                                    xf_cur[:, f * 512:(f + 1) * 512],
                                    start=(f == 0), stop=(f == NC - 1)), inc=(s_t if f == NC - 1 else None))
                            if tc == 3 and hd % 4 == 3:
                                marks["w1q"][hd // 4] = s_t.n
                            if hd == 15:
                                marks["xfs"][tc % 2] = s_t.n
                            P.wait("scalar", s_t)
                            P.emit("scalar", lambda a, e=e, dh=dh, hd=hd, ph_=ph_, h_sb=h_sb: a.activation(
                                h_sb[:, hd * 512:(hd + 1) * 512], ph_[:, :], AF.Relu,
                                bias=b1_c[:, e * 32 + dh * 16 + hd: e * 32 + dh * 16 + hd + 1]), inc=s_a)
                            marks["relu"][hd % 4] = s_a.n
                        relu_all = s_a.n
                        P.wait("tensor", s_a, relu_all)
                        for ti in range(4):
                            pe = p_e1 if ti % 2 == 0 else p_e2
                            P.wait("tensor", s_v, marks["stt"][ti % 2])
                            if tc == 0 and ti == 0:
                                P.wait("tensor", s_ld, ld_w2)
                            if dh == 0 and tc == 0 and ti == 0:
                                P.wait("tensor", s_lb, lb_mark)
                            if e == 0 and dh == 0:
                                P.wait("vector", s_v, init_mk[tc * 4 + ti])
                            for hd in range(16):
                                for hf in range(2):
                                    stop = (dh == 1 and hd == 15)
                                    P.emit("tensor", lambda t, hd=hd, hf=hf, ti=ti, pe=pe, h_sb=h_sb, w2h=w2h, stop=stop: t.matmul(
                                        pe[:, hf * 512:(hf + 1) * 512],
                                        h_sb[:, hd * 512 + ti * 128: hd * 512 + (ti + 1) * 128],
                                        w2h[:, hd * 1024 + hf * 512: hd * 1024 + (hf + 1) * 512],
                                        start=(hd == 0), stop=stop),
                                        inc=(s_t if (dh == 1 and hd == 15 and hf == 1) else None))
                                if tc == 3 and ti == 3 and hd % 4 == 3:
                                    pass
                            if dh == 0:
                                for hf in range(2):
                                    P.emit("tensor", lambda t, e=e, hf=hf, pe=pe: t.matmul(
                                        pe[:, hf * 512:(hf + 1) * 512], ones_r[64:65, :],
                                        vb_r[64:65, hf * 512:(hf + 1) * 512],
                                        start=False, stop=True), inc=(s_t if hf == 1 else None))
                            if tc == 3 and ti == 3:
                                marks["w2rd"][par2] = s_t.n
                            if dh == 0 and tc == 3 and ti == 3:
                                marks["b2"] = s_t.n
                            eo_done = s_t.n
                            P.wait("vector", s_t, eo_done)
                            tix = tc * 4 + ti
                            P.emit("vector", lambda v, e=e, tb=tb, tix=tix, pe=pe, acc=acc: v.scalar_tensor_tensor(
                                acc[:, tix * D:(tix + 1) * D], pe[:, :],
                                gates_t[:, (tb * NT + tix) * E + e:(tb * NT + tix) * E + e + 1],
                                acc[:, tix * D:(tix + 1) * D], OP.mult, OP.add), inc=s_v)
                            marks["stt"][ti % 2] = s_v.n
            P.barrier(("tensor", "vector", "sync"))
            P.emit("sync", lambda s, tb=tb, acc=acc: s.dma_start(
                A(y_out, tb * S * D, [[D, 128], [128 * D, NT], [1, D]]),
                A(acc, 0, [[NT * D, 128], [D, NT], [1, D]])), inc=s_st, incval=16)
        P.wait("sync", s_st)
        P.barrier()

    stack.close()
    P.replay()
    P.close()
    return nc


def _shard_inputs(inputs):
    x = np.ascontiguousarray(inputs["L0_attn_x"], dtype=np.float32)
    ren = {
        "aWq": "L0_attn_Wq", "abq": "L0_attn_bq", "aWk": "L0_attn_Wk", "abk": "L0_attn_bk",
        "aWv": "L0_attn_Wv", "abv": "L0_attn_bv", "aWc2q": "L0_attn_Wc2q", "abc2q": "L0_attn_bc2q",
        "aWo": "L0_attn_Wo", "abo": "L0_attn_bo", "ag1": "L0_attn_g1", "abeta1": "L0_attn_beta1",
        "ag2": "L0_attn_g2", "abeta2": "L0_attn_beta2", "aW1": "L0_attn_W1", "ab1": "L0_attn_b1",
        "aWm": "L0_attn_Wm", "abm": "L0_attn_bm", "aWqo": "L0_attn_Wqo", "abqo": "L0_attn_bqo",
        "aW2": "L0_attn_W2", "ab2": "L0_attn_b2",
        "fWq": "L0_ffn_Wq", "fbq": "L0_ffn_bq", "fWk": "L0_ffn_Wk", "fbk": "L0_ffn_bk",
        "fWv": "L0_ffn_Wv", "fbv": "L0_ffn_bv", "fWo": "L0_ffn_Wo", "fbo": "L0_ffn_bo",
        "fWg": "L0_ffn_Wg", "fbg": "L0_ffn_bg", "fW1": "L0_ffn_W1", "fb1": "L0_ffn_b1",
        "fW2": "L0_ffn_W2", "fb2": "L0_ffn_b2",
    }
    w = {k: np.ascontiguousarray(inputs[src], dtype=np.float32) for k, src in ren.items()}
    in_maps = []
    for c in range(8):
        m = {"x": x[2 * c:2 * c + 2]}
        m.update(w)
        in_maps.append(m)
    return in_maps


def kernel(**inputs):
    in_maps = _shard_inputs(inputs)
    nc = build()
    res = run_bass_kernel_spmd(nc, in_maps, core_ids=list(range(8)))
    return np.concatenate([r["y"] for r in res.results], axis=0)
